# version 01
# baseline (speedup 1.0000x reference)
import numpy as np
import concourse.bass as bass
import concourse.mybir as mybir
from concourse.bass_utils import run_bass_kernel_spmd
from concourse.aot_env import allow_ici_all_to_all
from contextlib import ExitStack

F32 = mybir.dt.float32
BF16 = mybir.dt.bfloat16
I8 = mybir.dt.int8
AF = mybir.ActivationFunctionType
OP = mybir.AluOpType
AX = mybir.AxisListType

P = 128
T = 2048
NT = 16
D = 1024
DC = 8
HID = 4096
HC = 32
NH = 16
R = 64
B = 8
BS = 256
NEG = -1e30

ENGINES = ("sync", "vector", "scalar", "tensor", "gpsimd")
_NPHASES = 9
_DBG = None


class Ctr:
    def __init__(self, h):
        self.h = h
        self.n = 0


class Prog:
    def __init__(self):
        self.q = {e: [] for e in ENGINES}
        self.cV = None
        self.cS = None

    def wait(self, eng, ctr, val=None):
        v = ctr.n if val is None else val
        if v > 0:
            self.q[eng].append(lambda e: e.wait_ge(ctr.h, v))

    def dma(self, eng, out, in_, sem, **kw):
        self.q[eng].append(
            lambda e: e.dma_start(out=out, in_=in_, **kw).then_inc(sem.h, 16))
        sem.n += 16
        return sem.n

    def op(self, eng, fn, inc=None, k=1):
        if inc is None:
            self.q[eng].append(fn)
            return None
        self.q[eng].append(lambda e: fn(e).then_inc(inc.h, k))
        inc.n += k
        return inc.n

    def vop(self, fn):
        self.wait("vector", self.cV)
        return self.op("vector", fn, inc=self.cV)

    def sop(self, fn):
        self.wait("scalar", self.cS)
        return self.op("scalar", fn, inc=self.cS)

    def barrier(self, ctrs):
        vals = {}
        for e in ENGINES:
            c = ctrs[e]
            self.q[e].append(lambda en, c=c: en.sem_inc(c.h, 1))
            c.n += 1
            vals[e] = c.n
        for e in ENGINES:
            for o in ENGINES:
                if o != e:
                    c, v = ctrs[o], vals[o]
                    self.q[e].append(lambda en, c=c, v=v: en.wait_ge(c.h, v))

    def flush(self, block):
        qs = self.q

        @block.sync
        def _(eng):
            for fn in qs["sync"]:
                fn(eng)

        @block.vector
        def _(eng):
            for fn in qs["vector"]:
                fn(eng)

        @block.scalar
        def _(eng):
            for fn in qs["scalar"]:
                fn(eng)

        @block.tensor
        def _(eng):
            for fn in qs["tensor"]:
                fn(eng)

        @block.gpsimd
        def _(eng):
            for fn in qs["gpsimd"]:
                fn(eng)


def build(nphases=9, dbg=None):
    nc = bass.Bass("TRN2", target_bir_lowering=False,
                   detect_race_conditions=False)

    x0_d = nc.dram_tensor("x0", [T, D], F32, kind="ExternalInput").ap()
    basis_d = nc.dram_tensor("basis_w", [R, D], F32, kind="ExternalInput").ap()
    core_d = nc.dram_tensor("core_m", [R, R], F32, kind="ExternalInput").ap()
    hres_d = nc.dram_tensor("head_res", [NH, R, R], F32,
                            kind="ExternalInput").ap()
    vw_d = nc.dram_tensor("v_w", [D, D], F32, kind="ExternalInput").ap()
    vb_d = nc.dram_tensor("v_b", [D], F32, kind="ExternalInput").ap()
    ow_d = nc.dram_tensor("o_w", [D, D], F32, kind="ExternalInput").ap()
    ob_d = nc.dram_tensor("o_b", [D], F32, kind="ExternalInput").ap()
    wk_d = nc.dram_tensor("w_k", [D, D], F32, kind="ExternalInput").ap()
    wv_d = nc.dram_tensor("w_v", [D, D], F32, kind="ExternalInput").ap()
    wqd_d = nc.dram_tensor("w_q", [D, D], F32, kind="ExternalInput").ap()
    w1_d = nc.dram_tensor("w1", [HID, D], F32, kind="ExternalInput").ap()
    b1_d = nc.dram_tensor("b1", [HID], F32, kind="ExternalInput").ap()
    w2_d = nc.dram_tensor("w2", [D, HID], F32, kind="ExternalInput").ap()
    b2_d = nc.dram_tensor("b2", [D], F32, kind="ExternalInput").ap()
    out_d = nc.dram_tensor("out", [T, D], F32, kind="ExternalOutput").ap()

    x1_d = nc.dram_tensor("x1_s", [T, D], F32).ap()
    x2_d = nc.dram_tensor("x2_s", [T, D], F32).ap()
    x2t_d = nc.dram_tensor("x2t_s", [T, D], F32).ap()
    x3_d = nc.dram_tensor("x3_s", [T, D], F32).ap()
    wq1_d = nc.dram_tensor("wq1_s", [P, DC * HID], BF16).ap()
    wq2_d = nc.dram_tensor("wq2_s", [P, HC * D], BF16).ap()

    pr = Prog()

    with ExitStack() as stack:
        sem = {}
        for name in ("s_in", "s_out", "s_gd", "s_cc",
                     "c_v", "c_s", "c_t", "c_g", "c_y",
                     "p0", "p1", "p2", "p3", "p4", "p5", "p6"):
            sem[name] = Ctr(stack.enter_context(nc.semaphore(name)))
        s_in, s_out, s_gd = sem["s_in"], sem["s_out"], sem["s_gd"]
        s_cc = sem["s_cc"]
        cV, cS, cT, cG, cY = (sem["c_v"], sem["c_s"], sem["c_t"],
                              sem["c_g"], sem["c_y"])
        pm = [sem[f"p{i}"] for i in range(7)]
        pr.cV, pr.cS = cV, cS
        bar_ctrs = {"sync": cY, "vector": cV, "scalar": cS,
                    "tensor": cT, "gpsimd": cG}

        def barrier():
            pr.barrier(bar_ctrs)

        class PBuf:
            def __init__(self, t, s):
                self.t = t
                self.s = s
                self.free = []

            def produce(self, emits):
                for (c, v) in self.free:
                    pr.wait("tensor", c, v)
                self.free = []
                for i, fn in enumerate(emits):
                    if i == len(emits) - 1:
                        pr.op("tensor", fn, inc=self.s)
                    else:
                        pr.op("tensor", fn)
                return self.s.n

            def consumed(self, ctr, val=None):
                self.free.append((ctr, ctr.n if val is None else val))

        idn_b = stack.enter_context(nc.sbuf_tensor("idn_b", [P, P], BF16))
        idn_f = stack.enter_context(nc.sbuf_tensor("idn_f", [P, P], F32))
        ones_r = stack.enter_context(nc.sbuf_tensor("ones_r", [1, P], BF16))
        ones_rf = stack.enter_context(nc.sbuf_tensor("ones_rf", [1, P], F32))
        b1row = stack.enter_context(nc.sbuf_tensor("b1row", [1, HID], BF16))
        b2row = stack.enter_context(nc.sbuf_tensor("b2row", [1, D], BF16))
        scr = stack.enter_context(nc.sbuf_tensor("scr", [P, HID], BF16))
        sp = stack.enter_context(nc.sbuf_tensor("sp", [P, 64], F32))
        xin = [stack.enter_context(nc.sbuf_tensor(f"xin{i}", [P, D], F32))
               for i in range(2)]

        def spc(i):
            return sp[:, i:i + 1]

        scrf = scr.bitcast(F32)

        pr.op("gpsimd", lambda e: e.memset(idn_b[:, :], 0.0), inc=cG)
        pr.wait("gpsimd", cG)
        pr.op("gpsimd", lambda e: e.affine_select(
            out=idn_b[:, :], in_=idn_b[:, :], compare_op=OP.not_equal,
            fill=1.0, base=0, pattern=[[-1, P]], channel_multiplier=1), inc=cG)
        pr.op("gpsimd", lambda e: e.memset(idn_f[:, :], 0.0), inc=cG)
        pr.wait("gpsimd", cG)
        pr.op("gpsimd", lambda e: e.affine_select(
            out=idn_f[:, :], in_=idn_f[:, :], compare_op=OP.not_equal,
            fill=1.0, base=0, pattern=[[-1, P]], channel_multiplier=1), inc=cG)
        pr.op("gpsimd", lambda e: e.memset(ones_r[:, :], 1.0), inc=cG)
        pr.op("gpsimd", lambda e: e.memset(ones_rf[:, :], 1.0), inc=cG)
        pr.op("gpsimd", lambda e: e.memset(sp[:, 33:34], 1.0), inc=cG)
        pr.op("gpsimd", lambda e: e.memset(sp[:, 30:31], 1e-5), inc=cG)
        pr.op("gpsimd", lambda e: e.memset(sp[:, 31:32], 1e-6), inc=cG)
        pr.dma("gpsimd", b1row[:, :], b1_d[None, :], s_gd)
        pr.dma("gpsimd", b2row[:, :], b2_d[None, :], s_gd)
        consts_g = cG.n

        class TrPool:
            def __init__(self, bufs):
                self.b = bufs
                self.i = 0

            def tr(self, src_ap, dst_ap, rows=P, cols=P,
                   src_ctr=None, src_val=None):
                pb = self.b[self.i % len(self.b)]
                self.i += 1
                if src_ctr is not None:
                    pr.wait("tensor", src_ctr, src_val)
                pr.wait("tensor", cG, consts_g)
                k = pb.produce([lambda e, s=src_ap, p=pb.t, rows=rows,
                                cols=cols: e.transpose(
                                    p[:cols, :rows], s, idn_b[:rows, :rows])])
                pr.wait("vector", pb.s, k)
                pr.vop(lambda e, p=pb.t, d=dst_ap, rows=rows, cols=cols:
                       e.tensor_copy(out=d, in_=p[:cols, :rows]))
                pb.consumed(cV)

        def ln_stats(xt, W):
            pr.vop(lambda e: e.reduce_sum(out=spc(3), in_=xt[:, :W],
                                          axis=AX.X))
            pr.sop(lambda e: e.activation(out=scr[:, :W], in_=xt[:, :W],
                                          func=AF.Square, accum_out=spc(4)))
            pr.wait("vector", cS)
            pr.vop(lambda e: e.tensor_scalar_mul(spc(0), spc(3), 1.0 / W))
            pr.vop(lambda e: e.tensor_scalar_mul(spc(1), spc(4), 1.0 / W))
            pr.vop(lambda e: e.tensor_mul(spc(3), spc(0), spc(0)))
            pr.vop(lambda e: e.tensor_sub(spc(1), spc(1), spc(3)))
            pr.wait("scalar", cV)
            pr.sop(lambda e: e.activation(out=spc(3), in_=spc(1),
                                          func=AF.Sqrt, bias=spc(30)))
            pr.wait("vector", cS)
            pr.vop(lambda e: e.reciprocal(out=spc(2), in_=spc(3)))

        def build_fm(wdst, w_hbm, rows, cols, tp, wtmp):
            nrt, ncc = rows // P, cols // P
            for rt in range(nrt):
                b = rt % 2
                pr.wait("gpsimd", cV)
                v = pr.dma("gpsimd", wtmp[b][:, :cols],
                           w_hbm[rt * P:(rt + 1) * P, :], s_gd)
                pr.wait("tensor", s_gd, v)
                for c in range(ncc):
                    tp.tr(wtmp[b][:, c * P:(c + 1) * P],
                          wdst[:, c * rows + rt * P: c * rows + (rt + 1) * P])

        st0 = ExitStack()
        vwT = st0.enter_context(nc.sbuf_tensor("vwT", [P, DC * D], BF16))
        owT = st0.enter_context(nc.sbuf_tensor("owT", [P, DC * D], BF16))
        basT = st0.enter_context(nc.sbuf_tensor("basT", [P, DC * R], BF16))
        hm = st0.enter_context(nc.sbuf_tensor("hm", [R, NH * R], F32))
        hm_b = st0.enter_context(nc.sbuf_tensor("hm_b", [R, NH * R], BF16))
        wtmp = [st0.enter_context(nc.sbuf_tensor(f"wtmp{i}", [P, D], BF16))
                for i in range(2)]
        hrbuf = st0.enter_context(nc.sbuf_tensor("hrbuf", [R, NH * R], F32))
        sm0 = ExitStack()
        tb0 = [PBuf(sm0.enter_context(
            nc.psum_tensor(f"tb0{i}", [P, 1024], BF16)), pm[i])
            for i in range(2)]
        pp0 = PBuf(sm0.enter_context(nc.psum_tensor("pp0", [P, 1024], F32)),
                   pm[2])
        tp0 = TrPool(tb0)

        if nphases >= 1:
            build_fm(vwT, vw_d, D, D, tp0, wtmp)
            build_fm(owT, ow_d, D, D, tp0, wtmp)
            pr.wait("gpsimd", cV)
            v = pr.dma("gpsimd", wtmp[0][:R, :D], basis_d, s_gd)
            pr.wait("tensor", s_gd, v)
            for c in range(DC):
                tp0.tr(wtmp[0][:R, c * P:(c + 1) * P],
                       basT[:, c * R:(c + 1) * R], rows=R, cols=P)
            pr.dma("sync", hm[:R, 0:R], core_d, s_in)
            pr.dma("sync",
                   hrbuf[:R, :].rearrange("p (h s) -> p h s", h=NH),
                   hres_d.transpose([1, 0, 2]), s_in)
            pr.wait("tensor", s_in)
            k = pp0.produce([lambda e: e.transpose(
                pp0.t[:R, 0:R], hm[:R, 0:R], idn_f[:R, :R])])
            pr.wait("vector", pp0.s, k)
            pr.wait("vector", s_in)
            pr.vop(lambda e: e.reduce_sum(
                out=scrf[:R, 0:R],
                in_=hrbuf[:R, :].rearrange("p (h s) -> p s h", h=NH),
                axis=AX.X))
            pr.vop(lambda e: e.tensor_add(
                hm[:R, 0:R], hm[:R, 0:R], pp0.t[:R, 0:R]))
            pp0.consumed(cV)
            pr.vop(lambda e: e.tensor_scalar_mul(
                hm[:R, 0:R], hm[:R, 0:R], 0.5 / NH))
            pr.vop(lambda e: e.tensor_scalar_mul(
                scrf[:R, 0:R], scrf[:R, 0:R], 1.0 / NH))
            pr.vop(lambda e: e.tensor_sub(
                scrf[:R, R:2 * R], hm[:R, 0:R], scrf[:R, 0:R]))
            for h in range(NH):
                pr.vop(lambda e, h=h: e.tensor_add(
                    hm[:R, h * R:(h + 1) * R],
                    hrbuf[:R, h * R:(h + 1) * R],
                    scrf[:R, R:2 * R]))
            pr.vop(lambda e: e.tensor_copy(out=hm_b[:R, :], in_=hm[:R, :]))
            barrier()

        if nphases >= 2:
            st1 = ExitStack()
            ulow = st1.enter_context(nc.sbuf_tensor("ulow", [P, P], BF16))
            nbigI = st1.enter_context(nc.sbuf_tensor("nbigI", [P, P], BF16))
            obias = st1.enter_context(nc.sbuf_tensor("obias", [1, D], BF16))
            pr.op("gpsimd", lambda e: e.memset(ulow[:, :], 1.0), inc=cG)
            pr.wait("gpsimd", cG)
            pr.op("gpsimd", lambda e: e.affine_select(
                out=ulow[:, :], in_=ulow[:, :], compare_op=OP.is_gt,
                fill=0.0, base=0, pattern=[[-1, P]], channel_multiplier=1),
                inc=cG)
            pr.op("gpsimd", lambda e: e.memset(nbigI[:, :], 0.0), inc=cG)
            pr.wait("gpsimd", cG)
            pr.op("gpsimd", lambda e: e.affine_select(
                out=nbigI[:, :], in_=nbigI[:, :], compare_op=OP.not_equal,
                fill=NEG, base=0, pattern=[[-1, P]], channel_multiplier=1),
                inc=cG)
            masks_g = cG.n
            pr.dma("sync", sp[:, 40:40 + DC],
                   vb_d.rearrange("(c p) -> p c", p=P), s_in,
                   allow_slow_non_contiguous=True)
            pr.dma("sync", scrf[0:1, 1024:1024 + D], ob_d[None, :], s_in)
            pr.wait("vector", s_in)
            pr.vop(lambda e: e.tensor_copy(
                out=wtmp[1][:, 0:DC], in_=sp[:, 40:40 + DC]))
            vb_ready = cV.n
            emits = []
            for c in range(DC):
                for nn in range(2):
                    emits.append(lambda e, c=c, nn=nn: e.matmul(
                        pp0.t[0:1, nn * 512:(nn + 1) * 512],
                        wtmp[1][:, c:c + 1],
                        owT[:, c * D + nn * 512: c * D + (nn + 1) * 512],
                        start=(c == 0), stop=(c == DC - 1)))
            pr.wait("tensor", cV, vb_ready)
            k = pp0.produce(emits)
            pr.wait("vector", pp0.s, k)
            pr.vop(lambda e: e.tensor_add(
                obias[0:1, :], pp0.t[0:1, 0:D], scrf[0:1, 1024:1024 + D]))
            pp0.consumed(cV)

            lat = st1.enter_context(nc.sbuf_tensor("lat", [P, T], BF16))
            ltf = st1.enter_context(nc.sbuf_tensor("ltf", [P, 8 * T], BF16))
            vaug = st1.enter_context(
                nc.sbuf_tensor("vaug", [P, NT * NH * 65], BF16))
            pTb = [st1.enter_context(nc.sbuf_tensor(f"pT{i}", [P, 1024], BF16))
                   for i in range(2)]
            xrs = [st1.enter_context(nc.sbuf_tensor(f"xrs{i}", [P, D], F32))
                   for i in range(2)]
            yT = ltf

            def dstage_ap(i):
                return scrf[0:1, i * 1024:(i + 1) * 1024]

            def dsb_ap(h, qc):
                p0 = 32 + h + 16 * qc
                return scrf[p0:p0 + 1, 0:1024]

            def rhi_ap(h, qc):
                p0 = 32 + h + 16 * qc
                return scr[p0:p0 + 1, 2048:3072]

            def rlo_ap(h, qc):
                p0 = 32 + h + 16 * qc
                return scr[p0:p0 + 1, 3072:4096]

            st1a = ExitStack()
            xlnT = st1a.enter_context(nc.sbuf_tensor("xlnT", [P, DC * T], BF16))
            xlnb = st1a.enter_context(nc.sbuf_tensor("xlnb", [P, D], BF16))

            for t in range(NT):
                b = t % 2
                pr.wait("sync", cV)
                pr.wait("sync", cS)
                pr.dma("sync", xin[b][:, :], x0_d[t * P:(t + 1) * P, :], s_in)
                pr.wait("vector", s_in)
                pr.wait("scalar", s_in)
                ln_stats(xin[b], D)
                pr.vop(lambda e, b=b: e.scalar_tensor_tensor(
                    out=xlnb[:, :], in0=xin[b][:, :], scalar=spc(0),
                    in1=spc(2).broadcast_to([P, D]),
                    op0=OP.subtract, op1=OP.mult))
                stt_v = cV.n
                for c in range(DC):
                    tp0.tr(xlnb[:, c * P:(c + 1) * P],
                           xlnT[:, c * T + t * P: c * T + (t + 1) * P],
                           src_ctr=cV, src_val=stt_v)

            for nq in range(4):
                emits = []
                for c in range(DC):
                    emits.append(lambda e, nq=nq, c=c: e.matmul(
                        pp0.t[:R, 0:512],
                        basT[:, c * R:(c + 1) * R],
                        xlnT[:, c * T + nq * 512: c * T + (nq + 1) * 512],
                        start=(c == 0), stop=(c == DC - 1)))
                pr.wait("tensor", cV)
                k = pp0.produce(emits)
                pr.wait("vector", pp0.s, k)
                pr.vop(lambda e, nq=nq: e.tensor_copy(
                    out=lat[:R, nq * 512:(nq + 1) * 512],
                    in_=pp0.t[:R, 0:512]))
                pp0.consumed(cV)
            pr.wait("sync", cV)
            pr.dma("sync", lat[R:2 * R, :], lat[0:R, :], s_in)

            for h in range(NH):
                half = h % 2
                for nq in range(4):
                    pr.wait("tensor", s_in)
                    k = pp0.produce([lambda e, h=h, nq=nq, half=half: e.matmul(
                        pp0.t[half * R:(half + 1) * R, 0:512],
                        hm_b[:R, h * R:(h + 1) * R],
                        lat[0:R, nq * 512:(nq + 1) * 512],
                        start=True, stop=True,
                        tile_position=(0, half * R))])
                    pr.wait("vector", pp0.s, k)
                    pr.vop(lambda e, h=h, nq=nq, half=half: e.tensor_copy(
                        out=ltf[half * R:(half + 1) * R,
                                (h // 2) * T + nq * 512:
                                (h // 2) * T + (nq + 1) * 512],
                        in_=pp0.t[half * R:(half + 1) * R, 0:512]))
                    pp0.consumed(cV)

            pr.vop(lambda e: e.reduce_max(
                out=spc(8), in_=lat[:, :], axis=AX.X,
                apply_absolute_value=True))
            pr.vop(lambda e: e.reduce_max(
                out=spc(9), in_=ltf[:, :], axis=AX.X,
                apply_absolute_value=True))
            pr.wait("tensor", cV)
            k = pp0.produce([
                lambda e: e.transpose(pp0.t[0:1, 0:P], sp[:, 8:9],
                                      idn_f[:, :]),
                lambda e: e.transpose(pp0.t[0:1, 256:256 + P], sp[:, 9:10],
                                      idn_f[:, :])])
            pr.wait("vector", pp0.s, k)
            pr.vop(lambda e: e.tensor_copy(
                out=scrf[0:1, 0:P], in_=pp0.t[0:1, 0:P]))
            pr.vop(lambda e: e.tensor_copy(
                out=scrf[0:1, P:2 * P], in_=pp0.t[0:1, 256:256 + P]))
            pp0.consumed(cV)
            pr.vop(lambda e: e.reduce_max(
                out=scrf[0:1, 300:301], in_=scrf[0:1, 0:P], axis=AX.X))
            pr.vop(lambda e: e.reduce_max(
                out=scrf[0:1, 301:302], in_=scrf[0:1, P:2 * P], axis=AX.X))
            pr.vop(lambda e: e.tensor_mul(
                scrf[0:1, 302:303], scrf[0:1, 300:301], scrf[0:1, 301:302]))
            pr.vop(lambda e: e.tensor_scalar_mul(
                scrf[0:1, 302:303], scrf[0:1, 302:303], -8.0))
            pr.wait("tensor", cV)
            k = pp0.produce([lambda e: e.matmul(
                pp0.t[:, 512:513], ones_rf[0:1, :], scrf[0:1, 302:303],
                start=True, stop=True)])
            pr.wait("vector", pp0.s, k)
            pr.vop(lambda e: e.tensor_copy(out=spc(12), in_=pp0.t[:, 512:513]))
            pp0.consumed(cV)
            negc = spc(12)

            pr.vop(lambda e: e.memset(
                vaug[:, :].rearrange("p (u x) -> p u x", x=65)[:, :, 64:65],
                1.0))
            for ut in range(NT):
                for half in range(2):
                    emits = []
                    for c in range(DC):
                        emits.append(lambda e, ut=ut, half=half, c=c: e.matmul(
                            pp0.t[:, 0:512],
                            xlnT[:, c * T + ut * P: c * T + (ut + 1) * P],
                            vwT[:, c * D + half * 512:
                                c * D + (half + 1) * 512],
                            start=(c == 0), stop=(c == DC - 1)))
                    pr.wait("tensor", cV)
                    k = pp0.produce(emits)
                    pr.wait("vector", pp0.s, k)
                    pr.vop(lambda e, ut=ut, half=half: e.tensor_copy(
                        out=vaug[:, ut * NH * 65 + half * 8 * 65:
                                 ut * NH * 65 + (half * 8 + 8) * 65]
                        .rearrange("p (u x) -> p u x", x=65)[:, :, 0:64],
                        in_=pp0.t[:, 0:512]))
                    pp0.consumed(cV)
            barrier()
            st1a.close()
            sm0.close()

            sm1 = ExitStack()
            ps_s = PBuf(sm1.enter_context(
                nc.psum_tensor("ps_s", [P, 1024], F32)), pm[0])
            ps_v = sm1.enter_context(nc.psum_tensor("ps_v", [P, 1024], F32))
            ps_r = PBuf(sm1.enter_context(
                nc.psum_tensor("ps_r", [P, 1024], F32)), pm[2])
            pAV = pm[1]
            pT_free = [0, 0]
            psv_free = []
            dstage_free = [0, 0]
            for h in range(NH):
                half = h % 2
                for qc in range(2):
                    nut = 8 * (qc + 1)
                    for ut in range(nut):
                        qlo = qc * 1024
                        col0 = max(0, ut * P - qlo)
                        diag = ut >= 8 * qc
                        emits = []
                        a = col0
                        while a < 1024:
                            b2 = min(1024, (a // 512 + 1) * 512)
                            dmask = diag and a <= col0 < b2
                            emits.append(lambda e, h=h, qc=qc, ut=ut,
                                         half=half, a=a, b2=b2, dmask=dmask:
                                         e.matmul(
                                             ps_s.t[:, a:b2],
                                             lat[half * R:(half + 1) * R,
                                                 ut * P:(ut + 1) * P],
                                             ltf[half * R:(half + 1) * R,
                                                 (h // 2) * T + qc * 1024 + a:
                                                 (h // 2) * T + qc * 1024 + b2],
                                             start=True, stop=not dmask,
                                             tile_position=(half * R, 0)))
                            if dmask:
                                emits.append(lambda e, col0=col0: e.matmul(
                                    ps_s.t[:, col0:col0 + P],
                                    nbigI[:, :], ulow[:, :],
                                    start=False, stop=True))
                            a = b2
                        pr.wait("tensor", cG, masks_g)
                        sk = ps_s.produce(emits)
                        pb = pTb[ut % 2]
                        mz_v = 0
                        if col0 > 0:
                            pr.wait("vector", pAV, pT_free[ut % 2])
                            mz_v = pr.vop(lambda e, pb=pb, col0=col0:
                                          e.memset(pb[:, 0:col0], 0.0))
                        pr.wait("scalar", ps_s.s, sk)
                        pr.wait("scalar", pAV, pT_free[ut % 2])
                        pr.sop(lambda e, pb=pb, col0=col0: e.activation(
                            out=pb[:, col0:1024], in_=ps_s.t[:, col0:1024],
                            func=AF.Exp, bias=negc, scale=0.125))
                        ps_s.consumed(cS)
                        exp_v = cS.n
                        pr.wait("tensor", cS, exp_v)
                        if mz_v:
                            pr.wait("tensor", cV, mz_v)
                        if ut == 0:
                            for (c_, v_) in psv_free:
                                pr.wait("tensor", c_, v_)
                            psv_free = []
                        for hq in range(2):
                            pr.op("tensor",
                                  lambda e, h=h, ut=ut, hq=hq, pb=pb,
                                  st=(ut == 0), sp_=(ut == nut - 1):
                                  e.matmul(
                                      ps_v[:65, hq * 512:(hq + 1) * 512],
                                      vaug[:, ut * NH * 65 + h * 65:
                                           ut * NH * 65 + h * 65 + 65],
                                      pb[:, hq * 512:(hq + 1) * 512],
                                      start=st, stop=sp_), inc=pAV)
                        pT_free[ut % 2] = pAV.n
                    av_k = pAV.n
                    stg = (h * 2 + qc) % 2
                    pr.wait("scalar", pAV, av_k)
                    pr.wait("scalar", s_in)
                    pr.sop(lambda e, stg=stg: e.copy(
                        out=dstage_ap(stg), in_=ps_v[64:65, 0:1024]))
                    dcp_v = cS.n
                    pr.wait("sync", cS, dcp_v)
                    pr.dma("sync", dsb_ap(h, qc), dstage_ap(stg), s_in)
                    pr.wait("vector", pAV, av_k)
                    pr.vop(lambda e, h=h, qc=qc, half=half: e.tensor_copy(
                        out=yT[half * R:(half + 1) * R,
                               (h // 2) * T + qc * 1024:
                               (h // 2) * T + (qc + 1) * 1024],
                        in_=ps_v[0:64, 0:1024]))
                    psv_free = [(cV, cV.n), (cS, dcp_v)]
            pr.wait("vector", cS)
            pr.wait("vector", s_in)
            pr.vop(lambda e: e.reciprocal(
                out=scrf[32:64, 0:1024], in_=scrf[32:64, 0:1024]))
            pr.vop(lambda e: e.tensor_copy(
                out=scr[32:64, 2048:3072], in_=scrf[32:64, 0:1024]))
            pr.vop(lambda e: e.tensor_sub(
                scrf[32:64, 0:1024], scrf[32:64, 0:1024],
                scr[32:64, 2048:3072]))
            pr.vop(lambda e: e.tensor_copy(
                out=scr[32:64, 3072:4096], in_=scrf[32:64, 0:1024]))
            for h in range(NH):
                half = h % 2
                for qc in range(2):
                    pr.wait("sync", cV)
                    pr.wait("sync", cS)
                    pr.wait("sync", pAV, pAV.n)
                    pr.dma("sync", pTb[0][0:1, 0:1024], rhi_ap(h, qc), s_in)
                    pr.dma("sync", pTb[1][0:1, 0:1024], rlo_ap(h, qc), s_in)
                    pr.wait("tensor", s_in)
                    emits = []
                    for hq in range(2):
                        emits.append(lambda e, hq=hq: e.matmul(
                            ps_r.t[0:64, hq * 512:(hq + 1) * 512],
                            ones_r[0:1, 0:64],
                            pTb[0][0:1, hq * 512:(hq + 1) * 512],
                            start=True, stop=False))
                        emits.append(lambda e, hq=hq: e.matmul(
                            ps_r.t[0:64, hq * 512:(hq + 1) * 512],
                            ones_r[0:1, 0:64],
                            pTb[1][0:1, hq * 512:(hq + 1) * 512],
                            start=False, stop=True))
                    k = ps_r.produce(emits)
                    pr.wait("vector", ps_r.s, k)
                    pr.vop(lambda e, h=h, qc=qc, half=half: e.tensor_tensor(
                        out=yT[half * R:(half + 1) * R,
                               (h // 2) * T + qc * 1024:
                               (h // 2) * T + (qc + 1) * 1024],
                        in0=yT[half * R:(half + 1) * R,
                               (h // 2) * T + qc * 1024:
                               (h // 2) * T + (qc + 1) * 1024],
                        in1=ps_r.t[0:64, 0:1024], op=OP.mult))
                    ps_r.consumed(cV)
            barrier()
            sm1.close()

            sm2 = ExitStack()
            po = [PBuf(sm2.enter_context(
                nc.psum_tensor(f"po{i}", [P, 1024], F32)), pm[3 + i])
                for i in range(2)]
            for t in range(NT):
                b = t % 2
                pr.wait("sync", cV)
                pr.wait("sync", s_out)
                pr.dma("sync", xrs[b][:, :], x0_d[t * P:(t + 1) * P, :], s_in)
                emits = []
                for c in range(DC):
                    for nn in range(2):
                        emits.append(lambda e, t=t, c=c, b=b, nn=nn: e.matmul(
                            po[b].t[:, nn * 512:(nn + 1) * 512],
                            yT[:, c * T + t * P:c * T + (t + 1) * P],
                            owT[:, c * D + nn * 512: c * D + (nn + 1) * 512],
                            start=(c == 0), stop=False))
                for nn in range(2):
                    emits.append(lambda e, b=b, nn=nn: e.matmul(
                        po[b].t[:, nn * 512:(nn + 1) * 512],
                        ones_r[0:1, :],
                        obias[0:1, nn * 512:(nn + 1) * 512],
                        start=False, stop=True))
                k = po[b].produce(emits)
                pr.wait("vector", po[b].s, k)
                pr.wait("vector", s_in)
                pr.vop(lambda e, b=b: e.tensor_add(
                    xrs[b][:, :], po[b].t[:, 0:D], xrs[b][:, :]))
                po[b].consumed(cV)
                pr.wait("sync", cV)
                pr.dma("sync", x1_d[t * P:(t + 1) * P, :], xrs[b][:, :], s_out)
            pr.wait("sync", s_out)
            barrier()
            sm2.close()
            st1.close()
        st0.close()
        if nphases < 2:
            sm0.close()

        st2 = ExitStack()
        w1q = w2q = None
        if nphases >= 3:
            w1q = st2.enter_context(nc.sbuf_tensor("w1q", [P, DC * HID], BF16))
            w2q = st2.enter_context(nc.sbuf_tensor("w2q", [P, HC * D], BF16))
            st2t = ExitStack()
            wf = st2t.enter_context(nc.sbuf_tensor("wf", [P, HID], F32))
            w8 = st2t.enter_context(nc.sbuf_tensor("w8", [P, HID], I8))
            sm2b = ExitStack()
            tb2 = [PBuf(sm2b.enter_context(
                nc.psum_tensor(f"tb2{i}", [P, 1024], BF16)), pm[i])
                for i in range(2)]
            pq2 = PBuf(sm2b.enter_context(
                nc.psum_tensor("pq2", [P, 512], F32)), pm[2])
            tp2 = TrPool(tb2)

            def quant_weight(w_hbm, nrt, ncc, wdst, rows, im):
                W = ncc * P
                pr.vop(lambda e: e.memset(spc(20), 0.0))
                for rt in range(nrt):
                    pr.wait("sync", cS)
                    pr.dma("sync", wf[:, :W],
                           w_hbm[rt * P:(rt + 1) * P, :], s_in)
                    pr.wait("scalar", s_in)
                    pr.wait("scalar", cV)
                    pr.sop(lambda e: e.activation(
                        out=scr[:, :W], in_=wf[:, :W],
                        func=AF.Abs, accum_out=spc(16)))
                    pr.wait("vector", cS)
                    pr.vop(lambda e: e.tensor_add(spc(20), spc(20), spc(16)))
                pr.wait("tensor", cV)
                pr.wait("tensor", cG, consts_g)
                k = pq2.produce([lambda e: e.matmul(
                    pq2.t[0:1, 0:1], sp[:, 33:34], spc(20),
                    start=True, stop=True)])
                pr.wait("vector", pq2.s, k)
                pr.vop(lambda e: e.tensor_copy(
                    out=scrf[0:1, 300:301], in_=pq2.t[0:1, 0:1]))
                pq2.consumed(cV)
                pr.vop(lambda e: e.tensor_scalar_mul(
                    scrf[0:1, 300:301], scrf[0:1, 300:301],
                    1.0 / (nrt * P * W)))
                pr.vop(lambda e: e.tensor_scalar_max(
                    scrf[0:1, 300:301], scrf[0:1, 300:301], 1e-5))
                pr.vop(lambda e: e.reciprocal(
                    out=scrf[0:1, 301:302], in_=scrf[0:1, 300:301]))
                pr.wait("tensor", cV)
                k = pq2.produce([
                    lambda e: e.matmul(pq2.t[:, 1:2], ones_rf[0:1, :],
                                       scrf[0:1, 300:301],
                                       start=True, stop=True),
                    lambda e: e.matmul(pq2.t[:, 2:3], ones_rf[0:1, :],
                                       scrf[0:1, 301:302],
                                       start=True, stop=True)])
                pr.wait("vector", pq2.s, k)
                pr.vop(lambda e, im=im: e.tensor_copy(
                    out=spc(im), in_=pq2.t[:, 1:2]))
                pr.vop(lambda e: e.tensor_copy(out=spc(18), in_=pq2.t[:, 2:3]))
                pq2.consumed(cV)
                for rt in range(nrt):
                    pr.wait("sync", cS)
                    pr.dma("sync", wf[:, :W],
                           w_hbm[rt * P:(rt + 1) * P, :], s_in)
                    pr.wait("scalar", s_in)
                    pr.wait("scalar", cV)
                    pr.sop(lambda e: e.activation(
                        out=w8[:, :W], in_=wf[:, :W],
                        func=AF.Copy, scale=spc(18)))
                    pr.sop(lambda e: e.activation(
                        out=sp[:, 61:62], in_=w8[:, W - 1:W], func=AF.Copy))
                    pr.wait("vector", cS)
                    pr.vop(lambda e: e.tensor_copy(
                        out=sp[:, 62:63], in_=sp[:, 61:62]))
                    pr.vop(lambda e: e.tensor_scalar(
                        out=scr[:, :W], in0=w8[:, :W],
                        scalar1=-1.0, scalar2=1.0,
                        op0=OP.max, op1=OP.min))
                    pr.vop(lambda e: e.tensor_copy(
                        out=sp[:, 63:64], in_=scr[:, W - 1:W]))
                    clamp_v = cV.n
                    for c in range(ncc):
                        tp2.tr(scr[:, c * P:(c + 1) * P],
                               wdst[:, c * rows + rt * P:
                                    c * rows + (rt + 1) * P],
                               src_ctr=cV, src_val=clamp_v)

            quant_weight(w1_d, HID // P, DC, w1q, HID, 21)
            quant_weight(w2_d, D // P, HC, w2q, D, 23)
            pr.vop(lambda e: e.reciprocal(out=spc(24), in_=spc(21)))
            pr.vop(lambda e: e.reciprocal(out=spc(25), in_=spc(23)))
            pr.wait("sync", cV)
            pr.dma("sync", wq1_d, w1q[:, :], s_out)
            pr.dma("sync", wq2_d, w2q[:, :], s_out)
            barrier()
            sm2b.close()
            st2t.close()

        def emit_ffn(xin_d, xout_d, w1qt, w2qt, stf, tag=""):
            xq8 = stf.enter_context(
                nc.sbuf_tensor(tag + "xq8", [P, NT * D], I8))
            xqT = stf.enter_context(
                nc.sbuf_tensor(tag + "xqT", [P, DC * P], BF16))
            hti = stf.enter_context(nc.sbuf_tensor(tag + "hti", [P, HID], F32))
            hq8 = stf.enter_context(nc.sbuf_tensor(tag + "hq8", [P, HID], I8))
            hqT = stf.enter_context(
                nc.sbuf_tensor(tag + "hqT", [P, HC * P], BF16))
            sxc = stf.enter_context(nc.sbuf_tensor(tag + "sxc", [P, NT], F32))
            exc = stf.enter_context(nc.sbuf_tensor(tag + "exc", [P, NT], F32))
            e2c = stf.enter_context(nc.sbuf_tensor(tag + "e2c", [P, 2], F32))
            sbB = stf.enter_context(nc.sbuf_tensor(tag + "sbB", [1, P], BF16))
            s2B = stf.enter_context(nc.sbuf_tensor(tag + "s2B", [1, P], BF16))
            xl2 = stf.enter_context(nc.sbuf_tensor(tag + "xl2", [P, D], F32))
            smf = ExitStack()
            tbf = [PBuf(smf.enter_context(
                nc.psum_tensor(tag + f"tbf{i}", [P, 1024], BF16)), pm[i])
                for i in range(2)]
            hp = [PBuf(smf.enter_context(
                nc.psum_tensor(tag + f"hpf{i}", [P, 512], F32)), pm[2 + i])
                for i in range(2)]
            pof = PBuf(smf.enter_context(
                nc.psum_tensor(tag + "poff", [P, 1024], F32)), pm[4])
            ptr = PBuf(smf.enter_context(
                nc.psum_tensor(tag + "ptrf", [P, 512], F32)), pm[5])
            tpf = TrPool(tbf)

            for t in range(NT):
                b = t % 2
                pr.wait("sync", cS)
                pr.wait("sync", cV)
                pr.dma("sync", xin[b][:, :], xin_d[t * P:(t + 1) * P, :], s_in)
                pr.wait("vector", s_in)
                pr.wait("scalar", s_in)
                ln_stats(xin[b], D)
                pr.wait("vector", cS)
                pr.vop(lambda e, b=b: e.scalar_tensor_tensor(
                    out=xl2[:, :], in0=xin[b][:, :], scalar=spc(0),
                    in1=spc(2).broadcast_to([P, D]),
                    op0=OP.subtract, op1=OP.mult))
                pr.vop(lambda e: e.tensor_mul(spc(5), spc(1), spc(2)))
                pr.vop(lambda e: e.tensor_mul(spc(5), spc(5), spc(2)))
                pr.wait("scalar", cV)
                pr.sop(lambda e: e.activation(
                    out=spc(6), in_=spc(5), func=AF.Sqrt, bias=spc(31)))
                pr.wait("vector", cS)
                pr.vop(lambda e: e.reciprocal(out=spc(6), in_=spc(6)))
                pr.vop(lambda e: e.reduce_max(
                    out=spc(7), in_=xl2[:, :], axis=AX.X,
                    apply_absolute_value=True))
                pr.vop(lambda e: e.tensor_mul(spc(7), spc(7), spc(6)))
                pr.vop(lambda e: e.tensor_scalar_max(spc(7), spc(7), 1e-5))
                pr.vop(lambda e: e.reciprocal(out=spc(8), in_=spc(7)))
                pr.vop(lambda e: e.tensor_scalar_mul(spc(8), spc(8), 127.0))
                pr.vop(lambda e: e.tensor_mul(spc(9), spc(8), spc(6)))
                pr.vop(lambda e, t=t: e.tensor_scalar_mul(
                    exc[:, t:t + 1], spc(7), 1.0 / 127.0))
                pr.vop(lambda e, t=t: e.tensor_mul(
                    exc[:, t:t + 1], exc[:, t:t + 1], spc(21)))
                pr.vop(lambda e, t=t: e.tensor_mul(
                    sxc[:, t:t + 1], spc(8), spc(24)))
                pr.wait("scalar", cV)
                pr.sop(lambda e, t=t: e.activation(
                    out=xq8[:, t * D:(t + 1) * D], in_=xl2[:, :],
                    func=AF.Copy, scale=spc(9)))

            for t in range(NT):
                b = t % 2
                pr.wait("vector", cS)
                pr.vop(lambda e, t=t: e.tensor_copy(
                    out=scr[:, 0:D], in_=xq8[:, t * D:(t + 1) * D]))
                pr.vop(lambda e: e.tensor_copy(
                    out=sp[:, 63:64], in_=scr[:, D - 1:D]))
                cp_v = cV.n
                for c in range(DC):
                    tpf.tr(scr[:, c * P:(c + 1) * P],
                           xqT[:, c * P:(c + 1) * P],
                           src_ctr=cV, src_val=cp_v)
                xqT_ready = cV.n
                pr.wait("tensor", cV, xqT_ready)
                k = ptr.produce([lambda e, t=t: e.transpose(
                    ptr.t[0:1, :P], sxc[:, t:t + 1], idn_f[:, :])])
                pr.wait("vector", ptr.s, k)
                pr.vop(lambda e: e.tensor_copy(
                    out=sbB[0:1, :], in_=ptr.t[0:1, :P]))
                ptr.consumed(cV)
                sb_ready = cV.n
                for n in range(8):
                    pb = n % 2
                    pr.wait("tensor", cV, max(xqT_ready, sb_ready))
                    emits = []
                    for c in range(DC):
                        emits.append(lambda e, n=n, c=c, pb=pb: e.matmul(
                            hp[pb].t[:, :],
                            xqT[:, c * P:(c + 1) * P],
                            w1qt[:, c * HID + n * 512:
                                 c * HID + (n + 1) * 512],
                            start=(c == 0), stop=False))
                    emits.append(lambda e, n=n, pb=pb: e.matmul(
                        hp[pb].t[:, :], sbB[0:1, :],
                        b1row[0:1, n * 512:(n + 1) * 512],
                        start=False, stop=True))
                    k = hp[pb].produce(emits)
                    pr.wait("scalar", hp[pb].s, k)
                    pr.sop(lambda e, n=n, pb=pb, t=t: e.activation(
                        out=hti[:, n * 512:(n + 1) * 512], in_=hp[pb].t[:, :],
                        func=AF.Gelu, scale=exc[:, t:t + 1]))
                    hp[pb].consumed(cS)
                pr.sop(lambda e: e.activation(
                    out=sp[:, 61:62], in_=hti[:, HID - 1:HID], func=AF.Copy))
                pr.wait("vector", cS)
                pr.vop(lambda e: e.tensor_copy(
                    out=sp[:, 62:63], in_=sp[:, 61:62]))
                pr.vop(lambda e: e.reduce_sum(
                    out=spc(3), in_=hti[:, :], axis=AX.X))
                pr.wait("scalar", cV)
                pr.sop(lambda e: e.activation(
                    out=scr[:, :], in_=hti[:, :], func=AF.Square,
                    accum_out=spc(4)))
                pr.sop(lambda e: e.activation(
                    out=sp[:, 61:62], in_=spc(4), func=AF.Copy))
                pr.wait("vector", cS)
                pr.vop(lambda e: e.tensor_copy(
                    out=sp[:, 62:63], in_=sp[:, 61:62]))
                pr.vop(lambda e: e.tensor_scalar_mul(spc(5), spc(4), 1.0 / HID))
                pr.wait("scalar", cV)
                pr.sop(lambda e: e.activation(
                    out=spc(6), in_=spc(5), func=AF.Sqrt, bias=spc(31)))
                pr.wait("vector", cS)
                pr.vop(lambda e: e.reciprocal(out=spc(6), in_=spc(6)))
                pr.vop(lambda e: e.reduce_max(
                    out=spc(7), in_=hti[:, :], axis=AX.X,
                    apply_absolute_value=True))
                pr.vop(lambda e: e.tensor_mul(spc(7), spc(7), spc(6)))
                pr.vop(lambda e: e.tensor_scalar_max(spc(7), spc(7), 1e-5))
                pr.vop(lambda e: e.reciprocal(out=spc(8), in_=spc(7)))
                pr.vop(lambda e: e.tensor_scalar_mul(spc(8), spc(8), 127.0))
                pr.vop(lambda e: e.tensor_mul(spc(9), spc(8), spc(6)))
                pr.vop(lambda e, b=b: e.tensor_scalar_mul(
                    e2c[:, b:b + 1], spc(7), 1.0 / 127.0))
                pr.vop(lambda e, b=b: e.tensor_mul(
                    e2c[:, b:b + 1], e2c[:, b:b + 1], spc(23)))
                pr.vop(lambda e: e.tensor_mul(spc(11), spc(8), spc(25)))
                pr.wait("scalar", cV)
                pr.sop(lambda e: e.activation(
                    out=hq8[:, :], in_=hti[:, :], func=AF.Copy, scale=spc(9)))
                pr.sop(lambda e: e.activation(
                    out=sp[:, 61:62], in_=hq8[:, HID - 1:HID], func=AF.Copy))
                pr.wait("tensor", cV)
                k = ptr.produce([lambda e: e.transpose(
                    ptr.t[0:1, :P], sp[:, 11:12], idn_f[:, :])])
                pr.wait("vector", ptr.s, k)
                pr.vop(lambda e: e.tensor_copy(
                    out=s2B[0:1, :], in_=ptr.t[0:1, :P]))
                ptr.consumed(cV)
                pr.wait("vector", cS)
                pr.vop(lambda e: e.tensor_copy(
                    out=sp[:, 62:63], in_=sp[:, 61:62]))
                pr.vop(lambda e: e.tensor_copy(out=scr[:, 0:HID], in_=hq8[:, :]))
                pr.vop(lambda e: e.tensor_copy(
                    out=sp[:, 63:64], in_=scr[:, HID - 1:HID]))
                hq_v = cV.n
                for c in range(HC):
                    tpf.tr(scr[:, c * P:(c + 1) * P],
                           hqT[:, c * P:(c + 1) * P],
                           src_ctr=cV, src_val=hq_v)
                hqT_ready = cV.n
                pr.wait("sync", cV)
                pr.wait("sync", cS)
                pr.wait("sync", s_out)
                pr.dma("sync", xin[b][:, :], xin_d[t * P:(t + 1) * P, :], s_in)
                pr.wait("tensor", cV, hqT_ready)
                emits = []
                for n in range(2):
                    for c in range(HC):
                        emits.append(lambda e, n=n, c=c: e.matmul(
                            pof.t[:, n * 512:(n + 1) * 512],
                            hqT[:, c * P:(c + 1) * P],
                            w2qt[:, c * D + n * 512: c * D + (n + 1) * 512],
                            start=(c == 0), stop=False))
                    emits.append(lambda e, n=n: e.matmul(
                        pof.t[:, n * 512:(n + 1) * 512], s2B[0:1, :],
                        b2row[0:1, n * 512:(n + 1) * 512],
                        start=False, stop=True))
                k = pof.produce(emits)
                pr.wait("vector", pof.s, k)
                pr.wait("vector", s_in)
                pr.vop(lambda e, b=b: e.scalar_tensor_tensor(
                    out=xin[b][:, :], in0=pof.t[:, 0:D],
                    scalar=e2c[:, b:b + 1],
                    in1=xin[b][:, :], op0=OP.mult, op1=OP.add))
                pof.consumed(cV)
                pr.wait("sync", cV)
                pr.dma("sync", xout_d[t * P:(t + 1) * P, :], xin[b][:, :],
                       s_out)
            pr.wait("sync", s_out)
            if dbg == "q2" and tag == "a":
                pr.wait("sync", cV)
                pr.wait("sync", cS)
                fl = out_d.flatten()
                pr.dma("sync", fl[0:P * 4096].rearrange("(p n) -> p n", p=P),
                       xq8.bitcast(F32)[:, :], s_out)
                pr.dma("sync",
                       fl[P * 4096:2 * P * 4096].rearrange("(p n) -> p n", p=P),
                       hti[:, :], s_out)
                pr.dma("sync",
                       fl[2 * P * 4096:2 * P * 4096 + 2048]
                       .rearrange("(p n) -> p n", p=1),
                       b1row.bitcast(F32)[0:1, :], s_out)
                pr.dma("sync",
                       fl[2 * P * 4096 + 2048:2 * P * 4096 + 2048 + P * 16]
                       .rearrange("(p n) -> p n", p=P),
                       exc[:, :], s_out)
                pr.dma("sync",
                       fl[2 * P * 4096 + 2048 + P * 16:
                          2 * P * 4096 + 2048 + P * 16 + P * 16]
                       .rearrange("(p n) -> p n", p=P),
                       sxc[:, :], s_out)
            barrier()
            smf.close()

        if nphases >= 4:
            stf1 = ExitStack()
            emit_ffn(x1_d, x2_d, w1q, w2q, stf1, tag="a")
            stf1.close()
        st2.close()

        if nphases >= 5:
            pr.wait("gpsimd", s_out)

            def _cc(e):
                with allow_ici_all_to_all():
                    return e.collective_compute(
                        "AllToAll", OP.bypass,
                        replica_groups=[list(range(8))],
                        ins=[x2_d], outs=[x2t_d])
            pr.op("gpsimd", _cc, inc=s_cc)

        if nphases >= 5:
            st3 = ExitStack()
            wkT = st3.enter_context(nc.sbuf_tensor("wkT", [P, DC * D], BF16))
            wvT = st3.enter_context(nc.sbuf_tensor("wvT", [P, DC * D], BF16))
            wqT = st3.enter_context(nc.sbuf_tensor("wqT", [P, DC * D], BF16))
            wtm3 = [st3.enter_context(nc.sbuf_tensor(f"wtm3{i}", [P, D], BF16))
                    for i in range(2)]
            S2 = [st3.enter_context(nc.sbuf_tensor(f"S2{i}", [P, 8192], F32))
                  for i in range(2)]
            prod = st3.enter_context(nc.sbuf_tensor("prod", [P, 8192], F32))
            kk = st3.enter_context(nc.sbuf_tensor("kk", [P, D], F32))
            vv = st3.enter_context(nc.sbuf_tensor("vv", [P, D], F32))
            qq = st3.enter_context(nc.sbuf_tensor("qq", [P, D], F32))
            kn = st3.enter_context(nc.sbuf_tensor("kn", [P, D], F32))
            rt3 = st3.enter_context(nc.sbuf_tensor("rt3", [P, D], F32))
            sq3 = st3.enter_context(nc.sbuf_tensor("sq3", [P, D], F32))
            xl3 = st3.enter_context(nc.sbuf_tensor("xl3", [P, D], BF16))
            x3T = st3.enter_context(nc.sbuf_tensor("x3T", [P, DC * P], BF16))
            sm3 = ExitStack()
            tb3 = [PBuf(sm3.enter_context(
                nc.psum_tensor(f"tb3{i}", [P, 1024], BF16)), pm[i])
                for i in range(2)]
            pq3 = [PBuf(sm3.enter_context(
                nc.psum_tensor(f"pq3{i}", [P, 512], F32)), pm[2 + i])
                for i in range(2)]
            tp3 = TrPool(tb3)
            build_fm(wkT, wk_d, D, D, tp3, wtm3)
            build_fm(wvT, wv_d, D, D, tp3, wtm3)
            build_fm(wqT, wqd_d, D, D, tp3, wtm3)

            pr.vop(lambda e: e.memset(S2[0][:, :], 0.0))
            pr.vop(lambda e: e.memset(S2[1][:, :], 0.0))
            pr.wait("sync", s_cc, 1)

            def S4(S):
                return S[:, :].rearrange("p (n i j) -> p n i j", i=8, j=8)

            def prod4():
                return prod[:, :].rearrange("p (n i j) -> p n i j", i=8, j=8)

            for t in range(B):
                for bt in range(2):
                    row0 = t * BS + bt * P
                    pr.wait("sync", cV)
                    pr.wait("sync", s_out)
                    pr.dma("sync", xin[bt][:, :], x2t_d[row0:row0 + P, :],
                           s_in)
                    pr.wait("vector", s_in)
                    pr.wait("scalar", s_in)
                    ln_stats(xin[bt], D)
                    pr.vop(lambda e, bt=bt: e.scalar_tensor_tensor(
                        out=xl3[:, :], in0=xin[bt][:, :], scalar=spc(0),
                        in1=spc(2).broadcast_to([P, D]),
                        op0=OP.subtract, op1=OP.mult))
                    xl3_v = cV.n
                    for c in range(DC):
                        tp3.tr(xl3[:, c * P:(c + 1) * P],
                               x3T[:, c * P:(c + 1) * P],
                               src_ctr=cV, src_val=xl3_v)
                    xT_ready = cV.n
                    for wi, (wT, dst) in enumerate(
                            ((wkT, kk), (wvT, vv), (wqT, qq))):
                        for n in range(2):
                            pb = (wi * 2 + n) % 2
                            pr.wait("tensor", cV, xT_ready)
                            emits = []
                            for c in range(DC):
                                emits.append(
                                    lambda e, wT=wT, n=n, c=c, pb=pb:
                                    e.matmul(
                                        pq3[pb].t[:, :],
                                        x3T[:, c * P:(c + 1) * P],
                                        wT[:, c * D + n * 512:
                                           c * D + (n + 1) * 512],
                                        start=(c == 0), stop=(c == DC - 1)))
                            k = pq3[pb].produce(emits)
                            pr.wait("vector", pq3[pb].s, k)
                            pr.wait("vector", cS)
                            pr.vop(lambda e, dst=dst, n=n, pb=pb:
                                   e.tensor_copy(
                                       out=dst[:, n * 512:(n + 1) * 512],
                                       in_=pq3[pb].t[:, :]))
                            pq3[pb].consumed(cV)
                    S = S2[bt]
                    pr.wait("scalar", cV)
                    pr.sop(lambda e: e.activation(
                        out=scrf[:, 0:D], in_=kk[:, :], func=AF.Square))
                    pr.wait("vector", cS)
                    pr.vop(lambda e: e.reduce_sum(
                        out=kn[:, 0:P],
                        in_=scrf[:, 0:D].rearrange("p (n j) -> p n j", j=8),
                        axis=AX.X))
                    pr.wait("scalar", cV)
                    pr.sop(lambda e: e.activation(
                        out=kn[:, P:2 * P], in_=kn[:, 0:P], func=AF.Sqrt))
                    pr.wait("vector", cS)
                    pr.vop(lambda e: e.tensor_scalar_add(
                        kn[:, P:2 * P], kn[:, P:2 * P], 1e-6))
                    pr.vop(lambda e: e.reciprocal(
                        out=kn[:, 0:P], in_=kn[:, P:2 * P]))
                    pr.vop(lambda e: e.tensor_tensor(
                        out=rt3[:, :], in0=kk[:, :],
                        in1=kn[:, 0:P].rearrange("p (n o) -> p n o", o=1)
                        .broadcast_to([P, P, 8]),
                        op=OP.mult))
                    kn4i = rt3[:, :].rearrange("p (n j) -> p n j", j=8) \
                        .unsqueeze(2).broadcast_to([P, P, 8, 8])
                    pr.vop(lambda e, S=S: e.tensor_tensor(
                        out=prod4(), in0=S4(S), in1=kn4i, op=OP.mult))
                    pr.vop(lambda e: e.reduce_sum(
                        out=sq3[:, :].rearrange("p (n i) -> p n i", i=8),
                        in_=prod4(), axis=AX.X))
                    pr.vop(lambda e: e.tensor_sub(
                        sq3[:, :], vv[:, :], sq3[:, :]))
                    pr.vop(lambda e: e.tensor_tensor(
                        out=prod4(),
                        in0=sq3[:, :].rearrange("p (n i) -> p n i", i=8)
                        .unsqueeze(3).broadcast_to([P, P, 8, 8]),
                        in1=kn4i, op=OP.mult))
                    pr.vop(lambda e, S=S: e.tensor_add(
                        prod[:, :], S[:, :], prod[:, :]))
                    pr.wait("scalar", cV)
                    pr.sop(lambda e, S=S: e.activation(
                        out=S[:, :], in_=prod[:, :], func=AF.Tanh))
                    pr.wait("vector", cS)
                    pr.vop(lambda e, S=S: e.tensor_tensor(
                        out=prod4(), in0=S4(S),
                        in1=qq[:, :].rearrange("p (n j) -> p n j", j=8)
                        .unsqueeze(2).broadcast_to([P, P, 8, 8]),
                        op=OP.mult))
                    pr.vop(lambda e: e.reduce_sum(
                        out=sq3[:, :].rearrange("p (n i) -> p n i", i=8),
                        in_=prod4(), axis=AX.X))
                    pr.wait("scalar", cV)
                    pr.sop(lambda e: e.activation(
                        out=rt3[:, :], in_=sq3[:, :], func=AF.Sigmoid))
                    pr.sop(lambda e: e.activation(
                        out=kn[:, :], in_=sq3[:, :], func=AF.Square))
                    pr.wait("vector", cS)
                    pr.vop(lambda e: e.tensor_mul(
                        rt3[:, :], kn[:, :], rt3[:, :]))
                    pr.vop(lambda e, bt=bt: e.tensor_add(
                        xin[bt][:, :], xin[bt][:, :], rt3[:, :]))
                    pr.wait("sync", cV)
                    pr.dma("sync", x3_d[row0:row0 + P, :], xin[bt][:, :],
                           s_out)
            pr.wait("sync", s_out)
            barrier()
            sm3.close()
            st3.close()

        if nphases >= 6:
            st4 = ExitStack()
            w1q2 = st4.enter_context(
                nc.sbuf_tensor("w1q2", [P, DC * HID], BF16))
            w2q2 = st4.enter_context(nc.sbuf_tensor("w2q2", [P, HC * D], BF16))
            pr.dma("sync", w1q2[:, :], wq1_d, s_in)
            pr.dma("sync", w2q2[:, :], wq2_d, s_in)
            pr.wait("tensor", s_in)
            emit_ffn(x3_d, out_d, w1q2, w2q2, st4, tag="b")
            st4.close()

        if dbg == "q2":
            pass
        if dbg == "q":
            pr.wait("sync", s_out)
            pr.wait("sync", cV)
            pr.wait("sync", cS)
            pr.dma("sync", out_d[0:P, 0:64], sp[:, :], s_out)
            pr.dma("sync", out_d[P:2 * P, 0:1024],
                   wq1_d.bitcast(F32)[:, 1024:2048], s_out)
            pr.dma("sync", out_d[2 * P:3 * P, 0:1024],
                   wq2_d.bitcast(F32)[:, 0:1024], s_out)
            if w2q is not None:
                pr.dma("sync", out_d[3 * P:4 * P, 0:1024],
                       w2q.bitcast(F32)[:, 0:1024], s_out)
        elif dbg is not None and dbg != "q2":
            dbg_src = {"x1": x1_d, "x2": x2_d, "x2t": x2t_d, "x3": x3_d}[dbg]
            pr.wait("sync", s_out)
            if dbg == "x2t":
                pr.wait("sync", s_cc, 1)
            pr.dma("sync", out_d, dbg_src, s_out)

        pr.wait("sync", s_out)
        pr.wait("gpsimd", s_out)

        with nc.Block() as block:
            pr.flush(block)

    return nc


def kernel(L0_attn_a_hidden_states, L0_attn_a_basis_w, L0_attn_a_core,
           L0_attn_a_head_residual, L0_attn_a_v_w, L0_attn_a_v_b,
           L0_attn_a_o_w, L0_attn_a_o_b, L0_attn_b_W_k, L0_attn_b_W_v,
           L0_attn_b_W_q, L0_ffn_w1, L0_ffn_b1, L0_ffn_w2, L0_ffn_b2):
    hs = np.ascontiguousarray(L0_attn_a_hidden_states, dtype=np.float32)
    nc = build(_NPHASES, _DBG)
    common = {
        "basis_w": np.ascontiguousarray(L0_attn_a_basis_w, np.float32),
        "core_m": np.ascontiguousarray(L0_attn_a_core, np.float32),
        "head_res": np.ascontiguousarray(L0_attn_a_head_residual, np.float32),
        "v_w": np.ascontiguousarray(L0_attn_a_v_w, np.float32),
        "v_b": np.ascontiguousarray(L0_attn_a_v_b, np.float32),
        "o_w": np.ascontiguousarray(L0_attn_a_o_w, np.float32),
        "o_b": np.ascontiguousarray(L0_attn_a_o_b, np.float32),
        "w_k": np.ascontiguousarray(L0_attn_b_W_k, np.float32),
        "w_v": np.ascontiguousarray(L0_attn_b_W_v, np.float32),
        "w_q": np.ascontiguousarray(L0_attn_b_W_q, np.float32),
        "w1": np.ascontiguousarray(L0_ffn_w1, np.float32),
        "b1": np.ascontiguousarray(L0_ffn_b1, np.float32),
        "w2": np.ascontiguousarray(L0_ffn_w2, np.float32),
        "b2": np.ascontiguousarray(L0_ffn_b2, np.float32),
    }
    in_maps = [dict(common, x0=np.ascontiguousarray(hs[i])) for i in range(8)]
    res = run_bass_kernel_spmd(nc, in_maps, core_ids=list(range(8)))
    out = np.empty((B, T, D), dtype=np.float32)
    for s in range(8):
        out[:, s * BS:(s + 1) * BS, :] = \
            res.results[s]["out"].reshape(B, BS, D)
    return out


# version 02
# speedup vs baseline: 1.0174x; 1.0174x over previous
import numpy as np
import concourse.bass as bass
import concourse.mybir as mybir
from concourse.bass_utils import run_bass_kernel_spmd
from concourse.aot_env import allow_ici_all_to_all
from contextlib import ExitStack

F32 = mybir.dt.float32
BF16 = mybir.dt.bfloat16
I8 = mybir.dt.int8
AF = mybir.ActivationFunctionType
OP = mybir.AluOpType
AX = mybir.AxisListType

P = 128
T = 2048
NT = 16
D = 1024
DC = 8
HID = 4096
HC = 32
NH = 16
R = 64
B = 8
BS = 256
NEG = -1e30

ENGINES = ("sync", "vector", "scalar", "tensor", "gpsimd")
_NPHASES = 9
_TRACE = False
_LAST_RES = None
_DBG = None


class Ctr:
    def __init__(self, h):
        self.h = h
        self.n = 0


class Prog:
    def __init__(self, nc=None):
        self.q = {e: [] for e in ENGINES}
        self.cV = None
        self.cS = None
        self.nc = nc
        self.phase = "init"

    def _add(self, eng, fn):
        self.q[eng].append((self.phase, fn))

    def wait(self, eng, ctr, val=None):
        v = ctr.n if val is None else val
        if v > 0:
            self._add(eng, lambda e: e.wait_ge(ctr.h, v))

    def dma(self, eng, out, in_, sem, **kw):
        self._add(
            eng,
            lambda e: e.dma_start(out=out, in_=in_, **kw).then_inc(sem.h, 16))
        sem.n += 16
        return sem.n

    def op(self, eng, fn, inc=None, k=1):
        if inc is None:
            self._add(eng, fn)
            return None
        self._add(eng, lambda e: fn(e).then_inc(inc.h, k))
        inc.n += k
        return inc.n

    def vop(self, fn):
        self.wait("vector", self.cV)
        return self.op("vector", fn, inc=self.cV)

    def sop(self, fn):
        self.wait("scalar", self.cS)
        return self.op("scalar", fn, inc=self.cS)

    def barrier(self, ctrs):
        vals = {}
        for e in ENGINES:
            c = ctrs[e]
            self._add(e, lambda en, c=c: en.sem_inc(c.h, 1))
            c.n += 1
            vals[e] = c.n
        for e in ENGINES:
            for o in ENGINES:
                if o != e:
                    c, v = ctrs[o], vals[o]
                    self._add(e, lambda en, c=c, v=v: en.wait_ge(c.h, v))

    def _emit(self, eng, name):
        nc, cur, sid = self.nc, None, None
        for lab, fn in self.q[name]:
            if lab != cur:
                if cur is not None:
                    nc.leave_named_scope(cur, sid, False)
                sid, _ = nc.enter_named_scope(lab, False)
                cur = lab
            fn(eng)
        if cur is not None:
            nc.leave_named_scope(cur, sid, False)

    def flush(self, block):
        @block.sync
        def _(eng):
            self._emit(eng, "sync")

        @block.vector
        def _(eng):
            self._emit(eng, "vector")

        @block.scalar
        def _(eng):
            self._emit(eng, "scalar")

        @block.tensor
        def _(eng):
            self._emit(eng, "tensor")

        @block.gpsimd
        def _(eng):
            self._emit(eng, "gpsimd")


def build(nphases=9, dbg=None):
    nc = bass.Bass("TRN2", target_bir_lowering=False,
                   detect_race_conditions=False)

    x0_d = nc.dram_tensor("x0", [T, D], F32, kind="ExternalInput").ap()
    basis_d = nc.dram_tensor("basis_w", [R, D], F32, kind="ExternalInput").ap()
    core_d = nc.dram_tensor("core_m", [R, R], F32, kind="ExternalInput").ap()
    hres_d = nc.dram_tensor("head_res", [NH, R, R], F32,
                            kind="ExternalInput").ap()
    vw_d = nc.dram_tensor("v_w", [D, D], F32, kind="ExternalInput").ap()
    vb_d = nc.dram_tensor("v_b", [D], F32, kind="ExternalInput").ap()
    ow_d = nc.dram_tensor("o_w", [D, D], F32, kind="ExternalInput").ap()
    ob_d = nc.dram_tensor("o_b", [D], F32, kind="ExternalInput").ap()
    wk_d = nc.dram_tensor("w_k", [D, D], F32, kind="ExternalInput").ap()
    wv_d = nc.dram_tensor("w_v", [D, D], F32, kind="ExternalInput").ap()
    wqd_d = nc.dram_tensor("w_q", [D, D], F32, kind="ExternalInput").ap()
    w1_d = nc.dram_tensor("w1", [HID, D], F32, kind="ExternalInput").ap()
    b1_d = nc.dram_tensor("b1", [HID], F32, kind="ExternalInput").ap()
    w2_d = nc.dram_tensor("w2", [D, HID], F32, kind="ExternalInput").ap()
    b2_d = nc.dram_tensor("b2", [D], F32, kind="ExternalInput").ap()
    out_d = nc.dram_tensor("out", [T, D], F32, kind="ExternalOutput").ap()

    x1_d = nc.dram_tensor("x1_s", [T, D], F32).ap()
    x2_d = nc.dram_tensor("x2_s", [T, D], F32).ap()
    x2t_d = nc.dram_tensor("x2t_s", [T, D], F32).ap()
    x3_d = nc.dram_tensor("x3_s", [T, D], F32).ap()
    wq1_d = nc.dram_tensor("wq1_s", [P, DC * HID], BF16).ap()
    wq2_d = nc.dram_tensor("wq2_s", [P, HC * D], BF16).ap()

    pr = Prog(nc)

    with ExitStack() as stack:
        sem = {}
        for name in ("s_in", "s_out", "s_gd", "s_cc",
                     "c_v", "c_s", "c_t", "c_g", "c_y",
                     "p0", "p1", "p2", "p3", "p4", "p5", "p6"):
            sem[name] = Ctr(stack.enter_context(nc.semaphore(name)))
        s_in, s_out, s_gd = sem["s_in"], sem["s_out"], sem["s_gd"]
        s_cc = sem["s_cc"]
        cV, cS, cT, cG, cY = (sem["c_v"], sem["c_s"], sem["c_t"],
                              sem["c_g"], sem["c_y"])
        pm = [sem[f"p{i}"] for i in range(7)]
        pr.cV, pr.cS = cV, cS
        bar_ctrs = {"sync": cY, "vector": cV, "scalar": cS,
                    "tensor": cT, "gpsimd": cG}

        def barrier():
            pr.barrier(bar_ctrs)

        class PBuf:
            def __init__(self, t, s):
                self.t = t
                self.s = s
                self.free = []

            def produce(self, emits):
                for (c, v) in self.free:
                    pr.wait("tensor", c, v)
                self.free = []
                for i, fn in enumerate(emits):
                    if i == len(emits) - 1:
                        pr.op("tensor", fn, inc=self.s)
                    else:
                        pr.op("tensor", fn)
                return self.s.n

            def consumed(self, ctr, val=None):
                self.free.append((ctr, ctr.n if val is None else val))

        idn_b = stack.enter_context(nc.sbuf_tensor("idn_b", [P, P], BF16))
        idn_f = stack.enter_context(nc.sbuf_tensor("idn_f", [P, P], F32))
        ones_r = stack.enter_context(nc.sbuf_tensor("ones_r", [1, P], BF16))
        ones_rf = stack.enter_context(nc.sbuf_tensor("ones_rf", [1, P], F32))
        b1row = stack.enter_context(nc.sbuf_tensor("b1row", [1, HID], BF16))
        b2row = stack.enter_context(nc.sbuf_tensor("b2row", [1, D], BF16))
        scr = stack.enter_context(nc.sbuf_tensor("scr", [P, HID], BF16))
        sp = stack.enter_context(nc.sbuf_tensor("sp", [P, 64], F32))
        xin = [stack.enter_context(nc.sbuf_tensor(f"xin{i}", [P, D], F32))
               for i in range(2)]

        def spc(i):
            return sp[:, i:i + 1]

        scrf = scr.bitcast(F32)

        pr.op("gpsimd", lambda e: e.memset(idn_b[:, :], 0.0), inc=cG)
        pr.wait("gpsimd", cG)
        pr.op("gpsimd", lambda e: e.affine_select(
            out=idn_b[:, :], in_=idn_b[:, :], compare_op=OP.not_equal,
            fill=1.0, base=0, pattern=[[-1, P]], channel_multiplier=1), inc=cG)
        pr.op("gpsimd", lambda e: e.memset(idn_f[:, :], 0.0), inc=cG)
        pr.wait("gpsimd", cG)
        pr.op("gpsimd", lambda e: e.affine_select(
            out=idn_f[:, :], in_=idn_f[:, :], compare_op=OP.not_equal,
            fill=1.0, base=0, pattern=[[-1, P]], channel_multiplier=1), inc=cG)
        pr.op("gpsimd", lambda e: e.memset(ones_r[:, :], 1.0), inc=cG)
        pr.op("gpsimd", lambda e: e.memset(ones_rf[:, :], 1.0), inc=cG)
        pr.op("gpsimd", lambda e: e.memset(sp[:, 33:34], 1.0), inc=cG)
        pr.op("gpsimd", lambda e: e.memset(sp[:, 30:31], 1e-5), inc=cG)
        pr.op("gpsimd", lambda e: e.memset(sp[:, 31:32], 1e-6), inc=cG)
        pr.dma("gpsimd", b1row[:, :], b1_d[None, :], s_gd)
        pr.dma("gpsimd", b2row[:, :], b2_d[None, :], s_gd)
        consts_g = cG.n

        class TrPool:
            def __init__(self, bufs):
                self.b = bufs
                self.i = 0

            def tr(self, src_ap, dst_ap, rows=P, cols=P,
                   src_ctr=None, src_val=None):
                pb = self.b[self.i % len(self.b)]
                self.i += 1
                if src_ctr is not None:
                    pr.wait("tensor", src_ctr, src_val)
                pr.wait("tensor", cG, consts_g)
                k = pb.produce([lambda e, s=src_ap, p=pb.t, rows=rows,
                                cols=cols: e.transpose(
                                    p[:cols, :rows], s, idn_b[:rows, :rows])])
                pr.wait("vector", pb.s, k)
                pr.vop(lambda e, p=pb.t, d=dst_ap, rows=rows, cols=cols:
                       e.tensor_copy(out=d, in_=p[:cols, :rows]))
                pb.consumed(cV)

        def ln_stats(xt, W):
            pr.vop(lambda e: e.reduce_sum(out=spc(3), in_=xt[:, :W],
                                          axis=AX.X))
            pr.sop(lambda e: e.activation(out=scr[:, :W], in_=xt[:, :W],
                                          func=AF.Square, accum_out=spc(4)))
            pr.wait("vector", cS)
            pr.vop(lambda e: e.tensor_scalar_mul(spc(0), spc(3), 1.0 / W))
            pr.vop(lambda e: e.tensor_scalar_mul(spc(1), spc(4), 1.0 / W))
            pr.vop(lambda e: e.tensor_mul(spc(3), spc(0), spc(0)))
            pr.vop(lambda e: e.tensor_sub(spc(1), spc(1), spc(3)))
            pr.wait("scalar", cV)
            pr.sop(lambda e: e.activation(out=spc(3), in_=spc(1),
                                          func=AF.Sqrt, bias=spc(30)))
            pr.wait("vector", cS)
            pr.vop(lambda e: e.reciprocal(out=spc(2), in_=spc(3)))

        def build_fm(wdst, w_hbm, rows, cols, tp, wtmp):
            nrt, ncc = rows // P, cols // P
            for rt in range(nrt):
                b = rt % 2
                pr.wait("gpsimd", cV)
                v = pr.dma("gpsimd", wtmp[b][:, :cols],
                           w_hbm[rt * P:(rt + 1) * P, :], s_gd)
                pr.wait("tensor", s_gd, v)
                for c in range(ncc):
                    tp.tr(wtmp[b][:, c * P:(c + 1) * P],
                          wdst[:, c * rows + rt * P: c * rows + (rt + 1) * P])

        pr.phase = "p0a_prep"
        st0 = ExitStack()
        vwT = st0.enter_context(nc.sbuf_tensor("vwT", [P, DC * D], BF16))
        owT = st0.enter_context(nc.sbuf_tensor("owT", [P, DC * D], BF16))
        basT = st0.enter_context(nc.sbuf_tensor("basT", [P, DC * R], BF16))
        hm = st0.enter_context(nc.sbuf_tensor("hm", [R, NH * R], F32))
        hm_b = st0.enter_context(nc.sbuf_tensor("hm_b", [R, NH * R], BF16))
        wtmp = [st0.enter_context(nc.sbuf_tensor(f"wtmp{i}", [P, D], BF16))
                for i in range(2)]
        hrbuf = st0.enter_context(nc.sbuf_tensor("hrbuf", [R, NH * R], F32))
        sm0 = ExitStack()
        tb0 = [PBuf(sm0.enter_context(
            nc.psum_tensor(f"tb0{i}", [P, 1024], BF16)), pm[i])
            for i in range(2)]
        pp0 = PBuf(sm0.enter_context(nc.psum_tensor("pp0", [P, 1024], F32)),
                   pm[2])
        tp0 = TrPool(tb0)

        if nphases >= 1:
            build_fm(vwT, vw_d, D, D, tp0, wtmp)
            build_fm(owT, ow_d, D, D, tp0, wtmp)
            pr.wait("gpsimd", cV)
            v = pr.dma("gpsimd", wtmp[0][:R, :D], basis_d, s_gd)
            pr.wait("tensor", s_gd, v)
            for c in range(DC):
                tp0.tr(wtmp[0][:R, c * P:(c + 1) * P],
                       basT[:, c * R:(c + 1) * R], rows=R, cols=P)
            pr.dma("sync", hm[:R, 0:R], core_d, s_in)
            pr.dma("sync",
                   hrbuf[:R, :].rearrange("p (h s) -> p h s", h=NH),
                   hres_d.transpose([1, 0, 2]), s_in)
            pr.wait("tensor", s_in)
            k = pp0.produce([lambda e: e.transpose(
                pp0.t[:R, 0:R], hm[:R, 0:R], idn_f[:R, :R])])
            pr.wait("vector", pp0.s, k)
            pr.wait("vector", s_in)
            pr.vop(lambda e: e.reduce_sum(
                out=scrf[:R, 0:R],
                in_=hrbuf[:R, :].rearrange("p (h s) -> p s h", h=NH),
                axis=AX.X))
            pr.vop(lambda e: e.tensor_add(
                hm[:R, 0:R], hm[:R, 0:R], pp0.t[:R, 0:R]))
            pp0.consumed(cV)
            pr.vop(lambda e: e.tensor_scalar_mul(
                hm[:R, 0:R], hm[:R, 0:R], 0.5 / NH))
            pr.vop(lambda e: e.tensor_scalar_mul(
                scrf[:R, 0:R], scrf[:R, 0:R], 1.0 / NH))
            pr.vop(lambda e: e.tensor_sub(
                scrf[:R, R:2 * R], hm[:R, 0:R], scrf[:R, 0:R]))
            for h in range(NH):
                pr.vop(lambda e, h=h: e.tensor_add(
                    hm[:R, h * R:(h + 1) * R],
                    hrbuf[:R, h * R:(h + 1) * R],
                    scrf[:R, R:2 * R]))
            pr.vop(lambda e: e.tensor_copy(out=hm_b[:R, :], in_=hm[:R, :]))
            barrier()

        pr.phase = "p1_attn_a"
        if nphases >= 2:
            st1 = ExitStack()
            ulow = st1.enter_context(nc.sbuf_tensor("ulow", [P, P], BF16))
            nbigI = st1.enter_context(nc.sbuf_tensor("nbigI", [P, P], BF16))
            obias = st1.enter_context(nc.sbuf_tensor("obias", [1, D], BF16))
            pr.op("gpsimd", lambda e: e.memset(ulow[:, :], 1.0), inc=cG)
            pr.wait("gpsimd", cG)
            pr.op("gpsimd", lambda e: e.affine_select(
                out=ulow[:, :], in_=ulow[:, :], compare_op=OP.is_gt,
                fill=0.0, base=0, pattern=[[-1, P]], channel_multiplier=1),
                inc=cG)
            pr.op("gpsimd", lambda e: e.memset(nbigI[:, :], 0.0), inc=cG)
            pr.wait("gpsimd", cG)
            pr.op("gpsimd", lambda e: e.affine_select(
                out=nbigI[:, :], in_=nbigI[:, :], compare_op=OP.not_equal,
                fill=NEG, base=0, pattern=[[-1, P]], channel_multiplier=1),
                inc=cG)
            masks_g = cG.n
            pr.dma("sync", sp[:, 40:40 + DC],
                   vb_d.rearrange("(c p) -> p c", p=P), s_in,
                   allow_slow_non_contiguous=True)
            pr.dma("sync", scrf[0:1, 1024:1024 + D], ob_d[None, :], s_in)
            pr.wait("vector", s_in)
            pr.vop(lambda e: e.tensor_copy(
                out=wtmp[1][:, 0:DC], in_=sp[:, 40:40 + DC]))
            vb_ready = cV.n
            emits = []
            for c in range(DC):
                for nn in range(2):
                    emits.append(lambda e, c=c, nn=nn: e.matmul(
                        pp0.t[0:1, nn * 512:(nn + 1) * 512],
                        wtmp[1][:, c:c + 1],
                        owT[:, c * D + nn * 512: c * D + (nn + 1) * 512],
                        start=(c == 0), stop=(c == DC - 1)))
            pr.wait("tensor", cV, vb_ready)
            k = pp0.produce(emits)
            pr.wait("vector", pp0.s, k)
            pr.vop(lambda e: e.tensor_add(
                obias[0:1, :], pp0.t[0:1, 0:D], scrf[0:1, 1024:1024 + D]))
            pp0.consumed(cV)

            lat = st1.enter_context(nc.sbuf_tensor("lat", [P, T], BF16))
            ltf = st1.enter_context(nc.sbuf_tensor("ltf", [P, 8 * T], BF16))
            vaug = st1.enter_context(
                nc.sbuf_tensor("vaug", [P, NT * NH * 65], BF16))
            pTb = [st1.enter_context(nc.sbuf_tensor(f"pT{i}", [P, 1024], BF16))
                   for i in range(2)]
            xrs = [st1.enter_context(nc.sbuf_tensor(f"xrs{i}", [P, D], F32))
                   for i in range(2)]
            yT = ltf

            def dstage_ap(i):
                return scrf[0:1, i * 1024:(i + 1) * 1024]

            def dsb_ap(h, qc):
                p0 = 32 + h + 16 * qc
                return scrf[p0:p0 + 1, 0:1024]

            def rhi_ap(h, qc):
                p0 = 32 + h + 16 * qc
                return scr[p0:p0 + 1, 2048:3072]

            def rlo_ap(h, qc):
                p0 = 32 + h + 16 * qc
                return scr[p0:p0 + 1, 3072:4096]

            st1a = ExitStack()
            xlnT = st1a.enter_context(nc.sbuf_tensor("xlnT", [P, DC * T], BF16))
            xlnb = st1a.enter_context(nc.sbuf_tensor("xlnb", [P, D], BF16))

            for t in range(NT):
                b = t % 2
                pr.wait("sync", cV)
                pr.wait("sync", cS)
                pr.dma("sync", xin[b][:, :], x0_d[t * P:(t + 1) * P, :], s_in)
                pr.wait("vector", s_in)
                pr.wait("scalar", s_in)
                ln_stats(xin[b], D)
                pr.vop(lambda e, b=b: e.scalar_tensor_tensor(
                    out=xlnb[:, :], in0=xin[b][:, :], scalar=spc(0),
                    in1=spc(2).broadcast_to([P, D]),
                    op0=OP.subtract, op1=OP.mult))
                stt_v = cV.n
                for c in range(DC):
                    tp0.tr(xlnb[:, c * P:(c + 1) * P],
                           xlnT[:, c * T + t * P: c * T + (t + 1) * P],
                           src_ctr=cV, src_val=stt_v)

            for nq in range(4):
                emits = []
                for c in range(DC):
                    emits.append(lambda e, nq=nq, c=c: e.matmul(
                        pp0.t[:R, 0:512],
                        basT[:, c * R:(c + 1) * R],
                        xlnT[:, c * T + nq * 512: c * T + (nq + 1) * 512],
                        start=(c == 0), stop=(c == DC - 1)))
                pr.wait("tensor", cV)
                k = pp0.produce(emits)
                pr.wait("vector", pp0.s, k)
                pr.vop(lambda e, nq=nq: e.tensor_copy(
                    out=lat[:R, nq * 512:(nq + 1) * 512],
                    in_=pp0.t[:R, 0:512]))
                pp0.consumed(cV)
            pr.wait("sync", cV)
            pr.dma("sync", lat[R:2 * R, :], lat[0:R, :], s_in)

            for h in range(NH):
                half = h % 2
                for nq in range(4):
                    pr.wait("tensor", s_in)
                    k = pp0.produce([lambda e, h=h, nq=nq, half=half: e.matmul(
                        pp0.t[half * R:(half + 1) * R, 0:512],
                        hm_b[:R, h * R:(h + 1) * R],
                        lat[0:R, nq * 512:(nq + 1) * 512],
                        start=True, stop=True,
                        tile_position=(0, half * R))])
                    pr.wait("vector", pp0.s, k)
                    pr.vop(lambda e, h=h, nq=nq, half=half: e.tensor_copy(
                        out=ltf[half * R:(half + 1) * R,
                                (h // 2) * T + nq * 512:
                                (h // 2) * T + (nq + 1) * 512],
                        in_=pp0.t[half * R:(half + 1) * R, 0:512]))
                    pp0.consumed(cV)

            pr.vop(lambda e: e.reduce_max(
                out=spc(8), in_=lat[:, :], axis=AX.X,
                apply_absolute_value=True))
            pr.vop(lambda e: e.reduce_max(
                out=spc(9), in_=ltf[:, :], axis=AX.X,
                apply_absolute_value=True))
            pr.wait("tensor", cV)
            k = pp0.produce([
                lambda e: e.transpose(pp0.t[0:1, 0:P], sp[:, 8:9],
                                      idn_f[:, :]),
                lambda e: e.transpose(pp0.t[0:1, 256:256 + P], sp[:, 9:10],
                                      idn_f[:, :])])
            pr.wait("vector", pp0.s, k)
            pr.vop(lambda e: e.tensor_copy(
                out=scrf[0:1, 0:P], in_=pp0.t[0:1, 0:P]))
            pr.vop(lambda e: e.tensor_copy(
                out=scrf[0:1, P:2 * P], in_=pp0.t[0:1, 256:256 + P]))
            pp0.consumed(cV)
            pr.vop(lambda e: e.reduce_max(
                out=scrf[0:1, 300:301], in_=scrf[0:1, 0:P], axis=AX.X))
            pr.vop(lambda e: e.reduce_max(
                out=scrf[0:1, 301:302], in_=scrf[0:1, P:2 * P], axis=AX.X))
            pr.vop(lambda e: e.tensor_mul(
                scrf[0:1, 302:303], scrf[0:1, 300:301], scrf[0:1, 301:302]))
            pr.vop(lambda e: e.tensor_scalar_mul(
                scrf[0:1, 302:303], scrf[0:1, 302:303], -8.0))
            pr.wait("tensor", cV)
            k = pp0.produce([lambda e: e.matmul(
                pp0.t[:, 512:513], ones_rf[0:1, :], scrf[0:1, 302:303],
                start=True, stop=True)])
            pr.wait("vector", pp0.s, k)
            pr.vop(lambda e: e.tensor_copy(out=spc(12), in_=pp0.t[:, 512:513]))
            pp0.consumed(cV)
            negc = spc(12)

            pr.vop(lambda e: e.memset(
                vaug[:, :].rearrange("p (u x) -> p u x", x=65)[:, :, 64:65],
                1.0))
            for ut in range(NT):
                for half in range(2):
                    emits = []
                    for c in range(DC):
                        emits.append(lambda e, ut=ut, half=half, c=c: e.matmul(
                            pp0.t[:, 0:512],
                            xlnT[:, c * T + ut * P: c * T + (ut + 1) * P],
                            vwT[:, c * D + half * 512:
                                c * D + (half + 1) * 512],
                            start=(c == 0), stop=(c == DC - 1)))
                    pr.wait("tensor", cV)
                    k = pp0.produce(emits)
                    pr.wait("vector", pp0.s, k)
                    pr.vop(lambda e, ut=ut, half=half: e.tensor_copy(
                        out=vaug[:, ut * NH * 65 + half * 8 * 65:
                                 ut * NH * 65 + (half * 8 + 8) * 65]
                        .rearrange("p (u x) -> p u x", x=65)[:, :, 0:64],
                        in_=pp0.t[:, 0:512]))
                    pp0.consumed(cV)
            barrier()
            st1a.close()
            sm0.close()

            sm1 = ExitStack()
            ps_s = PBuf(sm1.enter_context(
                nc.psum_tensor("ps_s", [P, 1024], F32)), pm[0])
            ps_v = sm1.enter_context(nc.psum_tensor("ps_v", [P, 1024], F32))
            ps_r = PBuf(sm1.enter_context(
                nc.psum_tensor("ps_r", [P, 1024], F32)), pm[2])
            pAV = pm[1]
            pT_free = [0, 0]
            psv_free = []
            dstage_free = [0, 0]
            for h in range(NH):
                half = h % 2
                for qc in range(2):
                    nut = 8 * (qc + 1)
                    for ut in range(nut):
                        qlo = qc * 1024
                        col0 = max(0, ut * P - qlo)
                        diag = ut >= 8 * qc
                        emits = []
                        a = col0
                        while a < 1024:
                            b2 = min(1024, (a // 512 + 1) * 512)
                            dmask = diag and a <= col0 < b2
                            emits.append(lambda e, h=h, qc=qc, ut=ut,
                                         half=half, a=a, b2=b2, dmask=dmask:
                                         e.matmul(
                                             ps_s.t[:, a:b2],
                                             lat[half * R:(half + 1) * R,
                                                 ut * P:(ut + 1) * P],
                                             ltf[half * R:(half + 1) * R,
                                                 (h // 2) * T + qc * 1024 + a:
                                                 (h // 2) * T + qc * 1024 + b2],
                                             start=True, stop=not dmask,
                                             tile_position=(half * R, 0)))
                            if dmask:
                                emits.append(lambda e, col0=col0: e.matmul(
                                    ps_s.t[:, col0:col0 + P],
                                    nbigI[:, :], ulow[:, :],
                                    start=False, stop=True))
                            a = b2
                        pr.wait("tensor", cG, masks_g)
                        sk = ps_s.produce(emits)
                        pb = pTb[ut % 2]
                        mz_v = 0
                        if col0 > 0:
                            pr.wait("vector", pAV, pT_free[ut % 2])
                            mz_v = pr.vop(lambda e, pb=pb, col0=col0:
                                          e.memset(pb[:, 0:col0], 0.0))
                        pr.wait("scalar", ps_s.s, sk)
                        pr.wait("scalar", pAV, pT_free[ut % 2])
                        pr.sop(lambda e, pb=pb, col0=col0: e.activation(
                            out=pb[:, col0:1024], in_=ps_s.t[:, col0:1024],
                            func=AF.Exp, bias=negc, scale=0.125))
                        ps_s.consumed(cS)
                        exp_v = cS.n
                        pr.wait("tensor", cS, exp_v)
                        if mz_v:
                            pr.wait("tensor", cV, mz_v)
                        if ut == 0:
                            for (c_, v_) in psv_free:
                                pr.wait("tensor", c_, v_)
                            psv_free = []
                        for hq in range(2):
                            pr.op("tensor",
                                  lambda e, h=h, ut=ut, hq=hq, pb=pb,
                                  st=(ut == 0), sp_=(ut == nut - 1):
                                  e.matmul(
                                      ps_v[:65, hq * 512:(hq + 1) * 512],
                                      vaug[:, ut * NH * 65 + h * 65:
                                           ut * NH * 65 + h * 65 + 65],
                                      pb[:, hq * 512:(hq + 1) * 512],
                                      start=st, stop=sp_), inc=pAV)
                        pT_free[ut % 2] = pAV.n
                    av_k = pAV.n
                    stg = (h * 2 + qc) % 2
                    pr.wait("scalar", pAV, av_k)
                    pr.wait("scalar", s_in)
                    pr.sop(lambda e, stg=stg: e.copy(
                        out=dstage_ap(stg), in_=ps_v[64:65, 0:1024]))
                    dcp_v = cS.n
                    pr.wait("sync", cS, dcp_v)
                    pr.dma("sync", dsb_ap(h, qc), dstage_ap(stg), s_in)
                    pr.wait("vector", pAV, av_k)
                    pr.vop(lambda e, h=h, qc=qc, half=half: e.tensor_copy(
                        out=yT[half * R:(half + 1) * R,
                               (h // 2) * T + qc * 1024:
                               (h // 2) * T + (qc + 1) * 1024],
                        in_=ps_v[0:64, 0:1024]))
                    psv_free = [(cV, cV.n), (cS, dcp_v)]
            pr.wait("vector", cS)
            pr.wait("vector", s_in)
            pr.vop(lambda e: e.reciprocal(
                out=scrf[32:64, 0:1024], in_=scrf[32:64, 0:1024]))
            pr.vop(lambda e: e.tensor_copy(
                out=scr[32:64, 2048:3072], in_=scrf[32:64, 0:1024]))
            pr.vop(lambda e: e.tensor_sub(
                scrf[32:64, 0:1024], scrf[32:64, 0:1024],
                scr[32:64, 2048:3072]))
            pr.vop(lambda e: e.tensor_copy(
                out=scr[32:64, 3072:4096], in_=scrf[32:64, 0:1024]))
            for h in range(NH):
                half = h % 2
                for qc in range(2):
                    pr.wait("sync", cV)
                    pr.wait("sync", cS)
                    pr.wait("sync", pAV, pAV.n)
                    pr.dma("sync", pTb[0][0:1, 0:1024], rhi_ap(h, qc), s_in)
                    pr.dma("sync", pTb[1][0:1, 0:1024], rlo_ap(h, qc), s_in)
                    pr.wait("tensor", s_in)
                    emits = []
                    for hq in range(2):
                        emits.append(lambda e, hq=hq: e.matmul(
                            ps_r.t[0:64, hq * 512:(hq + 1) * 512],
                            ones_r[0:1, 0:64],
                            pTb[0][0:1, hq * 512:(hq + 1) * 512],
                            start=True, stop=False))
                        emits.append(lambda e, hq=hq: e.matmul(
                            ps_r.t[0:64, hq * 512:(hq + 1) * 512],
                            ones_r[0:1, 0:64],
                            pTb[1][0:1, hq * 512:(hq + 1) * 512],
                            start=False, stop=True))
                    k = ps_r.produce(emits)
                    pr.wait("vector", ps_r.s, k)
                    pr.vop(lambda e, h=h, qc=qc, half=half: e.tensor_tensor(
                        out=yT[half * R:(half + 1) * R,
                               (h // 2) * T + qc * 1024:
                               (h // 2) * T + (qc + 1) * 1024],
                        in0=yT[half * R:(half + 1) * R,
                               (h // 2) * T + qc * 1024:
                               (h // 2) * T + (qc + 1) * 1024],
                        in1=ps_r.t[0:64, 0:1024], op=OP.mult))
                    ps_r.consumed(cV)
            barrier()
            sm1.close()

            sm2 = ExitStack()
            po = [PBuf(sm2.enter_context(
                nc.psum_tensor(f"po{i}", [P, 1024], F32)), pm[3 + i])
                for i in range(2)]
            for t in range(NT):
                b = t % 2
                pr.wait("sync", cV)
                pr.wait("sync", s_out)
                pr.dma("sync", xrs[b][:, :], x0_d[t * P:(t + 1) * P, :], s_in)
                emits = []
                for c in range(DC):
                    for nn in range(2):
                        emits.append(lambda e, t=t, c=c, b=b, nn=nn: e.matmul(
                            po[b].t[:, nn * 512:(nn + 1) * 512],
                            yT[:, c * T + t * P:c * T + (t + 1) * P],
                            owT[:, c * D + nn * 512: c * D + (nn + 1) * 512],
                            start=(c == 0), stop=False))
                for nn in range(2):
                    emits.append(lambda e, b=b, nn=nn: e.matmul(
                        po[b].t[:, nn * 512:(nn + 1) * 512],
                        ones_r[0:1, :],
                        obias[0:1, nn * 512:(nn + 1) * 512],
                        start=False, stop=True))
                k = po[b].produce(emits)
                pr.wait("vector", po[b].s, k)
                pr.wait("vector", s_in)
                pr.vop(lambda e, b=b: e.tensor_add(
                    xrs[b][:, :], po[b].t[:, 0:D], xrs[b][:, :]))
                po[b].consumed(cV)
                pr.wait("sync", cV)
                pr.dma("sync", x1_d[t * P:(t + 1) * P, :], xrs[b][:, :], s_out)
            pr.wait("sync", s_out)
            barrier()
            sm2.close()
            st1.close()
        st0.close()
        if nphases < 2:
            sm0.close()

        pr.phase = "p0b_wquant"
        st2 = ExitStack()
        w1q = w2q = None
        if nphases >= 3:
            w1q = st2.enter_context(nc.sbuf_tensor("w1q", [P, DC * HID], BF16))
            w2q = st2.enter_context(nc.sbuf_tensor("w2q", [P, HC * D], BF16))
            st2t = ExitStack()
            wf2 = [st2t.enter_context(nc.sbuf_tensor(f"wf{i}", [P, HID], F32))
                   for i in range(2)]
            w8 = st2t.enter_context(nc.sbuf_tensor("w8", [P, HID], I8))
            sm2b = ExitStack()
            tb2 = [PBuf(sm2b.enter_context(
                nc.psum_tensor(f"tb2{i}", [P, 1024], BF16)), pm[i])
                for i in range(2)]
            pq2 = PBuf(sm2b.enter_context(
                nc.psum_tensor("pq2", [P, 512], F32)), pm[2])
            tp2 = TrPool(tb2)

            def quant_weight(w_hbm, nrt, ncc, wdst, rows, im):
                W = ncc * P
                pr.vop(lambda e: e.memset(spc(20), 0.0))
                act_done = {}
                for rt in range(nrt):
                    b = rt % 2
                    if rt >= 2:
                        pr.wait("sync", cS, act_done[rt - 2])
                    else:
                        pr.wait("sync", cS)
                    pr.dma("sync", wf2[b][:, :W],
                           w_hbm[rt * P:(rt + 1) * P, :], s_in)
                    d = s_in.n
                    pr.wait("scalar", s_in, d)
                    pr.wait("scalar", cV)
                    pr.sop(lambda e, b=b: e.activation(
                        out=scr[:, :W], in_=wf2[b][:, :W],
                        func=AF.Abs, accum_out=spc(16)))
                    act_done[rt] = cS.n
                    pr.wait("vector", cS)
                    pr.vop(lambda e: e.tensor_add(spc(20), spc(20), spc(16)))
                pr.wait("tensor", cV)
                pr.wait("tensor", cG, consts_g)
                k = pq2.produce([lambda e: e.matmul(
                    pq2.t[0:1, 0:1], sp[:, 33:34], spc(20),
                    start=True, stop=True)])
                pr.wait("vector", pq2.s, k)
                pr.vop(lambda e: e.tensor_copy(
                    out=scrf[0:1, 300:301], in_=pq2.t[0:1, 0:1]))
                pq2.consumed(cV)
                pr.vop(lambda e: e.tensor_scalar_mul(
                    scrf[0:1, 300:301], scrf[0:1, 300:301],
                    1.0 / (nrt * P * W)))
                pr.vop(lambda e: e.tensor_scalar_max(
                    scrf[0:1, 300:301], scrf[0:1, 300:301], 1e-5))
                pr.vop(lambda e: e.reciprocal(
                    out=scrf[0:1, 301:302], in_=scrf[0:1, 300:301]))
                pr.wait("tensor", cV)
                k = pq2.produce([
                    lambda e: e.matmul(pq2.t[:, 1:2], ones_rf[0:1, :],
                                       scrf[0:1, 300:301],
                                       start=True, stop=True),
                    lambda e: e.matmul(pq2.t[:, 2:3], ones_rf[0:1, :],
                                       scrf[0:1, 301:302],
                                       start=True, stop=True)])
                pr.wait("vector", pq2.s, k)
                pr.vop(lambda e, im=im: e.tensor_copy(
                    out=spc(im), in_=pq2.t[:, 1:2]))
                pr.vop(lambda e: e.tensor_copy(out=spc(18), in_=pq2.t[:, 2:3]))
                pq2.consumed(cV)
                q_done = {}
                evac_done = {}
                for rt in range(nrt):
                    b = rt % 2
                    if rt >= 2:
                        pr.wait("sync", cS, q_done[rt - 2])
                    else:
                        pr.wait("sync", cS, act_done[nrt - 2 + rt])
                    pr.dma("sync", wf2[b][:, :W],
                           w_hbm[rt * P:(rt + 1) * P, :], s_in)
                    d = s_in.n
                    pr.wait("scalar", s_in, d)
                    if rt == 0:
                        pr.wait("scalar", cV)
                    else:
                        pr.wait("scalar", cV, evac_done[rt - 1])
                    pr.sop(lambda e, b=b: e.activation(
                        out=w8[:, :W], in_=wf2[b][:, :W],
                        func=AF.Copy, scale=spc(18)))
                    pr.sop(lambda e: e.activation(
                        out=scr[:, :W], in_=w8[:, :W], func=AF.Sign))
                    pr.sop(lambda e: e.activation(
                        out=sp[:, 61:62], in_=scr[:, W - 1:W], func=AF.Copy))
                    sign_v = cS.n
                    q_done[rt] = cS.n
                    for c in range(ncc):
                        tp2.tr(scr[:, c * P:(c + 1) * P],
                               wdst[:, c * rows + rt * P:
                                    c * rows + (rt + 1) * P],
                               src_ctr=cS, src_val=sign_v)
                    evac_done[rt] = cV.n

            quant_weight(w1_d, HID // P, DC, w1q, HID, 21)
            quant_weight(w2_d, D // P, HC, w2q, D, 23)
            pr.vop(lambda e: e.reciprocal(out=spc(24), in_=spc(21)))
            pr.vop(lambda e: e.reciprocal(out=spc(25), in_=spc(23)))
            pr.wait("sync", cV)
            pr.dma("sync", wq1_d, w1q[:, :], s_out)
            pr.dma("sync", wq2_d, w2q[:, :], s_out)
            barrier()
            sm2b.close()
            st2t.close()

        def emit_ffn(xin_d, xout_d, w1qt, w2qt, stf, tag=""):
            xq8 = stf.enter_context(
                nc.sbuf_tensor(tag + "xq8", [P, NT * D], I8))
            xqT = stf.enter_context(
                nc.sbuf_tensor(tag + "xqT", [P, DC * P], BF16))
            hti = stf.enter_context(nc.sbuf_tensor(tag + "hti", [P, HID], F32))
            hq8 = stf.enter_context(nc.sbuf_tensor(tag + "hq8", [P, HID], I8))
            hqT = stf.enter_context(
                nc.sbuf_tensor(tag + "hqT", [P, HC * P], BF16))
            sxc = stf.enter_context(nc.sbuf_tensor(tag + "sxc", [P, NT], F32))
            exc = stf.enter_context(nc.sbuf_tensor(tag + "exc", [P, NT], F32))
            e2c = stf.enter_context(nc.sbuf_tensor(tag + "e2c", [P, 2], F32))
            sbB = stf.enter_context(nc.sbuf_tensor(tag + "sbB", [1, P], BF16))
            s2B = stf.enter_context(nc.sbuf_tensor(tag + "s2B", [1, P], BF16))
            xl2 = stf.enter_context(nc.sbuf_tensor(tag + "xl2", [P, D], F32))
            smf = ExitStack()
            tbf = [PBuf(smf.enter_context(
                nc.psum_tensor(tag + f"tbf{i}", [P, 1024], BF16)), pm[i])
                for i in range(2)]
            hp = [PBuf(smf.enter_context(
                nc.psum_tensor(tag + f"hpf{i}", [P, 512], F32)), pm[2 + i])
                for i in range(2)]
            pof = PBuf(smf.enter_context(
                nc.psum_tensor(tag + "poff", [P, 1024], F32)), pm[4])
            ptr = PBuf(smf.enter_context(
                nc.psum_tensor(tag + "ptrf", [P, 512], F32)), pm[5])
            tpf = TrPool(tbf)

            for t in range(NT):
                b = t % 2
                pr.wait("sync", cS)
                pr.wait("sync", cV)
                pr.dma("sync", xin[b][:, :], xin_d[t * P:(t + 1) * P, :], s_in)
                pr.wait("vector", s_in)
                pr.wait("scalar", s_in)
                ln_stats(xin[b], D)
                pr.wait("vector", cS)
                pr.vop(lambda e, b=b: e.scalar_tensor_tensor(
                    out=xl2[:, :], in0=xin[b][:, :], scalar=spc(0),
                    in1=spc(2).broadcast_to([P, D]),
                    op0=OP.subtract, op1=OP.mult))
                pr.vop(lambda e: e.tensor_mul(spc(5), spc(1), spc(2)))
                pr.vop(lambda e: e.tensor_mul(spc(5), spc(5), spc(2)))
                pr.wait("scalar", cV)
                pr.sop(lambda e: e.activation(
                    out=spc(6), in_=spc(5), func=AF.Sqrt, bias=spc(31)))
                pr.wait("vector", cS)
                pr.vop(lambda e: e.reciprocal(out=spc(6), in_=spc(6)))
                pr.vop(lambda e: e.reduce_max(
                    out=spc(7), in_=xl2[:, :], axis=AX.X,
                    apply_absolute_value=True))
                pr.vop(lambda e: e.tensor_mul(spc(7), spc(7), spc(6)))
                pr.vop(lambda e: e.tensor_scalar_max(spc(7), spc(7), 1e-5))
                pr.vop(lambda e: e.reciprocal(out=spc(8), in_=spc(7)))
                pr.vop(lambda e: e.tensor_scalar_mul(spc(8), spc(8), 127.0))
                pr.vop(lambda e: e.tensor_mul(spc(9), spc(8), spc(6)))
                pr.vop(lambda e, t=t: e.tensor_scalar_mul(
                    exc[:, t:t + 1], spc(7), 1.0 / 127.0))
                pr.vop(lambda e, t=t: e.tensor_mul(
                    exc[:, t:t + 1], exc[:, t:t + 1], spc(21)))
                pr.vop(lambda e, t=t: e.tensor_mul(
                    sxc[:, t:t + 1], spc(8), spc(24)))
                pr.wait("scalar", cV)
                pr.sop(lambda e, t=t: e.activation(
                    out=xq8[:, t * D:(t + 1) * D], in_=xl2[:, :],
                    func=AF.Copy, scale=spc(9)))

            for t in range(NT):
                b = t % 2
                pr.wait("vector", cS)
                pr.vop(lambda e, t=t: e.tensor_copy(
                    out=scr[:, 0:D], in_=xq8[:, t * D:(t + 1) * D]))
                pr.vop(lambda e: e.tensor_copy(
                    out=sp[:, 63:64], in_=scr[:, D - 1:D]))
                cp_v = cV.n
                for c in range(DC):
                    tpf.tr(scr[:, c * P:(c + 1) * P],
                           xqT[:, c * P:(c + 1) * P],
                           src_ctr=cV, src_val=cp_v)
                xqT_ready = cV.n
                pr.wait("tensor", cV, xqT_ready)
                k = ptr.produce([lambda e, t=t: e.transpose(
                    ptr.t[0:1, :P], sxc[:, t:t + 1], idn_f[:, :])])
                pr.wait("vector", ptr.s, k)
                pr.vop(lambda e: e.tensor_copy(
                    out=sbB[0:1, :], in_=ptr.t[0:1, :P]))
                ptr.consumed(cV)
                sb_ready = cV.n
                for n in range(8):
                    pb = n % 2
                    pr.wait("tensor", cV, max(xqT_ready, sb_ready))
                    emits = []
                    for c in range(DC):
                        emits.append(lambda e, n=n, c=c, pb=pb: e.matmul(
                            hp[pb].t[:, :],
                            xqT[:, c * P:(c + 1) * P],
                            w1qt[:, c * HID + n * 512:
                                 c * HID + (n + 1) * 512],
                            start=(c == 0), stop=False))
                    emits.append(lambda e, n=n, pb=pb: e.matmul(
                        hp[pb].t[:, :], sbB[0:1, :],
                        b1row[0:1, n * 512:(n + 1) * 512],
                        start=False, stop=True))
                    k = hp[pb].produce(emits)
                    pr.wait("scalar", hp[pb].s, k)
                    pr.sop(lambda e, n=n, pb=pb, t=t: e.activation(
                        out=hti[:, n * 512:(n + 1) * 512], in_=hp[pb].t[:, :],
                        func=AF.Gelu, scale=exc[:, t:t + 1]))
                    hp[pb].consumed(cS)
                pr.sop(lambda e: e.activation(
                    out=sp[:, 61:62], in_=hti[:, HID - 1:HID], func=AF.Copy))
                pr.wait("vector", cS)
                pr.vop(lambda e: e.tensor_copy(
                    out=sp[:, 62:63], in_=sp[:, 61:62]))
                pr.vop(lambda e: e.reduce_sum(
                    out=spc(3), in_=hti[:, :], axis=AX.X))
                pr.wait("scalar", cV)
                pr.sop(lambda e: e.activation(
                    out=scr[:, :], in_=hti[:, :], func=AF.Square,
                    accum_out=spc(4)))
                pr.sop(lambda e: e.activation(
                    out=sp[:, 61:62], in_=spc(4), func=AF.Copy))
                pr.wait("vector", cS)
                pr.vop(lambda e: e.tensor_copy(
                    out=sp[:, 62:63], in_=sp[:, 61:62]))
                pr.vop(lambda e: e.tensor_scalar_mul(spc(5), spc(4), 1.0 / HID))
                pr.wait("scalar", cV)
                pr.sop(lambda e: e.activation(
                    out=spc(6), in_=spc(5), func=AF.Sqrt, bias=spc(31)))
                pr.wait("vector", cS)
                pr.vop(lambda e: e.reciprocal(out=spc(6), in_=spc(6)))
                pr.vop(lambda e: e.reduce_max(
                    out=spc(7), in_=hti[:, :], axis=AX.X,
                    apply_absolute_value=True))
                pr.vop(lambda e: e.tensor_mul(spc(7), spc(7), spc(6)))
                pr.vop(lambda e: e.tensor_scalar_max(spc(7), spc(7), 1e-5))
                pr.vop(lambda e: e.reciprocal(out=spc(8), in_=spc(7)))
                pr.vop(lambda e: e.tensor_scalar_mul(spc(8), spc(8), 127.0))
                pr.vop(lambda e: e.tensor_mul(spc(9), spc(8), spc(6)))
                pr.vop(lambda e, b=b: e.tensor_scalar_mul(
                    e2c[:, b:b + 1], spc(7), 1.0 / 127.0))
                pr.vop(lambda e, b=b: e.tensor_mul(
                    e2c[:, b:b + 1], e2c[:, b:b + 1], spc(23)))
                pr.vop(lambda e: e.tensor_mul(spc(11), spc(8), spc(25)))
                pr.wait("scalar", cV)
                pr.sop(lambda e: e.activation(
                    out=hq8[:, :], in_=hti[:, :], func=AF.Copy, scale=spc(9)))
                pr.sop(lambda e: e.activation(
                    out=sp[:, 61:62], in_=hq8[:, HID - 1:HID], func=AF.Copy))
                pr.wait("tensor", cV)
                k = ptr.produce([lambda e: e.transpose(
                    ptr.t[0:1, :P], sp[:, 11:12], idn_f[:, :])])
                pr.wait("vector", ptr.s, k)
                pr.vop(lambda e: e.tensor_copy(
                    out=s2B[0:1, :], in_=ptr.t[0:1, :P]))
                ptr.consumed(cV)
                pr.wait("vector", cS)
                pr.vop(lambda e: e.tensor_copy(
                    out=sp[:, 62:63], in_=sp[:, 61:62]))
                pr.vop(lambda e: e.tensor_copy(out=scr[:, 0:HID], in_=hq8[:, :]))
                pr.vop(lambda e: e.tensor_copy(
                    out=sp[:, 63:64], in_=scr[:, HID - 1:HID]))
                hq_v = cV.n
                for c in range(HC):
                    tpf.tr(scr[:, c * P:(c + 1) * P],
                           hqT[:, c * P:(c + 1) * P],
                           src_ctr=cV, src_val=hq_v)
                hqT_ready = cV.n
                pr.wait("sync", cV)
                pr.wait("sync", cS)
                pr.wait("sync", s_out)
                pr.dma("sync", xin[b][:, :], xin_d[t * P:(t + 1) * P, :], s_in)
                pr.wait("tensor", cV, hqT_ready)
                emits = []
                for n in range(2):
                    for c in range(HC):
                        emits.append(lambda e, n=n, c=c: e.matmul(
                            pof.t[:, n * 512:(n + 1) * 512],
                            hqT[:, c * P:(c + 1) * P],
                            w2qt[:, c * D + n * 512: c * D + (n + 1) * 512],
                            start=(c == 0), stop=False))
                    emits.append(lambda e, n=n: e.matmul(
                        pof.t[:, n * 512:(n + 1) * 512], s2B[0:1, :],
                        b2row[0:1, n * 512:(n + 1) * 512],
                        start=False, stop=True))
                k = pof.produce(emits)
                pr.wait("vector", pof.s, k)
                pr.wait("vector", s_in)
                pr.vop(lambda e, b=b: e.scalar_tensor_tensor(
                    out=xin[b][:, :], in0=pof.t[:, 0:D],
                    scalar=e2c[:, b:b + 1],
                    in1=xin[b][:, :], op0=OP.mult, op1=OP.add))
                pof.consumed(cV)
                pr.wait("sync", cV)
                pr.dma("sync", xout_d[t * P:(t + 1) * P, :], xin[b][:, :],
                       s_out)
            pr.wait("sync", s_out)
            if dbg == "q2" and tag == "a":
                pr.wait("sync", cV)
                pr.wait("sync", cS)
                fl = out_d.flatten()
                pr.dma("sync", fl[0:P * 4096].rearrange("(p n) -> p n", p=P),
                       xq8.bitcast(F32)[:, :], s_out)
                pr.dma("sync",
                       fl[P * 4096:2 * P * 4096].rearrange("(p n) -> p n", p=P),
                       hti[:, :], s_out)
                pr.dma("sync",
                       fl[2 * P * 4096:2 * P * 4096 + 2048]
                       .rearrange("(p n) -> p n", p=1),
                       b1row.bitcast(F32)[0:1, :], s_out)
                pr.dma("sync",
                       fl[2 * P * 4096 + 2048:2 * P * 4096 + 2048 + P * 16]
                       .rearrange("(p n) -> p n", p=P),
                       exc[:, :], s_out)
                pr.dma("sync",
                       fl[2 * P * 4096 + 2048 + P * 16:
                          2 * P * 4096 + 2048 + P * 16 + P * 16]
                       .rearrange("(p n) -> p n", p=P),
                       sxc[:, :], s_out)
            barrier()
            smf.close()

        pr.phase = "p2_ffn1"
        if nphases >= 4:
            stf1 = ExitStack()
            emit_ffn(x1_d, x2_d, w1q, w2q, stf1, tag="a")
            stf1.close()
        st2.close()

        pr.phase = "p2b_a2a"
        if nphases >= 5:
            pr.wait("gpsimd", s_out)

            def _cc(e):
                with allow_ici_all_to_all():
                    return e.collective_compute(
                        "AllToAll", OP.bypass,
                        replica_groups=[list(range(8))],
                        ins=[x2_d], outs=[x2t_d])
            pr.op("gpsimd", _cc, inc=s_cc)

        pr.phase = "p3a_attn_b"
        if nphases >= 5:
            st3 = ExitStack()
            wkT = st3.enter_context(nc.sbuf_tensor("wkT", [P, DC * D], BF16))
            wvT = st3.enter_context(nc.sbuf_tensor("wvT", [P, DC * D], BF16))
            wqT = st3.enter_context(nc.sbuf_tensor("wqT", [P, DC * D], BF16))
            wtm3 = [st3.enter_context(nc.sbuf_tensor(f"wtm3{i}", [P, D], BF16))
                    for i in range(2)]
            S2 = [st3.enter_context(nc.sbuf_tensor(f"S2{i}", [P, 8192], F32))
                  for i in range(2)]
            prod = st3.enter_context(nc.sbuf_tensor("prod", [P, 8192], F32))
            kk = st3.enter_context(nc.sbuf_tensor("kk", [P, D], F32))
            vv = st3.enter_context(nc.sbuf_tensor("vv", [P, D], F32))
            qq = st3.enter_context(nc.sbuf_tensor("qq", [P, D], F32))
            kn = st3.enter_context(nc.sbuf_tensor("kn", [P, D], F32))
            rt3 = st3.enter_context(nc.sbuf_tensor("rt3", [P, D], F32))
            sq3 = st3.enter_context(nc.sbuf_tensor("sq3", [P, D], F32))
            xl3 = st3.enter_context(nc.sbuf_tensor("xl3", [P, D], BF16))
            x3T = st3.enter_context(nc.sbuf_tensor("x3T", [P, DC * P], BF16))
            sm3 = ExitStack()
            tb3 = [PBuf(sm3.enter_context(
                nc.psum_tensor(f"tb3{i}", [P, 1024], BF16)), pm[i])
                for i in range(2)]
            pq3 = [PBuf(sm3.enter_context(
                nc.psum_tensor(f"pq3{i}", [P, 512], F32)), pm[2 + i])
                for i in range(2)]
            tp3 = TrPool(tb3)
            build_fm(wkT, wk_d, D, D, tp3, wtm3)
            build_fm(wvT, wv_d, D, D, tp3, wtm3)
            build_fm(wqT, wqd_d, D, D, tp3, wtm3)

            pr.vop(lambda e: e.memset(S2[0][:, :], 0.0))
            pr.vop(lambda e: e.memset(S2[1][:, :], 0.0))
            pr.wait("sync", s_cc, 1)

            def S4(S):
                return S[:, :].rearrange("p (n i j) -> p n i j", i=8, j=8)

            def prod4():
                return prod[:, :].rearrange("p (n i j) -> p n i j", i=8, j=8)

            for t in range(B):
                for bt in range(2):
                    row0 = t * BS + bt * P
                    pr.wait("sync", cV)
                    pr.wait("sync", s_out)
                    pr.dma("sync", xin[bt][:, :], x2t_d[row0:row0 + P, :],
                           s_in)
                    pr.wait("vector", s_in)
                    pr.wait("scalar", s_in)
                    ln_stats(xin[bt], D)
                    pr.vop(lambda e, bt=bt: e.scalar_tensor_tensor(
                        out=xl3[:, :], in0=xin[bt][:, :], scalar=spc(0),
                        in1=spc(2).broadcast_to([P, D]),
                        op0=OP.subtract, op1=OP.mult))
                    xl3_v = cV.n
                    for c in range(DC):
                        tp3.tr(xl3[:, c * P:(c + 1) * P],
                               x3T[:, c * P:(c + 1) * P],
                               src_ctr=cV, src_val=xl3_v)
                    xT_ready = cV.n
                    for wi, (wT, dst) in enumerate(
                            ((wkT, kk), (wvT, vv), (wqT, qq))):
                        for n in range(2):
                            pb = (wi * 2 + n) % 2
                            pr.wait("tensor", cV, xT_ready)
                            emits = []
                            for c in range(DC):
                                emits.append(
                                    lambda e, wT=wT, n=n, c=c, pb=pb:
                                    e.matmul(
                                        pq3[pb].t[:, :],
                                        x3T[:, c * P:(c + 1) * P],
                                        wT[:, c * D + n * 512:
                                           c * D + (n + 1) * 512],
                                        start=(c == 0), stop=(c == DC - 1)))
                            k = pq3[pb].produce(emits)
                            pr.wait("vector", pq3[pb].s, k)
                            pr.wait("vector", cS)
                            pr.vop(lambda e, dst=dst, n=n, pb=pb:
                                   e.tensor_copy(
                                       out=dst[:, n * 512:(n + 1) * 512],
                                       in_=pq3[pb].t[:, :]))
                            pq3[pb].consumed(cV)
                    S = S2[bt]
                    pr.wait("scalar", cV)
                    pr.sop(lambda e: e.activation(
                        out=scrf[:, 0:D], in_=kk[:, :], func=AF.Square))
                    pr.wait("vector", cS)
                    pr.vop(lambda e: e.reduce_sum(
                        out=kn[:, 0:P],
                        in_=scrf[:, 0:D].rearrange("p (n j) -> p n j", j=8),
                        axis=AX.X))
                    pr.wait("scalar", cV)
                    pr.sop(lambda e: e.activation(
                        out=kn[:, P:2 * P], in_=kn[:, 0:P], func=AF.Sqrt))
                    pr.wait("vector", cS)
                    pr.vop(lambda e: e.tensor_scalar_add(
                        kn[:, P:2 * P], kn[:, P:2 * P], 1e-6))
                    pr.vop(lambda e: e.reciprocal(
                        out=kn[:, 0:P], in_=kn[:, P:2 * P]))
                    pr.vop(lambda e: e.tensor_tensor(
                        out=rt3[:, :], in0=kk[:, :],
                        in1=kn[:, 0:P].rearrange("p (n o) -> p n o", o=1)
                        .broadcast_to([P, P, 8]),
                        op=OP.mult))
                    kn4i = rt3[:, :].rearrange("p (n j) -> p n j", j=8) \
                        .unsqueeze(2).broadcast_to([P, P, 8, 8])
                    pr.vop(lambda e, S=S: e.tensor_tensor(
                        out=prod4(), in0=S4(S), in1=kn4i, op=OP.mult))
                    pr.vop(lambda e: e.reduce_sum(
                        out=sq3[:, :].rearrange("p (n i) -> p n i", i=8),
                        in_=prod4(), axis=AX.X))
                    pr.vop(lambda e: e.tensor_sub(
                        sq3[:, :], vv[:, :], sq3[:, :]))
                    pr.vop(lambda e: e.tensor_tensor(
                        out=prod4(),
                        in0=sq3[:, :].rearrange("p (n i) -> p n i", i=8)
                        .unsqueeze(3).broadcast_to([P, P, 8, 8]),
                        in1=kn4i, op=OP.mult))
                    pr.vop(lambda e, S=S: e.tensor_add(
                        prod[:, :], S[:, :], prod[:, :]))
                    pr.wait("scalar", cV)
                    pr.sop(lambda e, S=S: e.activation(
                        out=S[:, :], in_=prod[:, :], func=AF.Tanh))
                    pr.wait("vector", cS)
                    pr.vop(lambda e, S=S: e.tensor_tensor(
                        out=prod4(), in0=S4(S),
                        in1=qq[:, :].rearrange("p (n j) -> p n j", j=8)
                        .unsqueeze(2).broadcast_to([P, P, 8, 8]),
                        op=OP.mult))
                    pr.vop(lambda e: e.reduce_sum(
                        out=sq3[:, :].rearrange("p (n i) -> p n i", i=8),
                        in_=prod4(), axis=AX.X))
                    pr.wait("scalar", cV)
                    pr.sop(lambda e: e.activation(
                        out=rt3[:, :], in_=sq3[:, :], func=AF.Sigmoid))
                    pr.sop(lambda e: e.activation(
                        out=kn[:, :], in_=sq3[:, :], func=AF.Square))
                    pr.wait("vector", cS)
                    pr.vop(lambda e: e.tensor_mul(
                        rt3[:, :], kn[:, :], rt3[:, :]))
                    pr.vop(lambda e, bt=bt: e.tensor_add(
                        xin[bt][:, :], xin[bt][:, :], rt3[:, :]))
                    pr.wait("sync", cV)
                    pr.dma("sync", x3_d[row0:row0 + P, :], xin[bt][:, :],
                           s_out)
            pr.wait("sync", s_out)
            barrier()
            sm3.close()
            st3.close()

        pr.phase = "p3b_ffn2"
        if nphases >= 6:
            st4 = ExitStack()
            w1q2 = st4.enter_context(
                nc.sbuf_tensor("w1q2", [P, DC * HID], BF16))
            w2q2 = st4.enter_context(nc.sbuf_tensor("w2q2", [P, HC * D], BF16))
            pr.dma("sync", w1q2[:, :], wq1_d, s_in)
            pr.dma("sync", w2q2[:, :], wq2_d, s_in)
            pr.wait("tensor", s_in)
            emit_ffn(x3_d, out_d, w1q2, w2q2, st4, tag="b")
            st4.close()

        if dbg == "q2":
            pass
        if dbg == "q":
            pr.wait("sync", s_out)
            pr.wait("sync", cV)
            pr.wait("sync", cS)
            pr.dma("sync", out_d[0:P, 0:64], sp[:, :], s_out)
            pr.dma("sync", out_d[P:2 * P, 0:1024],
                   wq1_d.bitcast(F32)[:, 1024:2048], s_out)
            pr.dma("sync", out_d[2 * P:3 * P, 0:1024],
                   wq2_d.bitcast(F32)[:, 0:1024], s_out)
            if w2q is not None:
                pr.dma("sync", out_d[3 * P:4 * P, 0:1024],
                       w2q.bitcast(F32)[:, 0:1024], s_out)
        elif dbg is not None and dbg != "q2":
            dbg_src = {"x1": x1_d, "x2": x2_d, "x2t": x2t_d, "x3": x3_d}[dbg]
            pr.wait("sync", s_out)
            if dbg == "x2t":
                pr.wait("sync", s_cc, 1)
            pr.dma("sync", out_d, dbg_src, s_out)

        pr.wait("sync", s_out)
        pr.wait("gpsimd", s_out)

        with nc.Block() as block:
            pr.flush(block)

    return nc


def kernel(L0_attn_a_hidden_states, L0_attn_a_basis_w, L0_attn_a_core,
           L0_attn_a_head_residual, L0_attn_a_v_w, L0_attn_a_v_b,
           L0_attn_a_o_w, L0_attn_a_o_b, L0_attn_b_W_k, L0_attn_b_W_v,
           L0_attn_b_W_q, L0_ffn_w1, L0_ffn_b1, L0_ffn_w2, L0_ffn_b2):
    hs = np.ascontiguousarray(L0_attn_a_hidden_states, dtype=np.float32)
    nc = build(_NPHASES, _DBG)
    common = {
        "basis_w": np.ascontiguousarray(L0_attn_a_basis_w, np.float32),
        "core_m": np.ascontiguousarray(L0_attn_a_core, np.float32),
        "head_res": np.ascontiguousarray(L0_attn_a_head_residual, np.float32),
        "v_w": np.ascontiguousarray(L0_attn_a_v_w, np.float32),
        "v_b": np.ascontiguousarray(L0_attn_a_v_b, np.float32),
        "o_w": np.ascontiguousarray(L0_attn_a_o_w, np.float32),
        "o_b": np.ascontiguousarray(L0_attn_a_o_b, np.float32),
        "w_k": np.ascontiguousarray(L0_attn_b_W_k, np.float32),
        "w_v": np.ascontiguousarray(L0_attn_b_W_v, np.float32),
        "w_q": np.ascontiguousarray(L0_attn_b_W_q, np.float32),
        "w1": np.ascontiguousarray(L0_ffn_w1, np.float32),
        "b1": np.ascontiguousarray(L0_ffn_b1, np.float32),
        "w2": np.ascontiguousarray(L0_ffn_w2, np.float32),
        "b2": np.ascontiguousarray(L0_ffn_b2, np.float32),
    }
    in_maps = [dict(common, x0=np.ascontiguousarray(hs[i])) for i in range(8)]
    res = run_bass_kernel_spmd(nc, in_maps, core_ids=list(range(8)),
                               trace=_TRACE)
    globals()["_LAST_RES"] = res
    out = np.empty((B, T, D), dtype=np.float32)
    for s in range(8):
        out[:, s * BS:(s + 1) * BS, :] = \
            res.results[s]["out"].reshape(B, BS, D)
    return out


# version 04
# speedup vs baseline: 1.1255x; 1.1062x over previous
import numpy as np
import concourse.bass as bass
import concourse.mybir as mybir
from concourse.bass_utils import run_bass_kernel_spmd
from concourse.aot_env import allow_ici_all_to_all
from contextlib import ExitStack

F32 = mybir.dt.float32
BF16 = mybir.dt.bfloat16
I8 = mybir.dt.int8
AF = mybir.ActivationFunctionType
OP = mybir.AluOpType
AX = mybir.AxisListType

P = 128
T = 2048
NT = 16
D = 1024
DC = 8
HID = 4096
HC = 32
NH = 16
R = 64
B = 8
BS = 256
NEG = -1e30

ENGINES = ("sync", "vector", "scalar", "tensor", "gpsimd")
_NPHASES = 9
_TRACE = False
_LAST_RES = None
_DBG = None


class Ctr:
    def __init__(self, h):
        self.h = h
        self.n = 0


class Prog:
    def __init__(self, nc=None):
        self.q = {e: [] for e in ENGINES}
        self.cV = None
        self.cS = None
        self.nc = nc
        self.phase = "init"

    def _add(self, eng, fn):
        self.q[eng].append((self.phase, fn))

    def wait(self, eng, ctr, val=None):
        v = ctr.n if val is None else val
        if v > 0:
            self._add(eng, lambda e: e.wait_ge(ctr.h, v))

    def dma(self, eng, out, in_, sem, **kw):
        self._add(
            eng,
            lambda e: e.dma_start(out=out, in_=in_, **kw).then_inc(sem.h, 16))
        sem.n += 16
        return sem.n

    def op(self, eng, fn, inc=None, k=1):
        if inc is None:
            self._add(eng, fn)
            return None
        self._add(eng, lambda e: fn(e).then_inc(inc.h, k))
        inc.n += k
        return inc.n

    def vop(self, fn):
        self.wait("vector", self.cV)
        return self.op("vector", fn, inc=self.cV)

    def sop(self, fn):
        self.wait("scalar", self.cS)
        return self.op("scalar", fn, inc=self.cS)

    def barrier(self, ctrs):
        vals = {}
        for e in ENGINES:
            c = ctrs[e]
            self._add(e, lambda en, c=c: en.sem_inc(c.h, 1))
            c.n += 1
            vals[e] = c.n
        for e in ENGINES:
            for o in ENGINES:
                if o != e:
                    c, v = ctrs[o], vals[o]
                    self._add(e, lambda en, c=c, v=v: en.wait_ge(c.h, v))

    def _emit(self, eng, name):
        nc, cur, sid = self.nc, None, None
        for lab, fn in self.q[name]:
            if lab != cur:
                if cur is not None:
                    nc.leave_named_scope(cur, sid, False)
                sid, _ = nc.enter_named_scope(lab, False)
                cur = lab
            fn(eng)
        if cur is not None:
            nc.leave_named_scope(cur, sid, False)

    def flush(self, block):
        @block.sync
        def _(eng):
            self._emit(eng, "sync")

        @block.vector
        def _(eng):
            self._emit(eng, "vector")

        @block.scalar
        def _(eng):
            self._emit(eng, "scalar")

        @block.tensor
        def _(eng):
            self._emit(eng, "tensor")

        @block.gpsimd
        def _(eng):
            self._emit(eng, "gpsimd")


def build(nphases=9, dbg=None):
    nc = bass.Bass("TRN2", target_bir_lowering=False,
                   detect_race_conditions=False)

    x0_d = nc.dram_tensor("x0", [T, D], F32, kind="ExternalInput").ap()
    basis_d = nc.dram_tensor("basis_w", [R, D], F32, kind="ExternalInput").ap()
    core_d = nc.dram_tensor("core_m", [R, R], F32, kind="ExternalInput").ap()
    hres_d = nc.dram_tensor("head_res", [NH, R, R], F32,
                            kind="ExternalInput").ap()
    vw_d = nc.dram_tensor("v_w", [D, D], F32, kind="ExternalInput").ap()
    vb_d = nc.dram_tensor("v_b", [D], F32, kind="ExternalInput").ap()
    ow_d = nc.dram_tensor("o_w", [D, D], F32, kind="ExternalInput").ap()
    ob_d = nc.dram_tensor("o_b", [D], F32, kind="ExternalInput").ap()
    wk_d = nc.dram_tensor("w_k", [D, D], F32, kind="ExternalInput").ap()
    wv_d = nc.dram_tensor("w_v", [D, D], F32, kind="ExternalInput").ap()
    wqd_d = nc.dram_tensor("w_q", [D, D], F32, kind="ExternalInput").ap()
    w1_d = nc.dram_tensor("w1", [HID, D], F32, kind="ExternalInput").ap()
    b1_d = nc.dram_tensor("b1", [HID], F32, kind="ExternalInput").ap()
    w2_d = nc.dram_tensor("w2", [D, HID], F32, kind="ExternalInput").ap()
    b2_d = nc.dram_tensor("b2", [D], F32, kind="ExternalInput").ap()
    out_d = nc.dram_tensor("out", [T, D], F32, kind="ExternalOutput").ap()

    x1_d = nc.dram_tensor("x1_s", [T, D], F32).ap()
    x2_d = nc.dram_tensor("x2_s", [T, D], F32).ap()
    x2t_d = nc.dram_tensor("x2t_s", [T, D], F32).ap()
    x3_d = nc.dram_tensor("x3_s", [T, D], F32).ap()
    wq1_d = nc.dram_tensor("wq1_s", [P, DC * HID], BF16).ap()
    wq2_d = nc.dram_tensor("wq2_s", [P, HC * D], BF16).ap()

    pr = Prog(nc)

    with ExitStack() as stack:
        sem = {}
        for name in ("s_in", "s_out", "s_gd", "s_cc",
                     "c_v", "c_s", "c_t", "c_g", "c_y",
                     "p0", "p1", "p2", "p3", "p4", "p5", "p6"):
            sem[name] = Ctr(stack.enter_context(nc.semaphore(name)))
        s_in, s_out, s_gd = sem["s_in"], sem["s_out"], sem["s_gd"]
        s_cc = sem["s_cc"]
        cV, cS, cT, cG, cY = (sem["c_v"], sem["c_s"], sem["c_t"],
                              sem["c_g"], sem["c_y"])
        pm = [sem[f"p{i}"] for i in range(7)]
        pr.cV, pr.cS = cV, cS
        bar_ctrs = {"sync": cY, "vector": cV, "scalar": cS,
                    "tensor": cT, "gpsimd": cG}

        def barrier():
            pr.barrier(bar_ctrs)

        class PBuf:
            def __init__(self, t, s):
                self.t = t
                self.s = s
                self.free = []

            def produce(self, emits):
                for (c, v) in self.free:
                    pr.wait("tensor", c, v)
                self.free = []
                for i, fn in enumerate(emits):
                    if i == len(emits) - 1:
                        pr.op("tensor", fn, inc=self.s)
                    else:
                        pr.op("tensor", fn)
                return self.s.n

            def consumed(self, ctr, val=None):
                self.free.append((ctr, ctr.n if val is None else val))

        idn_b = stack.enter_context(nc.sbuf_tensor("idn_b", [P, P], BF16))
        idn_f = stack.enter_context(nc.sbuf_tensor("idn_f", [P, P], F32))
        ones_r = stack.enter_context(nc.sbuf_tensor("ones_r", [1, P], BF16))
        ones_rf = stack.enter_context(nc.sbuf_tensor("ones_rf", [1, P], F32))
        b1row = stack.enter_context(nc.sbuf_tensor("b1row", [1, HID], BF16))
        b2row = stack.enter_context(nc.sbuf_tensor("b2row", [1, D], BF16))
        scr = stack.enter_context(nc.sbuf_tensor("scr", [P, HID], BF16))
        sp = stack.enter_context(nc.sbuf_tensor("sp", [P, 64], F32))
        xin = [stack.enter_context(nc.sbuf_tensor(f"xin{i}", [P, D], F32))
               for i in range(2)]

        def spc(i):
            return sp[:, i:i + 1]

        spb = sp.bitcast(BF16)

        scrf = scr.bitcast(F32)

        pr.op("gpsimd", lambda e: e.memset(idn_b[:, :], 0.0), inc=cG)
        pr.wait("gpsimd", cG)
        pr.op("gpsimd", lambda e: e.affine_select(
            out=idn_b[:, :], in_=idn_b[:, :], compare_op=OP.not_equal,
            fill=1.0, base=0, pattern=[[-1, P]], channel_multiplier=1), inc=cG)
        pr.op("gpsimd", lambda e: e.memset(idn_f[:, :], 0.0), inc=cG)
        pr.wait("gpsimd", cG)
        pr.op("gpsimd", lambda e: e.affine_select(
            out=idn_f[:, :], in_=idn_f[:, :], compare_op=OP.not_equal,
            fill=1.0, base=0, pattern=[[-1, P]], channel_multiplier=1), inc=cG)
        pr.op("gpsimd", lambda e: e.memset(ones_r[:, :], 1.0), inc=cG)
        pr.op("gpsimd", lambda e: e.memset(ones_rf[:, :], 1.0), inc=cG)
        pr.op("gpsimd", lambda e: e.memset(sp[:, 33:34], 1.0), inc=cG)
        pr.op("gpsimd", lambda e: e.memset(sp[:, 30:31], 1e-5), inc=cG)
        pr.op("gpsimd", lambda e: e.memset(sp[:, 31:32], 1e-6), inc=cG)
        pr.dma("gpsimd", b1row[:, :], b1_d[None, :], s_gd)
        pr.dma("gpsimd", b2row[:, :], b2_d[None, :], s_gd)
        consts_g = cG.n

        class TrPool:
            def __init__(self, bufs):
                self.b = bufs
                self.i = 0

            def trN(self, src_fn, dst_ap, k, src_ctr=None, src_val=None):
                pb = self.b[self.i % len(self.b)]
                self.i += 1
                if src_ctr is not None:
                    pr.wait("tensor", src_ctr, src_val)
                pr.wait("tensor", cG, consts_g)
                kk = pb.produce([
                    lambda e, j=j, p=pb.t: e.transpose(
                        p[:, j * P:(j + 1) * P], src_fn(j), idn_b[:, :])
                    for j in range(k)])
                pr.wait("vector", pb.s, kk)
                pr.vop(lambda e, p=pb.t, d=dst_ap, k=k: e.tensor_copy(
                    out=d, in_=p[:, 0:k * P]))
                pb.consumed(cV)

            def tr(self, src_ap, dst_ap, rows=P, cols=P,
                   src_ctr=None, src_val=None):
                pb = self.b[self.i % len(self.b)]
                self.i += 1
                if src_ctr is not None:
                    pr.wait("tensor", src_ctr, src_val)
                pr.wait("tensor", cG, consts_g)
                k = pb.produce([lambda e, s=src_ap, p=pb.t, rows=rows,
                                cols=cols: e.transpose(
                                    p[:cols, :rows], s, idn_b[:rows, :rows])])
                pr.wait("vector", pb.s, k)
                pr.vop(lambda e, p=pb.t, d=dst_ap, rows=rows, cols=cols:
                       e.tensor_copy(out=d, in_=p[:cols, :rows]))
                pb.consumed(cV)

        def ln_stats(xt, W):
            pr.vop(lambda e: e.reduce_sum(out=spc(3), in_=xt[:, :W],
                                          axis=AX.X))
            pr.sop(lambda e: e.activation(out=scr[:, :W], in_=xt[:, :W],
                                          func=AF.Square, accum_out=spc(4)))
            pr.wait("vector", cS)
            pr.vop(lambda e: e.tensor_scalar_mul(spc(0), spc(3), 1.0 / W))
            pr.vop(lambda e: e.tensor_scalar_mul(spc(1), spc(4), 1.0 / W))
            pr.vop(lambda e: e.tensor_mul(spc(3), spc(0), spc(0)))
            pr.vop(lambda e: e.tensor_sub(spc(1), spc(1), spc(3)))
            pr.wait("scalar", cV)
            pr.sop(lambda e: e.activation(out=spc(3), in_=spc(1),
                                          func=AF.Sqrt, bias=spc(30)))
            pr.wait("vector", cS)
            pr.vop(lambda e: e.reciprocal(out=spc(2), in_=spc(3)))

        def build_fm(wdst, w_hbm, rows, cols, tp, wtmp):
            nrt, ncc = rows // P, cols // P
            for rt in range(nrt):
                b = rt % 2
                pr.wait("gpsimd", cV)
                v = pr.dma("gpsimd", wtmp[b][:, :cols],
                           w_hbm[rt * P:(rt + 1) * P, :], s_gd)
                pr.wait("tensor", s_gd, v)
                for c in range(ncc):
                    tp.tr(wtmp[b][:, c * P:(c + 1) * P],
                          wdst[:, c * rows + rt * P: c * rows + (rt + 1) * P])

        pr.phase = "p0a_prep"
        st0 = ExitStack()
        vwT = st0.enter_context(nc.sbuf_tensor("vwT", [P, DC * D], BF16))
        owT = st0.enter_context(nc.sbuf_tensor("owT", [P, DC * D], BF16))
        basT = st0.enter_context(nc.sbuf_tensor("basT", [P, DC * R], BF16))
        hm = st0.enter_context(nc.sbuf_tensor("hm", [R, NH * R], F32))
        hm_b = st0.enter_context(nc.sbuf_tensor("hm_b", [R, NH * R], BF16))
        wtmp = [st0.enter_context(nc.sbuf_tensor(f"wtmp{i}", [P, D], BF16))
                for i in range(2)]
        hrbuf = st0.enter_context(nc.sbuf_tensor("hrbuf", [R, NH * R], F32))
        sm0 = ExitStack()
        tb0 = [PBuf(sm0.enter_context(
            nc.psum_tensor(f"tb0{i}", [P, 1024], BF16)), pm[i])
            for i in range(2)]
        pp0 = PBuf(sm0.enter_context(nc.psum_tensor("pp0", [P, 1024], F32)),
                   pm[2])
        tp0 = TrPool(tb0)

        if nphases >= 1:
            build_fm(vwT, vw_d, D, D, tp0, wtmp)
            build_fm(owT, ow_d, D, D, tp0, wtmp)
            pr.wait("gpsimd", cV)
            v = pr.dma("gpsimd", wtmp[0][:R, :D], basis_d, s_gd)
            pr.wait("tensor", s_gd, v)
            for c in range(DC):
                tp0.tr(wtmp[0][:R, c * P:(c + 1) * P],
                       basT[:, c * R:(c + 1) * R], rows=R, cols=P)
            pr.dma("sync", hm[:R, 0:R], core_d, s_in)
            pr.dma("sync",
                   hrbuf[:R, :].rearrange("p (h s) -> p h s", h=NH),
                   hres_d.transpose([1, 0, 2]), s_in)
            pr.wait("tensor", s_in)
            k = pp0.produce([lambda e: e.transpose(
                pp0.t[:R, 0:R], hm[:R, 0:R], idn_f[:R, :R])])
            pr.wait("vector", pp0.s, k)
            pr.wait("vector", s_in)
            pr.vop(lambda e: e.reduce_sum(
                out=scrf[:R, 0:R],
                in_=hrbuf[:R, :].rearrange("p (h s) -> p s h", h=NH),
                axis=AX.X))
            pr.vop(lambda e: e.tensor_add(
                hm[:R, 0:R], hm[:R, 0:R], pp0.t[:R, 0:R]))
            pp0.consumed(cV)
            pr.vop(lambda e: e.tensor_scalar_mul(
                hm[:R, 0:R], hm[:R, 0:R], 0.5 / NH))
            pr.vop(lambda e: e.tensor_scalar_mul(
                scrf[:R, 0:R], scrf[:R, 0:R], 1.0 / NH))
            pr.vop(lambda e: e.tensor_sub(
                scrf[:R, R:2 * R], hm[:R, 0:R], scrf[:R, 0:R]))
            for h in range(NH):
                pr.vop(lambda e, h=h: e.tensor_add(
                    hm[:R, h * R:(h + 1) * R],
                    hrbuf[:R, h * R:(h + 1) * R],
                    scrf[:R, R:2 * R]))
            pr.vop(lambda e: e.tensor_copy(out=hm_b[:R, :], in_=hm[:R, :]))
            barrier()

        pr.phase = "p1_attn_a"
        if nphases >= 2:
            st1 = ExitStack()
            ulow = st1.enter_context(nc.sbuf_tensor("ulow", [P, P], BF16))
            nbigI = st1.enter_context(nc.sbuf_tensor("nbigI", [P, P], BF16))
            obias = st1.enter_context(nc.sbuf_tensor("obias", [1, D], BF16))
            pr.op("gpsimd", lambda e: e.memset(ulow[:, :], 1.0), inc=cG)
            pr.wait("gpsimd", cG)
            pr.op("gpsimd", lambda e: e.affine_select(
                out=ulow[:, :], in_=ulow[:, :], compare_op=OP.is_gt,
                fill=0.0, base=0, pattern=[[-1, P]], channel_multiplier=1),
                inc=cG)
            pr.op("gpsimd", lambda e: e.memset(nbigI[:, :], 0.0), inc=cG)
            pr.wait("gpsimd", cG)
            pr.op("gpsimd", lambda e: e.affine_select(
                out=nbigI[:, :], in_=nbigI[:, :], compare_op=OP.not_equal,
                fill=NEG, base=0, pattern=[[-1, P]], channel_multiplier=1),
                inc=cG)
            masks_g = cG.n
            pr.dma("sync", sp[:, 40:40 + DC],
                   vb_d.rearrange("(c p) -> p c", p=P), s_in,
                   allow_slow_non_contiguous=True)
            pr.dma("sync", scrf[0:1, 1024:1024 + D], ob_d[None, :], s_in)
            pr.wait("vector", s_in)
            pr.vop(lambda e: e.tensor_copy(
                out=wtmp[1][:, 0:DC], in_=sp[:, 40:40 + DC]))
            vb_ready = cV.n
            emits = []
            for c in range(DC):
                for nn in range(2):
                    emits.append(lambda e, c=c, nn=nn: e.matmul(
                        pp0.t[0:1, nn * 512:(nn + 1) * 512],
                        wtmp[1][:, c:c + 1],
                        owT[:, c * D + nn * 512: c * D + (nn + 1) * 512],
                        start=(c == 0), stop=(c == DC - 1)))
            pr.wait("tensor", cV, vb_ready)
            k = pp0.produce(emits)
            pr.wait("vector", pp0.s, k)
            pr.vop(lambda e: e.tensor_add(
                obias[0:1, :], pp0.t[0:1, 0:D], scrf[0:1, 1024:1024 + D]))
            pp0.consumed(cV)

            lat = st1.enter_context(nc.sbuf_tensor("lat", [P, T], BF16))
            ltf = st1.enter_context(nc.sbuf_tensor("ltf", [P, 8 * T], BF16))
            vaug = st1.enter_context(
                nc.sbuf_tensor("vaug", [P, NT * NH * 65], BF16))
            pTb = [st1.enter_context(nc.sbuf_tensor(f"pT{i}", [P, 1024], BF16))
                   for i in range(2)]
            xrs = [st1.enter_context(nc.sbuf_tensor(f"xrs{i}", [P, D], F32))
                   for i in range(2)]
            yT = ltf

            def dstage_ap(i):
                return scrf[0:1, i * 1024:(i + 1) * 1024]

            def dsb_ap(h, qc):
                p0 = 32 + h + 16 * qc
                return scrf[p0:p0 + 1, 0:1024]

            def rhi_ap(h, qc):
                p0 = 32 + h + 16 * qc
                return scr[p0:p0 + 1, 2048:3072]

            def rlo_ap(h, qc):
                p0 = 32 + h + 16 * qc
                return scr[p0:p0 + 1, 3072:4096]

            st1a = ExitStack()
            xlnT = st1a.enter_context(nc.sbuf_tensor("xlnT", [P, DC * T], BF16))
            xlnb = st1a.enter_context(nc.sbuf_tensor("xlnb", [P, D], BF16))

            for t in range(NT):
                b = t % 2
                pr.wait("sync", cV)
                pr.wait("sync", cS)
                pr.dma("sync", xin[b][:, :], x0_d[t * P:(t + 1) * P, :], s_in)
                pr.wait("vector", s_in)
                pr.wait("scalar", s_in)
                ln_stats(xin[b], D)
                pr.vop(lambda e, b=b: e.scalar_tensor_tensor(
                    out=xlnb[:, :], in0=xin[b][:, :], scalar=spc(0),
                    in1=spc(2).broadcast_to([P, D]),
                    op0=OP.subtract, op1=OP.mult))
                stt_v = cV.n
                for c in range(DC):
                    tp0.tr(xlnb[:, c * P:(c + 1) * P],
                           xlnT[:, c * T + t * P: c * T + (t + 1) * P],
                           src_ctr=cV, src_val=stt_v)

            for nq in range(4):
                emits = []
                for c in range(DC):
                    emits.append(lambda e, nq=nq, c=c: e.matmul(
                        pp0.t[:R, 0:512],
                        basT[:, c * R:(c + 1) * R],
                        xlnT[:, c * T + nq * 512: c * T + (nq + 1) * 512],
                        start=(c == 0), stop=(c == DC - 1)))
                pr.wait("tensor", cV)
                k = pp0.produce(emits)
                pr.wait("vector", pp0.s, k)
                pr.vop(lambda e, nq=nq: e.tensor_copy(
                    out=lat[:R, nq * 512:(nq + 1) * 512],
                    in_=pp0.t[:R, 0:512]))
                pp0.consumed(cV)
            pr.wait("sync", cV)
            pr.dma("sync", lat[R:2 * R, :], lat[0:R, :], s_in)

            for h in range(NH):
                half = h % 2
                for nq in range(4):
                    pr.wait("tensor", s_in)
                    k = pp0.produce([lambda e, h=h, nq=nq, half=half: e.matmul(
                        pp0.t[half * R:(half + 1) * R, 0:512],
                        hm_b[:R, h * R:(h + 1) * R],
                        lat[0:R, nq * 512:(nq + 1) * 512],
                        start=True, stop=True,
                        tile_position=(0, half * R))])
                    pr.wait("vector", pp0.s, k)
                    pr.vop(lambda e, h=h, nq=nq, half=half: e.tensor_copy(
                        out=ltf[half * R:(half + 1) * R,
                                (h // 2) * T + nq * 512:
                                (h // 2) * T + (nq + 1) * 512],
                        in_=pp0.t[half * R:(half + 1) * R, 0:512]))
                    pp0.consumed(cV)

            pr.vop(lambda e: e.reduce_max(
                out=spc(8), in_=lat[:, :], axis=AX.X,
                apply_absolute_value=True))
            pr.vop(lambda e: e.reduce_max(
                out=spc(9), in_=ltf[:, :], axis=AX.X,
                apply_absolute_value=True))
            pr.wait("tensor", cV)
            k = pp0.produce([
                lambda e: e.transpose(pp0.t[0:1, 0:P], sp[:, 8:9],
                                      idn_f[:, :]),
                lambda e: e.transpose(pp0.t[0:1, 256:256 + P], sp[:, 9:10],
                                      idn_f[:, :])])
            pr.wait("vector", pp0.s, k)
            pr.vop(lambda e: e.tensor_copy(
                out=scrf[0:1, 0:P], in_=pp0.t[0:1, 0:P]))
            pr.vop(lambda e: e.tensor_copy(
                out=scrf[0:1, P:2 * P], in_=pp0.t[0:1, 256:256 + P]))
            pp0.consumed(cV)
            pr.vop(lambda e: e.reduce_max(
                out=scrf[0:1, 300:301], in_=scrf[0:1, 0:P], axis=AX.X))
            pr.vop(lambda e: e.reduce_max(
                out=scrf[0:1, 301:302], in_=scrf[0:1, P:2 * P], axis=AX.X))
            pr.vop(lambda e: e.tensor_mul(
                scrf[0:1, 302:303], scrf[0:1, 300:301], scrf[0:1, 301:302]))
            pr.vop(lambda e: e.tensor_scalar_mul(
                scrf[0:1, 302:303], scrf[0:1, 302:303], -8.0))
            pr.wait("tensor", cV)
            k = pp0.produce([lambda e: e.matmul(
                pp0.t[:, 512:513], ones_rf[0:1, :], scrf[0:1, 302:303],
                start=True, stop=True)])
            pr.wait("vector", pp0.s, k)
            pr.vop(lambda e: e.tensor_copy(out=spc(12), in_=pp0.t[:, 512:513]))
            pp0.consumed(cV)
            negc = spc(12)

            pr.vop(lambda e: e.memset(
                vaug[:, :].rearrange("p (u x) -> p u x", x=65)[:, :, 64:65],
                1.0))
            for ut in range(NT):
                for half in range(2):
                    emits = []
                    for c in range(DC):
                        emits.append(lambda e, ut=ut, half=half, c=c: e.matmul(
                            pp0.t[:, 0:512],
                            xlnT[:, c * T + ut * P: c * T + (ut + 1) * P],
                            vwT[:, c * D + half * 512:
                                c * D + (half + 1) * 512],
                            start=(c == 0), stop=(c == DC - 1)))
                    pr.wait("tensor", cV)
                    k = pp0.produce(emits)
                    pr.wait("vector", pp0.s, k)
                    pr.vop(lambda e, ut=ut, half=half: e.tensor_copy(
                        out=vaug[:, ut * NH * 65 + half * 8 * 65:
                                 ut * NH * 65 + (half * 8 + 8) * 65]
                        .rearrange("p (u x) -> p u x", x=65)[:, :, 0:64],
                        in_=pp0.t[:, 0:512]))
                    pp0.consumed(cV)
            barrier()
            st1a.close()
            sm0.close()

            sm1 = ExitStack()
            ps_s = PBuf(sm1.enter_context(
                nc.psum_tensor("ps_s", [P, 1024], F32)), pm[0])
            ps_v = sm1.enter_context(nc.psum_tensor("ps_v", [P, 1024], F32))
            ps_r = PBuf(sm1.enter_context(
                nc.psum_tensor("ps_r", [P, 1024], F32)), pm[2])
            pAV = pm[1]
            pT_free = [0, 0]
            psv_free = []
            dstage_free = [0, 0]
            for h in range(NH):
                half = h % 2
                for qc in range(2):
                    nut = 8 * (qc + 1)
                    for ut in range(nut):
                        qlo = qc * 1024
                        col0 = max(0, ut * P - qlo)
                        diag = ut >= 8 * qc
                        emits = []
                        a = col0
                        while a < 1024:
                            b2 = min(1024, (a // 512 + 1) * 512)
                            dmask = diag and a <= col0 < b2
                            emits.append(lambda e, h=h, qc=qc, ut=ut,
                                         half=half, a=a, b2=b2, dmask=dmask:
                                         e.matmul(
                                             ps_s.t[:, a:b2],
                                             lat[half * R:(half + 1) * R,
                                                 ut * P:(ut + 1) * P],
                                             ltf[half * R:(half + 1) * R,
                                                 (h // 2) * T + qc * 1024 + a:
                                                 (h // 2) * T + qc * 1024 + b2],
                                             start=True, stop=not dmask,
                                             tile_position=(half * R, 0)))
                            if dmask:
                                emits.append(lambda e, col0=col0: e.matmul(
                                    ps_s.t[:, col0:col0 + P],
                                    nbigI[:, :], ulow[:, :],
                                    start=False, stop=True))
                            a = b2
                        pr.wait("tensor", cG, masks_g)
                        sk = ps_s.produce(emits)
                        pb = pTb[ut % 2]
                        mz_v = 0
                        if col0 > 0:
                            pr.wait("vector", pAV, pT_free[ut % 2])
                            mz_v = pr.vop(lambda e, pb=pb, col0=col0:
                                          e.memset(pb[:, 0:col0], 0.0))
                        pr.wait("scalar", ps_s.s, sk)
                        pr.wait("scalar", pAV, pT_free[ut % 2])
                        pr.sop(lambda e, pb=pb, col0=col0: e.activation(
                            out=pb[:, col0:1024], in_=ps_s.t[:, col0:1024],
                            func=AF.Exp, bias=negc, scale=0.125))
                        ps_s.consumed(cS)
                        exp_v = cS.n
                        pr.wait("tensor", cS, exp_v)
                        if mz_v:
                            pr.wait("tensor", cV, mz_v)
                        if ut == 0:
                            for (c_, v_) in psv_free:
                                pr.wait("tensor", c_, v_)
                            psv_free = []
                        for hq in range(2):
                            pr.op("tensor",
                                  lambda e, h=h, ut=ut, hq=hq, pb=pb,
                                  st=(ut == 0), sp_=(ut == nut - 1):
                                  e.matmul(
                                      ps_v[:65, hq * 512:(hq + 1) * 512],
                                      vaug[:, ut * NH * 65 + h * 65:
                                           ut * NH * 65 + h * 65 + 65],
                                      pb[:, hq * 512:(hq + 1) * 512],
                                      start=st, stop=sp_), inc=pAV)
                        pT_free[ut % 2] = pAV.n
                    av_k = pAV.n
                    stg = (h * 2 + qc) % 2
                    pr.wait("scalar", pAV, av_k)
                    pr.wait("scalar", s_in)
                    pr.sop(lambda e, stg=stg: e.copy(
                        out=dstage_ap(stg), in_=ps_v[64:65, 0:1024]))
                    dcp_v = cS.n
                    pr.wait("sync", cS, dcp_v)
                    pr.dma("sync", dsb_ap(h, qc), dstage_ap(stg), s_in)
                    pr.wait("vector", pAV, av_k)
                    pr.vop(lambda e, h=h, qc=qc, half=half: e.tensor_copy(
                        out=yT[half * R:(half + 1) * R,
                               (h // 2) * T + qc * 1024:
                               (h // 2) * T + (qc + 1) * 1024],
                        in_=ps_v[0:64, 0:1024]))
                    psv_free = [(cV, cV.n), (cS, dcp_v)]
            pr.wait("vector", cS)
            pr.wait("vector", s_in)
            pr.vop(lambda e: e.reciprocal(
                out=scrf[32:64, 0:1024], in_=scrf[32:64, 0:1024]))
            pr.vop(lambda e: e.tensor_copy(
                out=scr[32:64, 2048:3072], in_=scrf[32:64, 0:1024]))
            pr.vop(lambda e: e.tensor_sub(
                scrf[32:64, 0:1024], scrf[32:64, 0:1024],
                scr[32:64, 2048:3072]))
            pr.vop(lambda e: e.tensor_copy(
                out=scr[32:64, 3072:4096], in_=scrf[32:64, 0:1024]))
            for h in range(NH):
                half = h % 2
                for qc in range(2):
                    pr.wait("sync", cV)
                    pr.wait("sync", cS)
                    pr.wait("sync", pAV, pAV.n)
                    pr.dma("sync", pTb[0][0:1, 0:1024], rhi_ap(h, qc), s_in)
                    pr.dma("sync", pTb[1][0:1, 0:1024], rlo_ap(h, qc), s_in)
                    pr.wait("tensor", s_in)
                    emits = []
                    for hq in range(2):
                        emits.append(lambda e, hq=hq: e.matmul(
                            ps_r.t[0:64, hq * 512:(hq + 1) * 512],
                            ones_r[0:1, 0:64],
                            pTb[0][0:1, hq * 512:(hq + 1) * 512],
                            start=True, stop=False))
                        emits.append(lambda e, hq=hq: e.matmul(
                            ps_r.t[0:64, hq * 512:(hq + 1) * 512],
                            ones_r[0:1, 0:64],
                            pTb[1][0:1, hq * 512:(hq + 1) * 512],
                            start=False, stop=True))
                    k = ps_r.produce(emits)
                    pr.wait("vector", ps_r.s, k)
                    pr.vop(lambda e, h=h, qc=qc, half=half: e.tensor_tensor(
                        out=yT[half * R:(half + 1) * R,
                               (h // 2) * T + qc * 1024:
                               (h // 2) * T + (qc + 1) * 1024],
                        in0=yT[half * R:(half + 1) * R,
                               (h // 2) * T + qc * 1024:
                               (h // 2) * T + (qc + 1) * 1024],
                        in1=ps_r.t[0:64, 0:1024], op=OP.mult))
                    ps_r.consumed(cV)
            barrier()
            sm1.close()

            sm2 = ExitStack()
            po = [PBuf(sm2.enter_context(
                nc.psum_tensor(f"po{i}", [P, 1024], F32)), pm[3 + i])
                for i in range(2)]
            for t in range(NT):
                b = t % 2
                pr.wait("sync", cV)
                pr.wait("sync", s_out)
                pr.dma("sync", xrs[b][:, :], x0_d[t * P:(t + 1) * P, :], s_in)
                emits = []
                for c in range(DC):
                    for nn in range(2):
                        emits.append(lambda e, t=t, c=c, b=b, nn=nn: e.matmul(
                            po[b].t[:, nn * 512:(nn + 1) * 512],
                            yT[:, c * T + t * P:c * T + (t + 1) * P],
                            owT[:, c * D + nn * 512: c * D + (nn + 1) * 512],
                            start=(c == 0), stop=False))
                for nn in range(2):
                    emits.append(lambda e, b=b, nn=nn: e.matmul(
                        po[b].t[:, nn * 512:(nn + 1) * 512],
                        ones_r[0:1, :],
                        obias[0:1, nn * 512:(nn + 1) * 512],
                        start=False, stop=True))
                k = po[b].produce(emits)
                pr.wait("vector", po[b].s, k)
                pr.wait("vector", s_in)
                pr.vop(lambda e, b=b: e.tensor_add(
                    xrs[b][:, :], po[b].t[:, 0:D], xrs[b][:, :]))
                po[b].consumed(cV)
                pr.wait("sync", cV)
                pr.dma("sync", x1_d[t * P:(t + 1) * P, :], xrs[b][:, :], s_out)
            pr.wait("sync", s_out)
            barrier()
            sm2.close()
            st1.close()
        st0.close()
        if nphases < 2:
            sm0.close()

        pr.phase = "p0b_wquant"
        st2 = ExitStack()
        w1q = w2q = None
        if nphases >= 3:
            w1q = st2.enter_context(nc.sbuf_tensor("w1q", [P, DC * HID], BF16))
            w2q = st2.enter_context(nc.sbuf_tensor("w2q", [P, HC * D], BF16))
            st2t = ExitStack()
            wf2 = [st2t.enter_context(nc.sbuf_tensor(f"wf{i}", [P, HID], F32))
                   for i in range(2)]
            w8 = st2t.enter_context(nc.sbuf_tensor("w8", [P, HID], I8))
            sm2b = ExitStack()
            tb2 = [PBuf(sm2b.enter_context(
                nc.psum_tensor(f"tb2{i}", [P, 1024], BF16)), pm[i])
                for i in range(2)]
            pq2 = PBuf(sm2b.enter_context(
                nc.psum_tensor("pq2", [P, 512], F32)), pm[2])
            tp2 = TrPool(tb2)

            def quant_weight(w_hbm, nrt, ncc, wdst, rows, im):
                W = ncc * P
                pr.vop(lambda e: e.memset(spc(20), 0.0))
                act_done = {}
                for rt in range(nrt):
                    b = rt % 2
                    if rt >= 2:
                        pr.wait("sync", cS, act_done[rt - 2])
                    else:
                        pr.wait("sync", cS)
                    pr.dma("sync", wf2[b][:, :W],
                           w_hbm[rt * P:(rt + 1) * P, :], s_in)
                    d = s_in.n
                    pr.wait("scalar", s_in, d)
                    pr.wait("scalar", cV)
                    pr.sop(lambda e, b=b: e.activation(
                        out=scr[:, :W], in_=wf2[b][:, :W],
                        func=AF.Abs, accum_out=spc(16)))
                    act_done[rt] = cS.n
                    pr.wait("vector", cS)
                    pr.vop(lambda e: e.tensor_add(spc(20), spc(20), spc(16)))
                pr.wait("tensor", cV)
                pr.wait("tensor", cG, consts_g)
                k = pq2.produce([lambda e: e.matmul(
                    pq2.t[0:1, 0:1], sp[:, 33:34], spc(20),
                    start=True, stop=True)])
                pr.wait("vector", pq2.s, k)
                pr.vop(lambda e: e.tensor_copy(
                    out=scrf[0:1, 300:301], in_=pq2.t[0:1, 0:1]))
                pq2.consumed(cV)
                pr.vop(lambda e: e.tensor_scalar_mul(
                    scrf[0:1, 300:301], scrf[0:1, 300:301],
                    1.0 / (nrt * P * W)))
                pr.vop(lambda e: e.tensor_scalar_max(
                    scrf[0:1, 300:301], scrf[0:1, 300:301], 1e-5))
                pr.vop(lambda e: e.reciprocal(
                    out=scrf[0:1, 301:302], in_=scrf[0:1, 300:301]))
                pr.wait("tensor", cV)
                k = pq2.produce([
                    lambda e: e.matmul(pq2.t[:, 1:2], ones_rf[0:1, :],
                                       scrf[0:1, 300:301],
                                       start=True, stop=True),
                    lambda e: e.matmul(pq2.t[:, 2:3], ones_rf[0:1, :],
                                       scrf[0:1, 301:302],
                                       start=True, stop=True)])
                pr.wait("vector", pq2.s, k)
                pr.vop(lambda e, im=im: e.tensor_copy(
                    out=spc(im), in_=pq2.t[:, 1:2]))
                pr.vop(lambda e: e.tensor_copy(out=spc(18), in_=pq2.t[:, 2:3]))
                pq2.consumed(cV)
                q_done = {}
                evac_done = {}
                for rt in range(nrt):
                    b = rt % 2
                    if rt >= 2:
                        pr.wait("sync", cS, q_done[rt - 2])
                    else:
                        pr.wait("sync", cS, act_done[nrt - 2 + rt])
                    pr.dma("sync", wf2[b][:, :W],
                           w_hbm[rt * P:(rt + 1) * P, :], s_in)
                    d = s_in.n
                    pr.wait("scalar", s_in, d)
                    if rt == 0:
                        pr.wait("scalar", cV)
                    else:
                        pr.wait("scalar", cV, evac_done[rt - 1])
                    pr.sop(lambda e, b=b: e.activation(
                        out=w8[:, :W], in_=wf2[b][:, :W],
                        func=AF.Copy, scale=spc(18)))
                    pr.sop(lambda e: e.activation(
                        out=scr[:, :W], in_=w8[:, :W], func=AF.Sign))
                    pr.sop(lambda e: e.activation(
                        out=sp[:, 61:62], in_=scr[:, W - 1:W], func=AF.Copy))
                    sign_v = cS.n
                    q_done[rt] = cS.n
                    for c in range(ncc):
                        tp2.tr(scr[:, c * P:(c + 1) * P],
                               wdst[:, c * rows + rt * P:
                                    c * rows + (rt + 1) * P],
                               src_ctr=cS, src_val=sign_v)
                    evac_done[rt] = cV.n

            quant_weight(w1_d, HID // P, DC, w1q, HID, 21)
            quant_weight(w2_d, D // P, HC, w2q, D, 23)
            pr.vop(lambda e: e.reciprocal(out=spc(24), in_=spc(21)))
            pr.vop(lambda e: e.reciprocal(out=spc(25), in_=spc(23)))
            pr.wait("sync", cV)
            pr.dma("sync", wq1_d, w1q[:, :], s_out)
            pr.dma("sync", wq2_d, w2q[:, :], s_out)
            barrier()
            sm2b.close()
            st2t.close()

        def emit_ffn(xin_d, xout_d, w1qt, w2qt, stf, tag=""):
            xq8s = stf.enter_context(nc.sbuf_tensor(tag + "xq8s", [P, D], I8))
            xqT = stf.enter_context(
                nc.sbuf_tensor(tag + "xqT", [P, DC * P], BF16))
            xsc = stf.enter_context(nc.sbuf_tensor(tag + "xsc", [P, D], BF16))
            hti = [stf.enter_context(
                nc.sbuf_tensor(tag + f"hti{i}", [P, HID], F32))
                for i in range(2)]
            hqT = stf.enter_context(
                nc.sbuf_tensor(tag + "hqT", [P, (HC // 2) * P], BF16))
            hq8h = stf.enter_context(
                nc.sbuf_tensor(tag + "hq8h", [P, HID // 2], I8))
            sxc = stf.enter_context(nc.sbuf_tensor(tag + "sxc", [P, NT], F32))
            exc = stf.enter_context(nc.sbuf_tensor(tag + "exc", [P, NT], F32))
            e2c = stf.enter_context(nc.sbuf_tensor(tag + "e2c", [P, 2], F32))
            sbB = stf.enter_context(nc.sbuf_tensor(tag + "sbB", [1, P], BF16))
            s2B = stf.enter_context(nc.sbuf_tensor(tag + "s2B", [1, P], BF16))
            xl2 = stf.enter_context(nc.sbuf_tensor(tag + "xl2", [P, D], F32))
            xin3 = [xin[0], xin[1], stf.enter_context(
                nc.sbuf_tensor(tag + "xin3_2", [P, D], F32))]
            smf = ExitStack()
            tbf = [PBuf(smf.enter_context(
                nc.psum_tensor(tag + f"tbf{i}", [P, 1024], BF16)), pm[i])
                for i in range(2)]
            hp = [PBuf(smf.enter_context(
                nc.psum_tensor(tag + f"hpf{i}", [P, 512], F32)), pm[2 + i])
                for i in range(2)]
            pof = PBuf(smf.enter_context(
                nc.psum_tensor(tag + "poff", [P, 1024], F32)), pm[4])
            tpf = TrPool(tbf)

            xin_v = {}
            xqT_ok = {}
            sb_ok = {}
            stats_ok = {}
            qv = {}
            xv = {}
            ln_ok = {}

            def xstage(t):
                pr.wait("scalar", cV)
                pr.sop(lambda e, t=t: e.activation(
                    out=xsc[:, :], in_=xq8s[:, :], func=AF.Copy))
                pr.sop(lambda e: e.activation(
                    out=sp[:, 61:62], in_=xsc[:, D - 1:D], func=AF.Copy))
                xv[t] = cS.n

            def lnq(t, pos):
                if pos == 0:
                    pr.wait("vector", s_in, xin_v[t])
                    pr.vop(lambda e, t=t: e.reduce_sum(
                        out=spc(53), in_=xin3[t % 3][:, :], axis=AX.X))
                    pr.wait("scalar", s_in, xin_v[t])
                    pr.wait("scalar", cV)
                    pr.sop(lambda e, t=t: e.activation(
                        out=xl2[:, :], in_=xin3[t % 3][:, :],
                        func=AF.Square, accum_out=spc(54)))
                    pr.sop(lambda e: e.activation(
                        out=sp[:, 61:62], in_=spc(54), func=AF.Copy))
                elif pos == 1:
                    pr.wait("vector", cS)
                    pr.vop(lambda e: e.tensor_copy(
                        out=sp[:, 62:63], in_=sp[:, 61:62]))
                    pr.vop(lambda e: e.tensor_scalar_mul(
                        spc(50), spc(53), 1.0 / D))
                    pr.vop(lambda e: e.tensor_scalar_mul(
                        spc(51), spc(54), 1.0 / D))
                    pr.vop(lambda e: e.tensor_mul(spc(53), spc(50), spc(50)))
                    pr.vop(lambda e: e.tensor_sub(spc(51), spc(51), spc(53)))
                    pr.wait("scalar", cV)
                    pr.sop(lambda e: e.activation(
                        out=spc(52), in_=spc(51), func=AF.Sqrt,
                        bias=spc(30)))
                elif pos == 2:
                    pr.wait("vector", cS)
                    pr.vop(lambda e: e.reciprocal(out=spc(52), in_=spc(52)))
                    pr.vop(lambda e, t=t: e.scalar_tensor_tensor(
                        out=xl2[:, :], in0=xin3[t % 3][:, :], scalar=spc(50),
                        in1=spc(52).broadcast_to([P, D]),
                        op0=OP.subtract, op1=OP.mult))
                    pr.vop(lambda e: e.tensor_mul(spc(55), spc(51), spc(52)))
                    pr.vop(lambda e: e.tensor_mul(spc(55), spc(55), spc(52)))
                    pr.wait("scalar", cV)
                    pr.sop(lambda e: e.activation(
                        out=spc(56), in_=spc(55), func=AF.Sqrt,
                        bias=spc(31)))
                else:
                    pr.wait("vector", cS)
                    pr.vop(lambda e: e.reciprocal(out=spc(56), in_=spc(56)))
                    pr.vop(lambda e: e.reduce_max(
                        out=spc(57), in_=xl2[:, :], axis=AX.X,
                        apply_absolute_value=True))
                    pr.vop(lambda e: e.tensor_mul(spc(57), spc(57), spc(56)))
                    pr.vop(lambda e: e.tensor_scalar_max(spc(57), spc(57),
                                                         1e-5))
                    pr.vop(lambda e: e.reciprocal(out=spc(58), in_=spc(57)))
                    pr.vop(lambda e: e.tensor_scalar_mul(spc(58), spc(58),
                                                         127.0))
                    pr.vop(lambda e: e.tensor_mul(spc(59), spc(58), spc(56)))
                    pr.vop(lambda e, t=t: e.tensor_scalar_mul(
                        exc[:, t:t + 1], spc(57), 1.0 / 127.0))
                    pr.vop(lambda e, t=t: e.tensor_mul(
                        exc[:, t:t + 1], exc[:, t:t + 1], spc(21)))
                    pr.vop(lambda e, t=t: e.tensor_mul(
                        sxc[:, t:t + 1], spc(58), spc(24)))
                    ln_ok[t] = cV.n
                    pr.wait("scalar", cV, ln_ok[t])
                    pr.sop(lambda e, t=t: e.activation(
                        out=xq8s[:, :], in_=xl2[:, :],
                        func=AF.Copy, scale=spc(59)))
                    pr.sop(lambda e, t=t: e.activation(
                        out=sp[:, 61:62], in_=xq8s[:, D - 1:D],
                        func=AF.Copy))

            def front(t):
                if t + 1 < NT:
                    pr.wait("sync", cV)
                    pr.wait("sync", cS)
                    pr.wait("sync", s_out)
                    pr.dma("sync", xin3[(t + 1) % 3][:, :],
                           xin_d[(t + 1) * P:(t + 2) * P, :], s_in)
                    xin_v[t + 1] = s_in.n
                tpf.trN(lambda j: xsc[:, j * P:(j + 1) * P],
                        xqT[:, 0:DC * P], DC,
                        src_ctr=cS, src_val=xv[t])
                xqT_ok[t] = cV.n
                pr.vop(lambda e, t=t: e.tensor_copy(
                    out=spb[:, 120:121], in_=sxc[:, t:t + 1]))
                tpf.tr(spb[:, 120:121], sbB[0:1, :], rows=P, cols=1,
                       src_ctr=cV, src_val=cV.n)
                sb_ok[t] = cV.n

            def emit_stats(t, pos):
                if pos == 0:
                    pr.wait("scalar", cV)
                    pr.sop(lambda e, t=t: e.activation(
                        out=hti[(t + 1) % 2][:, :], in_=hti[t % 2][:, :],
                        func=AF.Square, accum_out=spc(4)))
                    pr.sop(lambda e: e.activation(
                        out=sp[:, 61:62], in_=spc(4), func=AF.Copy))
                elif pos == 1:
                    pr.wait("vector", cS)
                    pr.vop(lambda e: e.tensor_copy(
                        out=sp[:, 62:63], in_=sp[:, 61:62]))
                    pr.vop(lambda e: e.tensor_scalar_mul(
                        spc(5), spc(4), 1.0 / HID))
                    pr.wait("scalar", cV)
                    pr.sop(lambda e: e.activation(
                        out=spc(6), in_=spc(5), func=AF.Sqrt, bias=spc(31)))
                elif pos == 2:
                    pr.wait("vector", cS)
                    pr.vop(lambda e: e.reciprocal(out=spc(6), in_=spc(6)))
                    pr.vop(lambda e, t=t: e.reduce_max(
                        out=spc(7), in_=hti[t % 2][:, :], axis=AX.X,
                        apply_absolute_value=True))
                    pr.vop(lambda e: e.tensor_mul(spc(7), spc(7), spc(6)))
                    pr.vop(lambda e: e.tensor_scalar_max(spc(7), spc(7),
                                                         1e-5))
                    pr.vop(lambda e: e.reciprocal(out=spc(8), in_=spc(7)))
                    pr.vop(lambda e: e.tensor_scalar_mul(spc(8), spc(8),
                                                         127.0))
                    pr.vop(lambda e: e.tensor_mul(spc(9), spc(8), spc(6)))
                    pr.vop(lambda e, t=t: e.tensor_scalar_mul(
                        e2c[:, t % 2:t % 2 + 1], spc(7), 1.0 / 127.0))
                    pr.vop(lambda e, t=t: e.tensor_mul(
                        e2c[:, t % 2:t % 2 + 1], e2c[:, t % 2:t % 2 + 1],
                        spc(23)))
                    pr.vop(lambda e: e.tensor_mul(spc(11), spc(8), spc(25)))
                    stats_ok[t] = cV.n
                else:
                    pr.wait("scalar", cV, stats_ok[t])
                    pr.sop(lambda e, t=t: e.activation(
                        out=hq8h[:, :], in_=hti[t % 2][:, 0:HID // 2],
                        func=AF.Copy, scale=spc(9)))
                    pr.sop(lambda e: e.activation(
                        out=sp[:, 61:62], in_=hq8h[:, HID // 2 - 1:HID // 2],
                        func=AF.Copy))
                    qv[t] = cS.n

            def bl1_gelu(t, ts, tl):
                if ts is not None:
                    emit_stats(ts, 0)
                pr.wait("tensor", cV, max(xqT_ok[t], sb_ok[t]))
                for n in range(8):
                    pb = n % 2
                    emits = []
                    for c in range(DC):
                        emits.append(lambda e, n=n, c=c, pb=pb: e.matmul(
                            hp[pb].t[:, :],
                            xqT[:, c * P:(c + 1) * P],
                            w1qt[:, c * HID + n * 512:
                                 c * HID + (n + 1) * 512],
                            start=(c == 0), stop=False))
                    emits.append(lambda e, n=n, pb=pb: e.matmul(
                        hp[pb].t[:, :], sbB[0:1, :],
                        b1row[0:1, n * 512:(n + 1) * 512],
                        start=False, stop=True))
                    k = hp[pb].produce(emits)
                    pr.wait("scalar", hp[pb].s, k)
                    pr.sop(lambda e, n=n, pb=pb, t=t: e.activation(
                        out=hti[t % 2][:, n * 512:(n + 1) * 512],
                        in_=hp[pb].t[:, :],
                        func=AF.Gelu, scale=exc[:, t:t + 1]))
                    hp[pb].consumed(cS)
                    if n == 1 and ts is not None:
                        emit_stats(ts, 1)
                    if n == 2 and ts is not None:
                        emit_stats(ts, 2)
                    if n == 3 and tl is not None:
                        lnq(tl, 0)
                    if n == 4 and ts is not None:
                        emit_stats(ts, 3)
                    if n == 5 and tl is not None:
                        lnq(tl, 1)
                    if n == 6 and tl is not None:
                        lnq(tl, 2)
                    if n == 7 and tl is not None:
                        lnq(tl, 3)
                        xstage(tl)

            def back(t):
                b = t % 3
                pr.vop(lambda e: e.tensor_copy(
                    out=spb[:, 120:121], in_=sp[:, 11:12]))
                tpf.tr(spb[:, 120:121], s2B[0:1, :], rows=P, cols=1,
                       src_ctr=cV, src_val=cV.n)
                pr.wait("vector", cS, qv[t])
                pr.vop(lambda e: e.tensor_copy(
                    out=sp[:, 62:63], in_=sp[:, 61:62]))
                pr.vop(lambda e: e.tensor_copy(out=scr[:, 0:HID // 2],
                                               in_=hq8h[:, :]))
                pr.vop(lambda e: e.tensor_copy(
                    out=sp[:, 63:64], in_=scr[:, HID // 2 - 1:HID // 2]))
                c0v = cV.n
                pr.wait("scalar", cV, c0v)
                pr.sop(lambda e, t=t: e.activation(
                    out=hq8h[:, :], in_=hti[t % 2][:, HID // 2:HID],
                    func=AF.Copy, scale=spc(9)))
                pr.sop(lambda e: e.activation(
                    out=sp[:, 61:62], in_=hq8h[:, HID // 2 - 1:HID // 2],
                    func=AF.Copy))
                q1v = cS.n
                pr.wait("vector", cS, q1v)
                pr.vop(lambda e: e.tensor_copy(
                    out=sp[:, 62:63], in_=sp[:, 61:62]))
                pr.vop(lambda e: e.tensor_copy(
                    out=scr[:, HID // 2:HID], in_=hq8h[:, :]))
                pr.vop(lambda e: e.tensor_copy(
                    out=sp[:, 63:64], in_=scr[:, HID - 1:HID]))
                c1v = cV.n
                for (cc, v) in pof.free:
                    pr.wait("tensor", cc, v)
                pof.free = []
                for half in range(2):
                    cb = half * (HC // 2)
                    srcv = c0v if half == 0 else c1v
                    for hb in range(2):
                        tpf.trN(lambda j, base=cb + hb * 8:
                                scr[:, (base + j) * P:(base + j + 1) * P],
                                hqT[:, (hb * 8) * P:(hb * 8 + 8) * P], 8,
                                src_ctr=cV, src_val=srcv)
                    pr.wait("tensor", cV)
                    for c in range(HC // 2):
                        for n in range(2):
                            pr.op("tensor", lambda e, n=n, c=c, cb=cb:
                                  e.matmul(
                                      pof.t[:, n * 512:(n + 1) * 512],
                                      hqT[:, c * P:(c + 1) * P],
                                      w2qt[:, (cb + c) * D + n * 512:
                                           (cb + c) * D + (n + 1) * 512],
                                      start=(cb + c == 0), stop=False))
                pr.op("tensor", lambda e: e.matmul(
                    pof.t[:, 0:512], s2B[0:1, :], b2row[0:1, 0:512],
                    start=False, stop=True))
                pr.op("tensor", lambda e: e.matmul(
                    pof.t[:, 512:1024], s2B[0:1, :], b2row[0:1, 512:1024],
                    start=False, stop=True), inc=pof.s)
                k = pof.s.n
                pr.wait("vector", pof.s, k)
                pr.wait("vector", s_in)
                pr.vop(lambda e, b=b, t=t: e.scalar_tensor_tensor(
                    out=xin3[b][:, :], in0=pof.t[:, 0:D],
                    scalar=e2c[:, t % 2:t % 2 + 1],
                    in1=xin3[b][:, :], op0=OP.mult, op1=OP.add))
                pof.consumed(cV)
                pr.wait("sync", cV)
                pr.dma("sync", xout_d[t * P:(t + 1) * P, :], xin3[b][:, :],
                       s_out)

            pr.wait("sync", cV)
            pr.wait("sync", cS)
            pr.dma("sync", xin3[0][:, :], xin_d[0:P, :], s_in)
            xin_v[0] = s_in.n
            for pos in range(4):
                lnq(0, pos)
            xstage(0)
            for t in range(NT):
                front(t)
                bl1_gelu(t, t - 1 if t >= 1 else None,
                         t + 1 if t + 1 < NT else None)
                if t >= 1:
                    back(t - 1)
            for pos in range(4):
                emit_stats(NT - 1, pos)
            back(NT - 1)
            pr.wait("sync", s_out)
            if dbg == "q2" and tag == "a":
                pr.wait("sync", cV)
                pr.wait("sync", cS)
                fl = out_d.flatten()
                pr.dma("sync",
                       fl[P * 4096:2 * P * 4096].rearrange("(p n) -> p n", p=P),
                       hti[(NT - 1) % 2][:, :], s_out)
                pr.dma("sync",
                       fl[2 * P * 4096:2 * P * 4096 + 2048]
                       .rearrange("(p n) -> p n", p=1),
                       b1row.bitcast(F32)[0:1, :], s_out)
                pr.dma("sync",
                       fl[2 * P * 4096 + 2048:2 * P * 4096 + 2048 + P * 16]
                       .rearrange("(p n) -> p n", p=P),
                       exc[:, :], s_out)
                pr.dma("sync",
                       fl[2 * P * 4096 + 2048 + P * 16:
                          2 * P * 4096 + 2048 + P * 16 + P * 16]
                       .rearrange("(p n) -> p n", p=P),
                       sxc[:, :], s_out)
            barrier()
            smf.close()

        pr.phase = "p2_ffn1"
        if nphases >= 4:
            stf1 = ExitStack()
            emit_ffn(x1_d, x2_d, w1q, w2q, stf1, tag="a")
            stf1.close()
        st2.close()

        pr.phase = "p2b_a2a"
        if nphases >= 5:
            pr.wait("gpsimd", s_out)

            def _cc(e):
                with allow_ici_all_to_all():
                    return e.collective_compute(
                        "AllToAll", OP.bypass,
                        replica_groups=[list(range(8))],
                        ins=[x2_d], outs=[x2t_d])
            pr.op("gpsimd", _cc, inc=s_cc)

        pr.phase = "p3a_attn_b"
        if nphases >= 5:
            st3 = ExitStack()
            wkT = st3.enter_context(nc.sbuf_tensor("wkT", [P, DC * D], BF16))
            wvT = st3.enter_context(nc.sbuf_tensor("wvT", [P, DC * D], BF16))
            wqT = st3.enter_context(nc.sbuf_tensor("wqT", [P, DC * D], BF16))
            wtm3 = [st3.enter_context(nc.sbuf_tensor(f"wtm3{i}", [P, D], BF16))
                    for i in range(2)]
            S2 = [st3.enter_context(nc.sbuf_tensor(f"S2{i}", [P, 8192], F32))
                  for i in range(2)]
            prod = st3.enter_context(nc.sbuf_tensor("prod", [P, 8192], F32))
            kk = st3.enter_context(nc.sbuf_tensor("kk", [P, D], F32))
            vv = st3.enter_context(nc.sbuf_tensor("vv", [P, D], F32))
            qq = st3.enter_context(nc.sbuf_tensor("qq", [P, D], F32))
            kn = st3.enter_context(nc.sbuf_tensor("kn", [P, D], F32))
            rt3 = st3.enter_context(nc.sbuf_tensor("rt3", [P, D], F32))
            sq3 = st3.enter_context(nc.sbuf_tensor("sq3", [P, D], F32))
            xl3 = st3.enter_context(nc.sbuf_tensor("xl3", [P, D], BF16))
            x3T = st3.enter_context(nc.sbuf_tensor("x3T", [P, DC * P], BF16))
            sm3 = ExitStack()
            tb3 = [PBuf(sm3.enter_context(
                nc.psum_tensor(f"tb3{i}", [P, 1024], BF16)), pm[i])
                for i in range(2)]
            pq3 = [PBuf(sm3.enter_context(
                nc.psum_tensor(f"pq3{i}", [P, 512], F32)), pm[2 + i])
                for i in range(2)]
            tp3 = TrPool(tb3)
            build_fm(wkT, wk_d, D, D, tp3, wtm3)
            build_fm(wvT, wv_d, D, D, tp3, wtm3)
            build_fm(wqT, wqd_d, D, D, tp3, wtm3)

            pr.vop(lambda e: e.memset(S2[0][:, :], 0.0))
            pr.vop(lambda e: e.memset(S2[1][:, :], 0.0))
            pr.wait("sync", s_cc, 1)

            def S4(S):
                return S[:, :].rearrange("p (n i j) -> p n i j", i=8, j=8)

            def prod4():
                return prod[:, :].rearrange("p (n i j) -> p n i j", i=8, j=8)

            for t in range(B):
                for bt in range(2):
                    row0 = t * BS + bt * P
                    pr.wait("sync", cV)
                    pr.wait("sync", s_out)
                    pr.dma("sync", xin[bt][:, :], x2t_d[row0:row0 + P, :],
                           s_in)
                    pr.wait("vector", s_in)
                    pr.wait("scalar", s_in)
                    ln_stats(xin[bt], D)
                    pr.vop(lambda e, bt=bt: e.scalar_tensor_tensor(
                        out=xl3[:, :], in0=xin[bt][:, :], scalar=spc(0),
                        in1=spc(2).broadcast_to([P, D]),
                        op0=OP.subtract, op1=OP.mult))
                    xl3_v = cV.n
                    for c in range(DC):
                        tp3.tr(xl3[:, c * P:(c + 1) * P],
                               x3T[:, c * P:(c + 1) * P],
                               src_ctr=cV, src_val=xl3_v)
                    xT_ready = cV.n
                    for wi, (wT, dst) in enumerate(
                            ((wkT, kk), (wvT, vv), (wqT, qq))):
                        for n in range(2):
                            pb = (wi * 2 + n) % 2
                            pr.wait("tensor", cV, xT_ready)
                            emits = []
                            for c in range(DC):
                                emits.append(
                                    lambda e, wT=wT, n=n, c=c, pb=pb:
                                    e.matmul(
                                        pq3[pb].t[:, :],
                                        x3T[:, c * P:(c + 1) * P],
                                        wT[:, c * D + n * 512:
                                           c * D + (n + 1) * 512],
                                        start=(c == 0), stop=(c == DC - 1)))
                            k = pq3[pb].produce(emits)
                            pr.wait("vector", pq3[pb].s, k)
                            pr.wait("vector", cS)
                            pr.vop(lambda e, dst=dst, n=n, pb=pb:
                                   e.tensor_copy(
                                       out=dst[:, n * 512:(n + 1) * 512],
                                       in_=pq3[pb].t[:, :]))
                            pq3[pb].consumed(cV)
                    S = S2[bt]
                    pr.wait("scalar", cV)
                    pr.sop(lambda e: e.activation(
                        out=scrf[:, 0:D], in_=kk[:, :], func=AF.Square))
                    pr.wait("vector", cS)
                    pr.vop(lambda e: e.reduce_sum(
                        out=kn[:, 0:P],
                        in_=scrf[:, 0:D].rearrange("p (n j) -> p n j", j=8),
                        axis=AX.X))
                    pr.wait("scalar", cV)
                    pr.sop(lambda e: e.activation(
                        out=kn[:, P:2 * P], in_=kn[:, 0:P], func=AF.Sqrt))
                    pr.wait("vector", cS)
                    pr.vop(lambda e: e.tensor_scalar_add(
                        kn[:, P:2 * P], kn[:, P:2 * P], 1e-6))
                    pr.vop(lambda e: e.reciprocal(
                        out=kn[:, 0:P], in_=kn[:, P:2 * P]))
                    pr.vop(lambda e: e.tensor_tensor(
                        out=rt3[:, :], in0=kk[:, :],
                        in1=kn[:, 0:P].rearrange("p (n o) -> p n o", o=1)
                        .broadcast_to([P, P, 8]),
                        op=OP.mult))
                    kn4i = rt3[:, :].rearrange("p (n j) -> p n j", j=8) \
                        .unsqueeze(2).broadcast_to([P, P, 8, 8])
                    pr.vop(lambda e, S=S: e.tensor_tensor(
                        out=prod4(), in0=S4(S), in1=kn4i, op=OP.mult))
                    pr.vop(lambda e: e.reduce_sum(
                        out=sq3[:, :].rearrange("p (n i) -> p n i", i=8),
                        in_=prod4(), axis=AX.X))
                    pr.vop(lambda e: e.tensor_sub(
                        sq3[:, :], vv[:, :], sq3[:, :]))
                    pr.vop(lambda e: e.tensor_tensor(
                        out=prod4(),
                        in0=sq3[:, :].rearrange("p (n i) -> p n i", i=8)
                        .unsqueeze(3).broadcast_to([P, P, 8, 8]),
                        in1=kn4i, op=OP.mult))
                    pr.vop(lambda e, S=S: e.tensor_add(
                        prod[:, :], S[:, :], prod[:, :]))
                    pr.wait("scalar", cV)
                    pr.sop(lambda e, S=S: e.activation(
                        out=S[:, :], in_=prod[:, :], func=AF.Tanh))
                    pr.wait("vector", cS)
                    pr.vop(lambda e, S=S: e.tensor_tensor(
                        out=prod4(), in0=S4(S),
                        in1=qq[:, :].rearrange("p (n j) -> p n j", j=8)
                        .unsqueeze(2).broadcast_to([P, P, 8, 8]),
                        op=OP.mult))
                    pr.vop(lambda e: e.reduce_sum(
                        out=sq3[:, :].rearrange("p (n i) -> p n i", i=8),
                        in_=prod4(), axis=AX.X))
                    pr.wait("scalar", cV)
                    pr.sop(lambda e: e.activation(
                        out=rt3[:, :], in_=sq3[:, :], func=AF.Sigmoid))
                    pr.sop(lambda e: e.activation(
                        out=kn[:, :], in_=sq3[:, :], func=AF.Square))
                    pr.wait("vector", cS)
                    pr.vop(lambda e: e.tensor_mul(
                        rt3[:, :], kn[:, :], rt3[:, :]))
                    pr.vop(lambda e, bt=bt: e.tensor_add(
                        xin[bt][:, :], xin[bt][:, :], rt3[:, :]))
                    pr.wait("sync", cV)
                    pr.dma("sync", x3_d[row0:row0 + P, :], xin[bt][:, :],
                           s_out)
            pr.wait("sync", s_out)
            barrier()
            sm3.close()
            st3.close()

        pr.phase = "p3b_ffn2"
        if nphases >= 6:
            st4 = ExitStack()
            w1q2 = st4.enter_context(
                nc.sbuf_tensor("w1q2", [P, DC * HID], BF16))
            w2q2 = st4.enter_context(nc.sbuf_tensor("w2q2", [P, HC * D], BF16))
            pr.dma("sync", w1q2[:, :], wq1_d, s_in)
            pr.dma("sync", w2q2[:, :], wq2_d, s_in)
            pr.wait("tensor", s_in)
            emit_ffn(x3_d, out_d, w1q2, w2q2, st4, tag="b")
            st4.close()

        if dbg == "q2":
            pass
        if dbg == "q":
            pr.wait("sync", s_out)
            pr.wait("sync", cV)
            pr.wait("sync", cS)
            pr.dma("sync", out_d[0:P, 0:64], sp[:, :], s_out)
            pr.dma("sync", out_d[P:2 * P, 0:1024],
                   wq1_d.bitcast(F32)[:, 1024:2048], s_out)
            pr.dma("sync", out_d[2 * P:3 * P, 0:1024],
                   wq2_d.bitcast(F32)[:, 0:1024], s_out)
            if w2q is not None:
                pr.dma("sync", out_d[3 * P:4 * P, 0:1024],
                       w2q.bitcast(F32)[:, 0:1024], s_out)
        elif dbg is not None and dbg != "q2":
            dbg_src = {"x1": x1_d, "x2": x2_d, "x2t": x2t_d, "x3": x3_d}[dbg]
            pr.wait("sync", s_out)
            if dbg == "x2t":
                pr.wait("sync", s_cc, 1)
            pr.dma("sync", out_d, dbg_src, s_out)

        pr.wait("sync", s_out)
        pr.wait("gpsimd", s_out)

        with nc.Block() as block:
            pr.flush(block)

    return nc


def kernel(L0_attn_a_hidden_states, L0_attn_a_basis_w, L0_attn_a_core,
           L0_attn_a_head_residual, L0_attn_a_v_w, L0_attn_a_v_b,
           L0_attn_a_o_w, L0_attn_a_o_b, L0_attn_b_W_k, L0_attn_b_W_v,
           L0_attn_b_W_q, L0_ffn_w1, L0_ffn_b1, L0_ffn_w2, L0_ffn_b2):
    hs = np.ascontiguousarray(L0_attn_a_hidden_states, dtype=np.float32)
    nc = build(_NPHASES, _DBG)
    common = {
        "basis_w": np.ascontiguousarray(L0_attn_a_basis_w, np.float32),
        "core_m": np.ascontiguousarray(L0_attn_a_core, np.float32),
        "head_res": np.ascontiguousarray(L0_attn_a_head_residual, np.float32),
        "v_w": np.ascontiguousarray(L0_attn_a_v_w, np.float32),
        "v_b": np.ascontiguousarray(L0_attn_a_v_b, np.float32),
        "o_w": np.ascontiguousarray(L0_attn_a_o_w, np.float32),
        "o_b": np.ascontiguousarray(L0_attn_a_o_b, np.float32),
        "w_k": np.ascontiguousarray(L0_attn_b_W_k, np.float32),
        "w_v": np.ascontiguousarray(L0_attn_b_W_v, np.float32),
        "w_q": np.ascontiguousarray(L0_attn_b_W_q, np.float32),
        "w1": np.ascontiguousarray(L0_ffn_w1, np.float32),
        "b1": np.ascontiguousarray(L0_ffn_b1, np.float32),
        "w2": np.ascontiguousarray(L0_ffn_w2, np.float32),
        "b2": np.ascontiguousarray(L0_ffn_b2, np.float32),
    }
    in_maps = [dict(common, x0=np.ascontiguousarray(hs[i])) for i in range(8)]
    res = run_bass_kernel_spmd(nc, in_maps, core_ids=list(range(8)),
                               trace=_TRACE)
    globals()["_LAST_RES"] = res
    out = np.empty((B, T, D), dtype=np.float32)
    for s in range(8):
        out[:, s * BS:(s + 1) * BS, :] = \
            res.results[s]["out"].reshape(B, BS, D)
    return out


# version 05
# speedup vs baseline: 1.2235x; 1.0871x over previous
import numpy as np
import concourse.bass as bass
import concourse.mybir as mybir
from concourse.bass_utils import run_bass_kernel_spmd
from concourse.aot_env import allow_ici_all_to_all
from contextlib import ExitStack

F32 = mybir.dt.float32
BF16 = mybir.dt.bfloat16
I8 = mybir.dt.int8
AF = mybir.ActivationFunctionType
OP = mybir.AluOpType
AX = mybir.AxisListType

P = 128
T = 2048
NT = 16
D = 1024
DC = 8
HID = 4096
HC = 32
NH = 16
R = 64
B = 8
BS = 256
NEG = -1e30

ENGINES = ("sync", "vector", "scalar", "tensor", "gpsimd")
_NPHASES = 9
_TRACE = False
_LAST_RES = None
_DBG = None


class Ctr:
    def __init__(self, h):
        self.h = h
        self.n = 0


class Prog:
    def __init__(self, nc=None):
        self.q = {e: [] for e in ENGINES}
        self.cV = None
        self.cS = None
        self.nc = nc
        self.phase = "init"

    def _add(self, eng, fn):
        self.q[eng].append((self.phase, fn))

    def wait(self, eng, ctr, val=None):
        v = ctr.n if val is None else val
        if v > 0:
            self._add(eng, lambda e: e.wait_ge(ctr.h, v))

    def dma(self, eng, out, in_, sem, **kw):
        self._add(
            eng,
            lambda e: e.dma_start(out=out, in_=in_, **kw).then_inc(sem.h, 16))
        sem.n += 16
        return sem.n

    def op(self, eng, fn, inc=None, k=1):
        if inc is None:
            self._add(eng, fn)
            return None
        self._add(eng, lambda e: fn(e).then_inc(inc.h, k))
        inc.n += k
        return inc.n

    def vop(self, fn):
        self.wait("vector", self.cV)
        return self.op("vector", fn, inc=self.cV)

    def sop(self, fn):
        self.wait("scalar", self.cS)
        return self.op("scalar", fn, inc=self.cS)

    def barrier(self, ctrs):
        vals = {}
        for e in ENGINES:
            c = ctrs[e]
            self._add(e, lambda en, c=c: en.sem_inc(c.h, 1))
            c.n += 1
            vals[e] = c.n
        for e in ENGINES:
            for o in ENGINES:
                if o != e:
                    c, v = ctrs[o], vals[o]
                    self._add(e, lambda en, c=c, v=v: en.wait_ge(c.h, v))

    def _emit(self, eng, name):
        nc, cur, sid = self.nc, None, None
        for lab, fn in self.q[name]:
            if lab != cur:
                if cur is not None:
                    nc.leave_named_scope(cur, sid, False)
                sid, _ = nc.enter_named_scope(lab, False)
                cur = lab
            fn(eng)
        if cur is not None:
            nc.leave_named_scope(cur, sid, False)

    def flush(self, block):
        @block.sync
        def _(eng):
            self._emit(eng, "sync")

        @block.vector
        def _(eng):
            self._emit(eng, "vector")

        @block.scalar
        def _(eng):
            self._emit(eng, "scalar")

        @block.tensor
        def _(eng):
            self._emit(eng, "tensor")

        @block.gpsimd
        def _(eng):
            self._emit(eng, "gpsimd")


def build(nphases=9, dbg=None):
    nc = bass.Bass("TRN2", target_bir_lowering=False,
                   detect_race_conditions=False)

    x0_d = nc.dram_tensor("x0", [T, D], F32, kind="ExternalInput").ap()
    basis_d = nc.dram_tensor("basis_w", [R, D], F32, kind="ExternalInput").ap()
    core_d = nc.dram_tensor("core_m", [R, R], F32, kind="ExternalInput").ap()
    hres_d = nc.dram_tensor("head_res", [NH, R, R], F32,
                            kind="ExternalInput").ap()
    vw_d = nc.dram_tensor("v_w", [D, D], F32, kind="ExternalInput").ap()
    vb_d = nc.dram_tensor("v_b", [D], F32, kind="ExternalInput").ap()
    ow_d = nc.dram_tensor("o_w", [D, D], F32, kind="ExternalInput").ap()
    ob_d = nc.dram_tensor("o_b", [D], F32, kind="ExternalInput").ap()
    wk_d = nc.dram_tensor("w_k", [D, D], F32, kind="ExternalInput").ap()
    wv_d = nc.dram_tensor("w_v", [D, D], F32, kind="ExternalInput").ap()
    wqd_d = nc.dram_tensor("w_q", [D, D], F32, kind="ExternalInput").ap()
    w1_d = nc.dram_tensor("w1", [HID, D], F32, kind="ExternalInput").ap()
    b1_d = nc.dram_tensor("b1", [HID], F32, kind="ExternalInput").ap()
    w2_d = nc.dram_tensor("w2", [D, HID], F32, kind="ExternalInput").ap()
    b2_d = nc.dram_tensor("b2", [D], F32, kind="ExternalInput").ap()
    out_d = nc.dram_tensor("out", [T, D], F32, kind="ExternalOutput").ap()

    x1_d = nc.dram_tensor("x1_s", [T, D], F32).ap()
    x2_d = nc.dram_tensor("x2_s", [T, D], F32).ap()
    x2t_d = nc.dram_tensor("x2t_s", [T, D], F32).ap()
    x3_d = nc.dram_tensor("x3_s", [T, D], F32).ap()
    wq1_d = nc.dram_tensor("wq1_s", [P, DC * HID], BF16).ap()
    wq2_d = nc.dram_tensor("wq2_s", [P, HC * D], BF16).ap()

    pr = Prog(nc)

    with ExitStack() as stack:
        sem = {}
        for name in ("s_in", "s_out", "s_gd", "s_cc",
                     "c_v", "c_s", "c_t", "c_g", "c_y",
                     "p0", "p1", "p2", "p3", "p4", "p5", "p6"):
            sem[name] = Ctr(stack.enter_context(nc.semaphore(name)))
        s_in, s_out, s_gd = sem["s_in"], sem["s_out"], sem["s_gd"]
        s_cc = sem["s_cc"]
        cV, cS, cT, cG, cY = (sem["c_v"], sem["c_s"], sem["c_t"],
                              sem["c_g"], sem["c_y"])
        pm = [sem[f"p{i}"] for i in range(7)]
        pr.cV, pr.cS = cV, cS
        bar_ctrs = {"sync": cY, "vector": cV, "scalar": cS,
                    "tensor": cT, "gpsimd": cG}

        def barrier():
            pr.barrier(bar_ctrs)

        class PBuf:
            def __init__(self, t, s):
                self.t = t
                self.s = s
                self.free = []

            def produce(self, emits):
                for (c, v) in self.free:
                    pr.wait("tensor", c, v)
                self.free = []
                for i, fn in enumerate(emits):
                    if i == len(emits) - 1:
                        pr.op("tensor", fn, inc=self.s)
                    else:
                        pr.op("tensor", fn)
                return self.s.n

            def consumed(self, ctr, val=None):
                self.free.append((ctr, ctr.n if val is None else val))

        idn_b = stack.enter_context(nc.sbuf_tensor("idn_b", [P, P], BF16))
        idn_f = stack.enter_context(nc.sbuf_tensor("idn_f", [P, P], F32))
        ones_r = stack.enter_context(nc.sbuf_tensor("ones_r", [1, P], BF16))
        ones_rf = stack.enter_context(nc.sbuf_tensor("ones_rf", [1, P], F32))
        b1row = stack.enter_context(nc.sbuf_tensor("b1row", [1, HID], BF16))
        b2row = stack.enter_context(nc.sbuf_tensor("b2row", [1, D], BF16))
        scr = stack.enter_context(nc.sbuf_tensor("scr", [P, HID], BF16))
        sp = stack.enter_context(nc.sbuf_tensor("sp", [P, 64], F32))
        xin = [stack.enter_context(nc.sbuf_tensor(f"xin{i}", [P, D], F32))
               for i in range(2)]

        def spc(i):
            return sp[:, i:i + 1]

        spb = sp.bitcast(BF16)

        scrf = scr.bitcast(F32)

        pr.op("gpsimd", lambda e: e.memset(idn_b[:, :], 0.0), inc=cG)
        pr.wait("gpsimd", cG)
        pr.op("gpsimd", lambda e: e.affine_select(
            out=idn_b[:, :], in_=idn_b[:, :], compare_op=OP.not_equal,
            fill=1.0, base=0, pattern=[[-1, P]], channel_multiplier=1), inc=cG)
        pr.op("gpsimd", lambda e: e.memset(idn_f[:, :], 0.0), inc=cG)
        pr.wait("gpsimd", cG)
        pr.op("gpsimd", lambda e: e.affine_select(
            out=idn_f[:, :], in_=idn_f[:, :], compare_op=OP.not_equal,
            fill=1.0, base=0, pattern=[[-1, P]], channel_multiplier=1), inc=cG)
        pr.op("gpsimd", lambda e: e.memset(ones_r[:, :], 1.0), inc=cG)
        pr.op("gpsimd", lambda e: e.memset(ones_rf[:, :], 1.0), inc=cG)
        pr.op("gpsimd", lambda e: e.memset(sp[:, 33:34], 1.0), inc=cG)
        pr.op("gpsimd", lambda e: e.memset(sp[:, 30:31], 1e-5), inc=cG)
        pr.op("gpsimd", lambda e: e.memset(sp[:, 31:32], 1e-6), inc=cG)
        pr.dma("gpsimd", b1row[:, :], b1_d[None, :], s_gd)
        pr.dma("gpsimd", b2row[:, :], b2_d[None, :], s_gd)
        consts_g = cG.n

        class TrPool:
            def __init__(self, bufs):
                self.b = bufs
                self.i = 0

            def trN(self, src_fn, dst_ap, k, src_ctr=None, src_val=None):
                pb = self.b[self.i % len(self.b)]
                self.i += 1
                if src_ctr is not None:
                    pr.wait("tensor", src_ctr, src_val)
                pr.wait("tensor", cG, consts_g)
                kk = pb.produce([
                    lambda e, j=j, p=pb.t: e.transpose(
                        p[:, j * P:(j + 1) * P], src_fn(j), idn_b[:, :])
                    for j in range(k)])
                pr.wait("vector", pb.s, kk)
                pr.vop(lambda e, p=pb.t, d=dst_ap, k=k: e.tensor_copy(
                    out=d, in_=p[:, 0:k * P]))
                pb.consumed(cV)

            def tr(self, src_ap, dst_ap, rows=P, cols=P,
                   src_ctr=None, src_val=None):
                pb = self.b[self.i % len(self.b)]
                self.i += 1
                if src_ctr is not None:
                    pr.wait("tensor", src_ctr, src_val)
                pr.wait("tensor", cG, consts_g)
                k = pb.produce([lambda e, s=src_ap, p=pb.t, rows=rows,
                                cols=cols: e.transpose(
                                    p[:cols, :rows], s, idn_b[:rows, :rows])])
                pr.wait("vector", pb.s, k)
                pr.vop(lambda e, p=pb.t, d=dst_ap, rows=rows, cols=cols:
                       e.tensor_copy(out=d, in_=p[:cols, :rows]))
                pb.consumed(cV)

        def ln_stats(xt, W):
            pr.vop(lambda e: e.reduce_sum(out=spc(3), in_=xt[:, :W],
                                          axis=AX.X))
            pr.sop(lambda e: e.activation(out=scr[:, :W], in_=xt[:, :W],
                                          func=AF.Square, accum_out=spc(4)))
            pr.wait("vector", cS)
            pr.vop(lambda e: e.tensor_scalar_mul(spc(0), spc(3), 1.0 / W))
            pr.vop(lambda e: e.tensor_scalar_mul(spc(1), spc(4), 1.0 / W))
            pr.vop(lambda e: e.tensor_mul(spc(3), spc(0), spc(0)))
            pr.vop(lambda e: e.tensor_sub(spc(1), spc(1), spc(3)))
            pr.wait("scalar", cV)
            pr.sop(lambda e: e.activation(out=spc(3), in_=spc(1),
                                          func=AF.Sqrt, bias=spc(30)))
            pr.wait("vector", cS)
            pr.vop(lambda e: e.reciprocal(out=spc(2), in_=spc(3)))

        def build_fm(wdst, w_hbm, rows, cols, tp, wtmp):
            nrt, ncc = rows // P, cols // P
            for rt in range(nrt):
                b = rt % 2
                pr.wait("gpsimd", cV)
                v = pr.dma("gpsimd", wtmp[b][:, :cols],
                           w_hbm[rt * P:(rt + 1) * P, :], s_gd)
                pr.wait("tensor", s_gd, v)
                for c in range(ncc):
                    tp.tr(wtmp[b][:, c * P:(c + 1) * P],
                          wdst[:, c * rows + rt * P: c * rows + (rt + 1) * P])

        pr.phase = "p0a_prep"
        st0 = ExitStack()
        vwT = st0.enter_context(nc.sbuf_tensor("vwT", [P, DC * D], BF16))
        owT = st0.enter_context(nc.sbuf_tensor("owT", [P, DC * D], BF16))
        basT = st0.enter_context(nc.sbuf_tensor("basT", [P, DC * R], BF16))
        hm = st0.enter_context(nc.sbuf_tensor("hm", [R, NH * R], F32))
        hm_b = st0.enter_context(nc.sbuf_tensor("hm_b", [R, NH * R], BF16))
        wtmp = [st0.enter_context(nc.sbuf_tensor(f"wtmp{i}", [P, D], BF16))
                for i in range(2)]
        hrbuf = st0.enter_context(nc.sbuf_tensor("hrbuf", [R, NH * R], F32))
        sm0 = ExitStack()
        tb0 = [PBuf(sm0.enter_context(
            nc.psum_tensor(f"tb0{i}", [P, 1024], BF16)), pm[i])
            for i in range(2)]
        pp0 = PBuf(sm0.enter_context(nc.psum_tensor("pp0", [P, 1024], F32)),
                   pm[2])
        tp0 = TrPool(tb0)

        if nphases >= 1:
            build_fm(vwT, vw_d, D, D, tp0, wtmp)
            build_fm(owT, ow_d, D, D, tp0, wtmp)
            pr.wait("gpsimd", cV)
            v = pr.dma("gpsimd", wtmp[0][:R, :D], basis_d, s_gd)
            pr.wait("tensor", s_gd, v)
            for c in range(DC):
                tp0.tr(wtmp[0][:R, c * P:(c + 1) * P],
                       basT[:, c * R:(c + 1) * R], rows=R, cols=P)
            pr.dma("sync", hm[:R, 0:R], core_d, s_in)
            pr.dma("sync",
                   hrbuf[:R, :].rearrange("p (h s) -> p h s", h=NH),
                   hres_d.transpose([1, 0, 2]), s_in)
            pr.wait("tensor", s_in)
            k = pp0.produce([lambda e: e.transpose(
                pp0.t[:R, 0:R], hm[:R, 0:R], idn_f[:R, :R])])
            pr.wait("vector", pp0.s, k)
            pr.wait("vector", s_in)
            pr.vop(lambda e: e.reduce_sum(
                out=scrf[:R, 0:R],
                in_=hrbuf[:R, :].rearrange("p (h s) -> p s h", h=NH),
                axis=AX.X))
            pr.vop(lambda e: e.tensor_add(
                hm[:R, 0:R], hm[:R, 0:R], pp0.t[:R, 0:R]))
            pp0.consumed(cV)
            pr.vop(lambda e: e.tensor_scalar_mul(
                hm[:R, 0:R], hm[:R, 0:R], 0.5 / NH))
            pr.vop(lambda e: e.tensor_scalar_mul(
                scrf[:R, 0:R], scrf[:R, 0:R], 1.0 / NH))
            pr.vop(lambda e: e.tensor_sub(
                scrf[:R, R:2 * R], hm[:R, 0:R], scrf[:R, 0:R]))
            for h in range(NH):
                pr.vop(lambda e, h=h: e.tensor_add(
                    hm[:R, h * R:(h + 1) * R],
                    hrbuf[:R, h * R:(h + 1) * R],
                    scrf[:R, R:2 * R]))
            pr.vop(lambda e: e.tensor_copy(out=hm_b[:R, :], in_=hm[:R, :]))
            barrier()

        pr.phase = "p1_attn_a"
        if nphases >= 2:
            st1 = ExitStack()
            ulow = st1.enter_context(nc.sbuf_tensor("ulow", [P, P], BF16))
            nbigI = st1.enter_context(nc.sbuf_tensor("nbigI", [P, P], BF16))
            obias = st1.enter_context(nc.sbuf_tensor("obias", [1, D], BF16))
            pr.op("gpsimd", lambda e: e.memset(ulow[:, :], 1.0), inc=cG)
            pr.wait("gpsimd", cG)
            pr.op("gpsimd", lambda e: e.affine_select(
                out=ulow[:, :], in_=ulow[:, :], compare_op=OP.is_gt,
                fill=0.0, base=0, pattern=[[-1, P]], channel_multiplier=1),
                inc=cG)
            pr.op("gpsimd", lambda e: e.memset(nbigI[:, :], 0.0), inc=cG)
            pr.wait("gpsimd", cG)
            pr.op("gpsimd", lambda e: e.affine_select(
                out=nbigI[:, :], in_=nbigI[:, :], compare_op=OP.not_equal,
                fill=NEG, base=0, pattern=[[-1, P]], channel_multiplier=1),
                inc=cG)
            masks_g = cG.n
            pr.dma("sync", sp[:, 40:40 + DC],
                   vb_d.rearrange("(c p) -> p c", p=P), s_in,
                   allow_slow_non_contiguous=True)
            pr.dma("sync", scrf[0:1, 1024:1024 + D], ob_d[None, :], s_in)
            pr.wait("vector", s_in)
            pr.vop(lambda e: e.tensor_copy(
                out=wtmp[1][:, 0:DC], in_=sp[:, 40:40 + DC]))
            vb_ready = cV.n
            emits = []
            for c in range(DC):
                for nn in range(2):
                    emits.append(lambda e, c=c, nn=nn: e.matmul(
                        pp0.t[0:1, nn * 512:(nn + 1) * 512],
                        wtmp[1][:, c:c + 1],
                        owT[:, c * D + nn * 512: c * D + (nn + 1) * 512],
                        start=(c == 0), stop=(c == DC - 1)))
            pr.wait("tensor", cV, vb_ready)
            k = pp0.produce(emits)
            pr.wait("vector", pp0.s, k)
            pr.vop(lambda e: e.tensor_add(
                obias[0:1, :], pp0.t[0:1, 0:D], scrf[0:1, 1024:1024 + D]))
            pp0.consumed(cV)

            lat = st1.enter_context(nc.sbuf_tensor("lat", [P, T], BF16))
            ltf = st1.enter_context(nc.sbuf_tensor("ltf", [P, 8 * T], BF16))
            vaug = st1.enter_context(
                nc.sbuf_tensor("vaug", [P, NT * NH * 65], BF16))
            pTb = [st1.enter_context(nc.sbuf_tensor(f"pT{i}", [P, 1024], BF16))
                   for i in range(2)]
            xrs = [st1.enter_context(nc.sbuf_tensor(f"xrs{i}", [P, D], F32))
                   for i in range(2)]
            yT = ltf

            def dstage_ap(i):
                return scrf[0:1, i * 1024:(i + 1) * 1024]

            def dsb_ap(h, qc):
                p0 = 32 + h + 16 * qc
                return scrf[p0:p0 + 1, 0:1024]

            def rhi_ap(h, qc):
                p0 = 32 + h + 16 * qc
                return scr[p0:p0 + 1, 2048:3072]

            def rlo_ap(h, qc):
                p0 = 32 + h + 16 * qc
                return scr[p0:p0 + 1, 3072:4096]

            st1a = ExitStack()
            xlnT = st1a.enter_context(nc.sbuf_tensor("xlnT", [P, DC * T], BF16))
            xlnb = st1a.enter_context(nc.sbuf_tensor("xlnb", [P, D], BF16))

            for t in range(NT):
                b = t % 2
                pr.wait("sync", cV)
                pr.wait("sync", cS)
                pr.dma("sync", xin[b][:, :], x0_d[t * P:(t + 1) * P, :], s_in)
                pr.wait("vector", s_in)
                pr.wait("scalar", s_in)
                ln_stats(xin[b], D)
                pr.vop(lambda e, b=b: e.scalar_tensor_tensor(
                    out=xlnb[:, :], in0=xin[b][:, :], scalar=spc(0),
                    in1=spc(2).broadcast_to([P, D]),
                    op0=OP.subtract, op1=OP.mult))
                stt_v = cV.n
                for c in range(DC):
                    tp0.tr(xlnb[:, c * P:(c + 1) * P],
                           xlnT[:, c * T + t * P: c * T + (t + 1) * P],
                           src_ctr=cV, src_val=stt_v)

            for nq in range(4):
                emits = []
                for c in range(DC):
                    emits.append(lambda e, nq=nq, c=c: e.matmul(
                        pp0.t[:R, 0:512],
                        basT[:, c * R:(c + 1) * R],
                        xlnT[:, c * T + nq * 512: c * T + (nq + 1) * 512],
                        start=(c == 0), stop=(c == DC - 1)))
                pr.wait("tensor", cV)
                k = pp0.produce(emits)
                pr.wait("vector", pp0.s, k)
                pr.vop(lambda e, nq=nq: e.tensor_copy(
                    out=lat[:R, nq * 512:(nq + 1) * 512],
                    in_=pp0.t[:R, 0:512]))
                pp0.consumed(cV)
            pr.wait("sync", cV)
            pr.dma("sync", lat[R:2 * R, :], lat[0:R, :], s_in)

            for h in range(NH):
                half = h % 2
                for nq in range(4):
                    pr.wait("tensor", s_in)
                    k = pp0.produce([lambda e, h=h, nq=nq, half=half: e.matmul(
                        pp0.t[half * R:(half + 1) * R, 0:512],
                        hm_b[:R, h * R:(h + 1) * R],
                        lat[0:R, nq * 512:(nq + 1) * 512],
                        start=True, stop=True,
                        tile_position=(0, half * R))])
                    pr.wait("vector", pp0.s, k)
                    pr.vop(lambda e, h=h, nq=nq, half=half: e.tensor_copy(
                        out=ltf[half * R:(half + 1) * R,
                                (h // 2) * T + nq * 512:
                                (h // 2) * T + (nq + 1) * 512],
                        in_=pp0.t[half * R:(half + 1) * R, 0:512]))
                    pp0.consumed(cV)

            pr.vop(lambda e: e.reduce_max(
                out=spc(8), in_=lat[:, :], axis=AX.X,
                apply_absolute_value=True))
            pr.vop(lambda e: e.reduce_max(
                out=spc(9), in_=ltf[:, :], axis=AX.X,
                apply_absolute_value=True))
            pr.wait("tensor", cV)
            k = pp0.produce([
                lambda e: e.transpose(pp0.t[0:1, 0:P], sp[:, 8:9],
                                      idn_f[:, :]),
                lambda e: e.transpose(pp0.t[0:1, 256:256 + P], sp[:, 9:10],
                                      idn_f[:, :])])
            pr.wait("vector", pp0.s, k)
            pr.vop(lambda e: e.tensor_copy(
                out=scrf[0:1, 0:P], in_=pp0.t[0:1, 0:P]))
            pr.vop(lambda e: e.tensor_copy(
                out=scrf[0:1, P:2 * P], in_=pp0.t[0:1, 256:256 + P]))
            pp0.consumed(cV)
            pr.vop(lambda e: e.reduce_max(
                out=scrf[0:1, 300:301], in_=scrf[0:1, 0:P], axis=AX.X))
            pr.vop(lambda e: e.reduce_max(
                out=scrf[0:1, 301:302], in_=scrf[0:1, P:2 * P], axis=AX.X))
            pr.vop(lambda e: e.tensor_mul(
                scrf[0:1, 302:303], scrf[0:1, 300:301], scrf[0:1, 301:302]))
            pr.vop(lambda e: e.tensor_scalar_mul(
                scrf[0:1, 302:303], scrf[0:1, 302:303], -8.0))
            pr.wait("tensor", cV)
            k = pp0.produce([lambda e: e.matmul(
                pp0.t[:, 512:513], ones_rf[0:1, :], scrf[0:1, 302:303],
                start=True, stop=True)])
            pr.wait("vector", pp0.s, k)
            pr.vop(lambda e: e.tensor_copy(out=spc(12), in_=pp0.t[:, 512:513]))
            pp0.consumed(cV)
            negc = spc(12)

            pr.vop(lambda e: e.memset(
                vaug[:, :].rearrange("p (u x) -> p u x", x=65)[:, :, 64:65],
                1.0))
            for ut in range(NT):
                for half in range(2):
                    emits = []
                    for c in range(DC):
                        emits.append(lambda e, ut=ut, half=half, c=c: e.matmul(
                            pp0.t[:, 0:512],
                            xlnT[:, c * T + ut * P: c * T + (ut + 1) * P],
                            vwT[:, c * D + half * 512:
                                c * D + (half + 1) * 512],
                            start=(c == 0), stop=(c == DC - 1)))
                    pr.wait("tensor", cV)
                    k = pp0.produce(emits)
                    pr.wait("vector", pp0.s, k)
                    pr.vop(lambda e, ut=ut, half=half: e.tensor_copy(
                        out=vaug[:, ut * NH * 65 + half * 8 * 65:
                                 ut * NH * 65 + (half * 8 + 8) * 65]
                        .rearrange("p (u x) -> p u x", x=65)[:, :, 0:64],
                        in_=pp0.t[:, 0:512]))
                    pp0.consumed(cV)
            barrier()
            st1a.close()
            sm0.close()

            sm1 = ExitStack()
            ps_sb = [PBuf(sm1.enter_context(
                nc.psum_tensor(f"ps_s{i}", [P, 1024], F32)), pm[i and 5])
                for i in range(2)]
            ps_v = sm1.enter_context(nc.psum_tensor("ps_v", [P, 1024], F32))
            ps_r = PBuf(sm1.enter_context(
                nc.psum_tensor("ps_r", [P, 1024], F32)), pm[2])
            pAV = pm[1]
            pT_free = [0, 0]
            psv_free = []
            for h in range(NH):
                half = h % 2
                for qc in range(2):
                    nut = 8 * (qc + 1)
                    exp_v = {}
                    mz_v = {}
                    col0s = {}

                    def scores(ut, h=h, qc=qc, half=half, nut=nut,
                               exp_v=exp_v, mz_v=mz_v, col0s=col0s):
                        qlo = qc * 1024
                        col0 = max(0, ut * P - qlo)
                        col0s[ut] = col0
                        diag = ut >= 8 * qc
                        sb = ps_sb[ut % 2]
                        emits = []
                        a = col0
                        while a < 1024:
                            b2 = min(1024, (a // 512 + 1) * 512)
                            dmask = diag and a <= col0 < b2
                            emits.append(lambda e, ut=ut, a=a, b2=b2,
                                         dmask=dmask, sb=sb:
                                         e.matmul(
                                             sb.t[:, a:b2],
                                             lat[half * R:(half + 1) * R,
                                                 ut * P:(ut + 1) * P],
                                             ltf[half * R:(half + 1) * R,
                                                 (h // 2) * T + qc * 1024 + a:
                                                 (h // 2) * T + qc * 1024 + b2],
                                             start=True, stop=not dmask,
                                             tile_position=(half * R, 0)))
                            if dmask:
                                emits.append(lambda e, col0=col0, sb=sb:
                                             e.matmul(
                                                 sb.t[:, col0:col0 + P],
                                                 nbigI[:, :], ulow[:, :],
                                                 start=False, stop=True))
                            a = b2
                        pr.wait("tensor", cG, masks_g)
                        sk = sb.produce(emits)
                        pb = pTb[ut % 2]
                        mz_v[ut] = 0
                        if col0 > 0:
                            pr.wait("vector", pAV, pT_free[ut % 2])
                            mz_v[ut] = pr.vop(lambda e, pb=pb, col0=col0:
                                              e.memset(pb[:, 0:col0], 0.0))
                        pr.wait("scalar", sb.s, sk)
                        pr.wait("scalar", pAV, pT_free[ut % 2])
                        pr.sop(lambda e, pb=pb, col0=col0, sb=sb:
                               e.activation(
                                   out=pb[:, col0:1024],
                                   in_=sb.t[:, col0:1024],
                                   func=AF.Exp, bias=negc, scale=0.125))
                        sb.consumed(cS)
                        exp_v[ut] = cS.n

                    def av(ut, h=h, qc=qc, nut=nut,
                           exp_v=exp_v, mz_v=mz_v):
                        nonlocal psv_free
                        pb = pTb[ut % 2]
                        pr.wait("tensor", cS, exp_v[ut])
                        if mz_v[ut]:
                            pr.wait("tensor", cV, mz_v[ut])
                        if ut == 0:
                            for (c_, v_) in psv_free:
                                pr.wait("tensor", c_, v_)
                            psv_free = []
                        for hq in range(2):
                            pr.op("tensor",
                                  lambda e, ut=ut, hq=hq, pb=pb,
                                  st=(ut == 0), sp_=(ut == nut - 1):
                                  e.matmul(
                                      ps_v[:65, hq * 512:(hq + 1) * 512],
                                      vaug[:, ut * NH * 65 + h * 65:
                                           ut * NH * 65 + h * 65 + 65],
                                      pb[:, hq * 512:(hq + 1) * 512],
                                      start=st, stop=sp_), inc=pAV)
                        pT_free[ut % 2] = pAV.n

                    scores(0)
                    for ut in range(1, nut):
                        scores(ut)
                        av(ut - 1)
                    av(nut - 1)
                    av_k = pAV.n
                    stg = (h * 2 + qc) % 2
                    pr.wait("scalar", pAV, av_k)
                    pr.wait("scalar", s_in)
                    pr.sop(lambda e, stg=stg: e.copy(
                        out=dstage_ap(stg), in_=ps_v[64:65, 0:1024]))
                    dcp_v = cS.n
                    pr.wait("sync", cS, dcp_v)
                    pr.dma("sync", dsb_ap(h, qc), dstage_ap(stg), s_in)
                    pr.wait("vector", pAV, av_k)
                    pr.vop(lambda e, h=h, qc=qc, half=half: e.tensor_copy(
                        out=yT[half * R:(half + 1) * R,
                               (h // 2) * T + qc * 1024:
                               (h // 2) * T + (qc + 1) * 1024],
                        in_=ps_v[0:64, 0:1024]))
                    psv_free = [(cV, cV.n), (cS, dcp_v)]
            pr.wait("vector", cS)
            pr.wait("vector", s_in)
            pr.vop(lambda e: e.reciprocal(
                out=scrf[32:64, 0:1024], in_=scrf[32:64, 0:1024]))
            pr.vop(lambda e: e.tensor_copy(
                out=scr[32:64, 2048:3072], in_=scrf[32:64, 0:1024]))
            pr.vop(lambda e: e.tensor_sub(
                scrf[32:64, 0:1024], scrf[32:64, 0:1024],
                scr[32:64, 2048:3072]))
            pr.vop(lambda e: e.tensor_copy(
                out=scr[32:64, 3072:4096], in_=scrf[32:64, 0:1024]))
            for h in range(NH):
                half = h % 2
                for qc in range(2):
                    pr.wait("sync", cV)
                    pr.wait("sync", cS)
                    pr.wait("sync", pAV, pAV.n)
                    pr.dma("sync", pTb[0][0:1, 0:1024], rhi_ap(h, qc), s_in)
                    pr.dma("sync", pTb[1][0:1, 0:1024], rlo_ap(h, qc), s_in)
                    pr.wait("tensor", s_in)
                    emits = []
                    for hq in range(2):
                        emits.append(lambda e, hq=hq: e.matmul(
                            ps_r.t[0:64, hq * 512:(hq + 1) * 512],
                            ones_r[0:1, 0:64],
                            pTb[0][0:1, hq * 512:(hq + 1) * 512],
                            start=True, stop=False))
                        emits.append(lambda e, hq=hq: e.matmul(
                            ps_r.t[0:64, hq * 512:(hq + 1) * 512],
                            ones_r[0:1, 0:64],
                            pTb[1][0:1, hq * 512:(hq + 1) * 512],
                            start=False, stop=True))
                    k = ps_r.produce(emits)
                    pr.wait("vector", ps_r.s, k)
                    pr.vop(lambda e, h=h, qc=qc, half=half: e.tensor_tensor(
                        out=yT[half * R:(half + 1) * R,
                               (h // 2) * T + qc * 1024:
                               (h // 2) * T + (qc + 1) * 1024],
                        in0=yT[half * R:(half + 1) * R,
                               (h // 2) * T + qc * 1024:
                               (h // 2) * T + (qc + 1) * 1024],
                        in1=ps_r.t[0:64, 0:1024], op=OP.mult))
                    ps_r.consumed(cV)
            barrier()
            sm1.close()

            sm2 = ExitStack()
            po = [PBuf(sm2.enter_context(
                nc.psum_tensor(f"po{i}", [P, 1024], F32)), pm[3 + i])
                for i in range(2)]
            for t in range(NT):
                b = t % 2
                pr.wait("sync", cV)
                pr.wait("sync", s_out)
                pr.dma("sync", xrs[b][:, :], x0_d[t * P:(t + 1) * P, :], s_in)
                emits = []
                for c in range(DC):
                    for nn in range(2):
                        emits.append(lambda e, t=t, c=c, b=b, nn=nn: e.matmul(
                            po[b].t[:, nn * 512:(nn + 1) * 512],
                            yT[:, c * T + t * P:c * T + (t + 1) * P],
                            owT[:, c * D + nn * 512: c * D + (nn + 1) * 512],
                            start=(c == 0), stop=False))
                for nn in range(2):
                    emits.append(lambda e, b=b, nn=nn: e.matmul(
                        po[b].t[:, nn * 512:(nn + 1) * 512],
                        ones_r[0:1, :],
                        obias[0:1, nn * 512:(nn + 1) * 512],
                        start=False, stop=True))
                k = po[b].produce(emits)
                pr.wait("vector", po[b].s, k)
                pr.wait("vector", s_in)
                pr.vop(lambda e, b=b: e.tensor_add(
                    xrs[b][:, :], po[b].t[:, 0:D], xrs[b][:, :]))
                po[b].consumed(cV)
                pr.wait("sync", cV)
                pr.dma("sync", x1_d[t * P:(t + 1) * P, :], xrs[b][:, :], s_out)
            pr.wait("sync", s_out)
            barrier()
            sm2.close()
            st1.close()
        st0.close()
        if nphases < 2:
            sm0.close()

        pr.phase = "p0b_wquant"
        st2 = ExitStack()
        w1q = w2q = None
        if nphases >= 3:
            w1q = st2.enter_context(nc.sbuf_tensor("w1q", [P, DC * HID], BF16))
            w2q = st2.enter_context(nc.sbuf_tensor("w2q", [P, HC * D], BF16))
            st2t = ExitStack()
            wf2 = [st2t.enter_context(nc.sbuf_tensor(f"wf{i}", [P, HID], F32))
                   for i in range(2)]
            w8 = st2t.enter_context(nc.sbuf_tensor("w8", [P, HID], I8))
            sm2b = ExitStack()
            tb2 = [PBuf(sm2b.enter_context(
                nc.psum_tensor(f"tb2{i}", [P, 1024], BF16)), pm[i])
                for i in range(2)]
            pq2 = PBuf(sm2b.enter_context(
                nc.psum_tensor("pq2", [P, 512], F32)), pm[2])
            tp2 = TrPool(tb2)

            def quant_weight(w_hbm, nrt, ncc, wdst, rows, im):
                W = ncc * P
                pr.vop(lambda e: e.memset(spc(20), 0.0))
                act_done = {}
                for rt in range(nrt):
                    b = rt % 2
                    if rt >= 2:
                        pr.wait("sync", cS, act_done[rt - 2])
                    else:
                        pr.wait("sync", cS)
                    pr.dma("sync", wf2[b][:, :W],
                           w_hbm[rt * P:(rt + 1) * P, :], s_in)
                    d = s_in.n
                    pr.wait("scalar", s_in, d)
                    pr.wait("scalar", cV)
                    pr.sop(lambda e, b=b: e.activation(
                        out=scr[:, :W], in_=wf2[b][:, :W],
                        func=AF.Abs, accum_out=spc(16)))
                    act_done[rt] = cS.n
                    pr.wait("vector", cS)
                    pr.vop(lambda e: e.tensor_add(spc(20), spc(20), spc(16)))
                pr.wait("tensor", cV)
                pr.wait("tensor", cG, consts_g)
                k = pq2.produce([lambda e: e.matmul(
                    pq2.t[0:1, 0:1], sp[:, 33:34], spc(20),
                    start=True, stop=True)])
                pr.wait("vector", pq2.s, k)
                pr.vop(lambda e: e.tensor_copy(
                    out=scrf[0:1, 300:301], in_=pq2.t[0:1, 0:1]))
                pq2.consumed(cV)
                pr.vop(lambda e: e.tensor_scalar_mul(
                    scrf[0:1, 300:301], scrf[0:1, 300:301],
                    1.0 / (nrt * P * W)))
                pr.vop(lambda e: e.tensor_scalar_max(
                    scrf[0:1, 300:301], scrf[0:1, 300:301], 1e-5))
                pr.vop(lambda e: e.reciprocal(
                    out=scrf[0:1, 301:302], in_=scrf[0:1, 300:301]))
                pr.wait("tensor", cV)
                k = pq2.produce([
                    lambda e: e.matmul(pq2.t[:, 1:2], ones_rf[0:1, :],
                                       scrf[0:1, 300:301],
                                       start=True, stop=True),
                    lambda e: e.matmul(pq2.t[:, 2:3], ones_rf[0:1, :],
                                       scrf[0:1, 301:302],
                                       start=True, stop=True)])
                pr.wait("vector", pq2.s, k)
                pr.vop(lambda e, im=im: e.tensor_copy(
                    out=spc(im), in_=pq2.t[:, 1:2]))
                pr.vop(lambda e: e.tensor_copy(out=spc(18), in_=pq2.t[:, 2:3]))
                pq2.consumed(cV)
                q_done = {}
                evac_done = {}
                for rt in range(nrt):
                    b = rt % 2
                    if rt >= 2:
                        pr.wait("sync", cS, q_done[rt - 2])
                    else:
                        pr.wait("sync", cS, act_done[nrt - 2 + rt])
                    pr.dma("sync", wf2[b][:, :W],
                           w_hbm[rt * P:(rt + 1) * P, :], s_in)
                    d = s_in.n
                    pr.wait("scalar", s_in, d)
                    if rt == 0:
                        pr.wait("scalar", cV)
                    else:
                        pr.wait("scalar", cV, evac_done[rt - 1])
                    pr.sop(lambda e, b=b: e.activation(
                        out=w8[:, :W], in_=wf2[b][:, :W],
                        func=AF.Copy, scale=spc(18)))
                    pr.sop(lambda e: e.activation(
                        out=scr[:, :W], in_=w8[:, :W], func=AF.Sign))
                    pr.sop(lambda e: e.activation(
                        out=sp[:, 61:62], in_=scr[:, W - 1:W], func=AF.Copy))
                    sign_v = cS.n
                    q_done[rt] = cS.n
                    for c in range(ncc):
                        tp2.tr(scr[:, c * P:(c + 1) * P],
                               wdst[:, c * rows + rt * P:
                                    c * rows + (rt + 1) * P],
                               src_ctr=cS, src_val=sign_v)
                    evac_done[rt] = cV.n

            quant_weight(w1_d, HID // P, DC, w1q, HID, 21)
            quant_weight(w2_d, D // P, HC, w2q, D, 23)
            pr.vop(lambda e: e.reciprocal(out=spc(24), in_=spc(21)))
            pr.vop(lambda e: e.reciprocal(out=spc(25), in_=spc(23)))
            pr.wait("sync", cV)
            pr.dma("sync", wq1_d, w1q[:, :], s_out)
            pr.dma("sync", wq2_d, w2q[:, :], s_out)
            barrier()
            sm2b.close()
            st2t.close()

        def emit_ffn(xin_d, xout_d, w1qt, w2qt, stf, tag=""):
            xq8s = stf.enter_context(nc.sbuf_tensor(tag + "xq8s", [P, D], I8))
            xqT = stf.enter_context(
                nc.sbuf_tensor(tag + "xqT", [P, DC * P], BF16))
            xsc = stf.enter_context(nc.sbuf_tensor(tag + "xsc", [P, D], BF16))
            hti = [stf.enter_context(
                nc.sbuf_tensor(tag + f"hti{i}", [P, HID], F32))
                for i in range(2)]
            hqT = stf.enter_context(
                nc.sbuf_tensor(tag + "hqT", [P, (HC // 2) * P], BF16))
            hq8h = stf.enter_context(
                nc.sbuf_tensor(tag + "hq8h", [P, HID // 2], I8))
            sxc = stf.enter_context(nc.sbuf_tensor(tag + "sxc", [P, NT], F32))
            exc = stf.enter_context(nc.sbuf_tensor(tag + "exc", [P, NT], F32))
            e2c = stf.enter_context(nc.sbuf_tensor(tag + "e2c", [P, 2], F32))
            sbB = stf.enter_context(nc.sbuf_tensor(tag + "sbB", [1, P], BF16))
            s2B = stf.enter_context(nc.sbuf_tensor(tag + "s2B", [1, P], BF16))
            xl2 = stf.enter_context(nc.sbuf_tensor(tag + "xl2", [P, D], F32))
            xin3 = [xin[0], xin[1], stf.enter_context(
                nc.sbuf_tensor(tag + "xin3_2", [P, D], F32))]
            smf = ExitStack()
            tbf = [PBuf(smf.enter_context(
                nc.psum_tensor(tag + f"tbf{i}", [P, 1024], BF16)), pm[i])
                for i in range(2)]
            hp = [PBuf(smf.enter_context(
                nc.psum_tensor(tag + f"hpf{i}", [P, 512], F32)), pm[2 + i])
                for i in range(2)]
            pof = PBuf(smf.enter_context(
                nc.psum_tensor(tag + "poff", [P, 1024], F32)), pm[4])
            tpf = TrPool(tbf)

            xin_v = {}
            xqT_ok = {}
            sb_ok = {}
            stats_ok = {}
            qv = {}
            xv = {}
            ln_ok = {}

            def xstage(t):
                pr.wait("scalar", cV)
                pr.sop(lambda e, t=t: e.activation(
                    out=xsc[:, :], in_=xq8s[:, :], func=AF.Copy))
                pr.sop(lambda e: e.activation(
                    out=sp[:, 61:62], in_=xsc[:, D - 1:D], func=AF.Copy))
                xv[t] = cS.n

            def lnq(t, pos):
                if pos == 0:
                    pr.wait("vector", s_in, xin_v[t])
                    pr.vop(lambda e, t=t: e.reduce_sum(
                        out=spc(53), in_=xin3[t % 3][:, :], axis=AX.X))
                    pr.wait("scalar", s_in, xin_v[t])
                    pr.wait("scalar", cV)
                    pr.sop(lambda e, t=t: e.activation(
                        out=xl2[:, :], in_=xin3[t % 3][:, :],
                        func=AF.Square, accum_out=spc(54)))
                    pr.sop(lambda e: e.activation(
                        out=sp[:, 61:62], in_=spc(54), func=AF.Copy))
                elif pos == 1:
                    pr.wait("vector", cS)
                    pr.vop(lambda e: e.tensor_copy(
                        out=sp[:, 62:63], in_=sp[:, 61:62]))
                    pr.vop(lambda e: e.tensor_scalar_mul(
                        spc(50), spc(53), 1.0 / D))
                    pr.vop(lambda e: e.tensor_scalar_mul(
                        spc(51), spc(54), 1.0 / D))
                    pr.vop(lambda e: e.tensor_mul(spc(53), spc(50), spc(50)))
                    pr.vop(lambda e: e.tensor_sub(spc(51), spc(51), spc(53)))
                    pr.wait("scalar", cV)
                    pr.sop(lambda e: e.activation(
                        out=spc(52), in_=spc(51), func=AF.Sqrt,
                        bias=spc(30)))
                elif pos == 2:
                    pr.wait("vector", cS)
                    pr.vop(lambda e: e.reciprocal(out=spc(52), in_=spc(52)))
                    pr.vop(lambda e, t=t: e.scalar_tensor_tensor(
                        out=xl2[:, :], in0=xin3[t % 3][:, :], scalar=spc(50),
                        in1=spc(52).broadcast_to([P, D]),
                        op0=OP.subtract, op1=OP.mult))
                    pr.vop(lambda e: e.tensor_mul(spc(55), spc(51), spc(52)))
                    pr.vop(lambda e: e.tensor_mul(spc(55), spc(55), spc(52)))
                    pr.wait("scalar", cV)
                    pr.sop(lambda e: e.activation(
                        out=spc(56), in_=spc(55), func=AF.Sqrt,
                        bias=spc(31)))
                else:
                    pr.wait("vector", cS)
                    pr.vop(lambda e: e.reciprocal(out=spc(56), in_=spc(56)))
                    pr.vop(lambda e: e.reduce_max(
                        out=spc(57), in_=xl2[:, :], axis=AX.X,
                        apply_absolute_value=True))
                    pr.vop(lambda e: e.tensor_mul(spc(57), spc(57), spc(56)))
                    pr.vop(lambda e: e.tensor_scalar_max(spc(57), spc(57),
                                                         1e-5))
                    pr.vop(lambda e: e.reciprocal(out=spc(58), in_=spc(57)))
                    pr.vop(lambda e: e.tensor_scalar_mul(spc(58), spc(58),
                                                         127.0))
                    pr.vop(lambda e: e.tensor_mul(spc(59), spc(58), spc(56)))
                    pr.vop(lambda e, t=t: e.tensor_scalar_mul(
                        exc[:, t:t + 1], spc(57), 1.0 / 127.0))
                    pr.vop(lambda e, t=t: e.tensor_mul(
                        exc[:, t:t + 1], exc[:, t:t + 1], spc(21)))
                    pr.vop(lambda e, t=t: e.tensor_mul(
                        sxc[:, t:t + 1], spc(58), spc(24)))
                    ln_ok[t] = cV.n
                    pr.wait("scalar", cV, ln_ok[t])
                    pr.sop(lambda e, t=t: e.activation(
                        out=xq8s[:, :], in_=xl2[:, :],
                        func=AF.Copy, scale=spc(59)))
                    pr.sop(lambda e, t=t: e.activation(
                        out=sp[:, 61:62], in_=xq8s[:, D - 1:D],
                        func=AF.Copy))

            def front(t):
                if t + 1 < NT:
                    pr.wait("sync", cV)
                    pr.wait("sync", cS)
                    pr.wait("sync", s_out)
                    pr.dma("sync", xin3[(t + 1) % 3][:, :],
                           xin_d[(t + 1) * P:(t + 2) * P, :], s_in)
                    xin_v[t + 1] = s_in.n
                tpf.trN(lambda j: xsc[:, j * P:(j + 1) * P],
                        xqT[:, 0:DC * P], DC,
                        src_ctr=cS, src_val=xv[t])
                xqT_ok[t] = cV.n
                pr.vop(lambda e, t=t: e.tensor_copy(
                    out=spb[:, 120:121], in_=sxc[:, t:t + 1]))
                tpf.tr(spb[:, 120:121], sbB[0:1, :], rows=P, cols=1,
                       src_ctr=cV, src_val=cV.n)
                sb_ok[t] = cV.n

            def emit_stats(t, pos):
                if pos == 0:
                    pr.wait("scalar", cV)
                    pr.sop(lambda e, t=t: e.activation(
                        out=hti[(t + 1) % 2][:, :], in_=hti[t % 2][:, :],
                        func=AF.Square, accum_out=spc(4)))
                    pr.sop(lambda e: e.activation(
                        out=sp[:, 61:62], in_=spc(4), func=AF.Copy))
                elif pos == 1:
                    pr.wait("vector", cS)
                    pr.vop(lambda e: e.tensor_copy(
                        out=sp[:, 62:63], in_=sp[:, 61:62]))
                    pr.vop(lambda e: e.tensor_scalar_mul(
                        spc(5), spc(4), 1.0 / HID))
                    pr.wait("scalar", cV)
                    pr.sop(lambda e: e.activation(
                        out=spc(6), in_=spc(5), func=AF.Sqrt, bias=spc(31)))
                elif pos == 2:
                    pr.wait("vector", cS)
                    pr.vop(lambda e: e.reciprocal(out=spc(6), in_=spc(6)))
                    pr.vop(lambda e, t=t: e.reduce_max(
                        out=spc(7), in_=hti[t % 2][:, :], axis=AX.X,
                        apply_absolute_value=True))
                    pr.vop(lambda e: e.tensor_mul(spc(7), spc(7), spc(6)))
                    pr.vop(lambda e: e.tensor_scalar_max(spc(7), spc(7),
                                                         1e-5))
                    pr.vop(lambda e: e.reciprocal(out=spc(8), in_=spc(7)))
                    pr.vop(lambda e: e.tensor_scalar_mul(spc(8), spc(8),
                                                         127.0))
                    pr.vop(lambda e: e.tensor_mul(spc(9), spc(8), spc(6)))
                    pr.vop(lambda e, t=t: e.tensor_scalar_mul(
                        e2c[:, t % 2:t % 2 + 1], spc(7), 1.0 / 127.0))
                    pr.vop(lambda e, t=t: e.tensor_mul(
                        e2c[:, t % 2:t % 2 + 1], e2c[:, t % 2:t % 2 + 1],
                        spc(23)))
                    pr.vop(lambda e: e.tensor_mul(spc(11), spc(8), spc(25)))
                    stats_ok[t] = cV.n
                else:
                    pr.wait("scalar", cV, stats_ok[t])
                    pr.sop(lambda e, t=t: e.activation(
                        out=hq8h[:, :], in_=hti[t % 2][:, 0:HID // 2],
                        func=AF.Copy, scale=spc(9)))
                    pr.sop(lambda e: e.activation(
                        out=sp[:, 61:62], in_=hq8h[:, HID // 2 - 1:HID // 2],
                        func=AF.Copy))
                    qv[t] = cS.n

            def bl1_gelu(t, ts, tl):
                if ts is not None:
                    emit_stats(ts, 0)
                pr.wait("tensor", cV, max(xqT_ok[t], sb_ok[t]))
                for n in range(8):
                    pb = n % 2
                    emits = []
                    for c in range(DC):
                        emits.append(lambda e, n=n, c=c, pb=pb: e.matmul(
                            hp[pb].t[:, :],
                            xqT[:, c * P:(c + 1) * P],
                            w1qt[:, c * HID + n * 512:
                                 c * HID + (n + 1) * 512],
                            start=(c == 0), stop=False))
                    emits.append(lambda e, n=n, pb=pb: e.matmul(
                        hp[pb].t[:, :], sbB[0:1, :],
                        b1row[0:1, n * 512:(n + 1) * 512],
                        start=False, stop=True))
                    k = hp[pb].produce(emits)
                    pr.wait("scalar", hp[pb].s, k)
                    pr.sop(lambda e, n=n, pb=pb, t=t: e.activation(
                        out=hti[t % 2][:, n * 512:(n + 1) * 512],
                        in_=hp[pb].t[:, :],
                        func=AF.Gelu, scale=exc[:, t:t + 1]))
                    hp[pb].consumed(cS)
                    if n == 1 and ts is not None:
                        emit_stats(ts, 1)
                    if n == 2 and ts is not None:
                        emit_stats(ts, 2)
                    if n == 3 and tl is not None:
                        lnq(tl, 0)
                    if n == 4 and ts is not None:
                        emit_stats(ts, 3)
                    if n == 5 and tl is not None:
                        lnq(tl, 1)
                    if n == 6 and tl is not None:
                        lnq(tl, 2)
                    if n == 7 and tl is not None:
                        lnq(tl, 3)
                        xstage(tl)

            def back(t):
                b = t % 3
                pr.vop(lambda e: e.tensor_copy(
                    out=spb[:, 120:121], in_=sp[:, 11:12]))
                tpf.tr(spb[:, 120:121], s2B[0:1, :], rows=P, cols=1,
                       src_ctr=cV, src_val=cV.n)
                pr.wait("vector", cS, qv[t])
                pr.vop(lambda e: e.tensor_copy(
                    out=sp[:, 62:63], in_=sp[:, 61:62]))
                pr.vop(lambda e: e.tensor_copy(out=scr[:, 0:HID // 2],
                                               in_=hq8h[:, :]))
                pr.vop(lambda e: e.tensor_copy(
                    out=sp[:, 63:64], in_=scr[:, HID // 2 - 1:HID // 2]))
                c0v = cV.n
                pr.wait("scalar", cV, c0v)
                pr.sop(lambda e, t=t: e.activation(
                    out=hq8h[:, :], in_=hti[t % 2][:, HID // 2:HID],
                    func=AF.Copy, scale=spc(9)))
                pr.sop(lambda e: e.activation(
                    out=sp[:, 61:62], in_=hq8h[:, HID // 2 - 1:HID // 2],
                    func=AF.Copy))
                q1v = cS.n
                pr.wait("vector", cS, q1v)
                pr.vop(lambda e: e.tensor_copy(
                    out=sp[:, 62:63], in_=sp[:, 61:62]))
                pr.vop(lambda e: e.tensor_copy(
                    out=scr[:, HID // 2:HID], in_=hq8h[:, :]))
                pr.vop(lambda e: e.tensor_copy(
                    out=sp[:, 63:64], in_=scr[:, HID - 1:HID]))
                c1v = cV.n
                for (cc, v) in pof.free:
                    pr.wait("tensor", cc, v)
                pof.free = []
                for half in range(2):
                    cb = half * (HC // 2)
                    srcv = c0v if half == 0 else c1v
                    for hb in range(2):
                        tpf.trN(lambda j, base=cb + hb * 8:
                                scr[:, (base + j) * P:(base + j + 1) * P],
                                hqT[:, (hb * 8) * P:(hb * 8 + 8) * P], 8,
                                src_ctr=cV, src_val=srcv)
                    pr.wait("tensor", cV)
                    for c in range(HC // 2):
                        for n in range(2):
                            pr.op("tensor", lambda e, n=n, c=c, cb=cb:
                                  e.matmul(
                                      pof.t[:, n * 512:(n + 1) * 512],
                                      hqT[:, c * P:(c + 1) * P],
                                      w2qt[:, (cb + c) * D + n * 512:
                                           (cb + c) * D + (n + 1) * 512],
                                      start=(cb + c == 0), stop=False))
                pr.op("tensor", lambda e: e.matmul(
                    pof.t[:, 0:512], s2B[0:1, :], b2row[0:1, 0:512],
                    start=False, stop=True))
                pr.op("tensor", lambda e: e.matmul(
                    pof.t[:, 512:1024], s2B[0:1, :], b2row[0:1, 512:1024],
                    start=False, stop=True), inc=pof.s)
                k = pof.s.n
                pr.wait("vector", pof.s, k)
                pr.wait("vector", s_in)
                pr.vop(lambda e, b=b, t=t: e.scalar_tensor_tensor(
                    out=xin3[b][:, :], in0=pof.t[:, 0:D],
                    scalar=e2c[:, t % 2:t % 2 + 1],
                    in1=xin3[b][:, :], op0=OP.mult, op1=OP.add))
                pof.consumed(cV)
                pr.wait("sync", cV)
                pr.dma("sync", xout_d[t * P:(t + 1) * P, :], xin3[b][:, :],
                       s_out)

            pr.wait("sync", cV)
            pr.wait("sync", cS)
            pr.dma("sync", xin3[0][:, :], xin_d[0:P, :], s_in)
            xin_v[0] = s_in.n
            for pos in range(4):
                lnq(0, pos)
            xstage(0)
            for t in range(NT):
                front(t)
                bl1_gelu(t, t - 1 if t >= 1 else None,
                         t + 1 if t + 1 < NT else None)
                if t >= 1:
                    back(t - 1)
            for pos in range(4):
                emit_stats(NT - 1, pos)
            back(NT - 1)
            pr.wait("sync", s_out)
            if dbg == "q2" and tag == "a":
                pr.wait("sync", cV)
                pr.wait("sync", cS)
                fl = out_d.flatten()
                pr.dma("sync",
                       fl[P * 4096:2 * P * 4096].rearrange("(p n) -> p n", p=P),
                       hti[(NT - 1) % 2][:, :], s_out)
                pr.dma("sync",
                       fl[2 * P * 4096:2 * P * 4096 + 2048]
                       .rearrange("(p n) -> p n", p=1),
                       b1row.bitcast(F32)[0:1, :], s_out)
                pr.dma("sync",
                       fl[2 * P * 4096 + 2048:2 * P * 4096 + 2048 + P * 16]
                       .rearrange("(p n) -> p n", p=P),
                       exc[:, :], s_out)
                pr.dma("sync",
                       fl[2 * P * 4096 + 2048 + P * 16:
                          2 * P * 4096 + 2048 + P * 16 + P * 16]
                       .rearrange("(p n) -> p n", p=P),
                       sxc[:, :], s_out)
            barrier()
            smf.close()

        pr.phase = "p2_ffn1"
        if nphases >= 4:
            stf1 = ExitStack()
            emit_ffn(x1_d, x2_d, w1q, w2q, stf1, tag="a")
            stf1.close()
        st2.close()

        pr.phase = "p2b_a2a"
        if nphases >= 5:
            pr.wait("gpsimd", s_out)

            def _cc(e):
                with allow_ici_all_to_all():
                    return e.collective_compute(
                        "AllToAll", OP.bypass,
                        replica_groups=[list(range(8))],
                        ins=[x2_d], outs=[x2t_d])
            pr.op("gpsimd", _cc, inc=s_cc)

        pr.phase = "p3a_attn_b"
        if nphases >= 5:
            st3 = ExitStack()
            wkT = st3.enter_context(nc.sbuf_tensor("wkT", [P, DC * D], BF16))
            wvT = st3.enter_context(nc.sbuf_tensor("wvT", [P, DC * D], BF16))
            wqT = st3.enter_context(nc.sbuf_tensor("wqT", [P, DC * D], BF16))
            wtm3 = [st3.enter_context(nc.sbuf_tensor(f"wtm3{i}", [P, D], BF16))
                    for i in range(2)]
            S2 = [st3.enter_context(nc.sbuf_tensor(f"S2{i}", [P, 8192], F32))
                  for i in range(2)]
            prod = st3.enter_context(nc.sbuf_tensor("prod", [P, 8192], F32))
            kk = st3.enter_context(nc.sbuf_tensor("kk", [P, D], F32))
            vv = st3.enter_context(nc.sbuf_tensor("vv", [P, D], F32))
            qq = st3.enter_context(nc.sbuf_tensor("qq", [P, D], F32))
            kn = st3.enter_context(nc.sbuf_tensor("kn", [P, D], F32))
            rt3 = st3.enter_context(nc.sbuf_tensor("rt3", [P, D], F32))
            sq3 = st3.enter_context(nc.sbuf_tensor("sq3", [P, D], F32))
            xl3 = st3.enter_context(nc.sbuf_tensor("xl3", [P, D], BF16))
            x3T = st3.enter_context(nc.sbuf_tensor("x3T", [P, DC * P], BF16))
            sm3 = ExitStack()
            tb3 = [PBuf(sm3.enter_context(
                nc.psum_tensor(f"tb3{i}", [P, 1024], BF16)), pm[i])
                for i in range(2)]
            pq3 = [PBuf(sm3.enter_context(
                nc.psum_tensor(f"pq3{i}", [P, 512], F32)), pm[2 + i])
                for i in range(2)]
            tp3 = TrPool(tb3)
            build_fm(wkT, wk_d, D, D, tp3, wtm3)
            build_fm(wvT, wv_d, D, D, tp3, wtm3)
            build_fm(wqT, wqd_d, D, D, tp3, wtm3)

            pr.vop(lambda e: e.memset(S2[0][:, :], 0.0))
            pr.vop(lambda e: e.memset(S2[1][:, :], 0.0))
            pr.wait("sync", s_cc, 1)

            def S4(S):
                return S[:, :].rearrange("p (n i j) -> p n i j", i=8, j=8)

            def prod4():
                return prod[:, :].rearrange("p (n i j) -> p n i j", i=8, j=8)

            for t in range(B):
                for bt in range(2):
                    row0 = t * BS + bt * P
                    pr.wait("sync", cV)
                    pr.wait("sync", s_out)
                    pr.dma("sync", xin[bt][:, :], x2t_d[row0:row0 + P, :],
                           s_in)
                    pr.wait("vector", s_in)
                    pr.wait("scalar", s_in)
                    ln_stats(xin[bt], D)
                    pr.vop(lambda e, bt=bt: e.scalar_tensor_tensor(
                        out=xl3[:, :], in0=xin[bt][:, :], scalar=spc(0),
                        in1=spc(2).broadcast_to([P, D]),
                        op0=OP.subtract, op1=OP.mult))
                    xl3_v = cV.n
                    for c in range(DC):
                        tp3.tr(xl3[:, c * P:(c + 1) * P],
                               x3T[:, c * P:(c + 1) * P],
                               src_ctr=cV, src_val=xl3_v)
                    xT_ready = cV.n
                    for wi, (wT, dst) in enumerate(
                            ((wkT, kk), (wvT, vv), (wqT, qq))):
                        for n in range(2):
                            pb = (wi * 2 + n) % 2
                            pr.wait("tensor", cV, xT_ready)
                            emits = []
                            for c in range(DC):
                                emits.append(
                                    lambda e, wT=wT, n=n, c=c, pb=pb:
                                    e.matmul(
                                        pq3[pb].t[:, :],
                                        x3T[:, c * P:(c + 1) * P],
                                        wT[:, c * D + n * 512:
                                           c * D + (n + 1) * 512],
                                        start=(c == 0), stop=(c == DC - 1)))
                            k = pq3[pb].produce(emits)
                            pr.wait("vector", pq3[pb].s, k)
                            pr.wait("vector", cS)
                            pr.vop(lambda e, dst=dst, n=n, pb=pb:
                                   e.tensor_copy(
                                       out=dst[:, n * 512:(n + 1) * 512],
                                       in_=pq3[pb].t[:, :]))
                            pq3[pb].consumed(cV)
                    S = S2[bt]
                    pr.wait("scalar", cV)
                    pr.sop(lambda e: e.activation(
                        out=scrf[:, 0:D], in_=kk[:, :], func=AF.Square))
                    pr.wait("vector", cS)
                    pr.vop(lambda e: e.reduce_sum(
                        out=kn[:, 0:P],
                        in_=scrf[:, 0:D].rearrange("p (n j) -> p n j", j=8),
                        axis=AX.X))
                    pr.wait("scalar", cV)
                    pr.sop(lambda e: e.activation(
                        out=kn[:, P:2 * P], in_=kn[:, 0:P], func=AF.Sqrt))
                    pr.wait("vector", cS)
                    pr.vop(lambda e: e.tensor_scalar_add(
                        kn[:, P:2 * P], kn[:, P:2 * P], 1e-6))
                    pr.vop(lambda e: e.reciprocal(
                        out=kn[:, 0:P], in_=kn[:, P:2 * P]))
                    pr.vop(lambda e: e.tensor_tensor(
                        out=rt3[:, :], in0=kk[:, :],
                        in1=kn[:, 0:P].rearrange("p (n o) -> p n o", o=1)
                        .broadcast_to([P, P, 8]),
                        op=OP.mult))
                    kn4i = rt3[:, :].rearrange("p (n j) -> p n j", j=8) \
                        .unsqueeze(2).broadcast_to([P, P, 8, 8])
                    pr.vop(lambda e, S=S: e.tensor_tensor(
                        out=prod4(), in0=S4(S), in1=kn4i, op=OP.mult))
                    pr.vop(lambda e: e.reduce_sum(
                        out=sq3[:, :].rearrange("p (n i) -> p n i", i=8),
                        in_=prod4(), axis=AX.X))
                    pr.vop(lambda e: e.tensor_sub(
                        sq3[:, :], vv[:, :], sq3[:, :]))
                    pr.vop(lambda e: e.tensor_tensor(
                        out=prod4(),
                        in0=sq3[:, :].rearrange("p (n i) -> p n i", i=8)
                        .unsqueeze(3).broadcast_to([P, P, 8, 8]),
                        in1=kn4i, op=OP.mult))
                    pr.vop(lambda e, S=S: e.tensor_add(
                        prod[:, :], S[:, :], prod[:, :]))
                    pr.wait("scalar", cV)
                    pr.sop(lambda e, S=S: e.activation(
                        out=S[:, :], in_=prod[:, :], func=AF.Tanh))
                    pr.wait("vector", cS)
                    pr.vop(lambda e, S=S: e.tensor_tensor(
                        out=prod4(), in0=S4(S),
                        in1=qq[:, :].rearrange("p (n j) -> p n j", j=8)
                        .unsqueeze(2).broadcast_to([P, P, 8, 8]),
                        op=OP.mult))
                    pr.vop(lambda e: e.reduce_sum(
                        out=sq3[:, :].rearrange("p (n i) -> p n i", i=8),
                        in_=prod4(), axis=AX.X))
                    pr.wait("scalar", cV)
                    pr.sop(lambda e: e.activation(
                        out=rt3[:, :], in_=sq3[:, :], func=AF.Sigmoid))
                    pr.sop(lambda e: e.activation(
                        out=kn[:, :], in_=sq3[:, :], func=AF.Square))
                    pr.wait("vector", cS)
                    pr.vop(lambda e: e.tensor_mul(
                        rt3[:, :], kn[:, :], rt3[:, :]))
                    pr.vop(lambda e, bt=bt: e.tensor_add(
                        xin[bt][:, :], xin[bt][:, :], rt3[:, :]))
                    pr.wait("sync", cV)
                    pr.dma("sync", x3_d[row0:row0 + P, :], xin[bt][:, :],
                           s_out)
            pr.wait("sync", s_out)
            barrier()
            sm3.close()
            st3.close()

        pr.phase = "p3b_ffn2"
        if nphases >= 6:
            st4 = ExitStack()
            w1q2 = st4.enter_context(
                nc.sbuf_tensor("w1q2", [P, DC * HID], BF16))
            w2q2 = st4.enter_context(nc.sbuf_tensor("w2q2", [P, HC * D], BF16))
            pr.dma("sync", w1q2[:, :], wq1_d, s_in)
            pr.dma("sync", w2q2[:, :], wq2_d, s_in)
            pr.wait("tensor", s_in)
            emit_ffn(x3_d, out_d, w1q2, w2q2, st4, tag="b")
            st4.close()

        if dbg == "q2":
            pass
        if dbg == "q":
            pr.wait("sync", s_out)
            pr.wait("sync", cV)
            pr.wait("sync", cS)
            pr.dma("sync", out_d[0:P, 0:64], sp[:, :], s_out)
            pr.dma("sync", out_d[P:2 * P, 0:1024],
                   wq1_d.bitcast(F32)[:, 1024:2048], s_out)
            pr.dma("sync", out_d[2 * P:3 * P, 0:1024],
                   wq2_d.bitcast(F32)[:, 0:1024], s_out)
            if w2q is not None:
                pr.dma("sync", out_d[3 * P:4 * P, 0:1024],
                       w2q.bitcast(F32)[:, 0:1024], s_out)
        elif dbg is not None and dbg != "q2":
            dbg_src = {"x1": x1_d, "x2": x2_d, "x2t": x2t_d, "x3": x3_d}[dbg]
            pr.wait("sync", s_out)
            if dbg == "x2t":
                pr.wait("sync", s_cc, 1)
            pr.dma("sync", out_d, dbg_src, s_out)

        pr.wait("sync", s_out)
        pr.wait("gpsimd", s_out)

        with nc.Block() as block:
            pr.flush(block)

    return nc


def kernel(L0_attn_a_hidden_states, L0_attn_a_basis_w, L0_attn_a_core,
           L0_attn_a_head_residual, L0_attn_a_v_w, L0_attn_a_v_b,
           L0_attn_a_o_w, L0_attn_a_o_b, L0_attn_b_W_k, L0_attn_b_W_v,
           L0_attn_b_W_q, L0_ffn_w1, L0_ffn_b1, L0_ffn_w2, L0_ffn_b2):
    hs = np.ascontiguousarray(L0_attn_a_hidden_states, dtype=np.float32)
    nc = build(_NPHASES, _DBG)
    common = {
        "basis_w": np.ascontiguousarray(L0_attn_a_basis_w, np.float32),
        "core_m": np.ascontiguousarray(L0_attn_a_core, np.float32),
        "head_res": np.ascontiguousarray(L0_attn_a_head_residual, np.float32),
        "v_w": np.ascontiguousarray(L0_attn_a_v_w, np.float32),
        "v_b": np.ascontiguousarray(L0_attn_a_v_b, np.float32),
        "o_w": np.ascontiguousarray(L0_attn_a_o_w, np.float32),
        "o_b": np.ascontiguousarray(L0_attn_a_o_b, np.float32),
        "w_k": np.ascontiguousarray(L0_attn_b_W_k, np.float32),
        "w_v": np.ascontiguousarray(L0_attn_b_W_v, np.float32),
        "w_q": np.ascontiguousarray(L0_attn_b_W_q, np.float32),
        "w1": np.ascontiguousarray(L0_ffn_w1, np.float32),
        "b1": np.ascontiguousarray(L0_ffn_b1, np.float32),
        "w2": np.ascontiguousarray(L0_ffn_w2, np.float32),
        "b2": np.ascontiguousarray(L0_ffn_b2, np.float32),
    }
    in_maps = [dict(common, x0=np.ascontiguousarray(hs[i])) for i in range(8)]
    res = run_bass_kernel_spmd(nc, in_maps, core_ids=list(range(8)),
                               trace=_TRACE)
    globals()["_LAST_RES"] = res
    out = np.empty((B, T, D), dtype=np.float32)
    for s in range(8):
        out[:, s * BS:(s + 1) * BS, :] = \
            res.results[s]["out"].reshape(B, BS, D)
    return out


# version 06
# speedup vs baseline: 1.2397x; 1.0133x over previous
import numpy as np
import concourse.bass as bass
import concourse.mybir as mybir
from concourse.bass_utils import run_bass_kernel_spmd
from concourse.aot_env import allow_ici_all_to_all
from contextlib import ExitStack

F32 = mybir.dt.float32
BF16 = mybir.dt.bfloat16
I8 = mybir.dt.int8
AF = mybir.ActivationFunctionType
OP = mybir.AluOpType
AX = mybir.AxisListType

P = 128
T = 2048
NT = 16
D = 1024
DC = 8
HID = 4096
HC = 32
NH = 16
R = 64
B = 8
BS = 256
NEG = -1e30

ENGINES = ("sync", "vector", "scalar", "tensor", "gpsimd")
_NPHASES = 9
_TRACE = False
_LAST_RES = None
_DBG = None


class Ctr:
    def __init__(self, h):
        self.h = h
        self.n = 0


class Prog:
    def __init__(self, nc=None):
        self.q = {e: [] for e in ENGINES}
        self.cV = None
        self.cS = None
        self.nc = nc
        self.phase = "init"

    def _add(self, eng, fn):
        self.q[eng].append((self.phase, fn))

    def wait(self, eng, ctr, val=None):
        v = ctr.n if val is None else val
        if v > 0:
            self._add(eng, lambda e: e.wait_ge(ctr.h, v))

    def dma(self, eng, out, in_, sem, **kw):
        self._add(
            eng,
            lambda e: e.dma_start(out=out, in_=in_, **kw).then_inc(sem.h, 16))
        sem.n += 16
        return sem.n

    def op(self, eng, fn, inc=None, k=1):
        if inc is None:
            self._add(eng, fn)
            return None
        self._add(eng, lambda e: fn(e).then_inc(inc.h, k))
        inc.n += k
        return inc.n

    def vop(self, fn):
        self.wait("vector", self.cV)
        return self.op("vector", fn, inc=self.cV)

    def sop(self, fn):
        self.wait("scalar", self.cS)
        return self.op("scalar", fn, inc=self.cS)

    def barrier(self, ctrs):
        vals = {}
        for e in ENGINES:
            c = ctrs[e]
            self._add(e, lambda en, c=c: en.sem_inc(c.h, 1))
            c.n += 1
            vals[e] = c.n
        for e in ENGINES:
            for o in ENGINES:
                if o != e:
                    c, v = ctrs[o], vals[o]
                    self._add(e, lambda en, c=c, v=v: en.wait_ge(c.h, v))

    def _emit(self, eng, name):
        nc, cur, sid = self.nc, None, None
        for lab, fn in self.q[name]:
            if lab != cur:
                if cur is not None:
                    nc.leave_named_scope(cur, sid, False)
                sid, _ = nc.enter_named_scope(lab, False)
                cur = lab
            fn(eng)
        if cur is not None:
            nc.leave_named_scope(cur, sid, False)

    def flush(self, block):
        @block.sync
        def _(eng):
            self._emit(eng, "sync")

        @block.vector
        def _(eng):
            self._emit(eng, "vector")

        @block.scalar
        def _(eng):
            self._emit(eng, "scalar")

        @block.tensor
        def _(eng):
            self._emit(eng, "tensor")

        @block.gpsimd
        def _(eng):
            self._emit(eng, "gpsimd")


def build(nphases=9, dbg=None):
    nc = bass.Bass("TRN2", target_bir_lowering=False,
                   detect_race_conditions=False)

    x0_d = nc.dram_tensor("x0", [T, D], F32, kind="ExternalInput").ap()
    basis_d = nc.dram_tensor("basis_w", [R, D], F32, kind="ExternalInput").ap()
    core_d = nc.dram_tensor("core_m", [R, R], F32, kind="ExternalInput").ap()
    hres_d = nc.dram_tensor("head_res", [NH, R, R], F32,
                            kind="ExternalInput").ap()
    vw_d = nc.dram_tensor("v_w", [D, D], F32, kind="ExternalInput").ap()
    vb_d = nc.dram_tensor("v_b", [D], F32, kind="ExternalInput").ap()
    ow_d = nc.dram_tensor("o_w", [D, D], F32, kind="ExternalInput").ap()
    ob_d = nc.dram_tensor("o_b", [D], F32, kind="ExternalInput").ap()
    wk_d = nc.dram_tensor("w_k", [D, D], F32, kind="ExternalInput").ap()
    wv_d = nc.dram_tensor("w_v", [D, D], F32, kind="ExternalInput").ap()
    wqd_d = nc.dram_tensor("w_q", [D, D], F32, kind="ExternalInput").ap()
    w1_d = nc.dram_tensor("w1", [HID, D], F32, kind="ExternalInput").ap()
    b1_d = nc.dram_tensor("b1", [HID], F32, kind="ExternalInput").ap()
    w2_d = nc.dram_tensor("w2", [D, HID], F32, kind="ExternalInput").ap()
    b2_d = nc.dram_tensor("b2", [D], F32, kind="ExternalInput").ap()
    out_d = nc.dram_tensor("out", [T, D], F32, kind="ExternalOutput").ap()

    x1_d = nc.dram_tensor("x1_s", [T, D], F32).ap()
    x2_d = nc.dram_tensor("x2_s", [T, D], F32).ap()
    x2t_d = nc.dram_tensor("x2t_s", [T, D], F32).ap()
    x3_d = nc.dram_tensor("x3_s", [T, D], F32).ap()
    wq1_d = nc.dram_tensor("wq1_s", [P, DC * HID], BF16).ap()
    wq2_d = nc.dram_tensor("wq2_s", [P, HC * D], BF16).ap()

    pr = Prog(nc)

    with ExitStack() as stack:
        sem = {}
        for name in ("s_in", "s_out", "s_gd", "s_cc",
                     "c_v", "c_s", "c_t", "c_g", "c_y",
                     "p0", "p1", "p2", "p3", "p4", "p5", "p6"):
            sem[name] = Ctr(stack.enter_context(nc.semaphore(name)))
        s_in, s_out, s_gd = sem["s_in"], sem["s_out"], sem["s_gd"]
        s_cc = sem["s_cc"]
        cV, cS, cT, cG, cY = (sem["c_v"], sem["c_s"], sem["c_t"],
                              sem["c_g"], sem["c_y"])
        pm = [sem[f"p{i}"] for i in range(7)]
        pr.cV, pr.cS = cV, cS
        bar_ctrs = {"sync": cY, "vector": cV, "scalar": cS,
                    "tensor": cT, "gpsimd": cG}

        def barrier():
            pr.barrier(bar_ctrs)

        class PBuf:
            def __init__(self, t, s):
                self.t = t
                self.s = s
                self.free = []

            def produce(self, emits):
                for (c, v) in self.free:
                    pr.wait("tensor", c, v)
                self.free = []
                for i, fn in enumerate(emits):
                    if i == len(emits) - 1:
                        pr.op("tensor", fn, inc=self.s)
                    else:
                        pr.op("tensor", fn)
                return self.s.n

            def consumed(self, ctr, val=None):
                self.free.append((ctr, ctr.n if val is None else val))

        idn_b = stack.enter_context(nc.sbuf_tensor("idn_b", [P, P], BF16))
        idn_f = stack.enter_context(nc.sbuf_tensor("idn_f", [P, P], F32))
        ones_r = stack.enter_context(nc.sbuf_tensor("ones_r", [1, P], BF16))
        ones_rf = stack.enter_context(nc.sbuf_tensor("ones_rf", [1, P], F32))
        scr = stack.enter_context(nc.sbuf_tensor("scr", [P, HID], BF16))
        sp = stack.enter_context(nc.sbuf_tensor("sp", [P, 64], F32))
        xin = [stack.enter_context(nc.sbuf_tensor(f"xin{i}", [P, D], F32))
               for i in range(2)]

        def spc(i):
            return sp[:, i:i + 1]

        spb = sp.bitcast(BF16)

        scrf = scr.bitcast(F32)

        pr.op("gpsimd", lambda e: e.memset(idn_b[:, :], 0.0), inc=cG)
        pr.wait("gpsimd", cG)
        pr.op("gpsimd", lambda e: e.affine_select(
            out=idn_b[:, :], in_=idn_b[:, :], compare_op=OP.not_equal,
            fill=1.0, base=0, pattern=[[-1, P]], channel_multiplier=1), inc=cG)
        pr.op("gpsimd", lambda e: e.memset(idn_f[:, :], 0.0), inc=cG)
        pr.wait("gpsimd", cG)
        pr.op("gpsimd", lambda e: e.affine_select(
            out=idn_f[:, :], in_=idn_f[:, :], compare_op=OP.not_equal,
            fill=1.0, base=0, pattern=[[-1, P]], channel_multiplier=1), inc=cG)
        pr.op("gpsimd", lambda e: e.memset(ones_r[:, :], 1.0), inc=cG)
        pr.op("gpsimd", lambda e: e.memset(ones_rf[:, :], 1.0), inc=cG)
        pr.op("gpsimd", lambda e: e.memset(sp[:, 33:34], 1.0), inc=cG)
        pr.op("gpsimd", lambda e: e.memset(sp[:, 30:31], 1e-5), inc=cG)
        pr.op("gpsimd", lambda e: e.memset(sp[:, 31:32], 1e-6), inc=cG)
        consts_g = cG.n

        class TrPool:
            def __init__(self, bufs):
                self.b = bufs
                self.i = 0

            def trN(self, src_fn, dst_ap, k, src_ctr=None, src_val=None):
                pb = self.b[self.i % len(self.b)]
                self.i += 1
                if src_ctr is not None:
                    pr.wait("tensor", src_ctr, src_val)
                pr.wait("tensor", cG, consts_g)
                kk = pb.produce([
                    lambda e, j=j, p=pb.t: e.transpose(
                        p[:, j * P:(j + 1) * P], src_fn(j), idn_b[:, :])
                    for j in range(k)])
                pr.wait("vector", pb.s, kk)
                pr.vop(lambda e, p=pb.t, d=dst_ap, k=k: e.tensor_copy(
                    out=d, in_=p[:, 0:k * P]))
                pb.consumed(cV)

            def tr(self, src_ap, dst_ap, rows=P, cols=P,
                   src_ctr=None, src_val=None):
                pb = self.b[self.i % len(self.b)]
                self.i += 1
                if src_ctr is not None:
                    pr.wait("tensor", src_ctr, src_val)
                pr.wait("tensor", cG, consts_g)
                k = pb.produce([lambda e, s=src_ap, p=pb.t, rows=rows,
                                cols=cols: e.transpose(
                                    p[:cols, :rows], s, idn_b[:rows, :rows])])
                pr.wait("vector", pb.s, k)
                pr.vop(lambda e, p=pb.t, d=dst_ap, rows=rows, cols=cols:
                       e.tensor_copy(out=d, in_=p[:cols, :rows]))
                pb.consumed(cV)

        def ln_stats(xt, W):
            pr.vop(lambda e: e.reduce_sum(out=spc(3), in_=xt[:, :W],
                                          axis=AX.X))
            pr.sop(lambda e: e.activation(out=scr[:, :W], in_=xt[:, :W],
                                          func=AF.Square, accum_out=spc(4)))
            pr.wait("vector", cS)
            pr.vop(lambda e: e.tensor_scalar_mul(spc(0), spc(3), 1.0 / W))
            pr.vop(lambda e: e.tensor_scalar_mul(spc(1), spc(4), 1.0 / W))
            pr.vop(lambda e: e.tensor_mul(spc(3), spc(0), spc(0)))
            pr.vop(lambda e: e.tensor_sub(spc(1), spc(1), spc(3)))
            pr.wait("scalar", cV)
            pr.sop(lambda e: e.activation(out=spc(3), in_=spc(1),
                                          func=AF.Sqrt, bias=spc(30)))
            pr.wait("vector", cS)
            pr.vop(lambda e: e.reciprocal(out=spc(2), in_=spc(3)))

        def build_fm(wdst, w_hbm, rows, cols, tp, wtmp):
            nrt, ncc = rows // P, cols // P
            for rt in range(nrt):
                b = rt % 2
                pr.wait("gpsimd", cV)
                v = pr.dma("gpsimd", wtmp[b][:, :cols],
                           w_hbm[rt * P:(rt + 1) * P, :], s_gd)
                pr.wait("tensor", s_gd, v)
                for c in range(ncc):
                    tp.tr(wtmp[b][:, c * P:(c + 1) * P],
                          wdst[:, c * rows + rt * P: c * rows + (rt + 1) * P])

        pr.phase = "p0a_prep"
        st0 = ExitStack()
        vwT = st0.enter_context(nc.sbuf_tensor("vwT", [P, DC * D], BF16))
        owT = st0.enter_context(nc.sbuf_tensor("owT", [P, DC * D], BF16))
        basT = st0.enter_context(nc.sbuf_tensor("basT", [P, DC * R], BF16))
        hm = st0.enter_context(nc.sbuf_tensor("hm", [R, NH * R], F32))
        hm_b = st0.enter_context(nc.sbuf_tensor("hm_b", [R, NH * R], BF16))
        wtmp = [st0.enter_context(nc.sbuf_tensor(f"wtmp{i}", [P, D], BF16))
                for i in range(2)]
        hrbuf = st0.enter_context(nc.sbuf_tensor("hrbuf", [R, NH * R], F32))
        sm0 = ExitStack()
        tb0 = [PBuf(sm0.enter_context(
            nc.psum_tensor(f"tb0{i}", [P, 1024], BF16)), pm[i])
            for i in range(2)]
        pp0 = PBuf(sm0.enter_context(nc.psum_tensor("pp0", [P, 1024], F32)),
                   pm[2])
        tp0 = TrPool(tb0)

        if nphases >= 1:
            build_fm(vwT, vw_d, D, D, tp0, wtmp)
            build_fm(owT, ow_d, D, D, tp0, wtmp)
            pr.wait("gpsimd", cV)
            v = pr.dma("gpsimd", wtmp[0][:R, :D], basis_d, s_gd)
            pr.wait("tensor", s_gd, v)
            for c in range(DC):
                tp0.tr(wtmp[0][:R, c * P:(c + 1) * P],
                       basT[:, c * R:(c + 1) * R], rows=R, cols=P)
            pr.dma("sync", hm[:R, 0:R], core_d, s_in)
            pr.dma("sync",
                   hrbuf[:R, :].rearrange("p (h s) -> p h s", h=NH),
                   hres_d.transpose([1, 0, 2]), s_in)
            pr.wait("tensor", s_in)
            k = pp0.produce([lambda e: e.transpose(
                pp0.t[:R, 0:R], hm[:R, 0:R], idn_f[:R, :R])])
            pr.wait("vector", pp0.s, k)
            pr.wait("vector", s_in)
            pr.vop(lambda e: e.reduce_sum(
                out=scrf[:R, 0:R],
                in_=hrbuf[:R, :].rearrange("p (h s) -> p s h", h=NH),
                axis=AX.X))
            pr.vop(lambda e: e.tensor_add(
                hm[:R, 0:R], hm[:R, 0:R], pp0.t[:R, 0:R]))
            pp0.consumed(cV)
            pr.vop(lambda e: e.tensor_scalar_mul(
                hm[:R, 0:R], hm[:R, 0:R], 0.5 / NH))
            pr.vop(lambda e: e.tensor_scalar_mul(
                scrf[:R, 0:R], scrf[:R, 0:R], 1.0 / NH))
            pr.vop(lambda e: e.tensor_sub(
                scrf[:R, R:2 * R], hm[:R, 0:R], scrf[:R, 0:R]))
            for h in range(NH):
                pr.vop(lambda e, h=h: e.tensor_add(
                    hm[:R, h * R:(h + 1) * R],
                    hrbuf[:R, h * R:(h + 1) * R],
                    scrf[:R, R:2 * R]))
            pr.vop(lambda e: e.tensor_copy(out=hm_b[:R, :], in_=hm[:R, :]))
            barrier()

        pr.phase = "p1_attn_a"
        if nphases >= 2:
            st1 = ExitStack()
            ulow = st1.enter_context(nc.sbuf_tensor("ulow", [P, P], BF16))
            nbigI = st1.enter_context(nc.sbuf_tensor("nbigI", [P, P], BF16))
            obias = st1.enter_context(nc.sbuf_tensor("obias", [1, D], BF16))
            pr.op("gpsimd", lambda e: e.memset(ulow[:, :], 1.0), inc=cG)
            pr.wait("gpsimd", cG)
            pr.op("gpsimd", lambda e: e.affine_select(
                out=ulow[:, :], in_=ulow[:, :], compare_op=OP.is_gt,
                fill=0.0, base=0, pattern=[[-1, P]], channel_multiplier=1),
                inc=cG)
            pr.op("gpsimd", lambda e: e.memset(nbigI[:, :], 0.0), inc=cG)
            pr.wait("gpsimd", cG)
            pr.op("gpsimd", lambda e: e.affine_select(
                out=nbigI[:, :], in_=nbigI[:, :], compare_op=OP.not_equal,
                fill=NEG, base=0, pattern=[[-1, P]], channel_multiplier=1),
                inc=cG)
            masks_g = cG.n
            pr.dma("sync", sp[:, 40:40 + DC],
                   vb_d.rearrange("(c p) -> p c", p=P), s_in,
                   allow_slow_non_contiguous=True)
            pr.dma("sync", scrf[0:1, 1024:1024 + D], ob_d[None, :], s_in)
            pr.wait("vector", s_in)
            pr.vop(lambda e: e.tensor_copy(
                out=wtmp[1][:, 0:DC], in_=sp[:, 40:40 + DC]))
            vb_ready = cV.n
            emits = []
            for c in range(DC):
                for nn in range(2):
                    emits.append(lambda e, c=c, nn=nn: e.matmul(
                        pp0.t[0:1, nn * 512:(nn + 1) * 512],
                        wtmp[1][:, c:c + 1],
                        owT[:, c * D + nn * 512: c * D + (nn + 1) * 512],
                        start=(c == 0), stop=(c == DC - 1)))
            pr.wait("tensor", cV, vb_ready)
            k = pp0.produce(emits)
            pr.wait("vector", pp0.s, k)
            pr.vop(lambda e: e.tensor_add(
                obias[0:1, :], pp0.t[0:1, 0:D], scrf[0:1, 1024:1024 + D]))
            pp0.consumed(cV)

            lat = st1.enter_context(nc.sbuf_tensor("lat", [P, T], BF16))
            ltf = st1.enter_context(nc.sbuf_tensor("ltf", [P, 8 * T], BF16))
            vaug = st1.enter_context(
                nc.sbuf_tensor("vaug", [P, NT * NH * 65], BF16))
            pTb = [st1.enter_context(nc.sbuf_tensor(f"pT{i}", [P, 1024], BF16))
                   for i in range(2)]
            xrs = [st1.enter_context(nc.sbuf_tensor(f"xrs{i}", [P, D], F32))
                   for i in range(2)]
            yT = ltf

            def dstage_ap(i):
                return scrf[0:1, i * 1024:(i + 1) * 1024]

            def dsb_ap(h, qc):
                p0 = 32 + h + 16 * qc
                return scrf[p0:p0 + 1, 0:1024]

            def rhi_ap(h, qc):
                p0 = 32 + h + 16 * qc
                return scr[p0:p0 + 1, 2048:3072]

            def rlo_ap(h, qc):
                p0 = 32 + h + 16 * qc
                return scr[p0:p0 + 1, 3072:4096]

            st1a = ExitStack()
            xlnT = st1a.enter_context(nc.sbuf_tensor("xlnT", [P, DC * T], BF16))
            xlnb = st1a.enter_context(nc.sbuf_tensor("xlnb", [P, D], BF16))

            for t in range(NT):
                b = t % 2
                pr.wait("sync", cV)
                pr.wait("sync", cS)
                pr.dma("sync", xin[b][:, :], x0_d[t * P:(t + 1) * P, :], s_in)
                pr.wait("vector", s_in)
                pr.wait("scalar", s_in)
                ln_stats(xin[b], D)
                pr.vop(lambda e, b=b: e.scalar_tensor_tensor(
                    out=xlnb[:, :], in0=xin[b][:, :], scalar=spc(0),
                    in1=spc(2).broadcast_to([P, D]),
                    op0=OP.subtract, op1=OP.mult))
                stt_v = cV.n
                for c in range(DC):
                    tp0.tr(xlnb[:, c * P:(c + 1) * P],
                           xlnT[:, c * T + t * P: c * T + (t + 1) * P],
                           src_ctr=cV, src_val=stt_v)

            for nq in range(4):
                emits = []
                for c in range(DC):
                    emits.append(lambda e, nq=nq, c=c: e.matmul(
                        pp0.t[:R, 0:512],
                        basT[:, c * R:(c + 1) * R],
                        xlnT[:, c * T + nq * 512: c * T + (nq + 1) * 512],
                        start=(c == 0), stop=(c == DC - 1)))
                pr.wait("tensor", cV)
                k = pp0.produce(emits)
                pr.wait("vector", pp0.s, k)
                pr.vop(lambda e, nq=nq: e.tensor_copy(
                    out=lat[:R, nq * 512:(nq + 1) * 512],
                    in_=pp0.t[:R, 0:512]))
                pp0.consumed(cV)
            pr.wait("sync", cV)
            pr.dma("sync", lat[R:2 * R, :], lat[0:R, :], s_in)

            for h in range(NH):
                half = h % 2
                for nq in range(4):
                    pr.wait("tensor", s_in)
                    k = pp0.produce([lambda e, h=h, nq=nq, half=half: e.matmul(
                        pp0.t[half * R:(half + 1) * R, 0:512],
                        hm_b[:R, h * R:(h + 1) * R],
                        lat[0:R, nq * 512:(nq + 1) * 512],
                        start=True, stop=True,
                        tile_position=(0, half * R))])
                    pr.wait("vector", pp0.s, k)
                    pr.vop(lambda e, h=h, nq=nq, half=half: e.tensor_copy(
                        out=ltf[half * R:(half + 1) * R,
                                (h // 2) * T + nq * 512:
                                (h // 2) * T + (nq + 1) * 512],
                        in_=pp0.t[half * R:(half + 1) * R, 0:512]))
                    pp0.consumed(cV)

            pr.vop(lambda e: e.reduce_max(
                out=spc(8), in_=lat[:, :], axis=AX.X,
                apply_absolute_value=True))
            pr.vop(lambda e: e.reduce_max(
                out=spc(9), in_=ltf[:, :], axis=AX.X,
                apply_absolute_value=True))
            pr.wait("tensor", cV)
            k = pp0.produce([
                lambda e: e.transpose(pp0.t[0:1, 0:P], sp[:, 8:9],
                                      idn_f[:, :]),
                lambda e: e.transpose(pp0.t[0:1, 256:256 + P], sp[:, 9:10],
                                      idn_f[:, :])])
            pr.wait("vector", pp0.s, k)
            pr.vop(lambda e: e.tensor_copy(
                out=scrf[0:1, 0:P], in_=pp0.t[0:1, 0:P]))
            pr.vop(lambda e: e.tensor_copy(
                out=scrf[0:1, P:2 * P], in_=pp0.t[0:1, 256:256 + P]))
            pp0.consumed(cV)
            pr.vop(lambda e: e.reduce_max(
                out=scrf[0:1, 300:301], in_=scrf[0:1, 0:P], axis=AX.X))
            pr.vop(lambda e: e.reduce_max(
                out=scrf[0:1, 301:302], in_=scrf[0:1, P:2 * P], axis=AX.X))
            pr.vop(lambda e: e.tensor_mul(
                scrf[0:1, 302:303], scrf[0:1, 300:301], scrf[0:1, 301:302]))
            pr.vop(lambda e: e.tensor_scalar_mul(
                scrf[0:1, 302:303], scrf[0:1, 302:303], -8.0))
            pr.wait("tensor", cV)
            k = pp0.produce([lambda e: e.matmul(
                pp0.t[:, 512:513], ones_rf[0:1, :], scrf[0:1, 302:303],
                start=True, stop=True)])
            pr.wait("vector", pp0.s, k)
            pr.vop(lambda e: e.tensor_copy(out=spc(12), in_=pp0.t[:, 512:513]))
            pp0.consumed(cV)
            negc = spc(12)

            pr.vop(lambda e: e.memset(
                vaug[:, :].rearrange("p (u x) -> p u x", x=65)[:, :, 64:65],
                1.0))
            for ut in range(NT):
                for half in range(2):
                    emits = []
                    for c in range(DC):
                        emits.append(lambda e, ut=ut, half=half, c=c: e.matmul(
                            pp0.t[:, 0:512],
                            xlnT[:, c * T + ut * P: c * T + (ut + 1) * P],
                            vwT[:, c * D + half * 512:
                                c * D + (half + 1) * 512],
                            start=(c == 0), stop=(c == DC - 1)))
                    pr.wait("tensor", cV)
                    k = pp0.produce(emits)
                    pr.wait("vector", pp0.s, k)
                    pr.vop(lambda e, ut=ut, half=half: e.tensor_copy(
                        out=vaug[:, ut * NH * 65 + half * 8 * 65:
                                 ut * NH * 65 + (half * 8 + 8) * 65]
                        .rearrange("p (u x) -> p u x", x=65)[:, :, 0:64],
                        in_=pp0.t[:, 0:512]))
                    pp0.consumed(cV)
            barrier()
            st1a.close()
            sm0.close()

            sm1 = ExitStack()
            ps_sb = [PBuf(sm1.enter_context(
                nc.psum_tensor(f"ps_s{i}", [P, 1024], F32)), pm[i and 5])
                for i in range(2)]
            ps_v = sm1.enter_context(nc.psum_tensor("ps_v", [P, 1024], F32))
            ps_r = PBuf(sm1.enter_context(
                nc.psum_tensor("ps_r", [P, 1024], F32)), pm[2])
            pAV = pm[1]
            pT_free = [0, 0]
            psv_free = []
            for h in range(NH):
                half = h % 2
                for qc in range(2):
                    nut = 8 * (qc + 1)
                    exp_v = {}
                    mz_v = {}
                    col0s = {}

                    def scores(ut, h=h, qc=qc, half=half, nut=nut,
                               exp_v=exp_v, mz_v=mz_v, col0s=col0s):
                        qlo = qc * 1024
                        col0 = max(0, ut * P - qlo)
                        col0s[ut] = col0
                        diag = ut >= 8 * qc
                        sb = ps_sb[ut % 2]
                        emits = []
                        a = col0
                        while a < 1024:
                            b2 = min(1024, (a // 512 + 1) * 512)
                            dmask = diag and a <= col0 < b2
                            emits.append(lambda e, ut=ut, a=a, b2=b2,
                                         dmask=dmask, sb=sb:
                                         e.matmul(
                                             sb.t[:, a:b2],
                                             lat[half * R:(half + 1) * R,
                                                 ut * P:(ut + 1) * P],
                                             ltf[half * R:(half + 1) * R,
                                                 (h // 2) * T + qc * 1024 + a:
                                                 (h // 2) * T + qc * 1024 + b2],
                                             start=True, stop=not dmask,
                                             tile_position=(half * R, 0)))
                            if dmask:
                                emits.append(lambda e, col0=col0, sb=sb:
                                             e.matmul(
                                                 sb.t[:, col0:col0 + P],
                                                 nbigI[:, :], ulow[:, :],
                                                 start=False, stop=True))
                            a = b2
                        pr.wait("tensor", cG, masks_g)
                        sk = sb.produce(emits)
                        pb = pTb[ut % 2]
                        mz_v[ut] = 0
                        if col0 > 0:
                            pr.wait("vector", pAV, pT_free[ut % 2])
                            mz_v[ut] = pr.vop(lambda e, pb=pb, col0=col0:
                                              e.memset(pb[:, 0:col0], 0.0))
                        pr.wait("scalar", sb.s, sk)
                        pr.wait("scalar", pAV, pT_free[ut % 2])
                        pr.sop(lambda e, pb=pb, col0=col0, sb=sb:
                               e.activation(
                                   out=pb[:, col0:1024],
                                   in_=sb.t[:, col0:1024],
                                   func=AF.Exp, bias=negc, scale=0.125))
                        sb.consumed(cS)
                        exp_v[ut] = cS.n

                    def av(ut, h=h, qc=qc, nut=nut,
                           exp_v=exp_v, mz_v=mz_v):
                        nonlocal psv_free
                        pb = pTb[ut % 2]
                        pr.wait("tensor", cS, exp_v[ut])
                        if mz_v[ut]:
                            pr.wait("tensor", cV, mz_v[ut])
                        if ut == 0:
                            for (c_, v_) in psv_free:
                                pr.wait("tensor", c_, v_)
                            psv_free = []
                        for hq in range(2):
                            pr.op("tensor",
                                  lambda e, ut=ut, hq=hq, pb=pb,
                                  st=(ut == 0), sp_=(ut == nut - 1):
                                  e.matmul(
                                      ps_v[:65, hq * 512:(hq + 1) * 512],
                                      vaug[:, ut * NH * 65 + h * 65:
                                           ut * NH * 65 + h * 65 + 65],
                                      pb[:, hq * 512:(hq + 1) * 512],
                                      start=st, stop=sp_), inc=pAV)
                        pT_free[ut % 2] = pAV.n

                    scores(0)
                    for ut in range(1, nut):
                        scores(ut)
                        av(ut - 1)
                    av(nut - 1)
                    av_k = pAV.n
                    stg = (h * 2 + qc) % 2
                    pr.wait("scalar", pAV, av_k)
                    pr.wait("scalar", s_in)
                    pr.sop(lambda e, stg=stg: e.copy(
                        out=dstage_ap(stg), in_=ps_v[64:65, 0:1024]))
                    dcp_v = cS.n
                    pr.wait("sync", cS, dcp_v)
                    pr.dma("sync", dsb_ap(h, qc), dstage_ap(stg), s_in)
                    pr.wait("vector", pAV, av_k)
                    pr.vop(lambda e, h=h, qc=qc, half=half: e.tensor_copy(
                        out=yT[half * R:(half + 1) * R,
                               (h // 2) * T + qc * 1024:
                               (h // 2) * T + (qc + 1) * 1024],
                        in_=ps_v[0:64, 0:1024]))
                    psv_free = [(cV, cV.n), (cS, dcp_v)]
            pr.wait("vector", cS)
            pr.wait("vector", s_in)
            pr.vop(lambda e: e.reciprocal(
                out=scrf[32:64, 0:1024], in_=scrf[32:64, 0:1024]))
            pr.vop(lambda e: e.tensor_copy(
                out=scr[32:64, 2048:3072], in_=scrf[32:64, 0:1024]))
            pr.vop(lambda e: e.tensor_sub(
                scrf[32:64, 0:1024], scrf[32:64, 0:1024],
                scr[32:64, 2048:3072]))
            pr.vop(lambda e: e.tensor_copy(
                out=scr[32:64, 3072:4096], in_=scrf[32:64, 0:1024]))
            for h in range(NH):
                half = h % 2
                for qc in range(2):
                    pr.wait("sync", cV)
                    pr.wait("sync", cS)
                    pr.wait("sync", pAV, pAV.n)
                    pr.dma("sync", pTb[0][0:1, 0:1024], rhi_ap(h, qc), s_in)
                    pr.dma("sync", pTb[1][0:1, 0:1024], rlo_ap(h, qc), s_in)
                    pr.wait("tensor", s_in)
                    emits = []
                    for hq in range(2):
                        emits.append(lambda e, hq=hq: e.matmul(
                            ps_r.t[0:64, hq * 512:(hq + 1) * 512],
                            ones_r[0:1, 0:64],
                            pTb[0][0:1, hq * 512:(hq + 1) * 512],
                            start=True, stop=False))
                        emits.append(lambda e, hq=hq: e.matmul(
                            ps_r.t[0:64, hq * 512:(hq + 1) * 512],
                            ones_r[0:1, 0:64],
                            pTb[1][0:1, hq * 512:(hq + 1) * 512],
                            start=False, stop=True))
                    k = ps_r.produce(emits)
                    pr.wait("vector", ps_r.s, k)
                    pr.vop(lambda e, h=h, qc=qc, half=half: e.tensor_tensor(
                        out=yT[half * R:(half + 1) * R,
                               (h // 2) * T + qc * 1024:
                               (h // 2) * T + (qc + 1) * 1024],
                        in0=yT[half * R:(half + 1) * R,
                               (h // 2) * T + qc * 1024:
                               (h // 2) * T + (qc + 1) * 1024],
                        in1=ps_r.t[0:64, 0:1024], op=OP.mult))
                    ps_r.consumed(cV)
            barrier()
            sm1.close()

            sm2 = ExitStack()
            po = [PBuf(sm2.enter_context(
                nc.psum_tensor(f"po{i}", [P, 1024], F32)), pm[3 + i])
                for i in range(2)]
            for t in range(NT):
                b = t % 2
                pr.wait("sync", cV)
                pr.wait("sync", s_out)
                pr.dma("sync", xrs[b][:, :], x0_d[t * P:(t + 1) * P, :], s_in)
                emits = []
                for c in range(DC):
                    for nn in range(2):
                        emits.append(lambda e, t=t, c=c, b=b, nn=nn: e.matmul(
                            po[b].t[:, nn * 512:(nn + 1) * 512],
                            yT[:, c * T + t * P:c * T + (t + 1) * P],
                            owT[:, c * D + nn * 512: c * D + (nn + 1) * 512],
                            start=(c == 0), stop=False))
                for nn in range(2):
                    emits.append(lambda e, b=b, nn=nn: e.matmul(
                        po[b].t[:, nn * 512:(nn + 1) * 512],
                        ones_r[0:1, :],
                        obias[0:1, nn * 512:(nn + 1) * 512],
                        start=False, stop=True))
                k = po[b].produce(emits)
                pr.wait("vector", po[b].s, k)
                pr.wait("vector", s_in)
                pr.vop(lambda e, b=b: e.tensor_add(
                    xrs[b][:, :], po[b].t[:, 0:D], xrs[b][:, :]))
                po[b].consumed(cV)
                pr.wait("sync", cV)
                pr.dma("sync", x1_d[t * P:(t + 1) * P, :], xrs[b][:, :], s_out)
            pr.wait("sync", s_out)
            barrier()
            sm2.close()
            st1.close()
        st0.close()
        if nphases < 2:
            sm0.close()

        pr.phase = "p0b_wquant"
        st2 = ExitStack()
        w1q = w2q = None
        if nphases >= 3:
            w1q = st2.enter_context(nc.sbuf_tensor("w1q", [P, DC * HID], BF16))
            w2q = st2.enter_context(nc.sbuf_tensor("w2q", [P, HC * D], BF16))
            st2t = ExitStack()
            wf2 = [st2t.enter_context(nc.sbuf_tensor(f"wf{i}", [P, HID], F32))
                   for i in range(2)]
            w8 = st2t.enter_context(nc.sbuf_tensor("w8", [P, HID], I8))
            sm2b = ExitStack()
            tb2 = [PBuf(sm2b.enter_context(
                nc.psum_tensor(f"tb2{i}", [P, 1024], BF16)), pm[i])
                for i in range(2)]
            pq2 = PBuf(sm2b.enter_context(
                nc.psum_tensor("pq2", [P, 512], F32)), pm[2])
            tp2 = TrPool(tb2)

            def quant_weight(w_hbm, nrt, ncc, wdst, rows, im):
                W = ncc * P
                pr.vop(lambda e: e.memset(spc(20), 0.0))
                act_done = {}
                for rt in range(nrt):
                    b = rt % 2
                    if rt >= 2:
                        pr.wait("sync", cS, act_done[rt - 2])
                    else:
                        pr.wait("sync", cS)
                    pr.dma("sync", wf2[b][:, :W],
                           w_hbm[rt * P:(rt + 1) * P, :], s_in)
                    d = s_in.n
                    pr.wait("scalar", s_in, d)
                    pr.wait("scalar", cV)
                    pr.sop(lambda e, b=b: e.activation(
                        out=scr[:, :W], in_=wf2[b][:, :W],
                        func=AF.Abs, accum_out=spc(16)))
                    act_done[rt] = cS.n
                    pr.wait("vector", cS)
                    pr.vop(lambda e: e.tensor_add(spc(20), spc(20), spc(16)))
                pr.wait("tensor", cV)
                pr.wait("tensor", cG, consts_g)
                k = pq2.produce([lambda e: e.matmul(
                    pq2.t[0:1, 0:1], sp[:, 33:34], spc(20),
                    start=True, stop=True)])
                pr.wait("vector", pq2.s, k)
                pr.vop(lambda e: e.tensor_copy(
                    out=scrf[0:1, 300:301], in_=pq2.t[0:1, 0:1]))
                pq2.consumed(cV)
                pr.vop(lambda e: e.tensor_scalar_mul(
                    scrf[0:1, 300:301], scrf[0:1, 300:301],
                    1.0 / (nrt * P * W)))
                pr.vop(lambda e: e.tensor_scalar_max(
                    scrf[0:1, 300:301], scrf[0:1, 300:301], 1e-5))
                pr.vop(lambda e: e.reciprocal(
                    out=scrf[0:1, 301:302], in_=scrf[0:1, 300:301]))
                pr.wait("tensor", cV)
                k = pq2.produce([
                    lambda e: e.matmul(pq2.t[:, 1:2], ones_rf[0:1, :],
                                       scrf[0:1, 300:301],
                                       start=True, stop=True),
                    lambda e: e.matmul(pq2.t[:, 2:3], ones_rf[0:1, :],
                                       scrf[0:1, 301:302],
                                       start=True, stop=True)])
                pr.wait("vector", pq2.s, k)
                pr.vop(lambda e, im=im: e.tensor_copy(
                    out=spc(im), in_=pq2.t[:, 1:2]))
                pr.vop(lambda e: e.tensor_copy(out=spc(18), in_=pq2.t[:, 2:3]))
                pq2.consumed(cV)
                q_done = {}
                evac_done = {}
                for rt in range(nrt):
                    b = rt % 2
                    if rt >= 2:
                        pr.wait("sync", cS, q_done[rt - 2])
                    else:
                        pr.wait("sync", cS, act_done[nrt - 2 + rt])
                    pr.dma("sync", wf2[b][:, :W],
                           w_hbm[rt * P:(rt + 1) * P, :], s_in)
                    d = s_in.n
                    pr.wait("scalar", s_in, d)
                    if rt == 0:
                        pr.wait("scalar", cV)
                    else:
                        pr.wait("scalar", cV, evac_done[rt - 1])
                    pr.sop(lambda e, b=b: e.activation(
                        out=w8[:, :W], in_=wf2[b][:, :W],
                        func=AF.Copy, scale=spc(18)))
                    pr.sop(lambda e: e.activation(
                        out=scr[:, :W], in_=w8[:, :W], func=AF.Sign))
                    pr.sop(lambda e: e.activation(
                        out=sp[:, 61:62], in_=scr[:, W - 1:W], func=AF.Copy))
                    sign_v = cS.n
                    q_done[rt] = cS.n
                    for c in range(ncc):
                        tp2.tr(scr[:, c * P:(c + 1) * P],
                               wdst[:, c * rows + rt * P:
                                    c * rows + (rt + 1) * P],
                               src_ctr=cS, src_val=sign_v)
                    evac_done[rt] = cV.n

            quant_weight(w1_d, HID // P, DC, w1q, HID, 21)
            quant_weight(w2_d, D // P, HC, w2q, D, 23)
            pr.vop(lambda e: e.reciprocal(out=spc(24), in_=spc(21)))
            pr.vop(lambda e: e.reciprocal(out=spc(25), in_=spc(23)))
            pr.wait("sync", cV)
            pr.dma("sync", wq1_d, w1q[:, :], s_out)
            pr.dma("sync", wq2_d, w2q[:, :], s_out)
            barrier()
            sm2b.close()
            st2t.close()

        def emit_ffn(xin_d, xout_d, w1qt, w2qt, stf, tag=""):
            xq8s = stf.enter_context(nc.sbuf_tensor(tag + "xq8s", [P, D], I8))
            xqT = stf.enter_context(
                nc.sbuf_tensor(tag + "xqT", [P, DC * P], BF16))
            xsc = stf.enter_context(nc.sbuf_tensor(tag + "xsc", [P, D], BF16))
            hti = [stf.enter_context(
                nc.sbuf_tensor(tag + f"hti{i}", [P, HID], F32))
                for i in range(2)]
            hqT = stf.enter_context(
                nc.sbuf_tensor(tag + "hqT", [P, (HC // 2) * P], BF16))
            hq8h = stf.enter_context(
                nc.sbuf_tensor(tag + "hq8h", [P, HID // 2], I8))
            sxc = stf.enter_context(nc.sbuf_tensor(tag + "sxc", [P, NT], F32))
            exc = stf.enter_context(nc.sbuf_tensor(tag + "exc", [P, NT], F32))
            e2c = stf.enter_context(nc.sbuf_tensor(tag + "e2c", [P, 2], F32))
            sbB = stf.enter_context(nc.sbuf_tensor(tag + "sbB", [1, P], BF16))
            s2B = stf.enter_context(nc.sbuf_tensor(tag + "s2B", [1, P], BF16))
            xl2 = stf.enter_context(nc.sbuf_tensor(tag + "xl2", [P, D], F32))
            xin3 = [xin[0], xin[1], stf.enter_context(
                nc.sbuf_tensor(tag + "xin3_2", [P, D], F32))]
            b1row = stf.enter_context(
                nc.sbuf_tensor(tag + "b1row", [1, HID], BF16))
            b2row = stf.enter_context(
                nc.sbuf_tensor(tag + "b2row", [1, D], BF16))
            pr.dma("gpsimd", b1row[:, :], b1_d[None, :], s_gd)
            pr.dma("gpsimd", b2row[:, :], b2_d[None, :], s_gd)
            brow_v = s_gd.n
            pr.wait("tensor", s_gd, brow_v)
            smf = ExitStack()
            tbf = [PBuf(smf.enter_context(
                nc.psum_tensor(tag + f"tbf{i}", [P, 1024], BF16)), pm[i])
                for i in range(2)]
            hp = [PBuf(smf.enter_context(
                nc.psum_tensor(tag + f"hpf{i}", [P, 512], F32)), pm[2 + i])
                for i in range(2)]
            pof = PBuf(smf.enter_context(
                nc.psum_tensor(tag + "poff", [P, 1024], F32)), pm[4])
            tpf = TrPool(tbf)

            xin_v = {}
            xqT_ok = {}
            sb_ok = {}
            stats_ok = {}
            qv = {}
            xv = {}
            ln_ok = {}

            def xstage(t):
                pr.wait("scalar", cV)
                pr.sop(lambda e, t=t: e.activation(
                    out=xsc[:, :], in_=xq8s[:, :], func=AF.Copy))
                pr.sop(lambda e: e.activation(
                    out=sp[:, 61:62], in_=xsc[:, D - 1:D], func=AF.Copy))
                xv[t] = cS.n

            def lnq(t, pos):
                if pos == 0:
                    pr.wait("vector", s_in, xin_v[t])
                    pr.vop(lambda e, t=t: e.reduce_sum(
                        out=spc(53), in_=xin3[t % 3][:, :], axis=AX.X))
                    pr.wait("scalar", s_in, xin_v[t])
                    pr.wait("scalar", cV)
                    pr.sop(lambda e, t=t: e.activation(
                        out=xl2[:, :], in_=xin3[t % 3][:, :],
                        func=AF.Square, accum_out=spc(54)))
                    pr.sop(lambda e: e.activation(
                        out=sp[:, 61:62], in_=spc(54), func=AF.Copy))
                elif pos == 1:
                    pr.wait("vector", cS)
                    pr.vop(lambda e: e.tensor_copy(
                        out=sp[:, 62:63], in_=sp[:, 61:62]))
                    pr.vop(lambda e: e.tensor_scalar_mul(
                        spc(50), spc(53), 1.0 / D))
                    pr.vop(lambda e: e.tensor_scalar_mul(
                        spc(51), spc(54), 1.0 / D))
                    pr.vop(lambda e: e.tensor_mul(spc(53), spc(50), spc(50)))
                    pr.vop(lambda e: e.tensor_sub(spc(51), spc(51), spc(53)))
                    pr.wait("scalar", cV)
                    pr.sop(lambda e: e.activation(
                        out=spc(52), in_=spc(51), func=AF.Sqrt,
                        bias=spc(30)))
                elif pos == 2:
                    pr.wait("vector", cS)
                    pr.vop(lambda e: e.reciprocal(out=spc(52), in_=spc(52)))
                    pr.vop(lambda e, t=t: e.scalar_tensor_tensor(
                        out=xl2[:, :], in0=xin3[t % 3][:, :], scalar=spc(50),
                        in1=spc(52).broadcast_to([P, D]),
                        op0=OP.subtract, op1=OP.mult))
                    pr.vop(lambda e: e.tensor_mul(spc(55), spc(51), spc(52)))
                    pr.vop(lambda e: e.tensor_mul(spc(55), spc(55), spc(52)))
                    pr.wait("scalar", cV)
                    pr.sop(lambda e: e.activation(
                        out=spc(56), in_=spc(55), func=AF.Sqrt,
                        bias=spc(31)))
                else:
                    pr.wait("vector", cS)
                    pr.vop(lambda e: e.reciprocal(out=spc(56), in_=spc(56)))
                    pr.vop(lambda e: e.reduce_max(
                        out=spc(57), in_=xl2[:, :], axis=AX.X,
                        apply_absolute_value=True))
                    pr.vop(lambda e: e.tensor_mul(spc(57), spc(57), spc(56)))
                    pr.vop(lambda e: e.tensor_scalar_max(spc(57), spc(57),
                                                         1e-5))
                    pr.vop(lambda e: e.reciprocal(out=spc(58), in_=spc(57)))
                    pr.vop(lambda e: e.tensor_scalar_mul(spc(58), spc(58),
                                                         127.0))
                    pr.vop(lambda e: e.tensor_mul(spc(59), spc(58), spc(56)))
                    pr.vop(lambda e, t=t: e.tensor_scalar_mul(
                        exc[:, t:t + 1], spc(57), 1.0 / 127.0))
                    pr.vop(lambda e, t=t: e.tensor_mul(
                        exc[:, t:t + 1], exc[:, t:t + 1], spc(21)))
                    pr.vop(lambda e, t=t: e.tensor_mul(
                        sxc[:, t:t + 1], spc(58), spc(24)))
                    ln_ok[t] = cV.n
                    pr.wait("scalar", cV, ln_ok[t])
                    pr.sop(lambda e, t=t: e.activation(
                        out=xq8s[:, :], in_=xl2[:, :],
                        func=AF.Copy, scale=spc(59)))
                    pr.sop(lambda e, t=t: e.activation(
                        out=sp[:, 61:62], in_=xq8s[:, D - 1:D],
                        func=AF.Copy))

            def front(t):
                if t + 1 < NT:
                    pr.wait("sync", cV)
                    pr.wait("sync", cS)
                    pr.wait("sync", s_out)
                    pr.dma("sync", xin3[(t + 1) % 3][:, :],
                           xin_d[(t + 1) * P:(t + 2) * P, :], s_in)
                    xin_v[t + 1] = s_in.n
                tpf.trN(lambda j: xsc[:, j * P:(j + 1) * P],
                        xqT[:, 0:DC * P], DC,
                        src_ctr=cS, src_val=xv[t])
                xqT_ok[t] = cV.n
                pr.vop(lambda e, t=t: e.tensor_copy(
                    out=spb[:, 120:121], in_=sxc[:, t:t + 1]))
                tpf.tr(spb[:, 120:121], sbB[0:1, :], rows=P, cols=1,
                       src_ctr=cV, src_val=cV.n)
                sb_ok[t] = cV.n

            def emit_stats(t, pos):
                if pos == 0:
                    pr.wait("scalar", cV)
                    pr.sop(lambda e, t=t: e.activation(
                        out=hti[(t + 1) % 2][:, :], in_=hti[t % 2][:, :],
                        func=AF.Square, accum_out=spc(4)))
                    pr.sop(lambda e: e.activation(
                        out=sp[:, 61:62], in_=spc(4), func=AF.Copy))
                elif pos == 1:
                    pr.wait("vector", cS)
                    pr.vop(lambda e: e.tensor_copy(
                        out=sp[:, 62:63], in_=sp[:, 61:62]))
                    pr.vop(lambda e: e.tensor_scalar_mul(
                        spc(5), spc(4), 1.0 / HID))
                    pr.wait("scalar", cV)
                    pr.sop(lambda e: e.activation(
                        out=spc(6), in_=spc(5), func=AF.Sqrt, bias=spc(31)))
                elif pos == 2:
                    pr.wait("vector", cS)
                    pr.vop(lambda e: e.reciprocal(out=spc(6), in_=spc(6)))
                    pr.vop(lambda e, t=t: e.reduce_max(
                        out=spc(7), in_=hti[t % 2][:, :], axis=AX.X,
                        apply_absolute_value=True))
                    pr.vop(lambda e: e.tensor_mul(spc(7), spc(7), spc(6)))
                    pr.vop(lambda e: e.tensor_scalar_max(spc(7), spc(7),
                                                         1e-5))
                    pr.vop(lambda e: e.reciprocal(out=spc(8), in_=spc(7)))
                    pr.vop(lambda e: e.tensor_scalar_mul(spc(8), spc(8),
                                                         127.0))
                    pr.vop(lambda e: e.tensor_mul(spc(9), spc(8), spc(6)))
                    pr.vop(lambda e, t=t: e.tensor_scalar_mul(
                        e2c[:, t % 2:t % 2 + 1], spc(7), 1.0 / 127.0))
                    pr.vop(lambda e, t=t: e.tensor_mul(
                        e2c[:, t % 2:t % 2 + 1], e2c[:, t % 2:t % 2 + 1],
                        spc(23)))
                    pr.vop(lambda e: e.tensor_mul(spc(11), spc(8), spc(25)))
                    stats_ok[t] = cV.n
                else:
                    pr.wait("scalar", cV, stats_ok[t])
                    pr.sop(lambda e, t=t: e.activation(
                        out=hq8h[:, :], in_=hti[t % 2][:, 0:HID // 2],
                        func=AF.Copy, scale=spc(9)))
                    pr.sop(lambda e: e.activation(
                        out=sp[:, 61:62], in_=hq8h[:, HID // 2 - 1:HID // 2],
                        func=AF.Copy))
                    qv[t] = cS.n

            def bl1_gelu(t, ts, tl):
                if ts is not None:
                    emit_stats(ts, 0)
                pr.wait("tensor", cV, max(xqT_ok[t], sb_ok[t]))
                for n in range(8):
                    pb = n % 2
                    emits = []
                    for c in range(DC):
                        emits.append(lambda e, n=n, c=c, pb=pb: e.matmul(
                            hp[pb].t[:, :],
                            xqT[:, c * P:(c + 1) * P],
                            w1qt[:, c * HID + n * 512:
                                 c * HID + (n + 1) * 512],
                            start=(c == 0), stop=False))
                    emits.append(lambda e, n=n, pb=pb: e.matmul(
                        hp[pb].t[:, :], sbB[0:1, :],
                        b1row[0:1, n * 512:(n + 1) * 512],
                        start=False, stop=True))
                    k = hp[pb].produce(emits)
                    pr.wait("scalar", hp[pb].s, k)
                    pr.sop(lambda e, n=n, pb=pb, t=t: e.activation(
                        out=hti[t % 2][:, n * 512:(n + 1) * 512],
                        in_=hp[pb].t[:, :],
                        func=AF.Gelu, scale=exc[:, t:t + 1]))
                    hp[pb].consumed(cS)
                    if n == 1 and ts is not None:
                        emit_stats(ts, 1)
                    if n == 2 and ts is not None:
                        emit_stats(ts, 2)
                    if n == 3 and tl is not None:
                        lnq(tl, 0)
                    if n == 4 and ts is not None:
                        emit_stats(ts, 3)
                    if n == 5 and tl is not None:
                        lnq(tl, 1)
                    if n == 6 and tl is not None:
                        lnq(tl, 2)
                    if n == 7 and tl is not None:
                        lnq(tl, 3)
                        xstage(tl)

            def back(t):
                b = t % 3
                pr.vop(lambda e: e.tensor_copy(
                    out=spb[:, 120:121], in_=sp[:, 11:12]))
                tpf.tr(spb[:, 120:121], s2B[0:1, :], rows=P, cols=1,
                       src_ctr=cV, src_val=cV.n)
                pr.wait("vector", cS, qv[t])
                pr.vop(lambda e: e.tensor_copy(
                    out=sp[:, 62:63], in_=sp[:, 61:62]))
                pr.vop(lambda e: e.tensor_copy(out=scr[:, 0:HID // 2],
                                               in_=hq8h[:, :]))
                pr.vop(lambda e: e.tensor_copy(
                    out=sp[:, 63:64], in_=scr[:, HID // 2 - 1:HID // 2]))
                c0v = cV.n
                pr.wait("scalar", cV, c0v)
                pr.sop(lambda e, t=t: e.activation(
                    out=hq8h[:, :], in_=hti[t % 2][:, HID // 2:HID],
                    func=AF.Copy, scale=spc(9)))
                pr.sop(lambda e: e.activation(
                    out=sp[:, 61:62], in_=hq8h[:, HID // 2 - 1:HID // 2],
                    func=AF.Copy))
                q1v = cS.n
                pr.wait("vector", cS, q1v)
                pr.vop(lambda e: e.tensor_copy(
                    out=sp[:, 62:63], in_=sp[:, 61:62]))
                pr.vop(lambda e: e.tensor_copy(
                    out=scr[:, HID // 2:HID], in_=hq8h[:, :]))
                pr.vop(lambda e: e.tensor_copy(
                    out=sp[:, 63:64], in_=scr[:, HID - 1:HID]))
                c1v = cV.n
                for (cc, v) in pof.free:
                    pr.wait("tensor", cc, v)
                pof.free = []
                for half in range(2):
                    cb = half * (HC // 2)
                    srcv = c0v if half == 0 else c1v
                    for hb in range(2):
                        tpf.trN(lambda j, base=cb + hb * 8:
                                scr[:, (base + j) * P:(base + j + 1) * P],
                                hqT[:, (hb * 8) * P:(hb * 8 + 8) * P], 8,
                                src_ctr=cV, src_val=srcv)
                    pr.wait("tensor", cV)
                    for c in range(HC // 2):
                        for n in range(2):
                            pr.op("tensor", lambda e, n=n, c=c, cb=cb:
                                  e.matmul(
                                      pof.t[:, n * 512:(n + 1) * 512],
                                      hqT[:, c * P:(c + 1) * P],
                                      w2qt[:, (cb + c) * D + n * 512:
                                           (cb + c) * D + (n + 1) * 512],
                                      start=(cb + c == 0), stop=False))
                pr.op("tensor", lambda e: e.matmul(
                    pof.t[:, 0:512], s2B[0:1, :], b2row[0:1, 0:512],
                    start=False, stop=True))
                pr.op("tensor", lambda e: e.matmul(
                    pof.t[:, 512:1024], s2B[0:1, :], b2row[0:1, 512:1024],
                    start=False, stop=True), inc=pof.s)
                k = pof.s.n
                pr.wait("vector", pof.s, k)
                pr.wait("vector", s_in)
                pr.vop(lambda e, b=b, t=t: e.scalar_tensor_tensor(
                    out=xin3[b][:, :], in0=pof.t[:, 0:D],
                    scalar=e2c[:, t % 2:t % 2 + 1],
                    in1=xin3[b][:, :], op0=OP.mult, op1=OP.add))
                pof.consumed(cV)
                pr.wait("sync", cV)
                pr.dma("sync", xout_d[t * P:(t + 1) * P, :], xin3[b][:, :],
                       s_out)

            pr.wait("sync", cV)
            pr.wait("sync", cS)
            pr.dma("sync", xin3[0][:, :], xin_d[0:P, :], s_in)
            xin_v[0] = s_in.n
            for pos in range(4):
                lnq(0, pos)
            xstage(0)
            for t in range(NT):
                front(t)
                bl1_gelu(t, t - 1 if t >= 1 else None,
                         t + 1 if t + 1 < NT else None)
                if t >= 1:
                    back(t - 1)
            for pos in range(4):
                emit_stats(NT - 1, pos)
            back(NT - 1)
            pr.wait("sync", s_out)
            if dbg == "q2" and tag == "a":
                pr.wait("sync", cV)
                pr.wait("sync", cS)
                fl = out_d.flatten()
                pr.dma("sync",
                       fl[P * 4096:2 * P * 4096].rearrange("(p n) -> p n", p=P),
                       hti[(NT - 1) % 2][:, :], s_out)
                pr.dma("sync",
                       fl[2 * P * 4096:2 * P * 4096 + 2048]
                       .rearrange("(p n) -> p n", p=1),
                       b1row.bitcast(F32)[0:1, :], s_out)
                pr.dma("sync",
                       fl[2 * P * 4096 + 2048:2 * P * 4096 + 2048 + P * 16]
                       .rearrange("(p n) -> p n", p=P),
                       exc[:, :], s_out)
                pr.dma("sync",
                       fl[2 * P * 4096 + 2048 + P * 16:
                          2 * P * 4096 + 2048 + P * 16 + P * 16]
                       .rearrange("(p n) -> p n", p=P),
                       sxc[:, :], s_out)
            barrier()
            smf.close()

        pr.phase = "p2_ffn1"
        if nphases >= 4:
            stf1 = ExitStack()
            emit_ffn(x1_d, x2_d, w1q, w2q, stf1, tag="a")
            stf1.close()
        st2.close()

        pr.phase = "p2b_a2a"
        if nphases >= 5:
            pr.wait("gpsimd", s_out)

            def _cc(e):
                with allow_ici_all_to_all():
                    return e.collective_compute(
                        "AllToAll", OP.bypass,
                        replica_groups=[list(range(8))],
                        ins=[x2_d], outs=[x2t_d])
            pr.op("gpsimd", _cc, inc=s_cc)

        pr.phase = "p3a_attn_b"
        if nphases >= 5:
            st3 = ExitStack()
            wkT = st3.enter_context(nc.sbuf_tensor("wkT", [P, DC * D], BF16))
            wvT = st3.enter_context(nc.sbuf_tensor("wvT", [P, DC * D], BF16))
            wqT = st3.enter_context(nc.sbuf_tensor("wqT", [P, DC * D], BF16))
            wtm3 = [st3.enter_context(nc.sbuf_tensor(f"wtm3{i}", [P, D], BF16))
                    for i in range(2)]
            S2 = [st3.enter_context(nc.sbuf_tensor(f"S2{i}", [P, 8192], F32))
                  for i in range(2)]
            prod = st3.enter_context(nc.sbuf_tensor("prod", [P, 8192], F32))
            kvq = [[st3.enter_context(
                nc.sbuf_tensor(f"kvq{r}{w}", [P, D], F32))
                for w in range(3)] for r in range(2)]
            kn = st3.enter_context(nc.sbuf_tensor("kn", [P, 256], F32))
            rt3 = st3.enter_context(nc.sbuf_tensor("rt3", [P, D], F32))
            sq3 = st3.enter_context(nc.sbuf_tensor("sq3", [P, D], F32))
            xl3 = st3.enter_context(nc.sbuf_tensor("xl3", [P, D], BF16))
            x3T = st3.enter_context(nc.sbuf_tensor("x3T", [P, DC * P], BF16))
            sm3 = ExitStack()
            tb3 = [PBuf(sm3.enter_context(
                nc.psum_tensor(f"tb3{i}", [P, 1024], BF16)), pm[i])
                for i in range(2)]
            pq3 = [PBuf(sm3.enter_context(
                nc.psum_tensor(f"pq3{i}", [P, 512], F32)), pm[2 + i])
                for i in range(2)]
            tp3 = TrPool(tb3)
            build_fm(wkT, wk_d, D, D, tp3, wtm3)
            build_fm(wvT, wv_d, D, D, tp3, wtm3)
            build_fm(wqT, wqd_d, D, D, tp3, wtm3)

            pr.vop(lambda e: e.memset(S2[0][:, :], 0.0))
            pr.vop(lambda e: e.memset(S2[1][:, :], 0.0))
            pr.wait("sync", s_cc, 1)

            def S4(S):
                return S[:, :].rearrange("p (n i j) -> p n i j", i=8, j=8)

            def prod4():
                return prod[:, :].rearrange("p (n i j) -> p n i j", i=8, j=8)

            kv_ok = {}

            def frontq(it):
                bt = it % 2
                t = it // 2
                row0 = t * BS + bt * P
                pr.wait("sync", cV)
                pr.wait("sync", s_out)
                pr.dma("sync", xin[bt][:, :], x2t_d[row0:row0 + P, :], s_in)
                pr.wait("vector", s_in)
                pr.wait("scalar", s_in)
                ln_stats(xin[bt], D)
                pr.vop(lambda e, bt=bt: e.scalar_tensor_tensor(
                    out=xl3[:, :], in0=xin[bt][:, :], scalar=spc(0),
                    in1=spc(2).broadcast_to([P, D]),
                    op0=OP.subtract, op1=OP.mult))
                xl3_v = cV.n
                tp3.trN(lambda j: xl3[:, j * P:(j + 1) * P],
                        x3T[:, 0:DC * P], DC, src_ctr=cV, src_val=xl3_v)
                xT_ready = cV.n
                for wi, wT in enumerate((wkT, wvT, wqT)):
                    dst = kvq[bt][wi]
                    for n in range(2):
                        pb = (wi * 2 + n) % 2
                        pr.wait("tensor", cV, xT_ready)
                        emits = []
                        for c in range(DC):
                            emits.append(
                                lambda e, wT=wT, n=n, c=c, pb=pb:
                                e.matmul(
                                    pq3[pb].t[:, :],
                                    x3T[:, c * P:(c + 1) * P],
                                    wT[:, c * D + n * 512:
                                       c * D + (n + 1) * 512],
                                    start=(c == 0), stop=(c == DC - 1)))
                        k = pq3[pb].produce(emits)
                        pr.wait("vector", pq3[pb].s, k)
                        pr.vop(lambda e, dst=dst, n=n, pb=pb:
                               e.tensor_copy(
                                   out=dst[:, n * 512:(n + 1) * 512],
                                   in_=pq3[pb].t[:, :]))
                        pq3[pb].consumed(cV)
                kv_ok[it] = cV.n

            def scan(it):
                bt = it % 2
                t = it // 2
                row0 = t * BS + bt * P
                S = S2[bt]
                kk, vv, qq = kvq[bt]
                S4S = S[:, :].rearrange("p (n i j) -> p n i j", i=8, j=8)
                pr4 = prod[:, :].rearrange("p (n i j) -> p n i j", i=8, j=8)
                v = {}

                def knbc(q):
                    return (rt3[:, q * 256:(q + 1) * 256]
                            .rearrange("p (n j) -> p n j", j=8)
                            .unsqueeze(2).broadcast_to([P, 32, 8, 8]))

                for q in range(4):
                    if q == 0:
                        pr.wait("scalar", cV, kv_ok[it])
                    pr.sop(lambda e, q=q: e.activation(
                        out=scrf[:, q * 256:(q + 1) * 256],
                        in_=kk[:, q * 256:(q + 1) * 256], func=AF.Square))
                    v[(0, q)] = cS.n
                for q in range(4):
                    pr.wait("vector", cS, v[(0, q)])
                    pr.vop(lambda e, q=q: e.reduce_sum(
                        out=kn[:, q * 32:(q + 1) * 32],
                        in_=scrf[:, q * 256:(q + 1) * 256]
                        .rearrange("p (n j) -> p n j", j=8), axis=AX.X))
                    v[(1, q)] = cV.n
                for q in range(4):
                    pr.wait("scalar", cV, v[(1, q)])
                    pr.sop(lambda e, q=q: e.activation(
                        out=kn[:, 128 + q * 32:128 + (q + 1) * 32],
                        in_=kn[:, q * 32:(q + 1) * 32], func=AF.Sqrt))
                    v[(2, q)] = cS.n
                for q in range(4):
                    pr.wait("vector", cS, v[(2, q)])
                    pr.vop(lambda e, q=q: e.tensor_scalar_add(
                        kn[:, 128 + q * 32:128 + (q + 1) * 32],
                        kn[:, 128 + q * 32:128 + (q + 1) * 32], 1e-6))
                    pr.vop(lambda e, q=q: e.reciprocal(
                        out=kn[:, q * 32:(q + 1) * 32],
                        in_=kn[:, 128 + q * 32:128 + (q + 1) * 32]))
                    pr.vop(lambda e, q=q: e.tensor_tensor(
                        out=rt3[:, q * 256:(q + 1) * 256],
                        in0=kk[:, q * 256:(q + 1) * 256],
                        in1=kn[:, q * 32:(q + 1) * 32]
                        .rearrange("p (n o) -> p n o", o=1)
                        .broadcast_to([P, 32, 8]),
                        op=OP.mult))
                for q in range(4):
                    nq = slice(q * 32, (q + 1) * 32)
                    s256 = slice(q * 256, (q + 1) * 256)
                    pr.vop(lambda e, q=q, nq=nq: e.tensor_tensor(
                        out=pr4[:, nq], in0=S4S[:, nq], in1=knbc(q),
                        op=OP.mult))
                    pr.vop(lambda e, q=q, nq=nq, s256=s256: e.reduce_sum(
                        out=sq3[:, s256].rearrange("p (n i) -> p n i", i=8),
                        in_=pr4[:, nq], axis=AX.X))
                    pr.vop(lambda e, s256=s256: e.tensor_sub(
                        sq3[:, s256], vv[:, s256], sq3[:, s256]))
                    pr.vop(lambda e, q=q, nq=nq, s256=s256: e.tensor_tensor(
                        out=pr4[:, nq],
                        in0=sq3[:, s256].rearrange("p (n i) -> p n i", i=8)
                        .unsqueeze(3).broadcast_to([P, 32, 8, 8]),
                        in1=knbc(q), op=OP.mult))
                    pr.vop(lambda e, q=q, S=S: e.tensor_add(
                        prod[:, q * 2048:(q + 1) * 2048],
                        S[:, q * 2048:(q + 1) * 2048],
                        prod[:, q * 2048:(q + 1) * 2048]))
                    v[(8, q)] = cV.n
                for q in range(4):
                    pr.wait("scalar", cV, v[(8, q)])
                    pr.sop(lambda e, q=q, S=S: e.activation(
                        out=S[:, q * 2048:(q + 1) * 2048],
                        in_=prod[:, q * 2048:(q + 1) * 2048], func=AF.Tanh))
                    v[(9, q)] = cS.n
                for q in range(4):
                    nq = slice(q * 32, (q + 1) * 32)
                    s256 = slice(q * 256, (q + 1) * 256)
                    pr.wait("vector", cS, v[(9, q)])
                    pr.vop(lambda e, q=q, nq=nq: e.tensor_tensor(
                        out=pr4[:, nq], in0=S4S[:, nq],
                        in1=qq[:, q * 256:(q + 1) * 256]
                        .rearrange("p (n j) -> p n j", j=8)
                        .unsqueeze(2).broadcast_to([P, 32, 8, 8]),
                        op=OP.mult))
                    pr.vop(lambda e, q=q, nq=nq, s256=s256: e.reduce_sum(
                        out=sq3[:, s256].rearrange("p (n i) -> p n i", i=8),
                        in_=pr4[:, nq], axis=AX.X))
                    v[(11, q)] = cV.n
                for q in range(4):
                    s256 = slice(q * 256, (q + 1) * 256)
                    pr.wait("scalar", cV, v[(11, q)])
                    pr.sop(lambda e, s256=s256: e.activation(
                        out=rt3[:, s256], in_=sq3[:, s256],
                        func=AF.Sigmoid))
                    pr.sop(lambda e, q=q, s256=s256: e.activation(
                        out=scrf[:, 1024 + q * 256:1024 + (q + 1) * 256],
                        in_=sq3[:, s256], func=AF.Square))
                    v[(12, q)] = cS.n
                for q in range(4):
                    s256 = slice(q * 256, (q + 1) * 256)
                    pr.wait("vector", cS, v[(12, q)])
                    pr.vop(lambda e, q=q, s256=s256: e.tensor_mul(
                        rt3[:, s256],
                        scrf[:, 1024 + q * 256:1024 + (q + 1) * 256],
                        rt3[:, s256]))
                    pr.vop(lambda e, bt=bt, s256=s256: e.tensor_add(
                        xin[bt][:, s256], xin[bt][:, s256], rt3[:, s256]))
                pr.wait("sync", cV)
                pr.dma("sync", x3_d[row0:row0 + P, :], xin[bt][:, :],
                       s_out)

            frontq(0)
            for it in range(2 * B):
                if it + 1 < 2 * B:
                    frontq(it + 1)
                scan(it)
            pr.wait("sync", s_out)
            barrier()
            sm3.close()
            st3.close()

        pr.phase = "p3b_ffn2"
        if nphases >= 6:
            st4 = ExitStack()
            w1q2 = st4.enter_context(
                nc.sbuf_tensor("w1q2", [P, DC * HID], BF16))
            w2q2 = st4.enter_context(nc.sbuf_tensor("w2q2", [P, HC * D], BF16))
            pr.dma("sync", w1q2[:, :], wq1_d, s_in)
            pr.dma("sync", w2q2[:, :], wq2_d, s_in)
            pr.wait("tensor", s_in)
            emit_ffn(x3_d, out_d, w1q2, w2q2, st4, tag="b")
            st4.close()

        if dbg == "q2":
            pass
        if dbg == "q":
            pr.wait("sync", s_out)
            pr.wait("sync", cV)
            pr.wait("sync", cS)
            pr.dma("sync", out_d[0:P, 0:64], sp[:, :], s_out)
            pr.dma("sync", out_d[P:2 * P, 0:1024],
                   wq1_d.bitcast(F32)[:, 1024:2048], s_out)
            pr.dma("sync", out_d[2 * P:3 * P, 0:1024],
                   wq2_d.bitcast(F32)[:, 0:1024], s_out)
            if w2q is not None:
                pr.dma("sync", out_d[3 * P:4 * P, 0:1024],
                       w2q.bitcast(F32)[:, 0:1024], s_out)
        elif dbg is not None and dbg != "q2":
            dbg_src = {"x1": x1_d, "x2": x2_d, "x2t": x2t_d, "x3": x3_d}[dbg]
            pr.wait("sync", s_out)
            if dbg == "x2t":
                pr.wait("sync", s_cc, 1)
            pr.dma("sync", out_d, dbg_src, s_out)

        pr.wait("sync", s_out)
        pr.wait("gpsimd", s_out)

        with nc.Block() as block:
            pr.flush(block)

    return nc


def kernel(L0_attn_a_hidden_states, L0_attn_a_basis_w, L0_attn_a_core,
           L0_attn_a_head_residual, L0_attn_a_v_w, L0_attn_a_v_b,
           L0_attn_a_o_w, L0_attn_a_o_b, L0_attn_b_W_k, L0_attn_b_W_v,
           L0_attn_b_W_q, L0_ffn_w1, L0_ffn_b1, L0_ffn_w2, L0_ffn_b2):
    hs = np.ascontiguousarray(L0_attn_a_hidden_states, dtype=np.float32)
    nc = build(_NPHASES, _DBG)
    common = {
        "basis_w": np.ascontiguousarray(L0_attn_a_basis_w, np.float32),
        "core_m": np.ascontiguousarray(L0_attn_a_core, np.float32),
        "head_res": np.ascontiguousarray(L0_attn_a_head_residual, np.float32),
        "v_w": np.ascontiguousarray(L0_attn_a_v_w, np.float32),
        "v_b": np.ascontiguousarray(L0_attn_a_v_b, np.float32),
        "o_w": np.ascontiguousarray(L0_attn_a_o_w, np.float32),
        "o_b": np.ascontiguousarray(L0_attn_a_o_b, np.float32),
        "w_k": np.ascontiguousarray(L0_attn_b_W_k, np.float32),
        "w_v": np.ascontiguousarray(L0_attn_b_W_v, np.float32),
        "w_q": np.ascontiguousarray(L0_attn_b_W_q, np.float32),
        "w1": np.ascontiguousarray(L0_ffn_w1, np.float32),
        "b1": np.ascontiguousarray(L0_ffn_b1, np.float32),
        "w2": np.ascontiguousarray(L0_ffn_w2, np.float32),
        "b2": np.ascontiguousarray(L0_ffn_b2, np.float32),
    }
    in_maps = [dict(common, x0=np.ascontiguousarray(hs[i])) for i in range(8)]
    res = run_bass_kernel_spmd(nc, in_maps, core_ids=list(range(8)),
                               trace=_TRACE)
    globals()["_LAST_RES"] = res
    out = np.empty((B, T, D), dtype=np.float32)
    for s in range(8):
        out[:, s * BS:(s + 1) * BS, :] = \
            res.results[s]["out"].reshape(B, BS, D)
    return out


# version 07
# speedup vs baseline: 1.2592x; 1.0157x over previous
import numpy as np
import concourse.bass as bass
import concourse.mybir as mybir
from concourse.bass_utils import run_bass_kernel_spmd
from concourse.aot_env import allow_ici_all_to_all
from contextlib import ExitStack

F32 = mybir.dt.float32
BF16 = mybir.dt.bfloat16
I8 = mybir.dt.int8
AF = mybir.ActivationFunctionType
OP = mybir.AluOpType
AX = mybir.AxisListType

P = 128
T = 2048
NT = 16
D = 1024
DC = 8
HID = 4096
HC = 32
NH = 16
R = 64
B = 8
BS = 256
NEG = -1e30

ENGINES = ("sync", "vector", "scalar", "tensor", "gpsimd")
_NPHASES = 9
_TRACE = False
_LAST_RES = None
_DBG = None


class Ctr:
    def __init__(self, h):
        self.h = h
        self.n = 0


class Prog:
    def __init__(self, nc=None):
        self.q = {e: [] for e in ENGINES}
        self.cV = None
        self.cS = None
        self.nc = nc
        self.phase = "init"

    def _add(self, eng, fn):
        self.q[eng].append((self.phase, fn))

    def wait(self, eng, ctr, val=None):
        v = ctr.n if val is None else val
        if v > 0:
            self._add(eng, lambda e: e.wait_ge(ctr.h, v))

    def dma(self, eng, out, in_, sem, **kw):
        self._add(
            eng,
            lambda e: e.dma_start(out=out, in_=in_, **kw).then_inc(sem.h, 16))
        sem.n += 16
        return sem.n

    def op(self, eng, fn, inc=None, k=1):
        if inc is None:
            self._add(eng, fn)
            return None
        self._add(eng, lambda e: fn(e).then_inc(inc.h, k))
        inc.n += k
        return inc.n

    def vop(self, fn):
        self.wait("vector", self.cV)
        return self.op("vector", fn, inc=self.cV)

    def sop(self, fn):
        self.wait("scalar", self.cS)
        return self.op("scalar", fn, inc=self.cS)

    def barrier(self, ctrs):
        vals = {}
        for e in ENGINES:
            c = ctrs[e]
            self._add(e, lambda en, c=c: en.sem_inc(c.h, 1))
            c.n += 1
            vals[e] = c.n
        for e in ENGINES:
            for o in ENGINES:
                if o != e:
                    c, v = ctrs[o], vals[o]
                    self._add(e, lambda en, c=c, v=v: en.wait_ge(c.h, v))

    def _emit(self, eng, name):
        nc, cur, sid = self.nc, None, None
        for lab, fn in self.q[name]:
            if lab != cur:
                if cur is not None:
                    nc.leave_named_scope(cur, sid, False)
                sid, _ = nc.enter_named_scope(lab, False)
                cur = lab
            fn(eng)
        if cur is not None:
            nc.leave_named_scope(cur, sid, False)

    def flush(self, block):
        @block.sync
        def _(eng):
            self._emit(eng, "sync")

        @block.vector
        def _(eng):
            self._emit(eng, "vector")

        @block.scalar
        def _(eng):
            self._emit(eng, "scalar")

        @block.tensor
        def _(eng):
            self._emit(eng, "tensor")

        @block.gpsimd
        def _(eng):
            self._emit(eng, "gpsimd")


def build(nphases=9, dbg=None):
    nc = bass.Bass("TRN2", target_bir_lowering=False,
                   detect_race_conditions=False)

    x0_d = nc.dram_tensor("x0", [T, D], F32, kind="ExternalInput").ap()
    basis_d = nc.dram_tensor("basis_w", [R, D], F32, kind="ExternalInput").ap()
    core_d = nc.dram_tensor("core_m", [R, R], F32, kind="ExternalInput").ap()
    hres_d = nc.dram_tensor("head_res", [NH, R, R], F32,
                            kind="ExternalInput").ap()
    vw_d = nc.dram_tensor("v_w", [D, D], F32, kind="ExternalInput").ap()
    vb_d = nc.dram_tensor("v_b", [D], F32, kind="ExternalInput").ap()
    ow_d = nc.dram_tensor("o_w", [D, D], F32, kind="ExternalInput").ap()
    ob_d = nc.dram_tensor("o_b", [D], F32, kind="ExternalInput").ap()
    wk_d = nc.dram_tensor("w_k", [D, D], F32, kind="ExternalInput").ap()
    wv_d = nc.dram_tensor("w_v", [D, D], F32, kind="ExternalInput").ap()
    wqd_d = nc.dram_tensor("w_q", [D, D], F32, kind="ExternalInput").ap()
    w1_d = nc.dram_tensor("w1", [HID, D], F32, kind="ExternalInput").ap()
    b1_d = nc.dram_tensor("b1", [HID], F32, kind="ExternalInput").ap()
    w2_d = nc.dram_tensor("w2", [D, HID], F32, kind="ExternalInput").ap()
    b2_d = nc.dram_tensor("b2", [D], F32, kind="ExternalInput").ap()
    out_d = nc.dram_tensor("out", [T, D], F32, kind="ExternalOutput").ap()

    x1_d = nc.dram_tensor("x1_s", [T, D], F32).ap()
    x2_d = nc.dram_tensor("x2_s", [T, D], F32).ap()
    x2t_d = nc.dram_tensor("x2t_s", [T, D], F32).ap()
    x3_d = nc.dram_tensor("x3_s", [T, D], F32).ap()
    wq1_d = nc.dram_tensor("wq1_s", [P, DC * HID], BF16).ap()
    wq2_d = nc.dram_tensor("wq2_s", [P, HC * D], BF16).ap()

    pr = Prog(nc)

    with ExitStack() as stack:
        sem = {}
        for name in ("s_in", "s_out", "s_gd", "s_cc",
                     "c_v", "c_s", "c_t", "c_g", "c_y",
                     "p0", "p1", "p2", "p3", "p4", "p5", "p6"):
            sem[name] = Ctr(stack.enter_context(nc.semaphore(name)))
        s_in, s_out, s_gd = sem["s_in"], sem["s_out"], sem["s_gd"]
        s_cc = sem["s_cc"]
        cV, cS, cT, cG, cY = (sem["c_v"], sem["c_s"], sem["c_t"],
                              sem["c_g"], sem["c_y"])
        pm = [sem[f"p{i}"] for i in range(7)]
        pr.cV, pr.cS = cV, cS
        bar_ctrs = {"sync": cY, "vector": cV, "scalar": cS,
                    "tensor": cT, "gpsimd": cG}

        def barrier():
            pr.barrier(bar_ctrs)

        class PBuf:
            def __init__(self, t, s):
                self.t = t
                self.s = s
                self.free = []

            def produce(self, emits):
                for (c, v) in self.free:
                    pr.wait("tensor", c, v)
                self.free = []
                for i, fn in enumerate(emits):
                    if i == len(emits) - 1:
                        pr.op("tensor", fn, inc=self.s)
                    else:
                        pr.op("tensor", fn)
                return self.s.n

            def consumed(self, ctr, val=None):
                self.free.append((ctr, ctr.n if val is None else val))

        idn_b = stack.enter_context(nc.sbuf_tensor("idn_b", [P, P], BF16))
        idn_f = stack.enter_context(nc.sbuf_tensor("idn_f", [P, P], F32))
        ones_r = stack.enter_context(nc.sbuf_tensor("ones_r", [1, P], BF16))
        ones_rf = stack.enter_context(nc.sbuf_tensor("ones_rf", [1, P], F32))
        scr = stack.enter_context(nc.sbuf_tensor("scr", [P, HID], BF16))
        sp = stack.enter_context(nc.sbuf_tensor("sp", [P, 64], F32))
        xin = [stack.enter_context(nc.sbuf_tensor(f"xin{i}", [P, D], F32))
               for i in range(2)]

        def spc(i):
            return sp[:, i:i + 1]

        spb = sp.bitcast(BF16)

        scrf = scr.bitcast(F32)

        pr.op("gpsimd", lambda e: e.memset(idn_b[:, :], 0.0), inc=cG)
        pr.wait("gpsimd", cG)
        pr.op("gpsimd", lambda e: e.affine_select(
            out=idn_b[:, :], in_=idn_b[:, :], compare_op=OP.not_equal,
            fill=1.0, base=0, pattern=[[-1, P]], channel_multiplier=1), inc=cG)
        pr.op("gpsimd", lambda e: e.memset(idn_f[:, :], 0.0), inc=cG)
        pr.wait("gpsimd", cG)
        pr.op("gpsimd", lambda e: e.affine_select(
            out=idn_f[:, :], in_=idn_f[:, :], compare_op=OP.not_equal,
            fill=1.0, base=0, pattern=[[-1, P]], channel_multiplier=1), inc=cG)
        pr.op("gpsimd", lambda e: e.memset(ones_r[:, :], 1.0), inc=cG)
        pr.op("gpsimd", lambda e: e.memset(ones_rf[:, :], 1.0), inc=cG)
        pr.op("gpsimd", lambda e: e.memset(sp[:, 33:34], 1.0), inc=cG)
        pr.op("gpsimd", lambda e: e.memset(sp[:, 30:31], 1e-5), inc=cG)
        pr.op("gpsimd", lambda e: e.memset(sp[:, 31:32], 1e-6), inc=cG)
        consts_g = cG.n

        class TrPool:
            def __init__(self, bufs):
                self.b = bufs
                self.i = 0

            def trN(self, src_fn, dst_ap, k, src_ctr=None, src_val=None):
                pb = self.b[self.i % len(self.b)]
                self.i += 1
                if src_ctr is not None:
                    pr.wait("tensor", src_ctr, src_val)
                pr.wait("tensor", cG, consts_g)
                kk = pb.produce([
                    lambda e, j=j, p=pb.t: e.transpose(
                        p[:, j * P:(j + 1) * P], src_fn(j), idn_b[:, :])
                    for j in range(k)])
                pr.wait("vector", pb.s, kk)
                if len(dst_ap.shape) == 3:
                    pr.vop(lambda e, p=pb.t, d=dst_ap, k=k: e.tensor_copy(
                        out=d, in_=p[:, 0:k * P]
                        .rearrange("p (c o) -> p c o", o=P)))
                else:
                    pr.vop(lambda e, p=pb.t, d=dst_ap, k=k: e.tensor_copy(
                        out=d, in_=p[:, 0:k * P]))
                pb.consumed(cV)

            def tr(self, src_ap, dst_ap, rows=P, cols=P,
                   src_ctr=None, src_val=None):
                pb = self.b[self.i % len(self.b)]
                self.i += 1
                if src_ctr is not None:
                    pr.wait("tensor", src_ctr, src_val)
                pr.wait("tensor", cG, consts_g)
                k = pb.produce([lambda e, s=src_ap, p=pb.t, rows=rows,
                                cols=cols: e.transpose(
                                    p[:cols, :rows], s, idn_b[:rows, :rows])])
                pr.wait("vector", pb.s, k)
                pr.vop(lambda e, p=pb.t, d=dst_ap, rows=rows, cols=cols:
                       e.tensor_copy(out=d, in_=p[:cols, :rows]))
                pb.consumed(cV)

        def ln_stats(xt, W):
            pr.vop(lambda e: e.reduce_sum(out=spc(3), in_=xt[:, :W],
                                          axis=AX.X))
            pr.sop(lambda e: e.activation(out=scr[:, :W], in_=xt[:, :W],
                                          func=AF.Square, accum_out=spc(4)))
            pr.wait("vector", cS)
            pr.vop(lambda e: e.tensor_scalar_mul(spc(0), spc(3), 1.0 / W))
            pr.vop(lambda e: e.tensor_scalar_mul(spc(1), spc(4), 1.0 / W))
            pr.vop(lambda e: e.tensor_mul(spc(3), spc(0), spc(0)))
            pr.vop(lambda e: e.tensor_sub(spc(1), spc(1), spc(3)))
            pr.wait("scalar", cV)
            pr.sop(lambda e: e.activation(out=spc(3), in_=spc(1),
                                          func=AF.Sqrt, bias=spc(30)))
            pr.wait("vector", cS)
            pr.vop(lambda e: e.reciprocal(out=spc(2), in_=spc(3)))

        def build_fm(wdst, w_hbm, rows, cols, tp, wtmp):
            nrt, ncc = rows // P, cols // P
            for rt in range(nrt):
                b = rt % 2
                pr.wait("gpsimd", cV)
                v = pr.dma("gpsimd", wtmp[b][:, :cols],
                           w_hbm[rt * P:(rt + 1) * P, :], s_gd)
                pr.wait("tensor", s_gd, v)
                for c in range(ncc):
                    tp.tr(wtmp[b][:, c * P:(c + 1) * P],
                          wdst[:, c * rows + rt * P: c * rows + (rt + 1) * P])

        pr.phase = "p0a_prep"
        st0 = ExitStack()
        vwT = st0.enter_context(nc.sbuf_tensor("vwT", [P, DC * D], BF16))
        owT = st0.enter_context(nc.sbuf_tensor("owT", [P, DC * D], BF16))
        basT = st0.enter_context(nc.sbuf_tensor("basT", [P, DC * R], BF16))
        hm = st0.enter_context(nc.sbuf_tensor("hm", [R, NH * R], F32))
        hm_b = st0.enter_context(nc.sbuf_tensor("hm_b", [R, NH * R], BF16))
        wtmp = [st0.enter_context(nc.sbuf_tensor(f"wtmp{i}", [P, D], BF16))
                for i in range(2)]
        hrbuf = st0.enter_context(nc.sbuf_tensor("hrbuf", [R, NH * R], F32))
        sm0 = ExitStack()
        tb0 = [PBuf(sm0.enter_context(
            nc.psum_tensor(f"tb0{i}", [P, 1024], BF16)), pm[i])
            for i in range(2)]
        pp0 = PBuf(sm0.enter_context(nc.psum_tensor("pp0", [P, 1024], F32)),
                   pm[2])
        tp0 = TrPool(tb0)

        if nphases >= 1:
            build_fm(vwT, vw_d, D, D, tp0, wtmp)
            build_fm(owT, ow_d, D, D, tp0, wtmp)
            pr.wait("gpsimd", cV)
            v = pr.dma("gpsimd", wtmp[0][:R, :D], basis_d, s_gd)
            pr.wait("tensor", s_gd, v)
            for c in range(DC):
                tp0.tr(wtmp[0][:R, c * P:(c + 1) * P],
                       basT[:, c * R:(c + 1) * R], rows=R, cols=P)
            pr.dma("sync", hm[:R, 0:R], core_d, s_in)
            pr.dma("sync",
                   hrbuf[:R, :].rearrange("p (h s) -> p h s", h=NH),
                   hres_d.transpose([1, 0, 2]), s_in)
            pr.wait("tensor", s_in)
            k = pp0.produce([lambda e: e.transpose(
                pp0.t[:R, 0:R], hm[:R, 0:R], idn_f[:R, :R])])
            pr.wait("vector", pp0.s, k)
            pr.wait("vector", s_in)
            pr.vop(lambda e: e.reduce_sum(
                out=scrf[:R, 0:R],
                in_=hrbuf[:R, :].rearrange("p (h s) -> p s h", h=NH),
                axis=AX.X))
            pr.vop(lambda e: e.tensor_add(
                hm[:R, 0:R], hm[:R, 0:R], pp0.t[:R, 0:R]))
            pp0.consumed(cV)
            pr.vop(lambda e: e.tensor_scalar_mul(
                hm[:R, 0:R], hm[:R, 0:R], 0.5 / NH))
            pr.vop(lambda e: e.tensor_scalar_mul(
                scrf[:R, 0:R], scrf[:R, 0:R], 1.0 / NH))
            pr.vop(lambda e: e.tensor_sub(
                scrf[:R, R:2 * R], hm[:R, 0:R], scrf[:R, 0:R]))
            for h in range(NH):
                pr.vop(lambda e, h=h: e.tensor_add(
                    hm[:R, h * R:(h + 1) * R],
                    hrbuf[:R, h * R:(h + 1) * R],
                    scrf[:R, R:2 * R]))
            pr.vop(lambda e: e.tensor_copy(out=hm_b[:R, :], in_=hm[:R, :]))
            barrier()

        pr.phase = "p1_attn_a"
        if nphases >= 2:
            st1 = ExitStack()
            ulow = st1.enter_context(nc.sbuf_tensor("ulow", [P, P], BF16))
            nbigI = st1.enter_context(nc.sbuf_tensor("nbigI", [P, P], BF16))
            obias = st1.enter_context(nc.sbuf_tensor("obias", [1, D], BF16))
            pr.op("gpsimd", lambda e: e.memset(ulow[:, :], 1.0), inc=cG)
            pr.wait("gpsimd", cG)
            pr.op("gpsimd", lambda e: e.affine_select(
                out=ulow[:, :], in_=ulow[:, :], compare_op=OP.is_gt,
                fill=0.0, base=0, pattern=[[-1, P]], channel_multiplier=1),
                inc=cG)
            pr.op("gpsimd", lambda e: e.memset(nbigI[:, :], 0.0), inc=cG)
            pr.wait("gpsimd", cG)
            pr.op("gpsimd", lambda e: e.affine_select(
                out=nbigI[:, :], in_=nbigI[:, :], compare_op=OP.not_equal,
                fill=NEG, base=0, pattern=[[-1, P]], channel_multiplier=1),
                inc=cG)
            masks_g = cG.n
            pr.dma("sync", sp[:, 40:40 + DC],
                   vb_d.rearrange("(c p) -> p c", p=P), s_in,
                   allow_slow_non_contiguous=True)
            pr.dma("sync", scrf[0:1, 1024:1024 + D], ob_d[None, :], s_in)
            pr.wait("vector", s_in)
            pr.vop(lambda e: e.tensor_copy(
                out=wtmp[1][:, 0:DC], in_=sp[:, 40:40 + DC]))
            vb_ready = cV.n
            emits = []
            for c in range(DC):
                for nn in range(2):
                    emits.append(lambda e, c=c, nn=nn: e.matmul(
                        pp0.t[0:1, nn * 512:(nn + 1) * 512],
                        wtmp[1][:, c:c + 1],
                        owT[:, c * D + nn * 512: c * D + (nn + 1) * 512],
                        start=(c == 0), stop=(c == DC - 1)))
            pr.wait("tensor", cV, vb_ready)
            k = pp0.produce(emits)
            pr.wait("vector", pp0.s, k)
            pr.vop(lambda e: e.tensor_add(
                obias[0:1, :], pp0.t[0:1, 0:D], scrf[0:1, 1024:1024 + D]))
            pp0.consumed(cV)

            lat = st1.enter_context(nc.sbuf_tensor("lat", [P, T], BF16))
            ltf = st1.enter_context(nc.sbuf_tensor("ltf", [P, 8 * T], BF16))
            vaug = st1.enter_context(
                nc.sbuf_tensor("vaug", [P, NT * NH * 65], BF16))
            pTb = [st1.enter_context(nc.sbuf_tensor(f"pT{i}", [P, 1024], BF16))
                   for i in range(2)]
            xrs = [st1.enter_context(nc.sbuf_tensor(f"xrs{i}", [P, D], F32))
                   for i in range(2)]
            yT = ltf

            def dstage_ap(i):
                return scrf[0:1, i * 1024:(i + 1) * 1024]

            def dsb_ap(h, qc):
                p0 = 32 + h + 16 * qc
                return scrf[p0:p0 + 1, 0:1024]

            def rhi_ap(h, qc):
                p0 = 32 + h + 16 * qc
                return scr[p0:p0 + 1, 2048:3072]

            def rlo_ap(h, qc):
                p0 = 32 + h + 16 * qc
                return scr[p0:p0 + 1, 3072:4096]

            st1a = ExitStack()
            xlnT = st1a.enter_context(nc.sbuf_tensor("xlnT", [P, DC * T], BF16))
            xlnb = st1a.enter_context(nc.sbuf_tensor("xlnb", [P, D], BF16))

            for t in range(NT):
                b = t % 2
                pr.wait("sync", cV)
                pr.wait("sync", cS)
                pr.dma("sync", xin[b][:, :], x0_d[t * P:(t + 1) * P, :], s_in)
                pr.wait("vector", s_in)
                pr.wait("scalar", s_in)
                ln_stats(xin[b], D)
                pr.vop(lambda e, b=b: e.scalar_tensor_tensor(
                    out=xlnb[:, :], in0=xin[b][:, :], scalar=spc(0),
                    in1=spc(2).broadcast_to([P, D]),
                    op0=OP.subtract, op1=OP.mult))
                stt_v = cV.n
                for c in range(DC):
                    tp0.tr(xlnb[:, c * P:(c + 1) * P],
                           xlnT[:, c * T + t * P: c * T + (t + 1) * P],
                           src_ctr=cV, src_val=stt_v)

            for nq in range(4):
                emits = []
                for c in range(DC):
                    emits.append(lambda e, nq=nq, c=c: e.matmul(
                        pp0.t[:R, 0:512],
                        basT[:, c * R:(c + 1) * R],
                        xlnT[:, c * T + nq * 512: c * T + (nq + 1) * 512],
                        start=(c == 0), stop=(c == DC - 1)))
                pr.wait("tensor", cV)
                k = pp0.produce(emits)
                pr.wait("vector", pp0.s, k)
                pr.vop(lambda e, nq=nq: e.tensor_copy(
                    out=lat[:R, nq * 512:(nq + 1) * 512],
                    in_=pp0.t[:R, 0:512]))
                pp0.consumed(cV)
            pr.wait("sync", cV)
            pr.dma("sync", lat[R:2 * R, :], lat[0:R, :], s_in)

            for h in range(NH):
                half = h % 2
                for nq in range(4):
                    pr.wait("tensor", s_in)
                    k = pp0.produce([lambda e, h=h, nq=nq, half=half: e.matmul(
                        pp0.t[half * R:(half + 1) * R, 0:512],
                        hm_b[:R, h * R:(h + 1) * R],
                        lat[0:R, nq * 512:(nq + 1) * 512],
                        start=True, stop=True,
                        tile_position=(0, half * R))])
                    pr.wait("vector", pp0.s, k)
                    pr.vop(lambda e, h=h, nq=nq, half=half: e.tensor_copy(
                        out=ltf[half * R:(half + 1) * R,
                                (h // 2) * T + nq * 512:
                                (h // 2) * T + (nq + 1) * 512],
                        in_=pp0.t[half * R:(half + 1) * R, 0:512]))
                    pp0.consumed(cV)

            pr.vop(lambda e: e.reduce_max(
                out=spc(8), in_=lat[:, :], axis=AX.X,
                apply_absolute_value=True))
            pr.vop(lambda e: e.reduce_max(
                out=spc(9), in_=ltf[:, :], axis=AX.X,
                apply_absolute_value=True))
            pr.wait("tensor", cV)
            k = pp0.produce([
                lambda e: e.transpose(pp0.t[0:1, 0:P], sp[:, 8:9],
                                      idn_f[:, :]),
                lambda e: e.transpose(pp0.t[0:1, 256:256 + P], sp[:, 9:10],
                                      idn_f[:, :])])
            pr.wait("vector", pp0.s, k)
            pr.vop(lambda e: e.tensor_copy(
                out=scrf[0:1, 0:P], in_=pp0.t[0:1, 0:P]))
            pr.vop(lambda e: e.tensor_copy(
                out=scrf[0:1, P:2 * P], in_=pp0.t[0:1, 256:256 + P]))
            pp0.consumed(cV)
            pr.vop(lambda e: e.reduce_max(
                out=scrf[0:1, 300:301], in_=scrf[0:1, 0:P], axis=AX.X))
            pr.vop(lambda e: e.reduce_max(
                out=scrf[0:1, 301:302], in_=scrf[0:1, P:2 * P], axis=AX.X))
            pr.vop(lambda e: e.tensor_mul(
                scrf[0:1, 302:303], scrf[0:1, 300:301], scrf[0:1, 301:302]))
            pr.vop(lambda e: e.tensor_scalar_mul(
                scrf[0:1, 302:303], scrf[0:1, 302:303], -8.0))
            pr.wait("tensor", cV)
            k = pp0.produce([lambda e: e.matmul(
                pp0.t[:, 512:513], ones_rf[0:1, :], scrf[0:1, 302:303],
                start=True, stop=True)])
            pr.wait("vector", pp0.s, k)
            pr.vop(lambda e: e.tensor_copy(out=spc(12), in_=pp0.t[:, 512:513]))
            pp0.consumed(cV)
            negc = spc(12)

            pr.vop(lambda e: e.memset(
                vaug[:, :].rearrange("p (u x) -> p u x", x=65)[:, :, 64:65],
                1.0))
            for ut in range(NT):
                for half in range(2):
                    emits = []
                    for c in range(DC):
                        emits.append(lambda e, ut=ut, half=half, c=c: e.matmul(
                            pp0.t[:, 0:512],
                            xlnT[:, c * T + ut * P: c * T + (ut + 1) * P],
                            vwT[:, c * D + half * 512:
                                c * D + (half + 1) * 512],
                            start=(c == 0), stop=(c == DC - 1)))
                    pr.wait("tensor", cV)
                    k = pp0.produce(emits)
                    pr.wait("vector", pp0.s, k)
                    pr.vop(lambda e, ut=ut, half=half: e.tensor_copy(
                        out=vaug[:, ut * NH * 65 + half * 8 * 65:
                                 ut * NH * 65 + (half * 8 + 8) * 65]
                        .rearrange("p (u x) -> p u x", x=65)[:, :, 0:64],
                        in_=pp0.t[:, 0:512]))
                    pp0.consumed(cV)
            barrier()
            st1a.close()
            sm0.close()

            sm1 = ExitStack()
            ps_sb = [PBuf(sm1.enter_context(
                nc.psum_tensor(f"ps_s{i}", [P, 1024], F32)), pm[i and 5])
                for i in range(2)]
            ps_v = sm1.enter_context(nc.psum_tensor("ps_v", [P, 1024], F32))
            ps_r = PBuf(sm1.enter_context(
                nc.psum_tensor("ps_r", [P, 1024], F32)), pm[2])
            pAV = pm[1]
            pT_free = [0, 0]
            psv_free = []
            for h in range(NH):
                half = h % 2
                for qc in range(2):
                    nut = 8 * (qc + 1)
                    exp_v = {}
                    mz_v = {}
                    col0s = {}

                    def scores(ut, h=h, qc=qc, half=half, nut=nut,
                               exp_v=exp_v, mz_v=mz_v, col0s=col0s):
                        qlo = qc * 1024
                        col0 = max(0, ut * P - qlo)
                        col0s[ut] = col0
                        diag = ut >= 8 * qc
                        sb = ps_sb[ut % 2]
                        emits = []
                        a = col0
                        while a < 1024:
                            b2 = min(1024, (a // 512 + 1) * 512)
                            dmask = diag and a <= col0 < b2
                            emits.append(lambda e, ut=ut, a=a, b2=b2,
                                         dmask=dmask, sb=sb:
                                         e.matmul(
                                             sb.t[:, a:b2],
                                             lat[half * R:(half + 1) * R,
                                                 ut * P:(ut + 1) * P],
                                             ltf[half * R:(half + 1) * R,
                                                 (h // 2) * T + qc * 1024 + a:
                                                 (h // 2) * T + qc * 1024 + b2],
                                             start=True, stop=not dmask,
                                             tile_position=(half * R, 0)))
                            if dmask:
                                emits.append(lambda e, col0=col0, sb=sb:
                                             e.matmul(
                                                 sb.t[:, col0:col0 + P],
                                                 nbigI[:, :], ulow[:, :],
                                                 start=False, stop=True))
                            a = b2
                        pr.wait("tensor", cG, masks_g)
                        sk = sb.produce(emits)
                        pb = pTb[ut % 2]
                        mz_v[ut] = 0
                        if col0 > 0:
                            pr.wait("vector", pAV, pT_free[ut % 2])
                            mz_v[ut] = pr.vop(lambda e, pb=pb, col0=col0:
                                              e.memset(pb[:, 0:col0], 0.0))
                        pr.wait("scalar", sb.s, sk)
                        pr.wait("scalar", pAV, pT_free[ut % 2])
                        pr.sop(lambda e, pb=pb, col0=col0, sb=sb:
                               e.activation(
                                   out=pb[:, col0:1024],
                                   in_=sb.t[:, col0:1024],
                                   func=AF.Exp, bias=negc, scale=0.125))
                        sb.consumed(cS)
                        exp_v[ut] = cS.n

                    def av(ut, h=h, qc=qc, nut=nut,
                           exp_v=exp_v, mz_v=mz_v):
                        nonlocal psv_free
                        pb = pTb[ut % 2]
                        pr.wait("tensor", cS, exp_v[ut])
                        if mz_v[ut]:
                            pr.wait("tensor", cV, mz_v[ut])
                        if ut == 0:
                            for (c_, v_) in psv_free:
                                pr.wait("tensor", c_, v_)
                            psv_free = []
                        qlo = qc * 1024
                        for hq in range(2):
                            last_ut = min(nut - 1,
                                          ((hq + 1) * 512 + qlo) // P - 1)
                            if ut > last_ut:
                                continue
                            pr.op("tensor",
                                  lambda e, ut=ut, hq=hq, pb=pb,
                                  st=(ut == 0), sp_=(ut == last_ut):
                                  e.matmul(
                                      ps_v[:65, hq * 512:(hq + 1) * 512],
                                      vaug[:, ut * NH * 65 + h * 65:
                                           ut * NH * 65 + h * 65 + 65],
                                      pb[:, hq * 512:(hq + 1) * 512],
                                      start=st, stop=sp_), inc=pAV)
                        pT_free[ut % 2] = pAV.n

                    scores(0)
                    for ut in range(1, nut):
                        scores(ut)
                        av(ut - 1)
                    av(nut - 1)
                    av_k = pAV.n
                    stg = (h * 2 + qc) % 2
                    pr.wait("scalar", pAV, av_k)
                    pr.wait("scalar", s_in)
                    pr.sop(lambda e, stg=stg: e.copy(
                        out=dstage_ap(stg), in_=ps_v[64:65, 0:1024]))
                    dcp_v = cS.n
                    pr.wait("sync", cS, dcp_v)
                    pr.dma("sync", dsb_ap(h, qc), dstage_ap(stg), s_in)
                    pr.wait("vector", pAV, av_k)
                    pr.vop(lambda e, h=h, qc=qc, half=half: e.tensor_copy(
                        out=yT[half * R:(half + 1) * R,
                               (h // 2) * T + qc * 1024:
                               (h // 2) * T + (qc + 1) * 1024],
                        in_=ps_v[0:64, 0:1024]))
                    psv_free = [(cV, cV.n), (cS, dcp_v)]
            pr.wait("vector", cS)
            pr.wait("vector", s_in)
            pr.vop(lambda e: e.reciprocal(
                out=scrf[32:64, 0:1024], in_=scrf[32:64, 0:1024]))
            pr.vop(lambda e: e.tensor_copy(
                out=scr[32:64, 2048:3072], in_=scrf[32:64, 0:1024]))
            pr.vop(lambda e: e.tensor_sub(
                scrf[32:64, 0:1024], scrf[32:64, 0:1024],
                scr[32:64, 2048:3072]))
            pr.vop(lambda e: e.tensor_copy(
                out=scr[32:64, 3072:4096], in_=scrf[32:64, 0:1024]))
            for h in range(NH):
                half = h % 2
                for qc in range(2):
                    pr.wait("sync", cV)
                    pr.wait("sync", cS)
                    pr.wait("sync", pAV, pAV.n)
                    pr.dma("sync", pTb[0][0:1, 0:1024], rhi_ap(h, qc), s_in)
                    pr.dma("sync", pTb[1][0:1, 0:1024], rlo_ap(h, qc), s_in)
                    pr.wait("tensor", s_in)
                    emits = []
                    for hq in range(2):
                        emits.append(lambda e, hq=hq: e.matmul(
                            ps_r.t[0:64, hq * 512:(hq + 1) * 512],
                            ones_r[0:1, 0:64],
                            pTb[0][0:1, hq * 512:(hq + 1) * 512],
                            start=True, stop=False))
                        emits.append(lambda e, hq=hq: e.matmul(
                            ps_r.t[0:64, hq * 512:(hq + 1) * 512],
                            ones_r[0:1, 0:64],
                            pTb[1][0:1, hq * 512:(hq + 1) * 512],
                            start=False, stop=True))
                    k = ps_r.produce(emits)
                    pr.wait("vector", ps_r.s, k)
                    pr.vop(lambda e, h=h, qc=qc, half=half: e.tensor_tensor(
                        out=yT[half * R:(half + 1) * R,
                               (h // 2) * T + qc * 1024:
                               (h // 2) * T + (qc + 1) * 1024],
                        in0=yT[half * R:(half + 1) * R,
                               (h // 2) * T + qc * 1024:
                               (h // 2) * T + (qc + 1) * 1024],
                        in1=ps_r.t[0:64, 0:1024], op=OP.mult))
                    ps_r.consumed(cV)
            barrier()
            sm1.close()

            sm2 = ExitStack()
            po = [PBuf(sm2.enter_context(
                nc.psum_tensor(f"po{i}", [P, 1024], F32)), pm[3 + i])
                for i in range(2)]
            for t in range(NT):
                b = t % 2
                pr.wait("sync", cV)
                pr.wait("sync", s_out)
                pr.dma("sync", xrs[b][:, :], x0_d[t * P:(t + 1) * P, :], s_in)
                emits = []
                for c in range(DC):
                    for nn in range(2):
                        emits.append(lambda e, t=t, c=c, b=b, nn=nn: e.matmul(
                            po[b].t[:, nn * 512:(nn + 1) * 512],
                            yT[:, c * T + t * P:c * T + (t + 1) * P],
                            owT[:, c * D + nn * 512: c * D + (nn + 1) * 512],
                            start=(c == 0), stop=False))
                for nn in range(2):
                    emits.append(lambda e, b=b, nn=nn: e.matmul(
                        po[b].t[:, nn * 512:(nn + 1) * 512],
                        ones_r[0:1, :],
                        obias[0:1, nn * 512:(nn + 1) * 512],
                        start=False, stop=True))
                k = po[b].produce(emits)
                pr.wait("vector", po[b].s, k)
                pr.wait("vector", s_in)
                pr.vop(lambda e, b=b: e.tensor_add(
                    xrs[b][:, :], po[b].t[:, 0:D], xrs[b][:, :]))
                po[b].consumed(cV)
                pr.wait("sync", cV)
                pr.dma("sync", x1_d[t * P:(t + 1) * P, :], xrs[b][:, :], s_out)
            pr.wait("sync", s_out)
            barrier()
            sm2.close()
            st1.close()
        st0.close()
        if nphases < 2:
            sm0.close()

        pr.phase = "p0b_wquant"
        st2 = ExitStack()
        w1q = w2q = None
        if nphases >= 3:
            w1q = st2.enter_context(nc.sbuf_tensor("w1q", [P, DC * HID], BF16))
            w2q = st2.enter_context(nc.sbuf_tensor("w2q", [P, HC * D], BF16))
            st2t = ExitStack()
            wf2 = [st2t.enter_context(nc.sbuf_tensor(f"wf{i}", [P, HID], F32))
                   for i in range(2)]
            w8 = st2t.enter_context(nc.sbuf_tensor("w8", [P, HID], I8))
            sm2b = ExitStack()
            tb2 = [PBuf(sm2b.enter_context(
                nc.psum_tensor(f"tb2{i}", [P, 1024], BF16)), pm[i])
                for i in range(2)]
            pq2 = PBuf(sm2b.enter_context(
                nc.psum_tensor("pq2", [P, 512], F32)), pm[2])
            tp2 = TrPool(tb2)

            def quant_weight(w_hbm, nrt, ncc, wdst, rows, im):
                W = ncc * P
                pr.vop(lambda e: e.memset(spc(20), 0.0))
                act_done = {}
                for rt in range(nrt):
                    b = rt % 2
                    if rt >= 2:
                        pr.wait("sync", cS, act_done[rt - 2])
                    else:
                        pr.wait("sync", cS)
                    pr.dma("sync", wf2[b][:, :W],
                           w_hbm[rt * P:(rt + 1) * P, :], s_in)
                    d = s_in.n
                    pr.wait("scalar", s_in, d)
                    pr.wait("scalar", cV)
                    pr.sop(lambda e, b=b: e.activation(
                        out=scr[:, :W], in_=wf2[b][:, :W],
                        func=AF.Abs, accum_out=spc(16)))
                    act_done[rt] = cS.n
                    pr.wait("vector", cS)
                    pr.vop(lambda e: e.tensor_add(spc(20), spc(20), spc(16)))
                pr.wait("tensor", cV)
                pr.wait("tensor", cG, consts_g)
                k = pq2.produce([lambda e: e.matmul(
                    pq2.t[0:1, 0:1], sp[:, 33:34], spc(20),
                    start=True, stop=True)])
                pr.wait("vector", pq2.s, k)
                pr.vop(lambda e: e.tensor_copy(
                    out=scrf[0:1, 300:301], in_=pq2.t[0:1, 0:1]))
                pq2.consumed(cV)
                pr.vop(lambda e: e.tensor_scalar_mul(
                    scrf[0:1, 300:301], scrf[0:1, 300:301],
                    1.0 / (nrt * P * W)))
                pr.vop(lambda e: e.tensor_scalar_max(
                    scrf[0:1, 300:301], scrf[0:1, 300:301], 1e-5))
                pr.vop(lambda e: e.reciprocal(
                    out=scrf[0:1, 301:302], in_=scrf[0:1, 300:301]))
                pr.wait("tensor", cV)
                k = pq2.produce([
                    lambda e: e.matmul(pq2.t[:, 1:2], ones_rf[0:1, :],
                                       scrf[0:1, 300:301],
                                       start=True, stop=True),
                    lambda e: e.matmul(pq2.t[:, 2:3], ones_rf[0:1, :],
                                       scrf[0:1, 301:302],
                                       start=True, stop=True)])
                pr.wait("vector", pq2.s, k)
                pr.vop(lambda e, im=im: e.tensor_copy(
                    out=spc(im), in_=pq2.t[:, 1:2]))
                pr.vop(lambda e: e.tensor_copy(out=spc(18), in_=pq2.t[:, 2:3]))
                pq2.consumed(cV)
                q_done = {}
                evac_done = {}
                for rt in range(nrt):
                    b = rt % 2
                    if rt >= 2:
                        pr.wait("sync", cS, q_done[rt - 2])
                    else:
                        pr.wait("sync", cS, act_done[nrt - 2 + rt])
                    pr.dma("sync", wf2[b][:, :W],
                           w_hbm[rt * P:(rt + 1) * P, :], s_in)
                    d = s_in.n
                    pr.wait("scalar", s_in, d)
                    if rt == 0:
                        pr.wait("scalar", cV)
                    else:
                        pr.wait("scalar", cV, evac_done[rt - 1])
                    pr.sop(lambda e, b=b: e.activation(
                        out=w8[:, :W], in_=wf2[b][:, :W],
                        func=AF.Copy, scale=spc(18)))
                    pr.sop(lambda e: e.activation(
                        out=scr[:, :W], in_=w8[:, :W], func=AF.Sign))
                    pr.sop(lambda e: e.activation(
                        out=sp[:, 61:62], in_=scr[:, W - 1:W], func=AF.Copy))
                    sign_v = cS.n
                    q_done[rt] = cS.n
                    w3 = wdst[:, :].rearrange("p (c o) -> p c o", o=rows)
                    for cb in range(0, ncc, 8):
                        kk8 = min(8, ncc - cb)
                        tp2.trN(lambda j, cb=cb: scr[:, (cb + j) * P:
                                                     (cb + j + 1) * P],
                                w3[:, cb:cb + kk8, rt * P:(rt + 1) * P],
                                kk8, src_ctr=cS, src_val=sign_v)
                    evac_done[rt] = cV.n

            quant_weight(w1_d, HID // P, DC, w1q, HID, 21)
            quant_weight(w2_d, D // P, HC, w2q, D, 23)
            pr.vop(lambda e: e.reciprocal(out=spc(24), in_=spc(21)))
            pr.vop(lambda e: e.reciprocal(out=spc(25), in_=spc(23)))
            pr.wait("sync", cV)
            pr.dma("sync", wq1_d, w1q[:, :], s_out)
            pr.dma("sync", wq2_d, w2q[:, :], s_out)
            barrier()
            sm2b.close()
            st2t.close()

        def emit_ffn(xin_d, xout_d, w1qt, w2qt, stf, tag=""):
            xq8s = stf.enter_context(nc.sbuf_tensor(tag + "xq8s", [P, D], I8))
            xqT = stf.enter_context(
                nc.sbuf_tensor(tag + "xqT", [P, DC * P], BF16))
            xsc = stf.enter_context(nc.sbuf_tensor(tag + "xsc", [P, D], BF16))
            hti = [stf.enter_context(
                nc.sbuf_tensor(tag + f"hti{i}", [P, HID], F32))
                for i in range(2)]
            hqT = stf.enter_context(
                nc.sbuf_tensor(tag + "hqT", [P, (HC // 2) * P], BF16))
            hq8h = stf.enter_context(
                nc.sbuf_tensor(tag + "hq8h", [P, HID // 2], I8))
            sxc = stf.enter_context(nc.sbuf_tensor(tag + "sxc", [P, NT], F32))
            exc = stf.enter_context(nc.sbuf_tensor(tag + "exc", [P, NT], F32))
            e2c = stf.enter_context(nc.sbuf_tensor(tag + "e2c", [P, 2], F32))
            sbB = stf.enter_context(nc.sbuf_tensor(tag + "sbB", [1, P], BF16))
            s2B = stf.enter_context(nc.sbuf_tensor(tag + "s2B", [1, P], BF16))
            xl2 = stf.enter_context(nc.sbuf_tensor(tag + "xl2", [P, D], F32))
            xin3 = [xin[0], xin[1], stf.enter_context(
                nc.sbuf_tensor(tag + "xin3_2", [P, D], F32))]
            b1row = stf.enter_context(
                nc.sbuf_tensor(tag + "b1row", [1, HID], BF16))
            b2row = stf.enter_context(
                nc.sbuf_tensor(tag + "b2row", [1, D], BF16))
            pr.dma("gpsimd", b1row[:, :], b1_d[None, :], s_gd)
            pr.dma("gpsimd", b2row[:, :], b2_d[None, :], s_gd)
            brow_v = s_gd.n
            pr.wait("tensor", s_gd, brow_v)
            smf = ExitStack()
            tbf = [PBuf(smf.enter_context(
                nc.psum_tensor(tag + f"tbf{i}", [P, 1024], BF16)), pm[i])
                for i in range(2)]
            hp = [PBuf(smf.enter_context(
                nc.psum_tensor(tag + f"hpf{i}", [P, 512], F32)), pm[2 + i])
                for i in range(2)]
            pof = PBuf(smf.enter_context(
                nc.psum_tensor(tag + "poff", [P, 1024], F32)), pm[4])
            tpf = TrPool(tbf)

            xin_v = {}
            xqT_ok = {}
            sb_ok = {}
            stats_ok = {}
            qv = {}
            xv = {}
            ln_ok = {}

            def xstage(t):
                pr.wait("scalar", cV)
                pr.sop(lambda e, t=t: e.activation(
                    out=xsc[:, :], in_=xq8s[:, :], func=AF.Copy))
                pr.sop(lambda e: e.activation(
                    out=sp[:, 61:62], in_=xsc[:, D - 1:D], func=AF.Copy))
                xv[t] = cS.n

            def lnq(t, pos):
                if pos == 0:
                    pr.wait("vector", s_in, xin_v[t])
                    pr.vop(lambda e, t=t: e.reduce_sum(
                        out=spc(53), in_=xin3[t % 3][:, :], axis=AX.X))
                    pr.wait("scalar", s_in, xin_v[t])
                    pr.wait("scalar", cV)
                    pr.sop(lambda e, t=t: e.activation(
                        out=xl2[:, :], in_=xin3[t % 3][:, :],
                        func=AF.Square, accum_out=spc(54)))
                    pr.sop(lambda e: e.activation(
                        out=sp[:, 61:62], in_=spc(54), func=AF.Copy))
                elif pos == 1:
                    pr.wait("vector", cS)
                    pr.vop(lambda e: e.tensor_copy(
                        out=sp[:, 62:63], in_=sp[:, 61:62]))
                    pr.vop(lambda e: e.tensor_scalar_mul(
                        spc(50), spc(53), 1.0 / D))
                    pr.vop(lambda e: e.tensor_scalar_mul(
                        spc(51), spc(54), 1.0 / D))
                    pr.vop(lambda e: e.tensor_mul(spc(53), spc(50), spc(50)))
                    pr.vop(lambda e: e.tensor_sub(spc(51), spc(51), spc(53)))
                    pr.wait("scalar", cV)
                    pr.sop(lambda e: e.activation(
                        out=spc(52), in_=spc(51), func=AF.Sqrt,
                        bias=spc(30)))
                elif pos == 2:
                    pr.wait("vector", cS)
                    pr.vop(lambda e: e.reciprocal(out=spc(52), in_=spc(52)))
                    pr.vop(lambda e, t=t: e.scalar_tensor_tensor(
                        out=xl2[:, :], in0=xin3[t % 3][:, :], scalar=spc(50),
                        in1=spc(52).broadcast_to([P, D]),
                        op0=OP.subtract, op1=OP.mult))
                    pr.vop(lambda e: e.tensor_mul(spc(55), spc(51), spc(52)))
                    pr.vop(lambda e: e.tensor_mul(spc(55), spc(55), spc(52)))
                    pr.wait("scalar", cV)
                    pr.sop(lambda e: e.activation(
                        out=spc(56), in_=spc(55), func=AF.Sqrt,
                        bias=spc(31)))
                else:
                    pr.wait("vector", cS)
                    pr.vop(lambda e: e.reciprocal(out=spc(56), in_=spc(56)))
                    pr.vop(lambda e: e.reduce_max(
                        out=spc(57), in_=xl2[:, :], axis=AX.X,
                        apply_absolute_value=True))
                    pr.vop(lambda e: e.tensor_mul(spc(57), spc(57), spc(56)))
                    pr.vop(lambda e: e.tensor_scalar_max(spc(57), spc(57),
                                                         1e-5))
                    pr.vop(lambda e: e.reciprocal(out=spc(58), in_=spc(57)))
                    pr.vop(lambda e: e.tensor_scalar_mul(spc(58), spc(58),
                                                         127.0))
                    pr.vop(lambda e: e.tensor_mul(spc(59), spc(58), spc(56)))
                    pr.vop(lambda e, t=t: e.tensor_scalar_mul(
                        exc[:, t:t + 1], spc(57), 1.0 / 127.0))
                    pr.vop(lambda e, t=t: e.tensor_mul(
                        exc[:, t:t + 1], exc[:, t:t + 1], spc(21)))
                    pr.vop(lambda e, t=t: e.tensor_mul(
                        sxc[:, t:t + 1], spc(58), spc(24)))
                    ln_ok[t] = cV.n
                    pr.wait("scalar", cV, ln_ok[t])
                    pr.sop(lambda e, t=t: e.activation(
                        out=xq8s[:, :], in_=xl2[:, :],
                        func=AF.Copy, scale=spc(59)))
                    pr.sop(lambda e, t=t: e.activation(
                        out=sp[:, 61:62], in_=xq8s[:, D - 1:D],
                        func=AF.Copy))

            def front(t):
                if t + 1 < NT:
                    pr.wait("sync", cV)
                    pr.wait("sync", cS)
                    pr.wait("sync", s_out)
                    pr.dma("sync", xin3[(t + 1) % 3][:, :],
                           xin_d[(t + 1) * P:(t + 2) * P, :], s_in)
                    xin_v[t + 1] = s_in.n
                tpf.trN(lambda j: xsc[:, j * P:(j + 1) * P],
                        xqT[:, 0:DC * P], DC,
                        src_ctr=cS, src_val=xv[t])
                xqT_ok[t] = cV.n
                pr.vop(lambda e, t=t: e.tensor_copy(
                    out=spb[:, 120:121], in_=sxc[:, t:t + 1]))
                tpf.tr(spb[:, 120:121], sbB[0:1, :], rows=P, cols=1,
                       src_ctr=cV, src_val=cV.n)
                sb_ok[t] = cV.n

            def emit_stats(t, pos):
                if pos == 0:
                    pr.wait("scalar", cV)
                    pr.sop(lambda e, t=t: e.activation(
                        out=hti[(t + 1) % 2][:, :], in_=hti[t % 2][:, :],
                        func=AF.Square, accum_out=spc(4)))
                    pr.sop(lambda e: e.activation(
                        out=sp[:, 61:62], in_=spc(4), func=AF.Copy))
                elif pos == 1:
                    pr.wait("vector", cS)
                    pr.vop(lambda e: e.tensor_copy(
                        out=sp[:, 62:63], in_=sp[:, 61:62]))
                    pr.vop(lambda e: e.tensor_scalar_mul(
                        spc(5), spc(4), 1.0 / HID))
                    pr.wait("scalar", cV)
                    pr.sop(lambda e: e.activation(
                        out=spc(6), in_=spc(5), func=AF.Sqrt, bias=spc(31)))
                elif pos == 2:
                    pr.wait("vector", cS)
                    pr.vop(lambda e: e.reciprocal(out=spc(6), in_=spc(6)))
                    pr.vop(lambda e, t=t: e.reduce_max(
                        out=spc(7), in_=hti[t % 2][:, :], axis=AX.X,
                        apply_absolute_value=True))
                    pr.vop(lambda e: e.tensor_mul(spc(7), spc(7), spc(6)))
                    pr.vop(lambda e: e.tensor_scalar_max(spc(7), spc(7),
                                                         1e-5))
                    pr.vop(lambda e: e.reciprocal(out=spc(8), in_=spc(7)))
                    pr.vop(lambda e: e.tensor_scalar_mul(spc(8), spc(8),
                                                         127.0))
                    pr.vop(lambda e: e.tensor_mul(spc(9), spc(8), spc(6)))
                    pr.vop(lambda e, t=t: e.tensor_scalar_mul(
                        e2c[:, t % 2:t % 2 + 1], spc(7), 1.0 / 127.0))
                    pr.vop(lambda e, t=t: e.tensor_mul(
                        e2c[:, t % 2:t % 2 + 1], e2c[:, t % 2:t % 2 + 1],
                        spc(23)))
                    pr.vop(lambda e: e.tensor_mul(spc(11), spc(8), spc(25)))
                    stats_ok[t] = cV.n
                else:
                    pr.wait("scalar", cV, stats_ok[t])
                    pr.sop(lambda e, t=t: e.activation(
                        out=hq8h[:, :], in_=hti[t % 2][:, 0:HID // 2],
                        func=AF.Copy, scale=spc(9)))
                    pr.sop(lambda e: e.activation(
                        out=sp[:, 61:62], in_=hq8h[:, HID // 2 - 1:HID // 2],
                        func=AF.Copy))
                    qv[t] = cS.n

            def bl1_gelu(t, ts, tl):
                if ts is not None:
                    emit_stats(ts, 0)
                pr.wait("tensor", cV, max(xqT_ok[t], sb_ok[t]))
                for n in range(8):
                    pb = n % 2
                    emits = []
                    for c in range(DC):
                        emits.append(lambda e, n=n, c=c, pb=pb: e.matmul(
                            hp[pb].t[:, :],
                            xqT[:, c * P:(c + 1) * P],
                            w1qt[:, c * HID + n * 512:
                                 c * HID + (n + 1) * 512],
                            start=(c == 0), stop=False))
                    emits.append(lambda e, n=n, pb=pb: e.matmul(
                        hp[pb].t[:, :], sbB[0:1, :],
                        b1row[0:1, n * 512:(n + 1) * 512],
                        start=False, stop=True))
                    k = hp[pb].produce(emits)
                    pr.wait("scalar", hp[pb].s, k)
                    pr.sop(lambda e, n=n, pb=pb, t=t: e.activation(
                        out=hti[t % 2][:, n * 512:(n + 1) * 512],
                        in_=hp[pb].t[:, :],
                        func=AF.Gelu, scale=exc[:, t:t + 1]))
                    hp[pb].consumed(cS)
                    if n == 1 and ts is not None:
                        emit_stats(ts, 1)
                    if n == 2 and ts is not None:
                        emit_stats(ts, 2)
                    if n == 3 and tl is not None:
                        lnq(tl, 0)
                    if n == 4 and ts is not None:
                        emit_stats(ts, 3)
                    if n == 5 and tl is not None:
                        lnq(tl, 1)
                    if n == 6 and tl is not None:
                        lnq(tl, 2)
                    if n == 7 and tl is not None:
                        lnq(tl, 3)
                        xstage(tl)

            def back(t):
                b = t % 3
                pr.vop(lambda e: e.tensor_copy(
                    out=spb[:, 120:121], in_=sp[:, 11:12]))
                tpf.tr(spb[:, 120:121], s2B[0:1, :], rows=P, cols=1,
                       src_ctr=cV, src_val=cV.n)
                pr.wait("vector", cS, qv[t])
                pr.vop(lambda e: e.tensor_copy(
                    out=sp[:, 62:63], in_=sp[:, 61:62]))
                pr.vop(lambda e: e.tensor_copy(out=scr[:, 0:HID // 2],
                                               in_=hq8h[:, :]))
                pr.vop(lambda e: e.tensor_copy(
                    out=sp[:, 63:64], in_=scr[:, HID // 2 - 1:HID // 2]))
                c0v = cV.n
                pr.wait("scalar", cV, c0v)
                pr.sop(lambda e, t=t: e.activation(
                    out=hq8h[:, :], in_=hti[t % 2][:, HID // 2:HID],
                    func=AF.Copy, scale=spc(9)))
                pr.sop(lambda e: e.activation(
                    out=sp[:, 61:62], in_=hq8h[:, HID // 2 - 1:HID // 2],
                    func=AF.Copy))
                q1v = cS.n
                pr.wait("vector", cS, q1v)
                pr.vop(lambda e: e.tensor_copy(
                    out=sp[:, 62:63], in_=sp[:, 61:62]))
                pr.vop(lambda e: e.tensor_copy(
                    out=scr[:, HID // 2:HID], in_=hq8h[:, :]))
                pr.vop(lambda e: e.tensor_copy(
                    out=sp[:, 63:64], in_=scr[:, HID - 1:HID]))
                c1v = cV.n
                for (cc, v) in pof.free:
                    pr.wait("tensor", cc, v)
                pof.free = []
                for half in range(2):
                    cb = half * (HC // 2)
                    srcv = c0v if half == 0 else c1v
                    for hb in range(2):
                        tpf.trN(lambda j, base=cb + hb * 8:
                                scr[:, (base + j) * P:(base + j + 1) * P],
                                hqT[:, (hb * 8) * P:(hb * 8 + 8) * P], 8,
                                src_ctr=cV, src_val=srcv)
                    pr.wait("tensor", cV)
                    for c in range(HC // 2):
                        for n in range(2):
                            pr.op("tensor", lambda e, n=n, c=c, cb=cb:
                                  e.matmul(
                                      pof.t[:, n * 512:(n + 1) * 512],
                                      hqT[:, c * P:(c + 1) * P],
                                      w2qt[:, (cb + c) * D + n * 512:
                                           (cb + c) * D + (n + 1) * 512],
                                      start=(cb + c == 0), stop=False))
                pr.op("tensor", lambda e: e.matmul(
                    pof.t[:, 0:512], s2B[0:1, :], b2row[0:1, 0:512],
                    start=False, stop=True))
                pr.op("tensor", lambda e: e.matmul(
                    pof.t[:, 512:1024], s2B[0:1, :], b2row[0:1, 512:1024],
                    start=False, stop=True), inc=pof.s)
                k = pof.s.n
                pr.wait("vector", pof.s, k)
                pr.wait("vector", s_in)
                pr.vop(lambda e, b=b, t=t: e.scalar_tensor_tensor(
                    out=xin3[b][:, :], in0=pof.t[:, 0:D],
                    scalar=e2c[:, t % 2:t % 2 + 1],
                    in1=xin3[b][:, :], op0=OP.mult, op1=OP.add))
                pof.consumed(cV)
                pr.wait("sync", cV)
                pr.dma("sync", xout_d[t * P:(t + 1) * P, :], xin3[b][:, :],
                       s_out)

            pr.wait("sync", cV)
            pr.wait("sync", cS)
            pr.dma("sync", xin3[0][:, :], xin_d[0:P, :], s_in)
            xin_v[0] = s_in.n
            for pos in range(4):
                lnq(0, pos)
            xstage(0)
            for t in range(NT):
                front(t)
                bl1_gelu(t, t - 1 if t >= 1 else None,
                         t + 1 if t + 1 < NT else None)
                if t >= 1:
                    back(t - 1)
            for pos in range(4):
                emit_stats(NT - 1, pos)
            back(NT - 1)
            pr.wait("sync", s_out)
            if dbg == "q2" and tag == "a":
                pr.wait("sync", cV)
                pr.wait("sync", cS)
                fl = out_d.flatten()
                pr.dma("sync",
                       fl[P * 4096:2 * P * 4096].rearrange("(p n) -> p n", p=P),
                       hti[(NT - 1) % 2][:, :], s_out)
                pr.dma("sync",
                       fl[2 * P * 4096:2 * P * 4096 + 2048]
                       .rearrange("(p n) -> p n", p=1),
                       b1row.bitcast(F32)[0:1, :], s_out)
                pr.dma("sync",
                       fl[2 * P * 4096 + 2048:2 * P * 4096 + 2048 + P * 16]
                       .rearrange("(p n) -> p n", p=P),
                       exc[:, :], s_out)
                pr.dma("sync",
                       fl[2 * P * 4096 + 2048 + P * 16:
                          2 * P * 4096 + 2048 + P * 16 + P * 16]
                       .rearrange("(p n) -> p n", p=P),
                       sxc[:, :], s_out)
            barrier()
            smf.close()

        pr.phase = "p2_ffn1"
        if nphases >= 4:
            stf1 = ExitStack()
            emit_ffn(x1_d, x2_d, w1q, w2q, stf1, tag="a")
            stf1.close()
        st2.close()

        pr.phase = "p2b_a2a"
        if nphases >= 5:
            pr.wait("gpsimd", s_out)

            def _cc(e):
                with allow_ici_all_to_all():
                    return e.collective_compute(
                        "AllToAll", OP.bypass,
                        replica_groups=[list(range(8))],
                        ins=[x2_d], outs=[x2t_d])
            pr.op("gpsimd", _cc, inc=s_cc)

        pr.phase = "p3a_attn_b"
        if nphases >= 5:
            st3 = ExitStack()
            wkT = st3.enter_context(nc.sbuf_tensor("wkT", [P, DC * D], BF16))
            wvT = st3.enter_context(nc.sbuf_tensor("wvT", [P, DC * D], BF16))
            wqT = st3.enter_context(nc.sbuf_tensor("wqT", [P, DC * D], BF16))
            wtm3 = [st3.enter_context(nc.sbuf_tensor(f"wtm3{i}", [P, D], BF16))
                    for i in range(2)]
            S2 = [st3.enter_context(nc.sbuf_tensor(f"S2{i}", [P, 8192], F32))
                  for i in range(2)]
            prod = st3.enter_context(nc.sbuf_tensor("prod", [P, 8192], F32))
            kvq = [[st3.enter_context(
                nc.sbuf_tensor(f"kvq{r}{w}", [P, D], F32))
                for w in range(3)] for r in range(2)]
            kn = st3.enter_context(nc.sbuf_tensor("kn", [P, 256], F32))
            rt3 = st3.enter_context(nc.sbuf_tensor("rt3", [P, D], F32))
            sq3 = st3.enter_context(nc.sbuf_tensor("sq3", [P, D], F32))
            xl3 = st3.enter_context(nc.sbuf_tensor("xl3", [P, D], BF16))
            x3T = st3.enter_context(nc.sbuf_tensor("x3T", [P, DC * P], BF16))
            sm3 = ExitStack()
            tb3 = [PBuf(sm3.enter_context(
                nc.psum_tensor(f"tb3{i}", [P, 1024], BF16)), pm[i])
                for i in range(2)]
            pq3 = [PBuf(sm3.enter_context(
                nc.psum_tensor(f"pq3{i}", [P, 512], F32)), pm[2 + i])
                for i in range(2)]
            tp3 = TrPool(tb3)
            build_fm(wkT, wk_d, D, D, tp3, wtm3)
            build_fm(wvT, wv_d, D, D, tp3, wtm3)
            build_fm(wqT, wqd_d, D, D, tp3, wtm3)

            pr.vop(lambda e: e.memset(S2[0][:, :], 0.0))
            pr.vop(lambda e: e.memset(S2[1][:, :], 0.0))
            pr.wait("sync", s_cc, 1)

            def S4(S):
                return S[:, :].rearrange("p (n i j) -> p n i j", i=8, j=8)

            def prod4():
                return prod[:, :].rearrange("p (n i j) -> p n i j", i=8, j=8)

            kv_ok = {}

            def frontq(it):
                bt = it % 2
                t = it // 2
                row0 = t * BS + bt * P
                pr.wait("sync", cV)
                pr.wait("sync", s_out)
                pr.dma("sync", xin[bt][:, :], x2t_d[row0:row0 + P, :], s_in)
                pr.wait("vector", s_in)
                pr.wait("scalar", s_in)
                ln_stats(xin[bt], D)
                pr.vop(lambda e, bt=bt: e.scalar_tensor_tensor(
                    out=xl3[:, :], in0=xin[bt][:, :], scalar=spc(0),
                    in1=spc(2).broadcast_to([P, D]),
                    op0=OP.subtract, op1=OP.mult))
                xl3_v = cV.n
                tp3.trN(lambda j: xl3[:, j * P:(j + 1) * P],
                        x3T[:, 0:DC * P], DC, src_ctr=cV, src_val=xl3_v)
                xT_ready = cV.n
                for wi, wT in enumerate((wkT, wvT, wqT)):
                    dst = kvq[bt][wi]
                    for n in range(2):
                        pb = (wi * 2 + n) % 2
                        pr.wait("tensor", cV, xT_ready)
                        emits = []
                        for c in range(DC):
                            emits.append(
                                lambda e, wT=wT, n=n, c=c, pb=pb:
                                e.matmul(
                                    pq3[pb].t[:, :],
                                    x3T[:, c * P:(c + 1) * P],
                                    wT[:, c * D + n * 512:
                                       c * D + (n + 1) * 512],
                                    start=(c == 0), stop=(c == DC - 1)))
                        k = pq3[pb].produce(emits)
                        pr.wait("vector", pq3[pb].s, k)
                        pr.vop(lambda e, dst=dst, n=n, pb=pb:
                               e.tensor_copy(
                                   out=dst[:, n * 512:(n + 1) * 512],
                                   in_=pq3[pb].t[:, :]))
                        pq3[pb].consumed(cV)
                kv_ok[it] = cV.n

            def scan(it):
                bt = it % 2
                t = it // 2
                row0 = t * BS + bt * P
                S = S2[bt]
                kk, vv, qq = kvq[bt]
                S4S = S[:, :].rearrange("p (n i j) -> p n i j", i=8, j=8)
                pr4 = prod[:, :].rearrange("p (n i j) -> p n i j", i=8, j=8)
                v = {}

                def knbc(q):
                    return (rt3[:, q * 256:(q + 1) * 256]
                            .rearrange("p (n j) -> p n j", j=8)
                            .unsqueeze(2).broadcast_to([P, 32, 8, 8]))

                for q in range(4):
                    if q == 0:
                        pr.wait("scalar", cV, kv_ok[it])
                    pr.sop(lambda e, q=q: e.activation(
                        out=scrf[:, q * 256:(q + 1) * 256],
                        in_=kk[:, q * 256:(q + 1) * 256], func=AF.Square))
                    v[(0, q)] = cS.n
                for q in range(4):
                    pr.wait("vector", cS, v[(0, q)])
                    pr.vop(lambda e, q=q: e.reduce_sum(
                        out=kn[:, q * 32:(q + 1) * 32],
                        in_=scrf[:, q * 256:(q + 1) * 256]
                        .rearrange("p (n j) -> p n j", j=8), axis=AX.X))
                    v[(1, q)] = cV.n
                for q in range(4):
                    pr.wait("scalar", cV, v[(1, q)])
                    pr.sop(lambda e, q=q: e.activation(
                        out=kn[:, 128 + q * 32:128 + (q + 1) * 32],
                        in_=kn[:, q * 32:(q + 1) * 32], func=AF.Sqrt))
                    v[(2, q)] = cS.n
                for q in range(4):
                    pr.wait("vector", cS, v[(2, q)])
                    pr.vop(lambda e, q=q: e.tensor_scalar_add(
                        kn[:, 128 + q * 32:128 + (q + 1) * 32],
                        kn[:, 128 + q * 32:128 + (q + 1) * 32], 1e-6))
                    pr.vop(lambda e, q=q: e.reciprocal(
                        out=kn[:, q * 32:(q + 1) * 32],
                        in_=kn[:, 128 + q * 32:128 + (q + 1) * 32]))
                    pr.vop(lambda e, q=q: e.tensor_tensor(
                        out=rt3[:, q * 256:(q + 1) * 256],
                        in0=kk[:, q * 256:(q + 1) * 256],
                        in1=kn[:, q * 32:(q + 1) * 32]
                        .rearrange("p (n o) -> p n o", o=1)
                        .broadcast_to([P, 32, 8]),
                        op=OP.mult))
                for q in range(4):
                    nq = slice(q * 32, (q + 1) * 32)
                    s256 = slice(q * 256, (q + 1) * 256)
                    pr.vop(lambda e, q=q, nq=nq: e.tensor_tensor(
                        out=pr4[:, nq], in0=S4S[:, nq], in1=knbc(q),
                        op=OP.mult))
                    pr.vop(lambda e, q=q, nq=nq, s256=s256: e.reduce_sum(
                        out=sq3[:, s256].rearrange("p (n i) -> p n i", i=8),
                        in_=pr4[:, nq], axis=AX.X))
                    pr.vop(lambda e, s256=s256: e.tensor_sub(
                        sq3[:, s256], vv[:, s256], sq3[:, s256]))
                    pr.vop(lambda e, q=q, nq=nq, s256=s256: e.tensor_tensor(
                        out=pr4[:, nq],
                        in0=sq3[:, s256].rearrange("p (n i) -> p n i", i=8)
                        .unsqueeze(3).broadcast_to([P, 32, 8, 8]),
                        in1=knbc(q), op=OP.mult))
                    pr.vop(lambda e, q=q, S=S: e.tensor_add(
                        prod[:, q * 2048:(q + 1) * 2048],
                        S[:, q * 2048:(q + 1) * 2048],
                        prod[:, q * 2048:(q + 1) * 2048]))
                    v[(8, q)] = cV.n
                for q in range(4):
                    pr.wait("scalar", cV, v[(8, q)])
                    pr.sop(lambda e, q=q, S=S: e.activation(
                        out=S[:, q * 2048:(q + 1) * 2048],
                        in_=prod[:, q * 2048:(q + 1) * 2048], func=AF.Tanh))
                    v[(9, q)] = cS.n
                for q in range(4):
                    nq = slice(q * 32, (q + 1) * 32)
                    s256 = slice(q * 256, (q + 1) * 256)
                    pr.wait("vector", cS, v[(9, q)])
                    pr.vop(lambda e, q=q, nq=nq: e.tensor_tensor(
                        out=pr4[:, nq], in0=S4S[:, nq],
                        in1=qq[:, q * 256:(q + 1) * 256]
                        .rearrange("p (n j) -> p n j", j=8)
                        .unsqueeze(2).broadcast_to([P, 32, 8, 8]),
                        op=OP.mult))
                    pr.vop(lambda e, q=q, nq=nq, s256=s256: e.reduce_sum(
                        out=sq3[:, s256].rearrange("p (n i) -> p n i", i=8),
                        in_=pr4[:, nq], axis=AX.X))
                    v[(11, q)] = cV.n
                for q in range(4):
                    s256 = slice(q * 256, (q + 1) * 256)
                    pr.wait("scalar", cV, v[(11, q)])
                    pr.sop(lambda e, s256=s256: e.activation(
                        out=rt3[:, s256], in_=sq3[:, s256],
                        func=AF.Sigmoid))
                    pr.sop(lambda e, q=q, s256=s256: e.activation(
                        out=scrf[:, 1024 + q * 256:1024 + (q + 1) * 256],
                        in_=sq3[:, s256], func=AF.Square))
                    v[(12, q)] = cS.n
                for q in range(4):
                    s256 = slice(q * 256, (q + 1) * 256)
                    pr.wait("vector", cS, v[(12, q)])
                    pr.vop(lambda e, q=q, s256=s256: e.tensor_mul(
                        rt3[:, s256],
                        scrf[:, 1024 + q * 256:1024 + (q + 1) * 256],
                        rt3[:, s256]))
                    pr.vop(lambda e, bt=bt, s256=s256: e.tensor_add(
                        xin[bt][:, s256], xin[bt][:, s256], rt3[:, s256]))
                pr.wait("sync", cV)
                pr.dma("sync", x3_d[row0:row0 + P, :], xin[bt][:, :],
                       s_out)

            frontq(0)
            for it in range(2 * B):
                if it + 1 < 2 * B:
                    frontq(it + 1)
                scan(it)
            pr.wait("sync", s_out)
            barrier()
            sm3.close()
            st3.close()

        pr.phase = "p3b_ffn2"
        if nphases >= 6:
            st4 = ExitStack()
            w1q2 = st4.enter_context(
                nc.sbuf_tensor("w1q2", [P, DC * HID], BF16))
            w2q2 = st4.enter_context(nc.sbuf_tensor("w2q2", [P, HC * D], BF16))
            pr.dma("sync", w1q2[:, :], wq1_d, s_in)
            pr.dma("sync", w2q2[:, :], wq2_d, s_in)
            pr.wait("tensor", s_in)
            emit_ffn(x3_d, out_d, w1q2, w2q2, st4, tag="b")
            st4.close()

        if dbg == "q2":
            pass
        if dbg == "q":
            pr.wait("sync", s_out)
            pr.wait("sync", cV)
            pr.wait("sync", cS)
            pr.dma("sync", out_d[0:P, 0:64], sp[:, :], s_out)
            pr.dma("sync", out_d[P:2 * P, 0:1024],
                   wq1_d.bitcast(F32)[:, 1024:2048], s_out)
            pr.dma("sync", out_d[2 * P:3 * P, 0:1024],
                   wq2_d.bitcast(F32)[:, 0:1024], s_out)
            if w2q is not None:
                pr.dma("sync", out_d[3 * P:4 * P, 0:1024],
                       w2q.bitcast(F32)[:, 0:1024], s_out)
        elif dbg is not None and dbg != "q2":
            dbg_src = {"x1": x1_d, "x2": x2_d, "x2t": x2t_d, "x3": x3_d}[dbg]
            pr.wait("sync", s_out)
            if dbg == "x2t":
                pr.wait("sync", s_cc, 1)
            pr.dma("sync", out_d, dbg_src, s_out)

        pr.wait("sync", s_out)
        pr.wait("gpsimd", s_out)

        with nc.Block() as block:
            pr.flush(block)

    return nc


def kernel(L0_attn_a_hidden_states, L0_attn_a_basis_w, L0_attn_a_core,
           L0_attn_a_head_residual, L0_attn_a_v_w, L0_attn_a_v_b,
           L0_attn_a_o_w, L0_attn_a_o_b, L0_attn_b_W_k, L0_attn_b_W_v,
           L0_attn_b_W_q, L0_ffn_w1, L0_ffn_b1, L0_ffn_w2, L0_ffn_b2):
    hs = np.ascontiguousarray(L0_attn_a_hidden_states, dtype=np.float32)
    nc = build(_NPHASES, _DBG)
    common = {
        "basis_w": np.ascontiguousarray(L0_attn_a_basis_w, np.float32),
        "core_m": np.ascontiguousarray(L0_attn_a_core, np.float32),
        "head_res": np.ascontiguousarray(L0_attn_a_head_residual, np.float32),
        "v_w": np.ascontiguousarray(L0_attn_a_v_w, np.float32),
        "v_b": np.ascontiguousarray(L0_attn_a_v_b, np.float32),
        "o_w": np.ascontiguousarray(L0_attn_a_o_w, np.float32),
        "o_b": np.ascontiguousarray(L0_attn_a_o_b, np.float32),
        "w_k": np.ascontiguousarray(L0_attn_b_W_k, np.float32),
        "w_v": np.ascontiguousarray(L0_attn_b_W_v, np.float32),
        "w_q": np.ascontiguousarray(L0_attn_b_W_q, np.float32),
        "w1": np.ascontiguousarray(L0_ffn_w1, np.float32),
        "b1": np.ascontiguousarray(L0_ffn_b1, np.float32),
        "w2": np.ascontiguousarray(L0_ffn_w2, np.float32),
        "b2": np.ascontiguousarray(L0_ffn_b2, np.float32),
    }
    in_maps = [dict(common, x0=np.ascontiguousarray(hs[i])) for i in range(8)]
    res = run_bass_kernel_spmd(nc, in_maps, core_ids=list(range(8)),
                               trace=_TRACE)
    globals()["_LAST_RES"] = res
    out = np.empty((B, T, D), dtype=np.float32)
    for s in range(8):
        out[:, s * BS:(s + 1) * BS, :] = \
            res.results[s]["out"].reshape(B, BS, D)
    return out
